# version 14
# baseline (speedup 1.0000x reference)
import math

import numpy as np
import ml_dtypes

import concourse.bass as bass
import concourse.mybir as mybir
from concourse.bass_utils import run_bass_kernel_spmd

F32 = mybir.dt.float32
BF16 = mybir.dt.bfloat16
I32 = mybir.dt.int32
AF = mybir.ActivationFunctionType
ALU = mybir.AluOpType
AX = mybir.AxisListType
bfdt = ml_dtypes.bfloat16

B, S, D = 2, 4096, 2048
NH, DN, DR, DV, RANK = 16, 128, 64, 128, 512
DQK = DN + DR
SCALE = DQK ** -0.5
EPS = 1e-5
IF = 8192
NC_ = 8
TOWN = 1024
PI = math.pi

ENGINES = ("sync", "scalar", "vector", "tensor", "gpsimd")


class K:

    def __init__(self, nc):
        self.nc = nc
        self.q = {e: [] for e in ENGINES}
        self.sems = {}
        self.cnt = {}
        self.waited = {e: {} for e in ENGINES}

    def sem(self, name):
        self.sems[name] = self.nc.alloc_semaphore(name)
        self.cnt[name] = 0

    def wait(self, e, sem, val):
        if val is None or val <= 0:
            return
        if self.waited[e].get(sem, 0) >= val:
            return
        self.waited[e][sem] = val
        h = self.sems[sem]
        self.q[e].append(lambda eng, h=h, v=val: eng.wait_ge(h, v))

    SELF_SEM = {"vector": "ve", "scalar": "ac", "gpsimd": "po"}

    def do(self, e, fn, inc=None):
        ssem = self.SELF_SEM.get(e)
        if ssem is not None:
            self.wait(e, ssem, self.cnt[ssem])
            self.cnt[ssem] += 1
            h = self.sems[ssem]
            if inc is None:
                self.q[e].append(
                    lambda eng, fn=fn, h=h: fn(eng).then_inc(h, 1))
                return self.cnt[ssem]
            sem2, amt2 = inc
            assert sem2 == ssem and amt2 == 1, (e, inc)
            self.q[e].append(lambda eng, fn=fn, h=h: fn(eng).then_inc(h, 1))
            return self.cnt[ssem]
        if inc is None:
            self.q[e].append(fn)
            return None
        sem, amt = inc
        self.cnt[sem] += amt
        h = self.sems[sem]
        self.q[e].append(lambda eng, fn=fn, h=h, a=amt: fn(eng).then_inc(h, a))
        return self.cnt[sem]

    def dma(self, e, fn, stream):
        if stream not in self.sems:
            self.sem(stream)
        self.cnt[stream] += 16
        h = self.sems[stream]
        self.q[e].append(lambda eng, fn=fn, h=h: fn(eng).then_inc(h, 16))
        return self.cnt[stream]

    def val(self, sem):
        return self.cnt.get(sem, 0)

    def barrier(self, engines=ENGINES, exclude=()):
        for e in engines:
            for s, v in self.cnt.items():
                if s in exclude:
                    continue
                self.wait(e, s, v)

    def finish(self):
        with self.nc.Block() as block:
            block.sync(lambda eng: [f(eng) for f in self.q["sync"]])
            block.scalar(lambda eng: [f(eng) for f in self.q["scalar"]])
            block.vector(lambda eng: [f(eng) for f in self.q["vector"]])
            block.tensor(lambda eng: [f(eng) for f in self.q["tensor"]])
            block.gpsimd(lambda eng: [f(eng) for f in self.q["gpsimd"]])


class Arena:

    def __init__(self, nc):
        self.nc = nc
        self.stack = []
        self.uid = 0

    def _nm(self, name):
        self.uid += 1
        return f"{name}_{self.uid}"

    def sb(self, name, shape, dt):
        g = self.nc.sbuf_tensor(self._nm(name), shape, dt)
        h = g.__enter__()
        self.stack.append(g)
        return h, len(self.stack) - 1

    def ps(self, name, shape, dt=F32):
        g = self.nc.psum_tensor(self._nm(name), shape, dt)
        h = g.__enter__()
        self.stack.append(g)
        return h, len(self.stack) - 1

    def free_to(self, mark):
        while len(self.stack) > mark:
            g = self.stack.pop()
            g.__exit__(None, None, None)

    def mark(self):
        return len(self.stack)


def np_consts():
    eye_bf = np.eye(128, dtype=bfdt)
    eye_f32 = np.eye(128, dtype=np.float32)
    ones_col_bf = np.ones((128, 1), dtype=bfdt)
    ones_row_f32 = np.ones((1, 128), dtype=np.float32)
    p64 = np.zeros((64, 64), dtype=bfdt)
    for j in range(32):
        p64[2 * j, 2 * j + 1] = 1.0
        p64[2 * j + 1, 2 * j] = 1.0
    dup = np.zeros((32, 64), dtype=np.float32)
    dups = np.zeros((32, 64), dtype=np.float32)
    for j in range(32):
        dup[j, 2 * j] = 1.0
        dup[j, 2 * j + 1] = 1.0
        dups[j, 2 * j] = -1.0
        dups[j, 2 * j + 1] = 1.0
    p128 = np.zeros((128, 128), dtype=bfdt)
    for j in range(64):
        p128[2 * j, 2 * j + 1] = 1.0
        p128[2 * j + 1, 2 * j] = 1.0
    dup128 = np.zeros((32, 128), dtype=np.float32)
    dups128 = np.zeros((32, 128), dtype=np.float32)
    for j in range(32):
        for base in (0, 64):
            dup128[j, base + 2 * j] = 1.0
            dup128[j, base + 2 * j + 1] = 1.0
            dups128[j, base + 2 * j] = -1.0
            dups128[j, base + 2 * j + 1] = 1.0
    return dict(eye_bf=eye_bf, eye_f32=eye_f32, ones_col_bf=ones_col_bf,
                ones_row_f32=ones_row_f32, p64=p64, dup=dup, dups=dups,
                p128=p128, dup128=dup128, dups128=dups128)


class Common:

    def __init__(self, nc):
        self.nc = nc
        self.k = K(nc)
        self.ar = Arena(nc)
        for s in ("pe", "ve", "ac", "po"):
            self.k.sem(s)

    def load_consts(self, with_ffn_scales=True, with_attn=False):
        nc, k, ar = self.nc, self.k, self.ar
        cn = np_consts()
        self.c_eye_bf, _ = ar.sb("c_eye_bf", [128, 128], BF16)
        self.c_eye_f32, _ = ar.sb("c_eye_f32", [128, 128], F32)
        self.c_ones_col, _ = ar.sb("c_ones_col", [128, 1], BF16)
        self.c_ones_row, _ = ar.sb("c_ones_row", [1, 128], F32)
        self.c_p128, _ = ar.sb("c_p128", [128, 128], BF16)
        self.c_dup128, _ = ar.sb("c_dup128", [32, 128], F32)
        self.c_dups128, _ = ar.sb("c_dups128", [32, 128], F32)
        self.c_negC, _ = ar.sb("c_negC", [128, 1], F32)
        self.negc_ready = self.k.do(
            "vector", lambda eng: eng.memset(self.c_negC[:, :], -30.0))
        pairs = [
            (self.c_eye_bf, cn["eye_bf"]), (self.c_eye_f32, cn["eye_f32"]),
            (self.c_ones_col, cn["ones_col_bf"]),
            (self.c_ones_row, cn["ones_row_f32"]),
            (self.c_p128, cn["p128"]), (self.c_dup128, cn["dup128"]),
            (self.c_dups128, cn["dups128"]),
        ]
        for sb, arr in pairs:
            dh = nc.inline_tensor(np.ascontiguousarray(arr))
            self.k.dma("sync", lambda eng, sb=sb, dh=dh: eng.dma_start(
                out=sb[:, :], in_=dh.ap()), "c0")
        self.consts_ready = self.k.val("c0")
        if with_ffn_scales:
            self.gsc, _ = ar.sb("gsc", [128, 64], F32)
            self.usc, _ = ar.sb("usc", [128, 64], F32)
            self.dsc_half, _ = ar.sb("dsc_half", [128, 16], F32)
            m0 = ar.mark()
            tmp, _ = ar.sb("sc_tmp", [64, 128], F32)
            pscale, _ = ar.ps("sc_ps", [128, 64], F32)
            jobs = [("gate_scale", 64, self.gsc), ("up_scale", 64, self.usc),
                    ("down_scale", 16, self.dsc_half)]
            for name, n, dst in jobs:
                src = self.in_ap(name)
                k.wait("sync", "pe", self.k.val("pe"))
                ld = k.dma("sync", lambda eng, src=src, n=n: eng.dma_start(
                    out=tmp[0:n, :], in_=src.rearrange("(a p) -> a p", p=128)),
                    "c0")
                k.wait("tensor", "c0", ld)
                k.wait("tensor", "ve", self.k.val("ve"))
                mm = k.do("tensor", lambda eng, n=n: eng.matmul(
                    pscale[:, 0:n], tmp[0:n, :], self.c_eye_f32[0:n, 0:n],
                    start=True, stop=True), inc=("pe", 1))
                k.wait("vector", "pe", mm)
                k.do("vector", lambda eng, n=n, dst=dst: eng.tensor_copy(
                    dst[:, 0:n], pscale[:, 0:n]), inc=("ve", 1))
            k.do("vector", lambda eng: eng.tensor_scalar_mul(
                self.dsc_half[:, :], self.dsc_half[:, :], 0.5),
                inc=("ve", 1))
            self.dsc_ready = self.k.val("ve")
            ar.free_to(m0)
            self.k.barrier()
        else:
            self.dsc_ready = 0

    def in_ap(self, name):
        return self.inputs[name]



def emit_prep_cast(cm):
    nc, k = cm.nc, cm.k
    cm.wg_bf = nc.dram_tensor("wg_bf", [IF, D], BF16, kind="Internal").ap()
    cm.wu_bf = nc.dram_tensor("wu_bf", [IF, D], BF16, kind="Internal").ap()
    cm.wd_bf = nc.dram_tensor("wd_bf", [D, IF], BF16, kind="Internal").ap()
    cm.pcast_done = {}
    for (name, bf_out, rows) in (("gate_w", cm.wg_bf, IF),
                                 ("up_w", cm.wu_bf, IF),
                                 ("down_w", cm.wd_bf, D)):
        srcw = cm.in_ap(name)
        vals = []
        for hh in range(2):
            r0, r1 = hh * rows // 2, (hh + 1) * rows // 2
            k.dma("gpsimd", lambda eng, srcw=srcw, bf=bf_out, r0=r0, r1=r1:
                  eng.dma_start(out=bf[r0:r1, :], in_=srcw[r0:r1, :]),
                  f"pcast{hh}")
            vals.append(k.val(f"pcast{hh}"))
        cm.pcast_done[name] = tuple(vals)


def emit_prep_tile(cm, phase="all"):
    nc, k, ar = cm.nc, cm.k, cm.ar
    if phase in ("all", "gu"):
        cm.wgT = nc.dram_tensor("wgT", [64, 16, 128, 128], BF16,
                                kind="Internal").ap()
        cm.wuT = nc.dram_tensor("wuT", [64, 16, 128, 128], BF16,
                                kind="Internal").ap()
        cm.wdT = nc.dram_tensor("wdT", [16, 64, 128, 128], BF16,
                                kind="Internal").ap()
        cm._pt_slab = [ar.sb(f"wprep_s{jj}", [128, IF], BF16)[0]
                       for jj in range(2)]
        cm._pt_slab2 = [ar.sb(f"wprep_t{jj}", [128, IF], BF16)[0]
                        for jj in range(2)]
        cm._pt_state = ({}, {})
    slab, slab2 = cm._pt_slab, cm._pt_slab2
    sh_done, wr_done = cm._pt_state

    alljobs = [("g", kc) for kc in range(16)] + \
              [("u", kc) for kc in range(16)] + \
              [("d", ic) for ic in range(64)]
    if phase == "gu":
        tjobs = list(enumerate(alljobs))[:32]
    elif phase == "d":
        tjobs = list(enumerate(alljobs))[32:]
    else:
        tjobs = list(enumerate(alljobs))
    gate = {"g": "gate_w", "u": "up_w", "d": "down_w"}
    for j2, (which, idx) in tjobs:
        par = j2 % 2
        cols = IF if which in ("g", "u") else D
        for hh in range(2):
            k.wait("sync", f"pcast{hh}", cm.pcast_done[gate[which]][hh])
        if j2 >= 2:
            k.wait("sync", "ve", sh_done[j2 - 2])
        bf = {"g": cm.wg_bf, "u": cm.wu_bf, "d": cm.wd_bf}[which]
        t = k.dma("sync", lambda eng, bf=bf, idx=idx, par=par, cols=cols:
                  eng.dma_start_transpose(
                      slab[par][:, 0:cols],
                      bf[:, idx * 128:(idx + 1) * 128]),
                  f"ptr{par}")
        k.wait("vector", f"ptr{par}", t)
        if j2 >= 2:
            k.wait("vector", f"ptw{par}", wr_done[j2 - 2])
        sh = k.do("vector", lambda eng, par=par, cols=cols: eng.tensor_scalar(
            slab2[par][:, 0:cols], slab[par][:, 0:cols], -128.0, None,
            ALU.add))
        sh_done[j2] = sh
        k.wait("scalar", "ve", sh)
        if which in ("g", "u"):
            dst = cm.wgT if which == "g" else cm.wuT
            w = k.dma("scalar", lambda eng, dst=dst, idx=idx, par=par:
                      eng.dma_start(
                          out=dst[:, idx, :, :].rearrange("m h i -> h m i"),
                          in_=slab2[par][:, 0:IF].rearrange(
                              "h (m i) -> h m i", m=64)),
                      f"ptw{par}")
        else:
            w = k.dma("scalar", lambda eng, idx=idx, par=par:
                      eng.dma_start(
                          out=cm.wdT[:, idx, :, :].rearrange("d i c -> i d c"),
                          in_=slab2[par][:, 0:D].rearrange(
                              "i (d c) -> i d c", d=16)),
                      f"ptw{par}")
        wr_done[j2] = w
        if j2 == 31:
            cm.wprep_gu = (k.val("ptw0"), k.val("ptw1"))
    if phase in ("all", "d"):
        cm.wprep_ptw = (k.val("ptw0"), k.val("ptw1"))


def emit_ffn_weight_prep(cm):
    m0 = cm.ar.mark()
    emit_prep_cast(cm)
    emit_prep_tile(cm)
    cm.k.barrier()
    cm.ar.free_to(m0)



def emit_tm_to_fm(cm, src_tm, dst_fm, T, t0=0):
    nc, k, ar = cm.nc, cm.k, cm.ar
    assert T % 128 == 0
    nt = T // 128
    m0 = ar.mark()
    xt = [ar.sb(f"t2f_x{t}", [128, D], F32)[0] for t in range(nt)]
    xo = [ar.sb(f"t2f_o{d}", [128, T], F32)[0] for d in range(16)]
    pt, _ = ar.ps("t2f_ps", [128, 512], F32)
    pt2, _ = ar.ps("t2f_ps2", [128, 512], F32)

    for t in range(nt):
        k.dma("sync", lambda eng, t=t: eng.dma_start(
            out=xt[t][:, :],
            in_=src_tm[t0 + t * 128:t0 + (t + 1) * 128, :]), "t2f")
    ld_all = k.val("t2f")
    evac_cnt = {}
    for d in range(16):
        for t in range(nt):
            par = (d * nt + t) // 4 % 2
            ps = pt if par == 0 else pt2
            k.wait("tensor", "t2f", ld_all)
            grp = (d * nt + t) // 4
            if grp >= 2:
                k.wait("tensor", "ve", evac_cnt.get(grp - 2, 0))
            k.do("tensor", lambda eng, d=d, t=t, ps=ps: eng.matmul(
                ps[:, (t % 4) * 128:(t % 4 + 1) * 128],
                xt[t][:, d * 128:(d + 1) * 128],
                cm.c_eye_f32[:, :], start=True, stop=True),
                inc=("pe", 1) if (d * nt + t) % 4 == 3 or t == nt - 1 else None)
            if (d * nt + t) % 4 == 3 or t == nt - 1:
                pedone = k.val("pe")
                k.wait("vector", "pe", pedone)
                lo = (t // 4) * 4
                nn = (t - lo + 1) * 128
                ev = k.do("vector", lambda eng, d=d, lo=lo, nn=nn, ps=ps:
                          eng.tensor_copy(
                              xo[d][:, lo * 128:lo * 128 + nn],
                              ps[:, 0:nn]),
                          inc=("ve", 1))
                evac_cnt[grp] = ev
    for d in range(16):
        k.wait("scalar", "ve", k.val("ve"))
        k.dma("scalar", lambda eng, d=d: eng.dma_start(
            out=dst_fm[d * 128:(d + 1) * 128, t0:t0 + T],
            in_=xo[d][:, :]), "t2fo")
    out_dA = k.val("t2fo")
    ar.free_to(m0)
    return ("t2fo", out_dA)



def emit_ln_pipeline(cm, x_fm, t0s, also_dram, last_sbuf=None,
                     keep_mark=False):
    nc, k, ar = cm.nc, cm.k, cm.ar
    T = 1024
    m0 = ar.mark()
    xbb = [ar.sb(f"lp_x{j}", [128, T], BF16)[0] for j in range(2)]
    xq = [ar.sb(f"lp_q{j}", [128, T], BF16)[0] for j in range(2)]
    xab = [ar.sb(f"lp_a{j}", [128, T], BF16)[0] for j in range(2)]
    stage = [ar.sb(f"lp_st{j}", [128, T], BF16)[0] for j in range(2)]
    rowbuf, _ = ar.sb("lp_rows", [1, 3 * T], F32)
    abc, _ = ar.sb("lp_abc", [128, T], BF16)
    bbc, _ = ar.sb("lp_bbc", [128, T], BF16)
    mpsum = ar.mark()
    prow, _ = ar.ps("lp_prow", [1, 2 * T], F32)
    pbc, _ = ar.ps("lp_pbc", [128, T], F32)
    pbc2, _ = ar.ps("lp_pbc2", [128, T], F32)
    mu = rowbuf[:, 0:T]
    arow = rowbuf[:, T:2 * T]
    brow = rowbuf[:, 2 * T:3 * T]

    k.wait("tensor", "c0", cm.consts_ready)
    st = dict(stats_pe=k.val("pe"), sq_ac=k.val("ac"), ve=k.val("ve"),
              prow_ve=k.val("ve"), bc_pe=k.val("pe"), evac_ve=k.val("ve"),
              app_ve=k.val("ve"), st0=k.val("lnst0"), st1=k.val("lnst1"))

    for ci, t0 in enumerate(t0s):
        is_last = (ci == len(t0s) - 1) and last_sbuf is not None
        p_stats, p_sq, p_ve = st["stats_pe"], st["sq_ac"], st["ve"]
        p_prow, p_bc, p_evac = st["prow_ve"], st["bc_pe"], st["evac_ve"]
        p_app, p_st0, p_st1 = st["app_ve"], st["st0"], st["st1"]
        lds, lds2, sq_ac, mmq = {}, {}, {}, {}

        def load_x(kc):
            par = kc % 2
            if kc >= 2:
                k.wait("gpsimd", "pe", mmq[kc - 2])
                k.wait("gpsimd", "ac", sq_ac[kc - 2])
            else:
                k.wait("gpsimd", "pe", p_stats)
                k.wait("gpsimd", "ac", p_sq)
                k.wait("gpsimd", "ve", p_ve)
            lds[kc] = k.dma("gpsimd", lambda eng, kc=kc, par=par, t0=t0:
                            eng.dma_start(
                                out=xbb[par][:, :],
                                in_=x_fm[kc * 128:(kc + 1) * 128,
                                         t0:t0 + T]), f"lpx{par}")

        def load_x2(kc):
            par = kc % 2
            if kc >= 2:
                k.wait("gpsimd", "ve", app_ve[kc - 2])
            else:
                k.wait("gpsimd", "ve", p_app)
            lds2[kc] = k.dma("gpsimd", lambda eng, kc=kc, par=par, t0=t0:
                             eng.dma_start(
                                 out=xab[par][:, :],
                                 in_=x_fm[kc * 128:(kc + 1) * 128,
                                          t0:t0 + T]), f"lpa{par}")

        load_x(0)
        load_x(1)
        for kc in range(16):
            par = kc % 2
            k.wait("scalar", f"lpx{par}", lds[kc])
            if kc >= 2:
                k.wait("scalar", "pe", mmq[kc - 2])
            else:
                k.wait("scalar", "pe", p_stats)
            sq_ac[kc] = k.do("scalar", lambda eng, kc=kc, par=par:
                             eng.activation(xq[par][:, :], xbb[par][:, :],
                                            AF.Square), inc=("ac", 1))
            k.wait("tensor", f"lpx{par}", lds[kc])
            k.wait("tensor", "ac", sq_ac[kc])
            if kc == 0:
                k.wait("tensor", "ve", p_prow)
            for nb in range(T // 512):
                k.do("tensor", lambda eng, nb=nb, kc=kc, par=par: eng.matmul(
                    prow[:, nb * 512:(nb + 1) * 512],
                    cm.c_ones_col[:, :],
                    xbb[par][:, nb * 512:(nb + 1) * 512],
                    start=(kc == 0), stop=(kc == 15)), inc=None)
                k.do("tensor", lambda eng, nb=nb, kc=kc, par=par: eng.matmul(
                    prow[:, T + nb * 512:T + (nb + 1) * 512],
                    cm.c_ones_col[:, :],
                    xq[par][:, nb * 512:(nb + 1) * 512],
                    start=(kc == 0), stop=(kc == 15)),
                    inc=("pe", 1))
            mmq[kc] = k.val("pe")
            if kc < 14:
                load_x(kc + 2)
            elif kc == 14:
                load_x2(0)
            else:
                load_x2(1)
        stats_pe = k.val("pe")
        k.wait("vector", "pe", stats_pe)
        k.do("vector", lambda eng: eng.tensor_scalar_mul(
            mu, prow[:, 0:T], 1.0 / D))
        k.do("vector", lambda eng: eng.tensor_scalar_mul(
            arow, prow[:, T:2 * T], 1.0 / D), inc=("ve", 1))
        prow_ve = k.val("ve")
        k.do("vector", lambda eng: eng.scalar_tensor_tensor(
            brow, mu, -1.0, mu, ALU.mult, ALU.mult))
        k.do("vector", lambda eng: eng.tensor_add(arow, arow, brow))
        vq = k.do("vector", lambda eng: eng.tensor_scalar_add(
            arow, arow, EPS), inc=("ve", 1))
        k.wait("scalar", "ve", vq)
        k.wait("scalar", "pe", p_bc)
        k.do("scalar", lambda eng: eng.activation(arow, arow, AF.Ln))
        rstd_ac = k.do("scalar", lambda eng: eng.activation(
            arow, arow, AF.Exp, scale=-0.5), inc=("ac", 1))
        k.wait("vector", "ac", rstd_ac)
        k.do("vector", lambda eng: eng.scalar_tensor_tensor(
            brow, mu, -1.0, arow, ALU.mult, ALU.mult), inc=("ve", 1))
        rowdone = k.val("ve")
        k.wait("tensor", "ve", rowdone)
        for nb in range(T // 512):
            k.do("tensor", lambda eng, nb=nb: eng.matmul(
                pbc[:, nb * 512:(nb + 1) * 512], cm.c_ones_row[:, :],
                arow[:, nb * 512:(nb + 1) * 512], start=True, stop=True))
            k.do("tensor", lambda eng, nb=nb: eng.matmul(
                pbc2[:, nb * 512:(nb + 1) * 512], cm.c_ones_row[:, :],
                brow[:, nb * 512:(nb + 1) * 512], start=True, stop=True),
                inc=("pe", 1) if nb == T // 512 - 1 else None)
        bc_pe = k.val("pe")
        k.wait("vector", "pe", bc_pe)
        k.do("vector", lambda eng: eng.tensor_copy(abc[:, :], pbc[:, :]))
        k.do("vector", lambda eng: eng.tensor_copy(bbc[:, :], pbc2[:, :]),
             inc=("ve", 1))
        evac_ve = k.val("ve")
        app_ve, st_dma = {}, {}
        for kc in range(16):
            par = kc % 2
            k.wait("vector", f"lpa{par}", lds2[kc])
            if is_last:
                dst = last_sbuf[:, kc * T:(kc + 1) * T]
            else:
                if kc >= 2:
                    k.wait("vector", f"lnst{par}", st_dma[kc - 2])
                elif par == 0:
                    k.wait("vector", "lnst0", p_st0)
                else:
                    k.wait("vector", "lnst1", p_st1)
                dst = stage[par][:, :]
            k.do("vector", lambda eng, par=par: eng.tensor_mul(
                xab[par][:, :], xab[par][:, :], abc[:, :]))
            app_ve[kc] = k.do("vector", lambda eng, dst=dst, par=par:
                              eng.tensor_tensor(
                                  dst, xab[par][:, :], bbc[:, :], ALU.add),
                              inc=("ve", 1))
            if kc < 14:
                load_x2(kc + 2)
            k.wait("scalar", "ve", app_ve[kc])
            srcap = (last_sbuf[:, kc * T:(kc + 1) * T] if is_last
                     else stage[par][:, :])
            st_dma[kc] = k.dma("scalar", lambda eng, kc=kc, srcap=srcap,
                               t0=t0: eng.dma_start(
                                   out=also_dram[kc * 128:(kc + 1) * 128,
                                                 t0:t0 + T],
                                   in_=srcap), f"lnst{par}")
        st = dict(stats_pe=stats_pe, sq_ac=sq_ac[15], ve=k.val("ve"),
                  prow_ve=prow_ve, bc_pe=bc_pe, evac_ve=evac_ve,
                  app_ve=app_ve[15], st0=k.val("lnst0"), st1=k.val("lnst1"))
    ar.free_to(mpsum if keep_mark else m0)
    return dict(ln_ve=st["app_ve"], st0=st["st0"], st1=st["st1"],
                pe=st["bc_pe"])


def emit_ln_chunk(cm, x_fm, t0, lnsb, also_dram=None, ldq="sync"):
    nc, k, ar = cm.nc, cm.k, cm.ar
    T = 1024
    m0 = ar.mark()
    stage = None
    if lnsb is None:
        assert also_dram is not None
        stage = [ar.sb(f"ln_st{j}", [128, T], BF16)[0] for j in range(2)]
    xs = [ar.sb(f"ln_x{j}", [128, T], F32)[0] for j in range(2)]
    xb = [ar.sb(f"ln_b{j}", [128, T], BF16)[0] for j in range(2)]
    xq = [ar.sb(f"ln_q{j}", [128, T], BF16)[0] for j in range(2)]
    rowbuf, _ = ar.sb("ln_rows", [1, 2 * T], F32)
    abc, _ = ar.sb("ln_abc", [128, T], F32)
    bbc, _ = ar.sb("ln_bbc", [128, T], F32)
    prow, _ = ar.ps("ln_prow", [1, 2 * T], F32)
    pbc, _ = ar.ps("ln_pbc", [128, T], F32)
    pbc2, _ = ar.ps("ln_pbc2", [128, T], F32)

    mu = rowbuf[:, 0:T]
    arow = rowbuf[:, T:2 * T]
    msq = arow
    var = arow
    sd = arow
    brow = abc[0:1, 0:T]

    k.wait("tensor", "c0", cm.consts_ready)
    lds, cast_ve, sq_ac, mmq = {}, {}, {}, {}

    def load_x(kc):
        par = kc % 2
        if kc >= 2:
            k.wait(ldq, "ve", cast_ve[kc - 2])
        lds[kc] = k.dma(ldq, lambda eng, kc=kc, par=par: eng.dma_start(
            out=xs[par][:, :],
            in_=x_fm[kc * 128:(kc + 1) * 128, t0:t0 + T]), f"lnx{par}")

    load_x(0)
    load_x(1)
    for kc in range(16):
        par = kc % 2
        k.wait("vector", f"lnx{par}", lds[kc])
        if kc >= 2:
            k.wait("vector", "pe", mmq[kc - 2])
        cast_ve[kc] = k.do("vector", lambda eng, kc=kc, par=par:
                           eng.tensor_copy(xb[par][:, :], xs[par][:, :]),
                           inc=("ve", 1))
        if kc < 14:
            load_x(kc + 2)
        k.wait("scalar", "ve", cast_ve[kc])
        if kc >= 2:
            k.wait("scalar", "pe", mmq[kc - 2])
        sq_ac[kc] = k.do("scalar", lambda eng, kc=kc, par=par: eng.activation(
            xq[par][:, :], xb[par][:, :], AF.Square), inc=("ac", 1))
        k.wait("tensor", "ac", sq_ac[kc])
        for nb in range(T // 512):
            k.do("tensor", lambda eng, nb=nb, kc=kc, par=par: eng.matmul(
                prow[:, nb * 512:(nb + 1) * 512],
                cm.c_ones_col[:, :],
                xb[par][:, nb * 512:(nb + 1) * 512],
                start=(kc == 0), stop=(kc == 15)), inc=None)
            k.do("tensor", lambda eng, nb=nb, kc=kc, par=par: eng.matmul(
                prow[:, T + nb * 512:T + (nb + 1) * 512],
                cm.c_ones_col[:, :],
                xq[par][:, nb * 512:(nb + 1) * 512],
                start=(kc == 0), stop=(kc == 15)),
                inc=("pe", 1) if nb == T // 512 - 1 else None)
        mmq[kc] = k.val("pe")
    stats_pe = k.val("pe")
    k.wait("vector", "pe", stats_pe)
    k.do("vector", lambda eng: eng.tensor_scalar_mul(mu, prow[:, 0:T], 1.0 / D))
    k.do("vector", lambda eng: eng.tensor_scalar_mul(
        msq, prow[:, T:2 * T], 1.0 / D))
    k.do("vector", lambda eng: eng.scalar_tensor_tensor(
        brow, mu, -1.0, mu, ALU.mult, ALU.mult))
    k.do("vector", lambda eng: eng.tensor_add(var, msq, brow))
    k.do("vector", lambda eng: eng.tensor_scalar_add(var, var, EPS),
         inc=("ve", 1))
    k.wait("scalar", "ve", k.val("ve"))
    k.do("scalar", lambda eng: eng.activation(sd, var, AF.Ln))
    k.do("scalar", lambda eng: eng.activation(arow, sd, AF.Exp, scale=-0.5),
         inc=("ac", 1))
    k.wait("vector", "ac", k.val("ac"))
    k.do("vector", lambda eng: eng.scalar_tensor_tensor(
        brow, mu, -1.0, arow, ALU.mult, ALU.mult), inc=("ve", 1))
    rowdone = k.val("ve")
    k.wait("tensor", "ve", rowdone)
    for nb in range(T // 512):
        k.do("tensor", lambda eng, nb=nb: eng.matmul(
            pbc[:, nb * 512:(nb + 1) * 512], cm.c_ones_row[:, :],
            arow[:, nb * 512:(nb + 1) * 512], start=True, stop=True))
        k.do("tensor", lambda eng, nb=nb: eng.matmul(
            pbc2[:, nb * 512:(nb + 1) * 512], cm.c_ones_row[:, :],
            brow[:, nb * 512:(nb + 1) * 512], start=True, stop=True),
            inc=("pe", 1) if nb == T // 512 - 1 else None)
    bc_pe = k.val("pe")
    k.wait("vector", "pe", bc_pe)
    k.do("vector", lambda eng: eng.tensor_copy(abc[:, :], pbc[:, :]))
    k.do("vector", lambda eng: eng.tensor_copy(bbc[:, :], pbc2[:, :]),
         inc=("ve", 1))
    bc_ve = k.val("ve")
    lds2, app_ve = {}, {}

    def load_x2(kc):
        par = kc % 2
        if kc >= 2:
            k.wait(ldq, "ve", app_ve[kc - 2])
        lds2[kc] = k.dma(ldq, lambda eng, kc=kc, par=par: eng.dma_start(
            out=xs[par][:, :],
            in_=x_fm[kc * 128:(kc + 1) * 128, t0:t0 + T]), f"lnx{par}")

    load_x2(0)
    load_x2(1)
    st_dma = {}
    for kc in range(16):
        par = kc % 2
        k.wait("vector", f"lnx{par}", lds2[kc])
        k.wait("vector", "ve", bc_ve)
        k.do("vector", lambda eng, kc=kc, par=par: eng.tensor_mul(
            xs[par][:, :], xs[par][:, :], abc[:, :]))
        if stage is not None:
            if kc >= 2:
                k.wait("vector", f"lnst{par}", st_dma[kc - 2])
            dst = stage[par][:, :]
        else:
            dst = lnsb[:, kc * T:(kc + 1) * T]
        app_ve[kc] = k.do("vector", lambda eng, dst=dst, par=par:
                          eng.tensor_tensor(
                              dst, xs[par][:, :],
                              bbc[:, :], ALU.add))
        if kc < 14:
            load_x2(kc + 2)
        if also_dram is not None:
            k.wait("scalar", "ve", app_ve[kc])
            srcap = stage[par][:, :] if stage is not None \
                else lnsb[:, kc * T:(kc + 1) * T]
            st_dma[kc] = k.dma("scalar", lambda eng, kc=kc, srcap=srcap,
                               t0=t0: eng.dma_start(
                                   out=also_dram[kc * 128:(kc + 1) * 128,
                                                 t0:t0 + T],
                                   in_=srcap), f"lnst{par}")
    res = dict(ln_ve=k.val("ve"))
    ar.free_to(m0)
    return res



def emit_ffn(cm, x_fm, t0, xout_fm, dbg_ln=None, ln_dram=None):
    nc, k, ar = cm.nc, cm.k, cm.ar
    T = 1024
    m0 = ar.mark()
    hsb, _ = ar.sb("ffn_h", [128, 64 * T], BF16)
    mln = ar.mark()
    lnsb, _ = ar.sb("ffn_ln", [128, 16 * T], BF16)
    if ln_dram is None:
        lnres = emit_ln_chunk(cm, x_fm, t0, lnsb, also_dram=dbg_ln)
        ln_ready_ve = lnres["ln_ve"]
        k.barrier()
    else:
        k.barrier(("sync",), exclude=("pcast0", "pcast1", "ptr0", "ptr1",
                                      "ptw0", "ptw1"))
        for kc in range(16):
            k.dma("sync", lambda eng, kc=kc: eng.dma_start(
                out=lnsb[:, kc * T:(kc + 1) * T],
                in_=ln_dram[kc * 128:(kc + 1) * 128, t0:t0 + T]), "fl")
        fl_all = k.val("fl")
        k.wait("tensor", "fl", fl_all)
        ln_ready_ve = 0
    wsg = [ar.sb(f"ffn_wg{j}", [128, 16 * 128], BF16)[0] for j in range(2)]
    wsu = [ar.sb(f"ffn_wu{j}", [128, 16 * 128], BF16)[0] for j in range(2)]
    sg = [ar.sb(f"ffn_sg{j}", [128, T], F32)[0] for j in range(2)]
    su = [ar.sb(f"ffn_su{j}", [128, T], F32)[0] for j in range(2)]
    pg = [ar.ps(f"ffn_pg{j}", [128, T], F32)[0] for j in range(2)]
    pu = [ar.ps(f"ffn_pu{j}", [128, T], F32)[0] for j in range(2)]

    k.wait("tensor", "ve", ln_ready_ve)
    k.wait("tensor", "ve", cm.dsc_ready)
    k.wait("sync", "ptw0", cm.wprep_gu[0])
    k.wait("sync", "ptw1", cm.wprep_gu[1])

    wld, mm_done, act_done, h_done = {}, {}, {}, {}

    def load_w(m):
        par = m % 2
        if m >= 2:
            k.wait("sync", "pe", mm_done[m - 2])
        k.dma("sync", lambda eng, m=m, par=par: eng.dma_start(
            out=wsg[par][:, :].rearrange("p (a f) -> p a f", a=16),
            in_=cm.wgT[m].rearrange("a h i -> h a i")), f"fw{par}")
        wld[m] = k.dma("sync", lambda eng, m=m, par=par: eng.dma_start(
            out=wsu[par][:, :].rearrange("p (a f) -> p a f", a=16),
            in_=cm.wuT[m].rearrange("a h i -> h a i")), f"fw{par}")

    load_w(0)
    load_w(1)
    for m in range(64):
        par = m % 2
        k.wait("tensor", f"fw{par}", wld[m])
        if m >= 2:
            k.wait("tensor", "ac", act_done[m - 2])
        for which in range(2):
            ps = pg[par] if which == 0 else pu[par]
            ws = wsg[par] if which == 0 else wsu[par]
            for half in range(2):
                c0 = half * 512
                for kc in range(16):
                    last = (which == 1 and half == 1 and kc == 15)
                    k.do("tensor", lambda eng, ps=ps, ws=ws, kc=kc, c0=c0:
                         eng.matmul(ps[:, c0:c0 + 512],
                                    ws[:, kc * 128:(kc + 1) * 128],
                                    lnsb[:, kc * T + c0:kc * T + c0 + 512],
                                    start=(kc == 0), stop=(kc == 15)),
                         inc=("pe", 1) if last else None)
        mm_done[m] = k.val("pe")
        if m < 62:
            load_w(m + 2)
        k.wait("scalar", "pe", mm_done[m])
        if m >= 2:
            k.wait("scalar", "ve", h_done[m - 2])
        k.do("scalar", lambda eng, m=m, par=par: eng.activation(
            sg[par][:, :], pg[par][:, :], AF.Silu,
            scale=cm.gsc[:, m:m + 1]))
        act_done[m] = k.do("scalar", lambda eng, m=m, par=par: eng.activation(
            su[par][:, :], pu[par][:, :], AF.Copy,
            scale=cm.usc[:, m:m + 1]), inc=("ac", 1))
        k.wait("vector", "ac", act_done[m])
        h_done[m] = k.do("vector", lambda eng, m=m, par=par: eng.tensor_mul(
            hsb[:, m * T:(m + 1) * T], sg[par][:, :], su[par][:, :]),
            inc=("ve", 1))
    h_all = k.val("ve")

    ar.free_to(mln)
    wsd = [ar.sb(f"ffn_wd{j}", [128, 64 * 128], BF16)[0] for j in range(2)]
    xr = [ar.sb(f"ffn_xr{j}", [128, T], F32)[0] for j in range(2)]
    xo = [ar.sb(f"ffn_xo{j}", [128, T], F32)[0] for j in range(2)]
    pd = [ar.ps(f"ffn_pd{j}", [128, T], F32)[0] for j in range(2)]

    dld, xld, dmm_done, dev_done, dst_done = {}, {}, {}, {}, {}

    def load_d(d):
        par = d % 2
        k.wait("sync", "ptw0", cm.wprep_ptw[0])
        k.wait("sync", "ptw1", cm.wprep_ptw[1])
        if d >= 2:
            k.wait("sync", "pe", dmm_done[d - 2])
            k.wait("scalar", "ve", dev_done[d - 2])
        else:
            k.wait("sync", "pe", mm_done[63])
            k.wait("sync", "ve", h_all)
            k.wait("scalar", "ve", h_all)
        dld[d] = k.dma("sync", lambda eng, d=d, par=par: eng.dma_start(
            out=wsd[par][:, :].rearrange("p (a f) -> p a f", a=64),
            in_=cm.wdT[d].rearrange("m i c -> i m c")), f"fd{par}")
        xld[d] = k.dma("scalar", lambda eng, d=d, par=par: eng.dma_start(
            out=xr[par][:, :],
            in_=x_fm[d * 128:(d + 1) * 128, t0:t0 + T]), f"fx{par}")

    load_d(0)
    load_d(1)
    k.wait("tensor", "ve", h_all)
    for d in range(16):
        par = d % 2
        k.wait("tensor", f"fd{par}", dld[d])
        if d >= 2:
            k.wait("tensor", "ve", dev_done[d - 2])
        for half in range(2):
            c0 = half * 512
            for m in range(64):
                last = (half == 1 and m == 63)
                k.do("tensor", lambda eng, d=d, par=par, m=m, c0=c0:
                     eng.matmul(pd[par][:, c0:c0 + 512],
                                wsd[par][:, m * 128:(m + 1) * 128],
                                hsb[:, m * T + c0:m * T + c0 + 512],
                                start=(m == 0), stop=(m == 63)),
                     inc=("pe", 1) if last else None)
        dmm_done[d] = k.val("pe")
        k.wait("vector", "pe", dmm_done[d])
        k.wait("vector", f"fx{par}", xld[d])
        if d >= 2:
            k.wait("vector", f"fo{par}", dst_done[d - 2])
        dev_done[d] = k.do("vector", lambda eng, d=d, par=par:
                           eng.scalar_tensor_tensor(
                               xo[par][:, :], pd[par][:, :],
                               cm.dsc_half[:, d:d + 1], xr[par][:, :],
                               ALU.mult, ALU.add), inc=("ve", 1))
        k.wait("scalar", "ve", dev_done[d])
        dst_done[d] = k.dma("scalar", lambda eng, d=d, par=par: eng.dma_start(
            out=xout_fm[d * 128:(d + 1) * 128, t0:t0 + T],
            in_=xo[par][:, :]), f"fo{par}")
        if d < 14:
            load_d(d + 2)
    res = dict(x_fo=(k.val("fo0"), k.val("fo1")))
    ar.free_to(m0)
    return res



def emit_fm_to_tm_out(cm, src_fm, dst_tm):
    nc, k, ar = cm.nc, cm.k, cm.ar
    m0 = ar.mark()
    xs = [ar.sb(f"f2t_x{d}", [128, 1024], F32)[0] for d in range(16)]
    ot = [ar.sb(f"f2t_o{t}", [128, D], F32)[0] for t in range(8)]
    ps = [ar.ps(f"f2t_p{j}", [128, 512], F32)[0] for j in range(2)]

    for d in range(16):
        k.dma("sync", lambda eng, d=d: eng.dma_start(
            out=xs[d][:, :], in_=src_fm[d * 128:(d + 1) * 128, :]), "f2t")
    ld_all = k.val("f2t")
    ev = {}
    for t in range(8):
        for d in range(16):
            g = (t * 16 + d) // 4
            par = g % 2
            k.wait("tensor", "f2t", ld_all)
            if g >= 2:
                k.wait("tensor", "ve", ev.get(g - 2, 0))
            k.do("tensor", lambda eng, t=t, d=d, par=par: eng.matmul(
                ps[par][:, (d % 4) * 128:(d % 4 + 1) * 128],
                xs[d][:, t * 128:(t + 1) * 128],
                cm.c_eye_f32[:, :], start=True, stop=True),
                inc=("pe", 1) if d % 4 == 3 else None)
            if d % 4 == 3:
                k.wait("vector", "pe", k.val("pe"))
                ev[g] = k.do("vector", lambda eng, t=t, d=d, par=par:
                             eng.tensor_copy(
                                 ot[t][:, (d - 3) * 128:(d + 1) * 128],
                                 ps[par][:, :]), inc=("ve", 1))
    for t in range(8):
        k.wait("gpsimd", "ve", k.val("ve"))
        k.dma("gpsimd", lambda eng, t=t: eng.dma_start(
            out=dst_tm[t * 128:(t + 1) * 128, :], in_=ot[t][:, :]), "outd")
    res = k.val("outd")
    ar.free_to(m0)
    return res



def emit_cols(cm, src1d, n, dst_sb, scale_after=None):
    nc, k, ar = cm.nc, cm.k, cm.ar
    m0 = ar.mark()
    tmp, _ = ar.sb("col_tmp", [64, 128], F32)
    pscale, _ = ar.ps("col_ps", [128, 64], F32)
    k.wait("sync", "pe", k.val("pe"))
    ld = k.dma("sync", lambda eng: eng.dma_start(
        out=tmp[0:n, :], in_=src1d.rearrange("(a p) -> a p", p=128)),
        "c0")
    k.wait("tensor", "c0", ld)
    k.wait("tensor", "ve", k.val("ve"))
    mm = k.do("tensor", lambda eng: eng.matmul(
        pscale[:, 0:n], tmp[0:n, :], cm.c_eye_f32[0:n, 0:n],
        start=True, stop=True), inc=("pe", 1))
    k.wait("vector", "pe", mm)
    r = k.do("vector", lambda eng: eng.tensor_copy(dst_sb[:, 0:n],
                                                   pscale[:, 0:n]))
    if scale_after is not None:
        r = k.do("vector", lambda eng: eng.tensor_scalar_mul(
            dst_sb[:, 0:n], dst_sb[:, 0:n], scale_after))
    ar.free_to(m0)
    k.barrier()
    return r



def emit_rope_tables(cm, fr_ap, C2, S2):
    nc, k, ar = cm.nc, cm.k, cm.ar
    TK = 4096
    m0 = ar.mark()
    th_tm = [ar.sb(f"th_tm{j}", [128, 32], F32)[0] for j in range(2)]
    th, _ = ar.sb("th_fm", [32, TK], F32)
    s2, _ = ar.sb("th_s2", [32, TK], F32)
    c2, _ = ar.sb("th_c2", [32, TK], F32)
    pth, _ = ar.ps("th_ps", [32, 512], F32)
    pbc, _ = ar.ps("th_pbc", [128, 512], F32)

    k.wait("tensor", "c0", cm.consts_ready)
    lds, mms, grp_ev = {}, {}, {}
    for t in range(32):
        par = t % 2
        if t >= 2:
            k.wait("sync", "pe", mms[t - 2])
        lds[t] = k.dma("sync", lambda eng, t=t, par=par: eng.dma_start(
            out=th_tm[par][:, :], in_=fr_ap[t * 128:(t + 1) * 128, :]),
            f"thl{par}")
        k.wait("tensor", f"thl{par}", lds[t])
        if t % 4 == 0 and t >= 4:
            k.wait("tensor", "ve", grp_ev[t // 4 - 1])
        mms[t] = k.do("tensor", lambda eng, t=t, par=par: eng.matmul(
            pth[0:32, (t % 4) * 128:(t % 4 + 1) * 128],
            th_tm[par][:, :], cm.c_eye_f32[:, :], start=True, stop=True),
            inc=("pe", 1))
        if t % 4 == 3:
            k.wait("vector", "pe", mms[t])
            grp_ev[t // 4] = k.do("vector", lambda eng, t=t: eng.tensor_copy(
                th[:, (t - 3) * 128:(t + 1) * 128], pth[0:32, :]))
    k.wait("vector", "pe", k.val("pe"))
    k.do("vector", lambda eng: eng.tensor_scalar(
        th[:, :], th[:, :], 0.5, -PI / 2, ALU.mult, ALU.add))
    k.wait("scalar", "ve", k.val("ve"))
    k.do("scalar", lambda eng: eng.activation(s2[:, :], th[:, :], AF.Sin))
    k.do("vector", lambda eng: eng.tensor_scalar(
        c2[:, :], th[:, :], -1.0, PI / 2, ALU.mult, ALU.add))
    k.wait("scalar", "ve", k.val("ve"))
    a1 = k.do("scalar", lambda eng: eng.activation(c2[:, :], c2[:, :], AF.Sin))
    k.wait("vector", "ac", a1)
    k.do("vector", lambda eng: eng.scalar_tensor_tensor(
        c2[:, :], s2[:, :], -2.0, c2[:, :], ALU.mult, ALU.mult))
    k.do("vector", lambda eng: eng.tensor_mul(s2[:, :], s2[:, :], s2[:, :]))
    vv = k.do("vector", lambda eng: eng.tensor_scalar(
        s2[:, :], s2[:, :], 2.0, -1.0, ALU.mult, ALU.add))
    k.wait("tensor", "ve", vv)
    for nb in range(TK // 512):
        m1 = k.do("tensor", lambda eng, nb=nb: eng.matmul(
            pbc[:, :], cm.c_dup128[:, :], s2[:, nb * 512:(nb + 1) * 512],
            start=True, stop=True), inc=("pe", 1))
        k.wait("vector", "pe", m1)
        cv = k.do("vector", lambda eng, nb=nb: eng.tensor_copy(
            C2[:, nb * 512:(nb + 1) * 512], pbc[:, :]))
        k.wait("tensor", "ve", cv)
        m2 = k.do("tensor", lambda eng, nb=nb: eng.matmul(
            pbc[:, :], cm.c_dups128[:, :], c2[:, nb * 512:(nb + 1) * 512],
            start=True, stop=True), inc=("pe", 1))
        k.wait("vector", "pe", m2)
        sv = k.do("vector", lambda eng, nb=nb: eng.tensor_copy(
            S2[:, nb * 512:(nb + 1) * 512], pbc[:, :]))
        k.wait("tensor", "ve", sv)
    ar.free_to(m0)


def emit_rope_apply(cm, io_bf, psw, t1, t2, C2s, S2s, W, out_ap):
    k = cm.k
    for c0 in range(0, W, 512):
        mm = k.do("tensor", lambda eng, c0=c0: eng.matmul(
            psw[:, c0:c0 + 512], cm.c_p128[:, :], io_bf[:, c0:c0 + 512],
            start=True, stop=True), inc=("pe", 1))
    k.do("vector", lambda eng: eng.tensor_tensor(
        t1[:, 0:W], io_bf[:, 0:W], C2s, ALU.mult))
    k.wait("vector", "pe", mm)
    k.do("vector", lambda eng: eng.tensor_tensor(
        t2[:, 0:W], psw[:, 0:W], S2s, ALU.mult))
    return k.do("vector", lambda eng: eng.tensor_tensor(
        out_ap, t1[:, 0:W], t2[:, 0:W], ALU.add))



def emit_qproj(cm, ln2_own, wq_ap, C2, S2, qn_sb, qr_sb):
    nc, k, ar = cm.nc, cm.k, cm.ar
    T = 1024
    m0 = ar.mark()
    wqb = [ar.sb(f"qp_w{j}", [128, 16 * 384], BF16)[0] for j in range(2)]
    qrb, _ = ar.sb("qp_qrb", [128, T], BF16)
    t1, _ = ar.sb("qp_t1", [128, T], F32)
    t2, _ = ar.sb("qp_t2", [128, T], F32)
    pqa, _ = ar.ps("qp_pa", [128, T], F32)
    pqb, _ = ar.ps("qp_pb", [128, T], F32)
    pqr, _ = ar.ps("qp_pr", [128, T], F32)
    psw, _ = ar.ps("qp_psw", [128, T], F32)

    wld, pair_done = {}, {}

    def load_wq(p):
        par = p % 2
        if p >= 2:
            k.wait("gpsimd", "pe", pair_done[p - 2])
        wld[p] = k.dma("gpsimd", lambda eng, p=p, par=par: eng.dma_start(
            out=wqb[par][:, :].rearrange("p (a f) -> p a f", a=16),
            in_=wq_ap[:, p * 384:(p + 1) * 384].rearrange(
                "(a p) c -> p a c", p=128)), f"qw{par}")

    load_wq(0)
    load_wq(1)
    for p in range(8):
        par = p % 2
        h = 2 * p
        k.wait("tensor", f"qw{par}", wld[p])
        k.wait("tensor", "ve", k.val("ve"))
        for kc in range(16):
            for half in range(2):
                c0 = half * 512
                k.do("tensor", lambda eng, kc=kc, c0=c0, par=par: eng.matmul(
                    pqa[:, c0:c0 + 512],
                    wqb[par][:, kc * 384:kc * 384 + 128],
                    ln2_own[:, kc * T + c0:kc * T + c0 + 512],
                    start=(kc == 0), stop=(kc == 15)))
                k.do("tensor", lambda eng, kc=kc, c0=c0, par=par: eng.matmul(
                    pqb[:, c0:c0 + 512],
                    wqb[par][:, kc * 384 + 192:kc * 384 + 320],
                    ln2_own[:, kc * T + c0:kc * T + c0 + 512],
                    start=(kc == 0), stop=(kc == 15)))
                k.do("tensor", lambda eng, kc=kc, c0=c0, par=par: eng.matmul(
                    pqr[0:64, c0:c0 + 512],
                    wqb[par][:, kc * 384 + 128:kc * 384 + 192],
                    ln2_own[:, kc * T + c0:kc * T + c0 + 512],
                    start=(kc == 0), stop=(kc == 15)))
                k.do("tensor", lambda eng, kc=kc, c0=c0, par=par: eng.matmul(
                    pqr[64:128, c0:c0 + 512],
                    wqb[par][:, kc * 384 + 320:kc * 384 + 384],
                    ln2_own[:, kc * T + c0:kc * T + c0 + 512],
                    start=(kc == 0), stop=(kc == 15)),
                    inc=("pe", 1) if (kc == 15 and half == 1) else None)
        mm = k.val("pe")
        k.wait("scalar", "pe", mm)
        k.do("scalar", lambda eng, h=h, par=par: eng.activation(
            qn_sb[:, h * T:(h + 1) * T], pqa[:, :], AF.Copy))
        k.do("scalar", lambda eng, h=h, par=par: eng.activation(
            qn_sb[:, (h + 1) * T:(h + 2) * T], pqb[:, :], AF.Copy))
        aa = k.do("scalar", lambda eng, par=par: eng.activation(
            qrb[:, :], pqr[:, :], AF.Copy))
        k.wait("tensor", "ac", aa)
        k.wait("vector", "ac", aa)
        emit_rope_apply(cm, qrb, psw, t1, t2,
                        C2[:, 0:T], S2[:, 0:T], T,
                        qr_sb[:, p * T:(p + 1) * T])
        pair_done[p] = k.val("pe")
        if p < 6:
            load_wq(p + 2)
        k.wait("tensor", "ac", aa)
        k.wait("tensor", "ve", k.val("ve"))
    ar.free_to(m0)
    k.barrier()



def emit_kv(cm, ln2d, wkva_ap, gcol, bcol, C2, S2, kvn_sb, krope_sb):
    nc, k, ar = cm.nc, cm.k, cm.ar
    TK = 4096
    m0 = ar.mark()
    kvc = kvn_sb
    m_kva = ar.mark()
    wka = [ar.sb(f"kv_w{j}", [128, 576], BF16)[0] for j in range(2)]
    blk = [ar.sb(f"kv_b{j}", [128, 512], BF16)[0] for j in range(2)]
    pm = [ar.ps(f"kv_p{j}", [128, 512], F32)[0] for j in range(5)]

    ev_cnt = {}
    for nb in range(8):
        lds = {}
        for kc in range(16):
            par = kc % 2
            if kc >= 2:
                k.wait("sync", "pe", lds[kc - 2][1])
                k.wait("gpsimd", "pe", lds[kc - 2][1])
            ld = k.dma("sync", lambda eng, kc=kc, nb=nb, par=par:
                       eng.dma_start(out=blk[par][:, :],
                                     in_=ln2d[kc * 128:(kc + 1) * 128,
                                              nb * 512:(nb + 1) * 512]),
                       f"kvb{par}")
            wl = k.dma("gpsimd", lambda eng, kc=kc, par=par: eng.dma_start(
                out=wka[par][:, :],
                in_=wkva_ap[kc * 128:(kc + 1) * 128, :]), f"kvw{par}")
            k.wait("tensor", f"kvb{par}", ld)
            k.wait("tensor", f"kvw{par}", wl)
            if nb > 0 and kc == 0:
                k.wait("tensor", "ac", ev_cnt[nb - 1])
            for m in range(5):
                kk = 128 if m < 4 else 64
                out_ap = pm[m][:, :] if m < 4 else pm[m][0:64, :]
                k.do("tensor", lambda eng, m=m, kc=kc, par=par, kk=kk,
                     out_ap=out_ap: eng.matmul(
                         out_ap,
                         wka[par][:, m * 128:m * 128 + kk],
                         blk[par][:, :],
                         start=(kc == 0), stop=(kc == 15)),
                     inc=("pe", 1) if (m == 4 and kc == 15) else None)
            lds[kc] = (ld, k.val("pe"))
        mmv = k.val("pe")
        k.wait("scalar", "pe", mmv)
        for m in range(4):
            k.do("scalar", lambda eng, m=m, nb=nb: eng.activation(
                kvc[:, m * TK + nb * 512:m * TK + (nb + 1) * 512],
                pm[m][:, :], AF.Copy))
        ev = k.do("scalar", lambda eng, nb=nb: eng.activation(
            krope_sb[0:64, nb * 512:(nb + 1) * 512], pm[4][0:64, :],
            AF.Copy))
        ev_cnt[nb] = ev
    kva_done_ac = k.val("ac")
    ar.free_to(m_kva)

    xq = [ar.sb(f"kvl_q{j}", [128, 512], BF16)[0] for j in range(2)]
    rb, _ = ar.sb("kvl_rows", [1, 2 * 512], F32)
    ab, _ = ar.sb("kvl_ab", [128, 512], F32)
    bb, _ = ar.sb("kvl_bb", [128, 512], F32)
    t1, _ = ar.sb("kvl_t1", [128, 512], F32)
    prow, _ = ar.ps("kvl_prow", [1, 2 * 512], F32)
    pb1, _ = ar.ps("kvl_pb1", [128, 512], F32)
    pb2, _ = ar.ps("kvl_pb2", [128, 512], F32)
    k.barrier()
    RK = float(RANK)
    bcast_ve = {}
    for nb in range(8):
        sqa = {}
        for kc in range(4):
            par = kc % 2
            if kc >= 2:
                k.wait("scalar", "pe", sqa[kc - 2])
            sq = k.do("scalar", lambda eng, kc=kc, nb=nb, par=par:
                      eng.activation(xq[par][:, :],
                                     kvc[:, kc * TK + nb * 512:
                                         kc * TK + (nb + 1) * 512],
                                     AF.Square))
            k.wait("tensor", "ac", sq)
            if nb > 0 and kc == 0:
                k.wait("tensor", "ve", bcast_ve[nb - 1])
            k.do("tensor", lambda eng, kc=kc, nb=nb, par=par: eng.matmul(
                prow[:, 0:512], cm.c_ones_col[:, :],
                kvc[:, kc * TK + nb * 512:kc * TK + (nb + 1) * 512],
                start=(kc == 0), stop=(kc == 3)))
            mmk = k.do("tensor", lambda eng, kc=kc, par=par: eng.matmul(
                prow[:, 512:1024], cm.c_ones_col[:, :], xq[par][:, :],
                start=(kc == 0), stop=(kc == 3)),
                inc=("pe", 1))
            sqa[kc] = mmk
        k.wait("vector", "pe", k.val("pe"))
        k.do("vector", lambda eng: eng.tensor_scalar_mul(
            rb[:, 0:512], prow[:, 0:512], 1.0 / RK))
        k.do("vector", lambda eng: eng.tensor_scalar_mul(
            rb[:, 512:1024], prow[:, 512:1024], 1.0 / RK))
        k.do("vector", lambda eng: eng.scalar_tensor_tensor(
            ab[0:1, 0:512], rb[:, 0:512], -1.0, rb[:, 0:512],
            ALU.mult, ALU.mult))
        k.do("vector", lambda eng: eng.tensor_add(
            rb[:, 512:1024], rb[:, 512:1024], ab[0:1, 0:512]))
        vq = k.do("vector", lambda eng: eng.tensor_scalar_add(
            rb[:, 512:1024], rb[:, 512:1024], EPS))
        k.wait("scalar", "ve", vq)
        k.do("scalar", lambda eng: eng.activation(
            rb[:, 512:1024], rb[:, 512:1024], AF.Ln))
        sdq = k.do("scalar", lambda eng: eng.activation(
            rb[:, 512:1024], rb[:, 512:1024], AF.Exp, scale=-0.5))
        k.wait("vector", "ac", sdq)
        vr = k.do("vector", lambda eng: eng.scalar_tensor_tensor(
            ab[0:1, 0:512], rb[:, 0:512], -1.0, rb[:, 512:1024],
            ALU.mult, ALU.mult))
        k.wait("tensor", "ve", vr)
        k.do("tensor", lambda eng: eng.matmul(
            pb1[:, :], cm.c_ones_row[:, :], rb[:, 512:1024],
            start=True, stop=True))
        bmm = k.do("tensor", lambda eng: eng.matmul(
            pb2[:, :], cm.c_ones_row[:, :], ab[0:1, 0:512],
            start=True, stop=True), inc=("pe", 1))
        k.wait("vector", "pe", bmm)
        k.do("vector", lambda eng: eng.tensor_copy(ab[:, :], pb1[:, :]))
        bcv = k.do("vector", lambda eng: eng.tensor_copy(bb[:, :], pb2[:, :]))
        for kc in range(4):
            k.do("vector", lambda eng, kc=kc, nb=nb: eng.tensor_tensor(
                t1[:, :], kvc[:, kc * TK + nb * 512:kc * TK + (nb + 1) * 512],
                ab[:, :], ALU.mult))
            k.do("vector", lambda eng: eng.tensor_tensor(
                t1[:, :], t1[:, :], bb[:, :], ALU.add))
            k.do("vector", lambda eng, kc=kc, nb=nb: eng.tensor_scalar(
                kvn_sb[:, kc * TK + nb * 512:kc * TK + (nb + 1) * 512],
                t1[:, :], gcol[:, kc:kc + 1], bcol[:, kc:kc + 1],
                ALU.mult, ALU.add))
        bcast_ve[nb] = k.val("ve")
    k.barrier()
    psw, _ = ar.ps("kv_psw", [128, 512], F32)
    t2b, _ = ar.sb("kv_t2", [64, 512], F32)
    t1b, _ = ar.sb("kv_t1b", [64, 512], F32)
    for nb in range(8):
        c0 = nb * 512
        mm = k.do("tensor", lambda eng, c0=c0: eng.matmul(
            psw[0:64, :], cm.c_p128[0:64, 0:64],
            krope_sb[0:64, c0:c0 + 512], start=True, stop=True),
            inc=("pe", 1))
        k.do("vector", lambda eng, c0=c0: eng.tensor_tensor(
            t1b[:, :], krope_sb[0:64, c0:c0 + 512], C2[0:64, c0:c0 + 512],
            ALU.mult))
        k.wait("vector", "pe", mm)
        k.do("vector", lambda eng, c0=c0: eng.tensor_tensor(
            t2b[:, :], psw[0:64, :], S2[0:64, c0:c0 + 512], ALU.mult))
        vv = k.do("vector", lambda eng, c0=c0: eng.tensor_tensor(
            krope_sb[0:64, c0:c0 + 512], t1b[:, :], t2b[:, :], ALU.add))
        k.do("vector", lambda eng, c0=c0: eng.tensor_copy(
            krope_sb[64:128, c0:c0 + 512], krope_sb[0:64, c0:c0 + 512]))
        k.wait("tensor", "ve", vv)
    ar.free_to(m0)
    k.barrier()



def emit_attention(cm, wkvb_ap, kvn_sb, krope_sb, qn_sb, qr_sb, attn_sb):
    nc, k, ar = cm.nc, cm.k, cm.ar
    T = 1024
    TK = 4096
    m0 = ar.mark()
    wkb = [ar.sb(f"at_wb{j}", [128, 4 * 256], BF16)[0] for j in range(2)]
    kno, _ = ar.sb("at_kn", [128, TK], BF16)
    vfm, _ = ar.sb("at_vf", [128, TK], BF16)
    vtm, _ = ar.sb("at_vt", [128, TK], BF16)
    p_sb = [ar.sb(f"at_p{j}", [128, T], BF16)[0] for j in range(2)]
    rrow, _ = ar.sb("at_rr", [1, T], F32)
    rbc, _ = ar.sb("at_rbc", [128, T], F32)
    psS = [ar.ps(f"at_s{j}", [128, T], F32)[0] for j in range(2)]
    psO, _ = ar.ps("at_o", [128, T], F32)
    psR, _ = ar.ps("at_r", [1, T], F32)

    k.wait("tensor", "ve", cm.negc_ready)
    wld = {}

    def load_wb(h):
        par = h % 2
        if h >= 2:
            k.wait("gpsimd", "ac", kvb_evac[h - 2])
        wld[h] = k.dma("gpsimd", lambda eng, h=h, par=par: eng.dma_start(
            out=wkb[par][:, :].rearrange("p (a f) -> p a f", a=4),
            in_=wkvb_ap[:, h * 256:(h + 1) * 256].rearrange(
                "(a p) c -> p a c", p=128)), f"awb{par}")

    kvb_evac = {}
    load_wb(0)
    load_wb(1)
    for h in range(16):
        par = h % 2
        k.wait("tensor", f"awb{par}", wld[h])
        for which in range(2):
            dst = kno if which == 0 else vfm
            for nb in range(8):
                pp = psS[nb % 2]
                for j in range(4):
                    k.do("tensor", lambda eng, j=j, nb=nb, par=par,
                         which=which, pp=pp: eng.matmul(
                             pp[:, 0:512],
                             wkb[par][:, j * 256 + which * 128:
                                      j * 256 + which * 128 + 128],
                             kvn_sb[:, j * TK + nb * 512:
                                    j * TK + (nb + 1) * 512],
                             start=(j == 0), stop=(j == 3)),
                         inc=("pe", 1) if j == 3 else None)
                mmv = k.val("pe")
                k.wait("scalar", "pe", mmv)
                ae = k.do("scalar", lambda eng, nb=nb, dst=dst, pp=pp:
                          eng.activation(dst[:, nb * 512:(nb + 1) * 512],
                                         pp[:, 0:512], AF.Copy))
                k.wait("tensor", "ac", ae)
        kvb_evac[h] = k.val("ac")
        if h < 14:
            load_wb(h + 2)
        for g4 in range(8):
            pp = psS[g4 % 2]
            for jj in range(4):
                sc = g4 * 4 + jj
                k.do("tensor", lambda eng, sc=sc, jj=jj, pp=pp: eng.matmul(
                    pp[:, jj * 128:(jj + 1) * 128],
                    vfm[:, sc * 128:(sc + 1) * 128],
                    cm.c_eye_bf[:, :], start=True, stop=True),
                    inc=("pe", 1) if jj == 3 else None)
            mmv = k.val("pe")
            k.wait("scalar", "pe", mmv)
            ve = k.do("scalar", lambda eng, g4=g4, pp=pp: eng.activation(
                vtm[:, g4 * 512:(g4 + 1) * 512], pp[:, 0:512], AF.Copy))
            k.wait("tensor", "ac", ve)
        exp_ac = {}
        pv_pe = {}
        for sc in range(32):
            par2 = sc % 2
            pp = psS[par2]
            if sc >= 2:
                k.wait("tensor", "ac", exp_ac[sc - 2])
            for half in range(2):
                c0 = half * 512
                k.do("tensor", lambda eng, sc=sc, c0=c0, h=h, pp=pp:
                     eng.matmul(pp[:, c0:c0 + 512],
                                kno[:, sc * 128:(sc + 1) * 128],
                                qn_sb[:, h * T + c0:h * T + c0 + 512],
                                start=True, stop=False))
                r0 = 64 * (h % 2)
                qrs = qr_sb[r0:r0 + 64,
                            (h // 2) * T + c0:(h // 2) * T + c0 + 512]
                krs = krope_sb[r0:r0 + 64, sc * 128:(sc + 1) * 128]
                k.do("tensor", lambda eng, qrs=qrs, krs=krs, c0=c0, pp=pp:
                     eng.matmul(pp[:, c0:c0 + 512], krs,
                                qrs, start=False, stop=True),
                     inc=("pe", 1) if c0 else None)
            mmv = k.val("pe")
            k.wait("scalar", "pe", mmv)
            if sc >= 2:
                k.wait("scalar", "pe", pv_pe[sc - 2])
            ea = k.do("scalar", lambda eng, sc=sc, par2=par2, pp=pp:
                      eng.activation(p_sb[par2][:, :], pp[:, :], AF.Exp,
                                     bias=cm.c_negC[:, 0:1], scale=SCALE))
            exp_ac[sc] = ea
            k.wait("tensor", "ac", ea)
            for half in range(2):
                c0 = half * 512
                k.do("tensor", lambda eng, sc=sc, c0=c0, par2=par2:
                     eng.matmul(psR[:, c0:c0 + 512], cm.c_ones_col[:, :],
                                p_sb[par2][:, c0:c0 + 512],
                                start=(sc == 0), stop=(sc == 31)))
                k.do("tensor", lambda eng, sc=sc, c0=c0, par2=par2:
                     eng.matmul(psO[:, c0:c0 + 512],
                                vtm[:, sc * 128:(sc + 1) * 128],
                                p_sb[par2][:, c0:c0 + 512],
                                start=(sc == 0), stop=(sc == 31)),
                     inc=("pe", 1) if c0 else None)
            pv_pe[sc] = k.val("pe")
        k.wait("vector", "pe", k.val("pe"))
        vv = k.do("vector", lambda eng: eng.reciprocal(rrow[:, :], psR[:, :]))
        k.wait("tensor", "ve", vv)
        bmm = k.do("tensor", lambda eng: eng.matmul(
            psS[0][:, 0:512], cm.c_ones_row[:, :], rrow[:, 0:512],
            start=True, stop=True), inc=("pe", 1))
        bmm2 = k.do("tensor", lambda eng: eng.matmul(
            psS[1][:, 0:512], cm.c_ones_row[:, :], rrow[:, 512:1024],
            start=True, stop=True), inc=("pe", 1))
        k.wait("vector", "pe", bmm2)
        k.do("vector", lambda eng: eng.tensor_copy(rbc[:, 0:512],
                                                   psS[0][:, 0:512]))
        k.do("vector", lambda eng: eng.tensor_copy(rbc[:, 512:1024],
                                                   psS[1][:, 0:512]))
        av = k.do("vector", lambda eng, h=h: eng.tensor_tensor(
            attn_sb[:, h * T:(h + 1) * T], psO[:, :], rbc[:, :], ALU.mult))
        k.wait("tensor", "ve", av)
    ar.free_to(m0)
    k.barrier()



def emit_wout(cm, wout_ap, attn_sb, x1fm, x2fm):
    nc, k, ar = cm.nc, cm.k, cm.ar
    T = 1024
    m0 = ar.mark()
    wsb, _ = ar.sb("wo_w", [128, 16 * D], BF16)
    xr = [ar.sb(f"wo_xr{j}", [128, T], F32)[0] for j in range(2)]
    xo = [ar.sb(f"wo_xo{j}", [128, T], F32)[0] for j in range(2)]
    pw = [ar.ps(f"wo_p{j}", [128, T], F32)[0] for j in range(2)]

    wl = k.dma("gpsimd", lambda eng: eng.dma_start(
        out=wsb[:, :].rearrange("p (a f) -> p a f", a=16),
        in_=wout_ap.rearrange("(a p) c -> p a c", p=128)), "wo")
    k.wait("tensor", "wo", wl)

    xl, mmd, evd, std = {}, {}, {}, {}

    def load_x(d):
        par = d % 2
        if d >= 2:
            k.wait("scalar", "ve", evd[d - 2])
        xl[d] = k.dma("scalar", lambda eng, d=d, par=par: eng.dma_start(
            out=xr[par][:, :], in_=x1fm[d * 128:(d + 1) * 128, 0:T]),
            f"wx{par}")

    load_x(0)
    load_x(1)
    for d in range(16):
        par = d % 2
        if d >= 2:
            k.wait("tensor", "ve", evd[d - 2])
        for h in range(16):
            for half in range(2):
                c0 = half * 512
                k.do("tensor", lambda eng, d=d, h=h, c0=c0, par=par:
                     eng.matmul(pw[par][:, c0:c0 + 512],
                                wsb[:, h * D + d * 128:h * D + (d + 1) * 128],
                                attn_sb[:, h * T + c0:h * T + c0 + 512],
                                start=(h == 0), stop=(h == 15)),
                     inc=("pe", 1) if (h == 15 and half == 1) else None)
        mmd[d] = k.val("pe")
        k.wait("vector", "pe", mmd[d])
        k.wait("vector", f"wx{par}", xl[d])
        if d >= 2:
            k.wait("vector", f"wst{par}", std[d - 2])
        evd[d] = k.do("vector", lambda eng, d=d, par=par: eng.tensor_tensor(
            xo[par][:, :], pw[par][:, :], xr[par][:, :], ALU.add))
        k.wait("scalar", "ve", evd[d])
        std[d] = k.dma("scalar", lambda eng, d=d, par=par: eng.dma_start(
            out=x2fm[d * 128:(d + 1) * 128, :], in_=xo[par][:, :]),
            f"wst{par}")
        if d < 14:
            load_x(d + 2)
    res = (k.val("wst0"), k.val("wst1"))
    ar.free_to(m0)
    k.barrier()
    return res



def build_L2(stop_after=None):
    nc = bass.Bass("TRN2", target_bir_lowering=False)
    cm = Common(nc)
    k, ar = cm.k, cm.ar

    inputs = {}
    inputs["x1fm"] = nc.dram_tensor("x1fm", [D, S], F32,
                                    kind="ExternalInput").ap()
    inputs["fr"] = nc.dram_tensor("fr", [S, DR // 2], F32,
                                  kind="ExternalInput").ap()
    inputs["Wq"] = nc.dram_tensor("Wq", [D, NH * DQK], F32,
                                  kind="ExternalInput").ap()
    inputs["Wkv_a"] = nc.dram_tensor("Wkv_a", [D, RANK + DR], F32,
                                     kind="ExternalInput").ap()
    inputs["kv_gamma"] = nc.dram_tensor("kv_gamma", [RANK], F32,
                                        kind="ExternalInput").ap()
    inputs["kv_beta"] = nc.dram_tensor("kv_beta", [RANK], F32,
                                       kind="ExternalInput").ap()
    inputs["Wkv_b"] = nc.dram_tensor("Wkv_b", [RANK, NH * (DN + DV)], F32,
                                     kind="ExternalInput").ap()
    inputs["Wout"] = nc.dram_tensor("Wout", [NH * DV, D], F32,
                                    kind="ExternalInput").ap()
    for nm, shp, dt in (("gate_w", [IF, D], I32), ("up_w", [IF, D], I32),
                        ("down_w", [D, IF], I32), ("gate_scale", [IF], F32),
                        ("up_scale", [IF], F32), ("down_scale", [D], F32)):
        inputs[nm] = nc.dram_tensor(nm, shp, dt, kind="ExternalInput").ap()
    cm.inputs = inputs

    out_tm = nc.dram_tensor("out_tm", [TOWN, D], F32,
                            kind="ExternalOutput").ap()
    ln2d = nc.dram_tensor("ln2d", [D, S], BF16, kind="Internal").ap()
    x2fm = nc.dram_tensor("x2fm", [D, TOWN], F32, kind="Internal").ap()
    x3fm = nc.dram_tensor("x3fm", [D, TOWN], F32, kind="Internal").ap()

    def _truncate():
        k.barrier()
        r = k.dma("gpsimd", lambda eng: eng.dma_start(
            out=out_tm, in_=inputs["x1fm"][0:TOWN, 0:D]), "outd")
        k.wait("sync", "outd", r)
        cm.k.finish()
        return nc

    cm.load_consts(with_ffn_scales=True)
    cm.gcol, _ = ar.sb("gcol", [128, 4], F32)
    cm.bcol, _ = ar.sb("bcol", [128, 4], F32)
    emit_cols(cm, inputs["kv_gamma"], 4, cm.gcol)
    emit_cols(cm, inputs["kv_beta"], 4, cm.bcol)
    if stop_after == "consts":
        return _truncate()
    emit_ffn_weight_prep(cm)
    k.barrier()
    if stop_after == "prep":
        return _truncate()

    m_pre = ar.mark()
    attn_sb, _ = ar.sb("attn_sb", [128, 16 * 1024], BF16)
    m_attn = ar.mark()
    qn_sb, _ = ar.sb("qn_sb", [128, 16 * 1024], BF16)
    qr_sb, _ = ar.sb("qr_sb", [128, 8 * 1024], BF16)
    C2, _ = ar.sb("C2", [128, S], F32)
    S2, _ = ar.sb("S2", [128, S], F32)

    emit_rope_tables(cm, inputs["fr"], C2, S2)
    if stop_after == "theta":
        return _truncate()

    for c in range(1, 4):
        emit_ln_chunk(cm, inputs["x1fm"], c * 1024, None, also_dram=ln2d)
        k.barrier()
    m_ln2 = ar.mark()
    ln2_own, _ = ar.sb("ln2_own", [128, 16 * 1024], BF16)
    emit_ln_chunk(cm, inputs["x1fm"], 0, ln2_own, also_dram=ln2d)
    k.barrier()
    if stop_after == "ln2":
        return _truncate()

    emit_qproj(cm, ln2_own, inputs["Wq"], C2, S2, qn_sb, qr_sb)
    ar.free_to(m_ln2)
    k.barrier()
    if stop_after == "qproj":
        return _truncate()
    kvn_sb, _ = ar.sb("kvn_sb", [128, 4 * S], BF16)
    krope_sb, _ = ar.sb("krope_sb", [128, S], BF16)
    emit_kv(cm, ln2d, inputs["Wkv_a"], cm.gcol, cm.bcol, C2, S2,
            kvn_sb, krope_sb)
    if stop_after == "kv":
        return _truncate()
    emit_attention(cm, inputs["Wkv_b"], kvn_sb, krope_sb, qn_sb, qr_sb,
                   attn_sb)
    ar.free_to(m_attn)
    k.barrier()
    if stop_after == "attn":
        return _truncate()
    r2 = emit_wout(cm, inputs["Wout"], attn_sb, inputs["x1fm"], x2fm)
    ar.free_to(m_pre)
    k.barrier()
    if stop_after == "wout":
        return _truncate()
    if stop_after == "x2":
        k.barrier()
        r = k.dma("gpsimd", lambda eng: eng.dma_start(
            out=out_tm.rearrange("a b -> (a b)"),
            in_=x2fm.rearrange("a b -> (a b)")), "outd")
        k.wait("sync", "outd", r)
        cm.k.finish()
        return nc
    r3 = emit_ffn(cm, x2fm, 0, x3fm)
    k.barrier()
    rout = emit_fm_to_tm_out(cm, x3fm, out_tm)
    k.wait("sync", "outd", rout)
    cm.k.finish()
    return nc



def build_L1():
    nc = bass.Bass("TRN2", target_bir_lowering=False)
    cm = Common(nc)
    k, ar = cm.k, cm.ar

    inputs = {}
    inputs["x_own"] = nc.dram_tensor("x_own", [TOWN, D], F32,
                                     kind="ExternalInput").ap()
    inputs["gate_w"] = nc.dram_tensor("gate_w", [IF, D], I32,
                                      kind="ExternalInput").ap()
    inputs["up_w"] = nc.dram_tensor("up_w", [IF, D], I32,
                                    kind="ExternalInput").ap()
    inputs["down_w"] = nc.dram_tensor("down_w", [D, IF], I32,
                                      kind="ExternalInput").ap()
    inputs["gate_scale"] = nc.dram_tensor("gate_scale", [IF], F32,
                                          kind="ExternalInput").ap()
    inputs["up_scale"] = nc.dram_tensor("up_scale", [IF], F32,
                                        kind="ExternalInput").ap()
    inputs["down_scale"] = nc.dram_tensor("down_scale", [D], F32,
                                          kind="ExternalInput").ap()
    cm.inputs = inputs

    x1fm = nc.dram_tensor("x1fm", [D, TOWN], F32, kind="ExternalOutput").ap()
    x0fm = nc.dram_tensor("x0fm", [D, TOWN], F32, kind="Internal").ap()

    cm.load_consts(with_ffn_scales=True)
    emit_ffn_weight_prep(cm)
    k.barrier()
    emit_tm_to_fm(cm, inputs["x_own"], x0fm, TOWN)
    k.barrier()

    r2 = emit_ffn(cm, x0fm, 0, x1fm)

    k.wait("sync", "fo0", r2["x_fo"][0])
    k.wait("sync", "fo1", r2["x_fo"][1])
    cm.k.finish()
    return nc


def build_FUSED(stop_after=None, dump_chunk=0, n_ffn1=4):
    nc = bass.Bass("TRN2", target_bir_lowering=False)
    cm = Common(nc)
    k, ar = cm.k, cm.ar

    inputs = {}
    inputs["x_full"] = nc.dram_tensor("x_full", [S, D], F32,
                                      kind="ExternalInput").ap()
    inputs["fr"] = nc.dram_tensor("fr", [S, DR // 2], F32,
                                  kind="ExternalInput").ap()
    inputs["Wq"] = nc.dram_tensor("Wq", [D, NH * DQK], F32,
                                  kind="ExternalInput").ap()
    inputs["Wkv_a"] = nc.dram_tensor("Wkv_a", [D, RANK + DR], F32,
                                     kind="ExternalInput").ap()
    inputs["kv_gamma"] = nc.dram_tensor("kv_gamma", [RANK], F32,
                                        kind="ExternalInput").ap()
    inputs["kv_beta"] = nc.dram_tensor("kv_beta", [RANK], F32,
                                       kind="ExternalInput").ap()
    inputs["Wkv_b"] = nc.dram_tensor("Wkv_b", [RANK, NH * (DN + DV)], F32,
                                     kind="ExternalInput").ap()
    inputs["Wout"] = nc.dram_tensor("Wout", [NH * DV, D], F32,
                                    kind="ExternalInput").ap()
    for nm, shp, dt in (("gate_w", [IF, D], I32), ("up_w", [IF, D], I32),
                        ("down_w", [D, IF], I32), ("gate_scale", [IF], F32),
                        ("up_scale", [IF], F32), ("down_scale", [D], F32)):
        inputs[nm] = nc.dram_tensor(nm, shp, dt, kind="ExternalInput").ap()
    cm.inputs = inputs

    out_tm = nc.dram_tensor("out_tm", [TOWN, D], F32,
                            kind="ExternalOutput").ap()
    x0fm = nc.dram_tensor("x0fm", [D, S], F32, kind="Internal").ap()
    x1fm = nc.dram_tensor("x1fm", [D, S], F32, kind="Internal").ap()
    ln2d = nc.dram_tensor("ln2d", [D, S], BF16, kind="Internal").ap()
    x2fm = nc.dram_tensor("x2fm", [D, TOWN], F32, kind="Internal").ap()
    x3fm = nc.dram_tensor("x3fm", [D, TOWN], F32, kind="Internal").ap()

    def _dump(src_fm):
        k.barrier()
        ov = out_tm.rearrange("a b -> (a b)").rearrange(
            "(c d) -> c d", c=D)
        r = k.dma("gpsimd", lambda eng: eng.dma_start(
            out=ov,
            in_=src_fm[:, dump_chunk * 1024:(dump_chunk + 1) * 1024]),
            "outd")
        k.wait("sync", "outd", r)
        cm.k.finish()
        return nc

    cm.load_consts(with_ffn_scales=True)
    cm.gcol, _ = ar.sb("gcol", [128, 4], F32)
    cm.bcol, _ = ar.sb("bcol", [128, 4], F32)
    emit_cols(cm, inputs["kv_gamma"], 4, cm.gcol)
    emit_cols(cm, inputs["kv_beta"], 4, cm.bcol)
    EXC = ("pcast0", "pcast1", "ptr0", "ptr1", "ptw0", "ptw1")
    emit_prep_cast(cm)

    ln1d = nc.dram_tensor("ln1d", [D, S], BF16, kind="Internal").ap()
    for c in range(4):
        emit_tm_to_fm(cm, inputs["x_full"], x0fm, 1024, t0=c * 1024)
        k.barrier(exclude=EXC)
    if stop_after == "x0":
        emit_prep_tile(cm)
        return _dump(x0fm)
    k.barrier(("sync", "vector", "scalar"), exclude=("pcast0", "pcast1"))
    mslab = ar.mark()
    emit_prep_tile(cm, phase="gu")
    k.wait("gpsimd", "t2fo", k.val("t2fo"))
    emit_ln_pipeline(cm, x0fm, [0, 1024, 2048, 3072], ln1d)
    emit_prep_tile(cm, phase="d")
    if stop_after == "lnd":
        return _dump(ln1d)
    k.wait("vector", "ptw0", cm.wprep_ptw[0])
    k.wait("vector", "ptw1", cm.wprep_ptw[1])
    ar.free_to(mslab)

    t0list = (n_ffn1 if isinstance(n_ffn1, (list, tuple))
              else [i * 1024 for i in range(n_ffn1)])
    for t0_ in t0list:
        emit_ffn(cm, x0fm, t0_, x1fm, ln_dram=ln1d)
        k.barrier()
    if stop_after == "x1":
        return _dump(x1fm)

    m_pre = ar.mark()
    attn_sb, _ = ar.sb("attn_sb", [128, 16 * 1024], BF16)
    m_attn = ar.mark()
    qn_sb, _ = ar.sb("qn_sb", [128, 16 * 1024], BF16)
    qr_sb, _ = ar.sb("qr_sb", [128, 8 * 1024], BF16)
    C2, _ = ar.sb("C2", [128, S], F32)
    S2, _ = ar.sb("S2", [128, S], F32)

    emit_rope_tables(cm, inputs["fr"], C2, S2)

    m_ln2 = ar.mark()
    lnres = emit_ln_pipeline(cm, x1fm, [1024, 2048, 3072, 0], ln2d,
                             last_sbuf=attn_sb, keep_mark=True)
    emit_qproj(cm, attn_sb, inputs["Wq"], C2, S2, qn_sb, qr_sb)
    k.wait("vector", "pe", k.val("pe"))
    ar.free_to(m_ln2)
    kvn_sb, _ = ar.sb("kvn_sb", [128, 4 * S], BF16)
    krope_sb, _ = ar.sb("krope_sb", [128, S], BF16)
    for e in ("sync", "gpsimd"):
        k.wait(e, "pe", k.val("pe"))
        k.wait(e, "ve", k.val("ve"))
        k.wait(e, "ac", k.val("ac"))
        k.wait(e, "lnst0", lnres["st0"])
        k.wait(e, "lnst1", lnres["st1"])
    emit_kv(cm, ln2d, inputs["Wkv_a"], cm.gcol, cm.bcol, C2, S2,
            kvn_sb, krope_sb)
    emit_attention(cm, inputs["Wkv_b"], kvn_sb, krope_sb, qn_sb, qr_sb,
                   attn_sb)
    ar.free_to(m_attn)
    k.barrier()
    emit_wout(cm, inputs["Wout"], attn_sb, x1fm, x2fm)
    ar.free_to(m_pre)
    k.barrier()
    emit_ffn(cm, x2fm, 0, x3fm)
    k.barrier()
    rout = emit_fm_to_tm_out(cm, x3fm, out_tm)
    k.wait("sync", "outd", rout)
    cm.k.finish()
    return nc



def _ffn_weight_inputs(inputs):
    return dict(
        gate_w=np.ascontiguousarray(inputs["L0_ffn_gate_w"]),
        up_w=np.ascontiguousarray(inputs["L0_ffn_up_w"]),
        down_w=np.ascontiguousarray(inputs["L0_ffn_down_w"]),
        gate_scale=np.ascontiguousarray(
            np.asarray(inputs["L0_ffn_gate_scale"], np.float32)),
        up_scale=np.ascontiguousarray(
            np.asarray(inputs["L0_ffn_up_scale"], np.float32)),
        down_scale=np.ascontiguousarray(
            np.asarray(inputs["L0_ffn_down_scale"], np.float32)),
    )


def kernel(**inputs):
    return kernel_fused(**inputs)


def kernel_fused(**inputs):
    x = np.asarray(inputs["L0_attn_hidden_states"], np.float32)
    wff = _ffn_weight_inputs(inputs)
    freqs = np.ascontiguousarray(
        np.asarray(inputs["L0_attn_freqs_cis"], np.float32))
    watt = dict(
        Wq=np.ascontiguousarray(np.asarray(inputs["L0_attn_Wq"], np.float32)),
        Wkv_a=np.ascontiguousarray(
            np.asarray(inputs["L0_attn_Wkv_a"], np.float32)),
        kv_gamma=np.ascontiguousarray(
            np.asarray(inputs["L0_attn_kv_gamma"], np.float32)),
        kv_beta=np.ascontiguousarray(
            np.asarray(inputs["L0_attn_kv_beta"], np.float32)),
        Wkv_b=np.ascontiguousarray(
            np.asarray(inputs["L0_attn_Wkv_b"], np.float32)),
        Wout=np.ascontiguousarray(
            np.asarray(inputs["L0_attn_Wout"], np.float32)),
    )
    nc = build_FUSED()
    in_maps = []
    for c in range(NC_):
        g, qc = c // 4, c % 4
        rows = (qc * 1024 + np.arange(S)) % S
        xr = np.ascontiguousarray(x[g][rows])
        fr = np.ascontiguousarray(freqs[rows % 2048])
        m = dict(wff)
        m.update(watt)
        m["x_full"] = xr
        m["fr"] = fr
        in_maps.append(m)
    res = run_bass_kernel_spmd(nc, in_maps, core_ids=list(range(NC_)))
    out = np.empty((B, S, D), np.float32)
    for c in range(NC_):
        g, qc = c // 4, c % 4
        out[g, qc * TOWN:(qc + 1) * TOWN] = res.results[c]["out_tm"]
    return out


def kernel_unfused(**inputs):
    x = np.ascontiguousarray(
        np.asarray(inputs["L0_attn_hidden_states"], np.float32)
    ).reshape(B * S, D)
    wff = _ffn_weight_inputs(inputs)

    nc1 = build_L1()
    in_maps = [dict(wff, x_own=np.ascontiguousarray(x[c * TOWN:(c + 1) * TOWN]))
               for c in range(NC_)]
    res1 = run_bass_kernel_spmd(nc1, in_maps, core_ids=list(range(NC_)))
    x1 = [np.asarray(res1.results[c]["x1fm"]) for c in range(NC_)]

    freqs = np.ascontiguousarray(
        np.asarray(inputs["L0_attn_freqs_cis"], np.float32))
    watt = dict(
        Wq=np.ascontiguousarray(np.asarray(inputs["L0_attn_Wq"], np.float32)),
        Wkv_a=np.ascontiguousarray(
            np.asarray(inputs["L0_attn_Wkv_a"], np.float32)),
        kv_gamma=np.ascontiguousarray(
            np.asarray(inputs["L0_attn_kv_gamma"], np.float32)),
        kv_beta=np.ascontiguousarray(
            np.asarray(inputs["L0_attn_kv_beta"], np.float32)),
        Wkv_b=np.ascontiguousarray(
            np.asarray(inputs["L0_attn_Wkv_b"], np.float32)),
        Wout=np.ascontiguousarray(
            np.asarray(inputs["L0_attn_Wout"], np.float32)),
    )
    nc2 = build_L2()
    in_maps2 = []
    for c in range(NC_):
        g, qc = c // 4, c % 4
        order = [g * 4 + ((qc + j) % 4) for j in range(4)]
        x1full = np.ascontiguousarray(
            np.concatenate([x1[o] for o in order], axis=1))
        pos = (qc * 1024 + np.arange(S)) % S
        fr = np.ascontiguousarray(freqs[pos % 2048])
        m = dict(wff)
        m.update(watt)
        m["x1fm"] = x1full
        m["fr"] = fr
        in_maps2.append(m)
    res2 = run_bass_kernel_spmd(nc2, in_maps2, core_ids=list(range(NC_)))
    out = np.empty((B, S, D), np.float32)
    for c in range(NC_):
        g, qc = c // 4, c % 4
        out[g, qc * TOWN:(qc + 1) * TOWN] = res2.results[c]["out_tm"]
    return out


# version 15
# speedup vs baseline: 1.0295x; 1.0295x over previous
import math

import numpy as np
import ml_dtypes

import concourse.bass as bass
import concourse.mybir as mybir
from concourse.bass_utils import run_bass_kernel_spmd

F32 = mybir.dt.float32
BF16 = mybir.dt.bfloat16
I32 = mybir.dt.int32
AF = mybir.ActivationFunctionType
ALU = mybir.AluOpType
AX = mybir.AxisListType
bfdt = ml_dtypes.bfloat16

B, S, D = 2, 4096, 2048
NH, DN, DR, DV, RANK = 16, 128, 64, 128, 512
DQK = DN + DR
SCALE = DQK ** -0.5
EPS = 1e-5
IF = 8192
NC_ = 8
TOWN = 1024
PI = math.pi

ENGINES = ("sync", "scalar", "vector", "tensor", "gpsimd")


class K:

    def __init__(self, nc):
        self.nc = nc
        self.q = {e: [] for e in ENGINES}
        self.sems = {}
        self.cnt = {}
        self.waited = {e: {} for e in ENGINES}

    def sem(self, name):
        self.sems[name] = self.nc.alloc_semaphore(name)
        self.cnt[name] = 0

    def wait(self, e, sem, val):
        if val is None or val <= 0:
            return
        if self.waited[e].get(sem, 0) >= val:
            return
        self.waited[e][sem] = val
        h = self.sems[sem]
        self.q[e].append(lambda eng, h=h, v=val: eng.wait_ge(h, v))

    SELF_SEM = {"vector": "ve", "scalar": "ac", "gpsimd": "po"}

    def do(self, e, fn, inc=None):
        ssem = self.SELF_SEM.get(e)
        if ssem is not None:
            self.wait(e, ssem, self.cnt[ssem])
            self.cnt[ssem] += 1
            h = self.sems[ssem]
            if inc is None:
                self.q[e].append(
                    lambda eng, fn=fn, h=h: fn(eng).then_inc(h, 1))
                return self.cnt[ssem]
            sem2, amt2 = inc
            assert sem2 == ssem and amt2 == 1, (e, inc)
            self.q[e].append(lambda eng, fn=fn, h=h: fn(eng).then_inc(h, 1))
            return self.cnt[ssem]
        if inc is None:
            self.q[e].append(fn)
            return None
        sem, amt = inc
        self.cnt[sem] += amt
        h = self.sems[sem]
        self.q[e].append(lambda eng, fn=fn, h=h, a=amt: fn(eng).then_inc(h, a))
        return self.cnt[sem]

    def dma(self, e, fn, stream):
        if stream not in self.sems:
            self.sem(stream)
        self.cnt[stream] += 16
        h = self.sems[stream]
        self.q[e].append(lambda eng, fn=fn, h=h: fn(eng).then_inc(h, 16))
        return self.cnt[stream]

    def val(self, sem):
        return self.cnt.get(sem, 0)

    def barrier(self, engines=ENGINES, exclude=()):
        for e in engines:
            for s, v in self.cnt.items():
                if s in exclude:
                    continue
                self.wait(e, s, v)

    def finish(self):
        with self.nc.Block() as block:
            block.sync(lambda eng: [f(eng) for f in self.q["sync"]])
            block.scalar(lambda eng: [f(eng) for f in self.q["scalar"]])
            block.vector(lambda eng: [f(eng) for f in self.q["vector"]])
            block.tensor(lambda eng: [f(eng) for f in self.q["tensor"]])
            block.gpsimd(lambda eng: [f(eng) for f in self.q["gpsimd"]])


class Arena:

    def __init__(self, nc):
        self.nc = nc
        self.stack = []
        self.uid = 0

    def _nm(self, name):
        self.uid += 1
        return f"{name}_{self.uid}"

    def sb(self, name, shape, dt):
        g = self.nc.sbuf_tensor(self._nm(name), shape, dt)
        h = g.__enter__()
        self.stack.append(g)
        return h, len(self.stack) - 1

    def ps(self, name, shape, dt=F32):
        g = self.nc.psum_tensor(self._nm(name), shape, dt)
        h = g.__enter__()
        self.stack.append(g)
        return h, len(self.stack) - 1

    def free_to(self, mark):
        while len(self.stack) > mark:
            g = self.stack.pop()
            g.__exit__(None, None, None)

    def mark(self):
        return len(self.stack)


def np_consts():
    eye_bf = np.eye(128, dtype=bfdt)
    eye_f32 = np.eye(128, dtype=np.float32)
    ones_col_bf = np.ones((128, 1), dtype=bfdt)
    ones_row_f32 = np.ones((1, 128), dtype=np.float32)
    p64 = np.zeros((64, 64), dtype=bfdt)
    for j in range(32):
        p64[2 * j, 2 * j + 1] = 1.0
        p64[2 * j + 1, 2 * j] = 1.0
    dup = np.zeros((32, 64), dtype=np.float32)
    dups = np.zeros((32, 64), dtype=np.float32)
    for j in range(32):
        dup[j, 2 * j] = 1.0
        dup[j, 2 * j + 1] = 1.0
        dups[j, 2 * j] = -1.0
        dups[j, 2 * j + 1] = 1.0
    p128 = np.zeros((128, 128), dtype=bfdt)
    for j in range(64):
        p128[2 * j, 2 * j + 1] = 1.0
        p128[2 * j + 1, 2 * j] = 1.0
    dup128 = np.zeros((32, 128), dtype=np.float32)
    dups128 = np.zeros((32, 128), dtype=np.float32)
    for j in range(32):
        for base in (0, 64):
            dup128[j, base + 2 * j] = 1.0
            dup128[j, base + 2 * j + 1] = 1.0
            dups128[j, base + 2 * j] = -1.0
            dups128[j, base + 2 * j + 1] = 1.0
    return dict(eye_bf=eye_bf, eye_f32=eye_f32, ones_col_bf=ones_col_bf,
                ones_row_f32=ones_row_f32, p64=p64, dup=dup, dups=dups,
                p128=p128, dup128=dup128, dups128=dups128)


class Common:

    def __init__(self, nc):
        self.nc = nc
        self.k = K(nc)
        self.ar = Arena(nc)
        for s in ("pe", "ve", "ac", "po"):
            self.k.sem(s)

    def load_consts(self, with_ffn_scales=True, with_attn=False):
        nc, k, ar = self.nc, self.k, self.ar
        cn = np_consts()
        self.c_eye_bf, _ = ar.sb("c_eye_bf", [128, 128], BF16)
        self.c_eye_f32, _ = ar.sb("c_eye_f32", [128, 128], F32)
        self.c_ones_col, _ = ar.sb("c_ones_col", [128, 1], BF16)
        self.c_ones_row, _ = ar.sb("c_ones_row", [1, 128], F32)
        self.c_p128, _ = ar.sb("c_p128", [128, 128], BF16)
        self.c_dup128, _ = ar.sb("c_dup128", [32, 128], F32)
        self.c_dups128, _ = ar.sb("c_dups128", [32, 128], F32)
        self.c_negC, _ = ar.sb("c_negC", [128, 1], F32)
        self.negc_ready = self.k.do(
            "vector", lambda eng: eng.memset(self.c_negC[:, :], -30.0))
        pairs = [
            (self.c_eye_bf, cn["eye_bf"]), (self.c_eye_f32, cn["eye_f32"]),
            (self.c_ones_col, cn["ones_col_bf"]),
            (self.c_ones_row, cn["ones_row_f32"]),
            (self.c_p128, cn["p128"]), (self.c_dup128, cn["dup128"]),
            (self.c_dups128, cn["dups128"]),
        ]
        for sb, arr in pairs:
            dh = nc.inline_tensor(np.ascontiguousarray(arr))
            self.k.dma("sync", lambda eng, sb=sb, dh=dh: eng.dma_start(
                out=sb[:, :], in_=dh.ap()), "c0")
        self.consts_ready = self.k.val("c0")
        if with_ffn_scales:
            self.gsc, _ = ar.sb("gsc", [128, 64], F32)
            self.usc, _ = ar.sb("usc", [128, 64], F32)
            self.dsc_half, _ = ar.sb("dsc_half", [128, 16], F32)
            m0 = ar.mark()
            tmp, _ = ar.sb("sc_tmp", [64, 128], F32)
            pscale, _ = ar.ps("sc_ps", [128, 64], F32)
            jobs = [("gate_scale", 64, self.gsc), ("up_scale", 64, self.usc),
                    ("down_scale", 16, self.dsc_half)]
            for name, n, dst in jobs:
                src = self.in_ap(name)
                k.wait("sync", "pe", self.k.val("pe"))
                ld = k.dma("sync", lambda eng, src=src, n=n: eng.dma_start(
                    out=tmp[0:n, :], in_=src.rearrange("(a p) -> a p", p=128)),
                    "c0")
                k.wait("tensor", "c0", ld)
                k.wait("tensor", "ve", self.k.val("ve"))
                mm = k.do("tensor", lambda eng, n=n: eng.matmul(
                    pscale[:, 0:n], tmp[0:n, :], self.c_eye_f32[0:n, 0:n],
                    start=True, stop=True), inc=("pe", 1))
                k.wait("vector", "pe", mm)
                k.do("vector", lambda eng, n=n, dst=dst: eng.tensor_copy(
                    dst[:, 0:n], pscale[:, 0:n]), inc=("ve", 1))
            k.do("vector", lambda eng: eng.tensor_scalar_mul(
                self.dsc_half[:, :], self.dsc_half[:, :], 0.5),
                inc=("ve", 1))
            self.dsc_ready = self.k.val("ve")
            ar.free_to(m0)
            self.k.barrier()
        else:
            self.dsc_ready = 0

    def in_ap(self, name):
        return self.inputs[name]



def emit_prep_cast(cm):
    nc, k = cm.nc, cm.k
    cm.wg_bf = nc.dram_tensor("wg_bf", [IF, D], BF16, kind="Internal").ap()
    cm.wu_bf = nc.dram_tensor("wu_bf", [IF, D], BF16, kind="Internal").ap()
    cm.wd_bf = nc.dram_tensor("wd_bf", [D, IF], BF16, kind="Internal").ap()
    cm.pcast_done = {}
    for (name, bf_out, rows) in (("gate_w", cm.wg_bf, IF),
                                 ("up_w", cm.wu_bf, IF),
                                 ("down_w", cm.wd_bf, D)):
        srcw = cm.in_ap(name)
        vals = []
        for hh in range(2):
            r0, r1 = hh * rows // 2, (hh + 1) * rows // 2
            k.dma("gpsimd", lambda eng, srcw=srcw, bf=bf_out, r0=r0, r1=r1:
                  eng.dma_start(out=bf[r0:r1, :], in_=srcw[r0:r1, :]),
                  f"pcast{hh}")
            vals.append(k.val(f"pcast{hh}"))
        cm.pcast_done[name] = tuple(vals)


def emit_prep_tile(cm, phase="all"):
    nc, k, ar = cm.nc, cm.k, cm.ar
    if phase in ("all", "gu"):
        cm.wgT = nc.dram_tensor("wgT", [64, 16, 128, 128], BF16,
                                kind="Internal").ap()
        cm.wuT = nc.dram_tensor("wuT", [64, 16, 128, 128], BF16,
                                kind="Internal").ap()
        cm.wdT = nc.dram_tensor("wdT", [16, 64, 128, 128], BF16,
                                kind="Internal").ap()
        cm._pt_slab = [ar.sb(f"wprep_s{jj}", [128, IF], BF16)[0]
                       for jj in range(2)]
        cm._pt_slab2 = [ar.sb(f"wprep_t{jj}", [128, IF], BF16)[0]
                        for jj in range(2)]
        cm._pt_state = ({}, {})
    slab, slab2 = cm._pt_slab, cm._pt_slab2
    sh_done, wr_done = cm._pt_state

    alljobs = [("g", kc) for kc in range(16)] + \
              [("u", kc) for kc in range(16)] + \
              [("d", ic) for ic in range(64)]
    if phase == "gu":
        tjobs = list(enumerate(alljobs))[:32]
    elif phase == "d":
        tjobs = list(enumerate(alljobs))[32:]
    else:
        tjobs = list(enumerate(alljobs))
    gate = {"g": "gate_w", "u": "up_w", "d": "down_w"}
    for j2, (which, idx) in tjobs:
        par = j2 % 2
        cols = IF if which in ("g", "u") else D
        for hh in range(2):
            k.wait("sync", f"pcast{hh}", cm.pcast_done[gate[which]][hh])
        if j2 >= 2:
            k.wait("sync", "ve", sh_done[j2 - 2])
        bf = {"g": cm.wg_bf, "u": cm.wu_bf, "d": cm.wd_bf}[which]
        t = k.dma("sync", lambda eng, bf=bf, idx=idx, par=par, cols=cols:
                  eng.dma_start_transpose(
                      slab[par][:, 0:cols],
                      bf[:, idx * 128:(idx + 1) * 128]),
                  f"ptr{par}")
        k.wait("vector", f"ptr{par}", t)
        if j2 >= 2:
            k.wait("vector", f"ptw{par}", wr_done[j2 - 2])
        sh = k.do("vector", lambda eng, par=par, cols=cols: eng.tensor_scalar(
            slab2[par][:, 0:cols], slab[par][:, 0:cols], -128.0, None,
            ALU.add))
        sh_done[j2] = sh
        k.wait("scalar", "ve", sh)
        if which in ("g", "u"):
            dst = cm.wgT if which == "g" else cm.wuT
            w = k.dma("scalar", lambda eng, dst=dst, idx=idx, par=par:
                      eng.dma_start(
                          out=dst[:, idx, :, :].rearrange("m h i -> h m i"),
                          in_=slab2[par][:, 0:IF].rearrange(
                              "h (m i) -> h m i", m=64)),
                      f"ptw{par}")
        else:
            w = k.dma("scalar", lambda eng, idx=idx, par=par:
                      eng.dma_start(
                          out=cm.wdT[:, idx, :, :].rearrange("d i c -> i d c"),
                          in_=slab2[par][:, 0:D].rearrange(
                              "i (d c) -> i d c", d=16)),
                      f"ptw{par}")
        wr_done[j2] = w
        if j2 == 31:
            cm.wprep_gu = (k.val("ptw0"), k.val("ptw1"))
    if phase in ("all", "d"):
        cm.wprep_ptw = (k.val("ptw0"), k.val("ptw1"))


def emit_prep_pe(cm, phase):
    nc, k, ar = cm.nc, cm.k, cm.ar
    if phase == "gu":
        cm.wgT = nc.dram_tensor("wgT", [64, 16, 128, 128], BF16,
                                kind="Internal").ap()
        cm.wuT = nc.dram_tensor("wuT", [64, 16, 128, 128], BF16,
                                kind="Internal").ap()
        cm.wdT = nc.dram_tensor("wdT", [16, 64, 128, 128], BF16,
                                kind="Internal").ap()
        jobs = []
        for b in range(64):
            jobs.append(("g", b))
            jobs.append(("u", b))
        cm.wprep_gu_m = []
    else:
        jobs = [("d", j) for j in range(64)]
    m0 = ar.mark()
    if phase == "gu":
        lw = [ar.sb(f"pp_l{j}", [128, 2048], BF16)[0] for j in range(2)]
        tb = [ar.sb(f"pp_t{j}", [128, 2048], BF16)[0] for j in range(2)]
        cm._pp_bufs = (lw, tb)
        seed_pe, seed_ac = k.val("pe"), k.val("ac")
        seed_wr = (0, 0)
    else:
        lw, tb = cm._pp_bufs
        seed_pe, seed_ac = cm._pp_pe_final, cm._pp_ac_final
        seed_wr = cm.wprep_gu
    mps = ar.mark()
    pst = [ar.ps(f"pp_p{phase}{j}", [128, 2048], F32)[0] for j in range(2)]
    srcs = {"g": cm.in_ap("gate_w"), "u": cm.in_ap("up_w"),
            "d": cm.in_ap("down_w")}
    lds, mm_done, ev_done, wr_done = {}, {}, {}, {}
    k.wait("tensor", "c0", cm.consts_ready)
    if phase == "d":
        k.wait("tensor", "ac", k.val("ac"))
    for j2, (which, idx) in enumerate(jobs):
        par = j2 % 2
        if which == "d":
            r0, c0 = (idx // 4) * 128, (idx % 4) * 2048
        else:
            r0, c0 = idx * 128, 0
        if j2 >= 2:
            k.wait("gpsimd", "pe", mm_done[j2 - 2])
        else:
            k.wait("gpsimd", "pe", seed_pe)
            k.wait("gpsimd", "ac", seed_ac)
        lds[j2] = k.dma("gpsimd", lambda eng, which=which, r0=r0, c0=c0,
                        par=par: eng.dma_start(
                            out=lw[par][:, :],
                            in_=srcs[which][r0:r0 + 128, c0:c0 + 2048]),
                        f"pls{par}")
        k.wait("tensor", f"pls{par}", lds[j2])
        if j2 >= 2:
            k.wait("tensor", "ac", ev_done[j2 - 2])
        for kc in range(16):
            k.do("tensor", lambda eng, kc=kc, par=par: eng.matmul(
                pst[par][:, kc * 128:(kc + 1) * 128],
                lw[par][:, kc * 128:(kc + 1) * 128],
                cm.c_eye_bf[:, :], start=True, stop=True),
                inc=("pe", 1) if kc == 15 else None)
        mm_done[j2] = k.val("pe")
        k.wait("scalar", "pe", mm_done[j2])
        if j2 >= 2:
            k.wait("scalar", f"ptw{par}", wr_done[j2 - 2])
        else:
            k.wait("scalar", f"ptw{par}", seed_wr[par])
        for q in range(4):
            k.do("scalar", lambda eng, q=q, par=par: eng.activation(
                tb[par][:, q * 512:(q + 1) * 512],
                pst[par][:, q * 512:(q + 1) * 512],
                AF.Copy, bias=-128.0),
                inc=("ac", 1) if q == 3 else None)
        ev_done[j2] = k.val("ac")
        k.wait("sync", "ac", ev_done[j2])
        if which in ("g", "u"):
            dst = cm.wgT if which == "g" else cm.wuT
            wr_done[j2] = k.dma("sync", lambda eng, dst=dst, idx=idx,
                                par=par: eng.dma_start(
                                    out=dst[idx].rearrange("a d h -> d a h"),
                                    in_=tb[par][:, :].rearrange(
                                        "d (a h) -> d a h", a=16)),
                                f"ptw{par}")
        else:
            wr_done[j2] = k.dma("sync", lambda eng, idx=idx, par=par:
                                eng.dma_start(
                                    out=cm.wdT[idx // 4,
                                               (idx % 4) * 16:
                                               (idx % 4 + 1) * 16].rearrange(
                                                   "m h c -> h m c"),
                                    in_=tb[par][:, :].rearrange(
                                        "h (m c) -> h m c", m=16)),
                                f"ptw{par}")
        if phase == "gu" and which == "u":
            cm.wprep_gu_m.append((k.val("ptw0"), k.val("ptw1")))
    if phase == "gu":
        cm.wprep_gu = (k.val("ptw0"), k.val("ptw1"))
        cm._pp_pe_final = k.val("pe")
        cm._pp_ac_final = k.val("ac")
    else:
        cm.wprep_ptw = (k.val("ptw0"), k.val("ptw1"))
    ar.free_to(mps)
    return m0


def emit_ffn_weight_prep(cm):
    m0 = cm.ar.mark()
    emit_prep_cast(cm)
    emit_prep_tile(cm)
    cm.k.barrier()
    cm.ar.free_to(m0)



def emit_tm_to_fm(cm, src_tm, dst_fm, T, t0=0):
    nc, k, ar = cm.nc, cm.k, cm.ar
    assert T % 128 == 0
    nt = T // 128
    m0 = ar.mark()
    xt = [ar.sb(f"t2f_x{t}", [128, D], F32)[0] for t in range(nt)]
    xo = [ar.sb(f"t2f_o{d}", [128, T], F32)[0] for d in range(16)]
    pt, _ = ar.ps("t2f_ps", [128, 512], F32)
    pt2, _ = ar.ps("t2f_ps2", [128, 512], F32)

    for t in range(nt):
        k.dma("sync", lambda eng, t=t: eng.dma_start(
            out=xt[t][:, :],
            in_=src_tm[t0 + t * 128:t0 + (t + 1) * 128, :]), "t2f")
    ld_all = k.val("t2f")
    evac_cnt = {}
    for d in range(16):
        for t in range(nt):
            par = (d * nt + t) // 4 % 2
            ps = pt if par == 0 else pt2
            k.wait("tensor", "t2f", ld_all)
            grp = (d * nt + t) // 4
            if grp >= 2:
                k.wait("tensor", "ve", evac_cnt.get(grp - 2, 0))
            k.do("tensor", lambda eng, d=d, t=t, ps=ps: eng.matmul(
                ps[:, (t % 4) * 128:(t % 4 + 1) * 128],
                xt[t][:, d * 128:(d + 1) * 128],
                cm.c_eye_f32[:, :], start=True, stop=True),
                inc=("pe", 1) if (d * nt + t) % 4 == 3 or t == nt - 1 else None)
            if (d * nt + t) % 4 == 3 or t == nt - 1:
                pedone = k.val("pe")
                k.wait("vector", "pe", pedone)
                lo = (t // 4) * 4
                nn = (t - lo + 1) * 128
                ev = k.do("vector", lambda eng, d=d, lo=lo, nn=nn, ps=ps:
                          eng.tensor_copy(
                              xo[d][:, lo * 128:lo * 128 + nn],
                              ps[:, 0:nn]),
                          inc=("ve", 1))
                evac_cnt[grp] = ev
    for d in range(16):
        k.wait("scalar", "ve", k.val("ve"))
        k.dma("scalar", lambda eng, d=d: eng.dma_start(
            out=dst_fm[d * 128:(d + 1) * 128, t0:t0 + T],
            in_=xo[d][:, :]), "t2fo")
    out_dA = k.val("t2fo")
    ar.free_to(m0)
    return ("t2fo", out_dA)



def emit_ln_pipeline(cm, x_fm, t0s, also_dram, last_sbuf=None,
                     keep_mark=False):
    nc, k, ar = cm.nc, cm.k, cm.ar
    T = 1024
    m0 = ar.mark()
    xbb = [ar.sb(f"lp_x{j}", [128, T], BF16)[0] for j in range(2)]
    xq = [ar.sb(f"lp_q{j}", [128, T], BF16)[0] for j in range(2)]
    xab = [ar.sb(f"lp_a{j}", [128, T], BF16)[0] for j in range(2)]
    stage = [ar.sb(f"lp_st{j}", [128, T], BF16)[0] for j in range(2)]
    rowbuf, _ = ar.sb("lp_rows", [1, 3 * T], F32)
    abc, _ = ar.sb("lp_abc", [128, T], BF16)
    bbc, _ = ar.sb("lp_bbc", [128, T], BF16)
    mpsum = ar.mark()
    prow, _ = ar.ps("lp_prow", [1, 2 * T], F32)
    pbc, _ = ar.ps("lp_pbc", [128, T], F32)
    pbc2, _ = ar.ps("lp_pbc2", [128, T], F32)
    mu = rowbuf[:, 0:T]
    arow = rowbuf[:, T:2 * T]
    brow = rowbuf[:, 2 * T:3 * T]

    k.wait("tensor", "c0", cm.consts_ready)
    st = dict(stats_pe=k.val("pe"), sq_ac=k.val("ac"), ve=k.val("ve"),
              prow_ve=k.val("ve"), bc_pe=k.val("pe"), evac_ve=k.val("ve"),
              app_ve=k.val("ve"), st0=k.val("lnst0"), st1=k.val("lnst1"))

    for ci, t0 in enumerate(t0s):
        is_last = (ci == len(t0s) - 1) and last_sbuf is not None
        p_stats, p_sq, p_ve = st["stats_pe"], st["sq_ac"], st["ve"]
        p_prow, p_bc, p_evac = st["prow_ve"], st["bc_pe"], st["evac_ve"]
        p_app, p_st0, p_st1 = st["app_ve"], st["st0"], st["st1"]
        lds, lds2, sq_ac, mmq = {}, {}, {}, {}

        def load_x(kc):
            par = kc % 2
            if kc >= 2:
                k.wait("gpsimd", "pe", mmq[kc - 2])
                k.wait("gpsimd", "ac", sq_ac[kc - 2])
            else:
                k.wait("gpsimd", "pe", p_stats)
                k.wait("gpsimd", "ac", p_sq)
                k.wait("gpsimd", "ve", p_ve)
            lds[kc] = k.dma("gpsimd", lambda eng, kc=kc, par=par, t0=t0:
                            eng.dma_start(
                                out=xbb[par][:, :],
                                in_=x_fm[kc * 128:(kc + 1) * 128,
                                         t0:t0 + T]), f"lpx{par}")

        def load_x2(kc):
            par = kc % 2
            if kc >= 2:
                k.wait("gpsimd", "ve", app_ve[kc - 2])
            else:
                k.wait("gpsimd", "ve", p_app)
            lds2[kc] = k.dma("gpsimd", lambda eng, kc=kc, par=par, t0=t0:
                             eng.dma_start(
                                 out=xab[par][:, :],
                                 in_=x_fm[kc * 128:(kc + 1) * 128,
                                          t0:t0 + T]), f"lpa{par}")

        load_x(0)
        load_x(1)
        for kc in range(16):
            par = kc % 2
            k.wait("scalar", f"lpx{par}", lds[kc])
            if kc >= 2:
                k.wait("scalar", "pe", mmq[kc - 2])
            else:
                k.wait("scalar", "pe", p_stats)
            sq_ac[kc] = k.do("scalar", lambda eng, kc=kc, par=par:
                             eng.activation(xq[par][:, :], xbb[par][:, :],
                                            AF.Square), inc=("ac", 1))
            k.wait("tensor", f"lpx{par}", lds[kc])
            k.wait("tensor", "ac", sq_ac[kc])
            if kc == 0:
                k.wait("tensor", "ve", p_prow)
            for nb in range(T // 512):
                k.do("tensor", lambda eng, nb=nb, kc=kc, par=par: eng.matmul(
                    prow[:, nb * 512:(nb + 1) * 512],
                    cm.c_ones_col[:, :],
                    xbb[par][:, nb * 512:(nb + 1) * 512],
                    start=(kc == 0), stop=(kc == 15)), inc=None)
                k.do("tensor", lambda eng, nb=nb, kc=kc, par=par: eng.matmul(
                    prow[:, T + nb * 512:T + (nb + 1) * 512],
                    cm.c_ones_col[:, :],
                    xq[par][:, nb * 512:(nb + 1) * 512],
                    start=(kc == 0), stop=(kc == 15)),
                    inc=("pe", 1))
            mmq[kc] = k.val("pe")
            if kc < 14:
                load_x(kc + 2)
            elif kc == 14:
                load_x2(0)
            else:
                load_x2(1)
        stats_pe = k.val("pe")
        k.wait("vector", "pe", stats_pe)
        k.do("vector", lambda eng: eng.tensor_scalar_mul(
            mu, prow[:, 0:T], 1.0 / D))
        k.do("vector", lambda eng: eng.tensor_scalar_mul(
            arow, prow[:, T:2 * T], 1.0 / D), inc=("ve", 1))
        prow_ve = k.val("ve")
        k.do("vector", lambda eng: eng.scalar_tensor_tensor(
            brow, mu, -1.0, mu, ALU.mult, ALU.mult))
        k.do("vector", lambda eng: eng.tensor_add(arow, arow, brow))
        vq = k.do("vector", lambda eng: eng.tensor_scalar_add(
            arow, arow, EPS), inc=("ve", 1))
        k.wait("scalar", "ve", vq)
        k.wait("scalar", "pe", p_bc)
        k.do("scalar", lambda eng: eng.activation(arow, arow, AF.Ln))
        rstd_ac = k.do("scalar", lambda eng: eng.activation(
            arow, arow, AF.Exp, scale=-0.5), inc=("ac", 1))
        k.wait("vector", "ac", rstd_ac)
        k.do("vector", lambda eng: eng.scalar_tensor_tensor(
            brow, mu, -1.0, arow, ALU.mult, ALU.mult), inc=("ve", 1))
        rowdone = k.val("ve")
        k.wait("tensor", "ve", rowdone)
        for nb in range(T // 512):
            k.do("tensor", lambda eng, nb=nb: eng.matmul(
                pbc[:, nb * 512:(nb + 1) * 512], cm.c_ones_row[:, :],
                arow[:, nb * 512:(nb + 1) * 512], start=True, stop=True))
            k.do("tensor", lambda eng, nb=nb: eng.matmul(
                pbc2[:, nb * 512:(nb + 1) * 512], cm.c_ones_row[:, :],
                brow[:, nb * 512:(nb + 1) * 512], start=True, stop=True),
                inc=("pe", 1) if nb == T // 512 - 1 else None)
        bc_pe = k.val("pe")
        k.wait("vector", "pe", bc_pe)
        k.do("vector", lambda eng: eng.tensor_copy(abc[:, :], pbc[:, :]))
        k.do("vector", lambda eng: eng.tensor_copy(bbc[:, :], pbc2[:, :]),
             inc=("ve", 1))
        evac_ve = k.val("ve")
        app_ve, st_dma = {}, {}
        for kc in range(16):
            par = kc % 2
            k.wait("vector", f"lpa{par}", lds2[kc])
            if is_last:
                dst = last_sbuf[:, kc * T:(kc + 1) * T]
            else:
                if kc >= 2:
                    k.wait("vector", f"lnst{par}", st_dma[kc - 2])
                elif par == 0:
                    k.wait("vector", "lnst0", p_st0)
                else:
                    k.wait("vector", "lnst1", p_st1)
                dst = stage[par][:, :]
            k.do("vector", lambda eng, par=par: eng.tensor_mul(
                xab[par][:, :], xab[par][:, :], abc[:, :]))
            app_ve[kc] = k.do("vector", lambda eng, dst=dst, par=par:
                              eng.tensor_tensor(
                                  dst, xab[par][:, :], bbc[:, :], ALU.add),
                              inc=("ve", 1))
            if kc < 14:
                load_x2(kc + 2)
            k.wait("scalar", "ve", app_ve[kc])
            srcap = (last_sbuf[:, kc * T:(kc + 1) * T] if is_last
                     else stage[par][:, :])
            st_dma[kc] = k.dma("scalar", lambda eng, kc=kc, srcap=srcap,
                               t0=t0: eng.dma_start(
                                   out=also_dram[kc * 128:(kc + 1) * 128,
                                                 t0:t0 + T],
                                   in_=srcap), f"lnst{par}")
        st = dict(stats_pe=stats_pe, sq_ac=sq_ac[15], ve=k.val("ve"),
                  prow_ve=prow_ve, bc_pe=bc_pe, evac_ve=evac_ve,
                  app_ve=app_ve[15], st0=k.val("lnst0"), st1=k.val("lnst1"))
    ar.free_to(mpsum if keep_mark else m0)
    return dict(ln_ve=st["app_ve"], st0=st["st0"], st1=st["st1"],
                pe=st["bc_pe"])


def emit_ln_chunk(cm, x_fm, t0, lnsb, also_dram=None, ldq="sync"):
    nc, k, ar = cm.nc, cm.k, cm.ar
    T = 1024
    m0 = ar.mark()
    stage = None
    if lnsb is None:
        assert also_dram is not None
        stage = [ar.sb(f"ln_st{j}", [128, T], BF16)[0] for j in range(2)]
    xs = [ar.sb(f"ln_x{j}", [128, T], F32)[0] for j in range(2)]
    xb = [ar.sb(f"ln_b{j}", [128, T], BF16)[0] for j in range(2)]
    xq = [ar.sb(f"ln_q{j}", [128, T], BF16)[0] for j in range(2)]
    rowbuf, _ = ar.sb("ln_rows", [1, 2 * T], F32)
    abc, _ = ar.sb("ln_abc", [128, T], F32)
    bbc, _ = ar.sb("ln_bbc", [128, T], F32)
    prow, _ = ar.ps("ln_prow", [1, 2 * T], F32)
    pbc, _ = ar.ps("ln_pbc", [128, T], F32)
    pbc2, _ = ar.ps("ln_pbc2", [128, T], F32)

    mu = rowbuf[:, 0:T]
    arow = rowbuf[:, T:2 * T]
    msq = arow
    var = arow
    sd = arow
    brow = abc[0:1, 0:T]

    k.wait("tensor", "c0", cm.consts_ready)
    lds, cast_ve, sq_ac, mmq = {}, {}, {}, {}

    def load_x(kc):
        par = kc % 2
        if kc >= 2:
            k.wait(ldq, "ve", cast_ve[kc - 2])
        lds[kc] = k.dma(ldq, lambda eng, kc=kc, par=par: eng.dma_start(
            out=xs[par][:, :],
            in_=x_fm[kc * 128:(kc + 1) * 128, t0:t0 + T]), f"lnx{par}")

    load_x(0)
    load_x(1)
    for kc in range(16):
        par = kc % 2
        k.wait("vector", f"lnx{par}", lds[kc])
        if kc >= 2:
            k.wait("vector", "pe", mmq[kc - 2])
        cast_ve[kc] = k.do("vector", lambda eng, kc=kc, par=par:
                           eng.tensor_copy(xb[par][:, :], xs[par][:, :]),
                           inc=("ve", 1))
        if kc < 14:
            load_x(kc + 2)
        k.wait("scalar", "ve", cast_ve[kc])
        if kc >= 2:
            k.wait("scalar", "pe", mmq[kc - 2])
        sq_ac[kc] = k.do("scalar", lambda eng, kc=kc, par=par: eng.activation(
            xq[par][:, :], xb[par][:, :], AF.Square), inc=("ac", 1))
        k.wait("tensor", "ac", sq_ac[kc])
        for nb in range(T // 512):
            k.do("tensor", lambda eng, nb=nb, kc=kc, par=par: eng.matmul(
                prow[:, nb * 512:(nb + 1) * 512],
                cm.c_ones_col[:, :],
                xb[par][:, nb * 512:(nb + 1) * 512],
                start=(kc == 0), stop=(kc == 15)), inc=None)
            k.do("tensor", lambda eng, nb=nb, kc=kc, par=par: eng.matmul(
                prow[:, T + nb * 512:T + (nb + 1) * 512],
                cm.c_ones_col[:, :],
                xq[par][:, nb * 512:(nb + 1) * 512],
                start=(kc == 0), stop=(kc == 15)),
                inc=("pe", 1) if nb == T // 512 - 1 else None)
        mmq[kc] = k.val("pe")
    stats_pe = k.val("pe")
    k.wait("vector", "pe", stats_pe)
    k.do("vector", lambda eng: eng.tensor_scalar_mul(mu, prow[:, 0:T], 1.0 / D))
    k.do("vector", lambda eng: eng.tensor_scalar_mul(
        msq, prow[:, T:2 * T], 1.0 / D))
    k.do("vector", lambda eng: eng.scalar_tensor_tensor(
        brow, mu, -1.0, mu, ALU.mult, ALU.mult))
    k.do("vector", lambda eng: eng.tensor_add(var, msq, brow))
    k.do("vector", lambda eng: eng.tensor_scalar_add(var, var, EPS),
         inc=("ve", 1))
    k.wait("scalar", "ve", k.val("ve"))
    k.do("scalar", lambda eng: eng.activation(sd, var, AF.Ln))
    k.do("scalar", lambda eng: eng.activation(arow, sd, AF.Exp, scale=-0.5),
         inc=("ac", 1))
    k.wait("vector", "ac", k.val("ac"))
    k.do("vector", lambda eng: eng.scalar_tensor_tensor(
        brow, mu, -1.0, arow, ALU.mult, ALU.mult), inc=("ve", 1))
    rowdone = k.val("ve")
    k.wait("tensor", "ve", rowdone)
    for nb in range(T // 512):
        k.do("tensor", lambda eng, nb=nb: eng.matmul(
            pbc[:, nb * 512:(nb + 1) * 512], cm.c_ones_row[:, :],
            arow[:, nb * 512:(nb + 1) * 512], start=True, stop=True))
        k.do("tensor", lambda eng, nb=nb: eng.matmul(
            pbc2[:, nb * 512:(nb + 1) * 512], cm.c_ones_row[:, :],
            brow[:, nb * 512:(nb + 1) * 512], start=True, stop=True),
            inc=("pe", 1) if nb == T // 512 - 1 else None)
    bc_pe = k.val("pe")
    k.wait("vector", "pe", bc_pe)
    k.do("vector", lambda eng: eng.tensor_copy(abc[:, :], pbc[:, :]))
    k.do("vector", lambda eng: eng.tensor_copy(bbc[:, :], pbc2[:, :]),
         inc=("ve", 1))
    bc_ve = k.val("ve")
    lds2, app_ve = {}, {}

    def load_x2(kc):
        par = kc % 2
        if kc >= 2:
            k.wait(ldq, "ve", app_ve[kc - 2])
        lds2[kc] = k.dma(ldq, lambda eng, kc=kc, par=par: eng.dma_start(
            out=xs[par][:, :],
            in_=x_fm[kc * 128:(kc + 1) * 128, t0:t0 + T]), f"lnx{par}")

    load_x2(0)
    load_x2(1)
    st_dma = {}
    for kc in range(16):
        par = kc % 2
        k.wait("vector", f"lnx{par}", lds2[kc])
        k.wait("vector", "ve", bc_ve)
        k.do("vector", lambda eng, kc=kc, par=par: eng.tensor_mul(
            xs[par][:, :], xs[par][:, :], abc[:, :]))
        if stage is not None:
            if kc >= 2:
                k.wait("vector", f"lnst{par}", st_dma[kc - 2])
            dst = stage[par][:, :]
        else:
            dst = lnsb[:, kc * T:(kc + 1) * T]
        app_ve[kc] = k.do("vector", lambda eng, dst=dst, par=par:
                          eng.tensor_tensor(
                              dst, xs[par][:, :],
                              bbc[:, :], ALU.add))
        if kc < 14:
            load_x2(kc + 2)
        if also_dram is not None:
            k.wait("scalar", "ve", app_ve[kc])
            srcap = stage[par][:, :] if stage is not None \
                else lnsb[:, kc * T:(kc + 1) * T]
            st_dma[kc] = k.dma("scalar", lambda eng, kc=kc, srcap=srcap,
                               t0=t0: eng.dma_start(
                                   out=also_dram[kc * 128:(kc + 1) * 128,
                                                 t0:t0 + T],
                                   in_=srcap), f"lnst{par}")
    res = dict(ln_ve=k.val("ve"))
    ar.free_to(m0)
    return res



def emit_ffn(cm, x_fm, t0, xout_fm, dbg_ln=None, ln_dram=None,
             gu_gate=None, mid_emit=None):
    nc, k, ar = cm.nc, cm.k, cm.ar
    T = 1024
    m0 = ar.mark()
    hsb, _ = ar.sb("ffn_h", [128, 64 * T], BF16)
    mln = ar.mark()
    lnsb, _ = ar.sb("ffn_ln", [128, 16 * T], BF16)
    if ln_dram is None:
        lnres = emit_ln_chunk(cm, x_fm, t0, lnsb, also_dram=dbg_ln)
        ln_ready_ve = lnres["ln_ve"]
        k.barrier()
    else:
        k.barrier(("sync",), exclude=("pcast0", "pcast1", "ptr0", "ptr1",
                                      "ptw0", "ptw1"))
        for kc in range(16):
            k.dma("sync", lambda eng, kc=kc: eng.dma_start(
                out=lnsb[:, kc * T:(kc + 1) * T],
                in_=ln_dram[kc * 128:(kc + 1) * 128, t0:t0 + T]), "fl")
        fl_all = k.val("fl")
        k.wait("tensor", "fl", fl_all)
        ln_ready_ve = 0
    wsg = [ar.sb(f"ffn_wg{j}", [128, 16 * 128], BF16)[0] for j in range(2)]
    wsu = [ar.sb(f"ffn_wu{j}", [128, 16 * 128], BF16)[0] for j in range(2)]
    sg = [ar.sb(f"ffn_sg{j}", [128, T], BF16)[0] for j in range(2)]
    su = [ar.sb(f"ffn_su{j}", [128, T], BF16)[0] for j in range(2)]
    pg = [ar.ps(f"ffn_pg{j}", [128, T], F32)[0] for j in range(2)]
    pu = [ar.ps(f"ffn_pu{j}", [128, T], F32)[0] for j in range(2)]

    k.wait("tensor", "ve", ln_ready_ve)
    k.wait("tensor", "ve", cm.dsc_ready)
    if gu_gate is None:
        k.wait("sync", "ptw0", cm.wprep_gu[0])
        k.wait("sync", "ptw1", cm.wprep_gu[1])

    wld, mm_done, act_done, h_done = {}, {}, {}, {}

    def load_w(m):
        par = m % 2
        if m >= 2:
            k.wait("sync", "pe", mm_done[m - 2])
        if gu_gate is not None:
            k.wait("sync", "ptw0", gu_gate[m][0])
            k.wait("sync", "ptw1", gu_gate[m][1])
        k.dma("sync", lambda eng, m=m, par=par: eng.dma_start(
            out=wsg[par][:, :].rearrange("p (a f) -> p a f", a=16),
            in_=cm.wgT[m].rearrange("a h i -> h a i")), f"fw{par}")
        wld[m] = k.dma("sync", lambda eng, m=m, par=par: eng.dma_start(
            out=wsu[par][:, :].rearrange("p (a f) -> p a f", a=16),
            in_=cm.wuT[m].rearrange("a h i -> h a i")), f"fw{par}")

    load_w(0)
    load_w(1)
    for m in range(64):
        par = m % 2
        k.wait("tensor", f"fw{par}", wld[m])
        if m >= 2:
            k.wait("tensor", "ac", act_done[m - 2])
        for which in range(2):
            ps = pg[par] if which == 0 else pu[par]
            ws = wsg[par] if which == 0 else wsu[par]
            for half in range(2):
                c0 = half * 512
                for kc in range(16):
                    last = (which == 1 and half == 1 and kc == 15)
                    k.do("tensor", lambda eng, ps=ps, ws=ws, kc=kc, c0=c0:
                         eng.matmul(ps[:, c0:c0 + 512],
                                    ws[:, kc * 128:(kc + 1) * 128],
                                    lnsb[:, kc * T + c0:kc * T + c0 + 512],
                                    start=(kc == 0), stop=(kc == 15)),
                         inc=("pe", 1) if last else None)
        mm_done[m] = k.val("pe")
        if m < 62:
            load_w(m + 2)
        k.wait("scalar", "pe", mm_done[m])
        if m >= 2:
            k.wait("scalar", "ve", h_done[m - 2])
        k.do("scalar", lambda eng, m=m, par=par: eng.activation(
            sg[par][:, :], pg[par][:, :], AF.Silu,
            scale=cm.gsc[:, m:m + 1]))
        act_done[m] = k.do("scalar", lambda eng, m=m, par=par: eng.activation(
            su[par][:, :], pu[par][:, :], AF.Copy,
            scale=cm.usc[:, m:m + 1]), inc=("ac", 1))
        k.wait("vector", "ac", act_done[m])
        h_done[m] = k.do("vector", lambda eng, m=m, par=par: eng.tensor_mul(
            hsb[:, m * T:(m + 1) * T], sg[par][:, :], su[par][:, :]),
            inc=("ve", 1))
    h_all = k.val("ve")

    ar.free_to(mln)
    if mid_emit is not None:
        mid_emit()
    wsd = [ar.sb(f"ffn_wd{j}", [128, 64 * 128], BF16)[0] for j in range(2)]
    xr = [ar.sb(f"ffn_xr{j}", [128, T], F32)[0] for j in range(2)]
    xo = [ar.sb(f"ffn_xo{j}", [128, T], F32)[0] for j in range(2)]
    pd = [ar.ps(f"ffn_pd{j}", [128, T], F32)[0] for j in range(2)]

    dld, xld, dmm_done, dev_done, dst_done = {}, {}, {}, {}, {}

    def load_d(d):
        par = d % 2
        k.wait("sync", "ptw0", cm.wprep_ptw[0])
        k.wait("sync", "ptw1", cm.wprep_ptw[1])
        if d < 2:
            k.wait("scalar", "ptw0", cm.wprep_ptw[0])
            k.wait("scalar", "ptw1", cm.wprep_ptw[1])
        if d >= 2:
            k.wait("sync", "pe", dmm_done[d - 2])
            k.wait("scalar", "ve", dev_done[d - 2])
        else:
            k.wait("sync", "pe", mm_done[63])
            k.wait("sync", "ve", h_all)
            k.wait("scalar", "ve", h_all)
        dld[d] = k.dma("sync", lambda eng, d=d, par=par: eng.dma_start(
            out=wsd[par][:, :].rearrange("p (a f) -> p a f", a=64),
            in_=cm.wdT[d].rearrange("m i c -> i m c")), f"fd{par}")
        xld[d] = k.dma("scalar", lambda eng, d=d, par=par: eng.dma_start(
            out=xr[par][:, :],
            in_=x_fm[d * 128:(d + 1) * 128, t0:t0 + T]), f"fx{par}")

    load_d(0)
    load_d(1)
    k.wait("tensor", "ve", h_all)
    k.wait("tensor", "ac", k.val("ac"))
    for d in range(16):
        par = d % 2
        k.wait("tensor", f"fd{par}", dld[d])
        if d >= 2:
            k.wait("tensor", "ve", dev_done[d - 2])
        for half in range(2):
            c0 = half * 512
            for m in range(64):
                last = (half == 1 and m == 63)
                k.do("tensor", lambda eng, d=d, par=par, m=m, c0=c0:
                     eng.matmul(pd[par][:, c0:c0 + 512],
                                wsd[par][:, m * 128:(m + 1) * 128],
                                hsb[:, m * T + c0:m * T + c0 + 512],
                                start=(m == 0), stop=(m == 63)),
                     inc=("pe", 1) if last else None)
        dmm_done[d] = k.val("pe")
        k.wait("vector", "pe", dmm_done[d])
        k.wait("vector", f"fx{par}", xld[d])
        if d >= 2:
            k.wait("vector", f"fo{par}", dst_done[d - 2])
        dev_done[d] = k.do("vector", lambda eng, d=d, par=par:
                           eng.scalar_tensor_tensor(
                               xo[par][:, :], pd[par][:, :],
                               cm.dsc_half[:, d:d + 1], xr[par][:, :],
                               ALU.mult, ALU.add), inc=("ve", 1))
        k.wait("scalar", "ve", dev_done[d])
        dst_done[d] = k.dma("scalar", lambda eng, d=d, par=par: eng.dma_start(
            out=xout_fm[d * 128:(d + 1) * 128, t0:t0 + T],
            in_=xo[par][:, :]), f"fo{par}")
        if d < 14:
            load_d(d + 2)
    res = dict(x_fo=(k.val("fo0"), k.val("fo1")))
    ar.free_to(m0)
    return res



def emit_fm_to_tm_out(cm, src_fm, dst_tm):
    nc, k, ar = cm.nc, cm.k, cm.ar
    m0 = ar.mark()
    xs = [ar.sb(f"f2t_x{d}", [128, 1024], F32)[0] for d in range(16)]
    ot = [ar.sb(f"f2t_o{t}", [128, D], F32)[0] for t in range(8)]
    ps = [ar.ps(f"f2t_p{j}", [128, 512], F32)[0] for j in range(2)]

    for d in range(16):
        k.dma("sync", lambda eng, d=d: eng.dma_start(
            out=xs[d][:, :], in_=src_fm[d * 128:(d + 1) * 128, :]), "f2t")
    ld_all = k.val("f2t")
    ev = {}
    for t in range(8):
        for d in range(16):
            g = (t * 16 + d) // 4
            par = g % 2
            k.wait("tensor", "f2t", ld_all)
            if g >= 2:
                k.wait("tensor", "ve", ev.get(g - 2, 0))
            k.do("tensor", lambda eng, t=t, d=d, par=par: eng.matmul(
                ps[par][:, (d % 4) * 128:(d % 4 + 1) * 128],
                xs[d][:, t * 128:(t + 1) * 128],
                cm.c_eye_f32[:, :], start=True, stop=True),
                inc=("pe", 1) if d % 4 == 3 else None)
            if d % 4 == 3:
                k.wait("vector", "pe", k.val("pe"))
                ev[g] = k.do("vector", lambda eng, t=t, d=d, par=par:
                             eng.tensor_copy(
                                 ot[t][:, (d - 3) * 128:(d + 1) * 128],
                                 ps[par][:, :]), inc=("ve", 1))
    for t in range(8):
        k.wait("gpsimd", "ve", k.val("ve"))
        k.dma("gpsimd", lambda eng, t=t: eng.dma_start(
            out=dst_tm[t * 128:(t + 1) * 128, :], in_=ot[t][:, :]), "outd")
    res = k.val("outd")
    ar.free_to(m0)
    return res



def emit_cols(cm, src1d, n, dst_sb, scale_after=None):
    nc, k, ar = cm.nc, cm.k, cm.ar
    m0 = ar.mark()
    tmp, _ = ar.sb("col_tmp", [64, 128], F32)
    pscale, _ = ar.ps("col_ps", [128, 64], F32)
    k.wait("sync", "pe", k.val("pe"))
    ld = k.dma("sync", lambda eng: eng.dma_start(
        out=tmp[0:n, :], in_=src1d.rearrange("(a p) -> a p", p=128)),
        "c0")
    k.wait("tensor", "c0", ld)
    k.wait("tensor", "ve", k.val("ve"))
    mm = k.do("tensor", lambda eng: eng.matmul(
        pscale[:, 0:n], tmp[0:n, :], cm.c_eye_f32[0:n, 0:n],
        start=True, stop=True), inc=("pe", 1))
    k.wait("vector", "pe", mm)
    r = k.do("vector", lambda eng: eng.tensor_copy(dst_sb[:, 0:n],
                                                   pscale[:, 0:n]))
    if scale_after is not None:
        r = k.do("vector", lambda eng: eng.tensor_scalar_mul(
            dst_sb[:, 0:n], dst_sb[:, 0:n], scale_after))
    ar.free_to(m0)
    k.barrier()
    return r



def emit_rope_tables(cm, fr_ap, C2, S2):
    nc, k, ar = cm.nc, cm.k, cm.ar
    TK = 4096
    m0 = ar.mark()
    th_tm = [ar.sb(f"th_tm{j}", [128, 32], F32)[0] for j in range(2)]
    th, _ = ar.sb("th_fm", [32, TK], F32)
    s2, _ = ar.sb("th_s2", [32, TK], F32)
    c2, _ = ar.sb("th_c2", [32, TK], F32)
    pth, _ = ar.ps("th_ps", [32, 512], F32)
    pbc, _ = ar.ps("th_pbc", [128, 512], F32)

    k.wait("tensor", "c0", cm.consts_ready)
    lds, mms, grp_ev = {}, {}, {}
    for t in range(32):
        par = t % 2
        if t >= 2:
            k.wait("sync", "pe", mms[t - 2])
        lds[t] = k.dma("sync", lambda eng, t=t, par=par: eng.dma_start(
            out=th_tm[par][:, :], in_=fr_ap[t * 128:(t + 1) * 128, :]),
            f"thl{par}")
        k.wait("tensor", f"thl{par}", lds[t])
        if t % 4 == 0 and t >= 4:
            k.wait("tensor", "ve", grp_ev[t // 4 - 1])
        mms[t] = k.do("tensor", lambda eng, t=t, par=par: eng.matmul(
            pth[0:32, (t % 4) * 128:(t % 4 + 1) * 128],
            th_tm[par][:, :], cm.c_eye_f32[:, :], start=True, stop=True),
            inc=("pe", 1))
        if t % 4 == 3:
            k.wait("vector", "pe", mms[t])
            grp_ev[t // 4] = k.do("vector", lambda eng, t=t: eng.tensor_copy(
                th[:, (t - 3) * 128:(t + 1) * 128], pth[0:32, :]))
    k.wait("vector", "pe", k.val("pe"))
    k.do("vector", lambda eng: eng.tensor_scalar(
        th[:, :], th[:, :], 0.5, -PI / 2, ALU.mult, ALU.add))
    k.wait("scalar", "ve", k.val("ve"))
    k.do("scalar", lambda eng: eng.activation(s2[:, :], th[:, :], AF.Sin))
    k.do("vector", lambda eng: eng.tensor_scalar(
        c2[:, :], th[:, :], -1.0, PI / 2, ALU.mult, ALU.add))
    k.wait("scalar", "ve", k.val("ve"))
    a1 = k.do("scalar", lambda eng: eng.activation(c2[:, :], c2[:, :], AF.Sin))
    k.wait("vector", "ac", a1)
    k.do("vector", lambda eng: eng.scalar_tensor_tensor(
        c2[:, :], s2[:, :], -2.0, c2[:, :], ALU.mult, ALU.mult))
    k.do("vector", lambda eng: eng.tensor_mul(s2[:, :], s2[:, :], s2[:, :]))
    vv = k.do("vector", lambda eng: eng.tensor_scalar(
        s2[:, :], s2[:, :], 2.0, -1.0, ALU.mult, ALU.add))
    k.wait("tensor", "ve", vv)
    for nb in range(TK // 512):
        m1 = k.do("tensor", lambda eng, nb=nb: eng.matmul(
            pbc[:, :], cm.c_dup128[:, :], s2[:, nb * 512:(nb + 1) * 512],
            start=True, stop=True), inc=("pe", 1))
        k.wait("vector", "pe", m1)
        cv = k.do("vector", lambda eng, nb=nb: eng.tensor_copy(
            C2[:, nb * 512:(nb + 1) * 512], pbc[:, :]))
        k.wait("tensor", "ve", cv)
        m2 = k.do("tensor", lambda eng, nb=nb: eng.matmul(
            pbc[:, :], cm.c_dups128[:, :], c2[:, nb * 512:(nb + 1) * 512],
            start=True, stop=True), inc=("pe", 1))
        k.wait("vector", "pe", m2)
        sv = k.do("vector", lambda eng, nb=nb: eng.tensor_copy(
            S2[:, nb * 512:(nb + 1) * 512], pbc[:, :]))
        k.wait("tensor", "ve", sv)
    ar.free_to(m0)


def emit_rope_apply(cm, io_bf, psw, t1, t2, C2s, S2s, W, out_ap):
    k = cm.k
    for c0 in range(0, W, 512):
        mm = k.do("tensor", lambda eng, c0=c0: eng.matmul(
            psw[:, c0:c0 + 512], cm.c_p128[:, :], io_bf[:, c0:c0 + 512],
            start=True, stop=True), inc=("pe", 1))
    k.do("vector", lambda eng: eng.tensor_tensor(
        t1[:, 0:W], io_bf[:, 0:W], C2s, ALU.mult))
    k.wait("vector", "pe", mm)
    k.do("vector", lambda eng: eng.tensor_tensor(
        t2[:, 0:W], psw[:, 0:W], S2s, ALU.mult))
    return k.do("vector", lambda eng: eng.tensor_tensor(
        out_ap, t1[:, 0:W], t2[:, 0:W], ALU.add))



def emit_qproj(cm, ln2_own, wq_ap, C2, S2, qn_sb, qr_sb):
    nc, k, ar = cm.nc, cm.k, cm.ar
    T = 1024
    m0 = ar.mark()
    wqb = [ar.sb(f"qp_w{j}", [128, 16 * 384], BF16)[0] for j in range(2)]
    qrb, _ = ar.sb("qp_qrb", [128, T], BF16)
    t1, _ = ar.sb("qp_t1", [128, T], F32)
    t2, _ = ar.sb("qp_t2", [128, T], F32)
    pqa, _ = ar.ps("qp_pa", [128, T], F32)
    pqb, _ = ar.ps("qp_pb", [128, T], F32)
    pqr, _ = ar.ps("qp_pr", [128, T], F32)
    psw, _ = ar.ps("qp_psw", [128, T], F32)

    wld, pair_done = {}, {}

    def load_wq(p):
        par = p % 2
        if p >= 2:
            k.wait("gpsimd", "pe", pair_done[p - 2])
        wld[p] = k.dma("gpsimd", lambda eng, p=p, par=par: eng.dma_start(
            out=wqb[par][:, :].rearrange("p (a f) -> p a f", a=16),
            in_=wq_ap[:, p * 384:(p + 1) * 384].rearrange(
                "(a p) c -> p a c", p=128)), f"qw{par}")

    load_wq(0)
    load_wq(1)
    for p in range(8):
        par = p % 2
        h = 2 * p
        k.wait("tensor", f"qw{par}", wld[p])
        k.wait("tensor", "ve", k.val("ve"))
        for kc in range(16):
            for half in range(2):
                c0 = half * 512
                k.do("tensor", lambda eng, kc=kc, c0=c0, par=par: eng.matmul(
                    pqa[:, c0:c0 + 512],
                    wqb[par][:, kc * 384:kc * 384 + 128],
                    ln2_own[:, kc * T + c0:kc * T + c0 + 512],
                    start=(kc == 0), stop=(kc == 15)))
                k.do("tensor", lambda eng, kc=kc, c0=c0, par=par: eng.matmul(
                    pqb[:, c0:c0 + 512],
                    wqb[par][:, kc * 384 + 192:kc * 384 + 320],
                    ln2_own[:, kc * T + c0:kc * T + c0 + 512],
                    start=(kc == 0), stop=(kc == 15)))
                k.do("tensor", lambda eng, kc=kc, c0=c0, par=par: eng.matmul(
                    pqr[0:64, c0:c0 + 512],
                    wqb[par][:, kc * 384 + 128:kc * 384 + 192],
                    ln2_own[:, kc * T + c0:kc * T + c0 + 512],
                    start=(kc == 0), stop=(kc == 15)))
                k.do("tensor", lambda eng, kc=kc, c0=c0, par=par: eng.matmul(
                    pqr[64:128, c0:c0 + 512],
                    wqb[par][:, kc * 384 + 320:kc * 384 + 384],
                    ln2_own[:, kc * T + c0:kc * T + c0 + 512],
                    start=(kc == 0), stop=(kc == 15)),
                    inc=("pe", 1) if (kc == 15 and half == 1) else None)
        mm = k.val("pe")
        k.wait("scalar", "pe", mm)
        k.do("scalar", lambda eng, h=h, par=par: eng.activation(
            qn_sb[:, h * T:(h + 1) * T], pqa[:, :], AF.Copy))
        k.do("scalar", lambda eng, h=h, par=par: eng.activation(
            qn_sb[:, (h + 1) * T:(h + 2) * T], pqb[:, :], AF.Copy))
        aa = k.do("scalar", lambda eng, par=par: eng.activation(
            qrb[:, :], pqr[:, :], AF.Copy))
        k.wait("tensor", "ac", aa)
        k.wait("vector", "ac", aa)
        emit_rope_apply(cm, qrb, psw, t1, t2,
                        C2[:, 0:T], S2[:, 0:T], T,
                        qr_sb[:, p * T:(p + 1) * T])
        pair_done[p] = k.val("pe")
        if p < 6:
            load_wq(p + 2)
        k.wait("tensor", "ac", aa)
        k.wait("tensor", "ve", k.val("ve"))
    ar.free_to(m0)
    k.barrier()



def emit_kv(cm, ln2d, wkva_ap, gcol, bcol, C2, S2, kvn_sb, krope_sb):
    nc, k, ar = cm.nc, cm.k, cm.ar
    TK = 4096
    m0 = ar.mark()
    kvc = kvn_sb
    m_kva = ar.mark()
    wka = [ar.sb(f"kv_w{j}", [128, 576], BF16)[0] for j in range(2)]
    blk = [ar.sb(f"kv_b{j}", [128, 512], BF16)[0] for j in range(2)]
    pm = [ar.ps(f"kv_p{j}", [128, 512], F32)[0] for j in range(5)]

    ev_cnt = {}
    for nb in range(8):
        lds = {}
        for kc in range(16):
            par = kc % 2
            if kc >= 2:
                k.wait("sync", "pe", lds[kc - 2][1])
                k.wait("gpsimd", "pe", lds[kc - 2][1])
            ld = k.dma("sync", lambda eng, kc=kc, nb=nb, par=par:
                       eng.dma_start(out=blk[par][:, :],
                                     in_=ln2d[kc * 128:(kc + 1) * 128,
                                              nb * 512:(nb + 1) * 512]),
                       f"kvb{par}")
            wl = k.dma("gpsimd", lambda eng, kc=kc, par=par: eng.dma_start(
                out=wka[par][:, :],
                in_=wkva_ap[kc * 128:(kc + 1) * 128, :]), f"kvw{par}")
            k.wait("tensor", f"kvb{par}", ld)
            k.wait("tensor", f"kvw{par}", wl)
            if nb > 0 and kc == 0:
                k.wait("tensor", "ac", ev_cnt[nb - 1])
            for m in range(5):
                kk = 128 if m < 4 else 64
                out_ap = pm[m][:, :] if m < 4 else pm[m][0:64, :]
                k.do("tensor", lambda eng, m=m, kc=kc, par=par, kk=kk,
                     out_ap=out_ap: eng.matmul(
                         out_ap,
                         wka[par][:, m * 128:m * 128 + kk],
                         blk[par][:, :],
                         start=(kc == 0), stop=(kc == 15)),
                     inc=("pe", 1) if (m == 4 and kc == 15) else None)
            lds[kc] = (ld, k.val("pe"))
        mmv = k.val("pe")
        k.wait("scalar", "pe", mmv)
        for m in range(4):
            k.do("scalar", lambda eng, m=m, nb=nb: eng.activation(
                kvc[:, m * TK + nb * 512:m * TK + (nb + 1) * 512],
                pm[m][:, :], AF.Copy))
        ev = k.do("scalar", lambda eng, nb=nb: eng.activation(
            krope_sb[0:64, nb * 512:(nb + 1) * 512], pm[4][0:64, :],
            AF.Copy))
        ev_cnt[nb] = ev
    kva_done_ac = k.val("ac")
    ar.free_to(m_kva)

    xq = [ar.sb(f"kvl_q{j}", [128, 512], BF16)[0] for j in range(2)]
    rb, _ = ar.sb("kvl_rows", [1, 2 * 512], F32)
    ab, _ = ar.sb("kvl_ab", [128, 512], F32)
    bb, _ = ar.sb("kvl_bb", [128, 512], F32)
    t1, _ = ar.sb("kvl_t1", [128, 512], F32)
    prow, _ = ar.ps("kvl_prow", [1, 2 * 512], F32)
    pb1, _ = ar.ps("kvl_pb1", [128, 512], F32)
    pb2, _ = ar.ps("kvl_pb2", [128, 512], F32)
    k.barrier()
    RK = float(RANK)
    bcast_ve = {}
    for nb in range(8):
        sqa = {}
        for kc in range(4):
            par = kc % 2
            if kc >= 2:
                k.wait("scalar", "pe", sqa[kc - 2])
            sq = k.do("scalar", lambda eng, kc=kc, nb=nb, par=par:
                      eng.activation(xq[par][:, :],
                                     kvc[:, kc * TK + nb * 512:
                                         kc * TK + (nb + 1) * 512],
                                     AF.Square))
            k.wait("tensor", "ac", sq)
            if nb > 0 and kc == 0:
                k.wait("tensor", "ve", bcast_ve[nb - 1])
            k.do("tensor", lambda eng, kc=kc, nb=nb, par=par: eng.matmul(
                prow[:, 0:512], cm.c_ones_col[:, :],
                kvc[:, kc * TK + nb * 512:kc * TK + (nb + 1) * 512],
                start=(kc == 0), stop=(kc == 3)))
            mmk = k.do("tensor", lambda eng, kc=kc, par=par: eng.matmul(
                prow[:, 512:1024], cm.c_ones_col[:, :], xq[par][:, :],
                start=(kc == 0), stop=(kc == 3)),
                inc=("pe", 1))
            sqa[kc] = mmk
        k.wait("vector", "pe", k.val("pe"))
        k.do("vector", lambda eng: eng.tensor_scalar_mul(
            rb[:, 0:512], prow[:, 0:512], 1.0 / RK))
        k.do("vector", lambda eng: eng.tensor_scalar_mul(
            rb[:, 512:1024], prow[:, 512:1024], 1.0 / RK))
        k.do("vector", lambda eng: eng.scalar_tensor_tensor(
            ab[0:1, 0:512], rb[:, 0:512], -1.0, rb[:, 0:512],
            ALU.mult, ALU.mult))
        k.do("vector", lambda eng: eng.tensor_add(
            rb[:, 512:1024], rb[:, 512:1024], ab[0:1, 0:512]))
        vq = k.do("vector", lambda eng: eng.tensor_scalar_add(
            rb[:, 512:1024], rb[:, 512:1024], EPS))
        k.wait("scalar", "ve", vq)
        k.do("scalar", lambda eng: eng.activation(
            rb[:, 512:1024], rb[:, 512:1024], AF.Ln))
        sdq = k.do("scalar", lambda eng: eng.activation(
            rb[:, 512:1024], rb[:, 512:1024], AF.Exp, scale=-0.5))
        k.wait("vector", "ac", sdq)
        vr = k.do("vector", lambda eng: eng.scalar_tensor_tensor(
            ab[0:1, 0:512], rb[:, 0:512], -1.0, rb[:, 512:1024],
            ALU.mult, ALU.mult))
        k.wait("tensor", "ve", vr)
        k.do("tensor", lambda eng: eng.matmul(
            pb1[:, :], cm.c_ones_row[:, :], rb[:, 512:1024],
            start=True, stop=True))
        bmm = k.do("tensor", lambda eng: eng.matmul(
            pb2[:, :], cm.c_ones_row[:, :], ab[0:1, 0:512],
            start=True, stop=True), inc=("pe", 1))
        k.wait("vector", "pe", bmm)
        k.do("vector", lambda eng: eng.tensor_copy(ab[:, :], pb1[:, :]))
        bcv = k.do("vector", lambda eng: eng.tensor_copy(bb[:, :], pb2[:, :]))
        for kc in range(4):
            k.do("vector", lambda eng, kc=kc, nb=nb: eng.tensor_tensor(
                t1[:, :], kvc[:, kc * TK + nb * 512:kc * TK + (nb + 1) * 512],
                ab[:, :], ALU.mult))
            k.do("vector", lambda eng: eng.tensor_tensor(
                t1[:, :], t1[:, :], bb[:, :], ALU.add))
            k.do("vector", lambda eng, kc=kc, nb=nb: eng.tensor_scalar(
                kvn_sb[:, kc * TK + nb * 512:kc * TK + (nb + 1) * 512],
                t1[:, :], gcol[:, kc:kc + 1], bcol[:, kc:kc + 1],
                ALU.mult, ALU.add))
        bcast_ve[nb] = k.val("ve")
    k.barrier()
    psw, _ = ar.ps("kv_psw", [128, 512], F32)
    t2b, _ = ar.sb("kv_t2", [64, 512], F32)
    t1b, _ = ar.sb("kv_t1b", [64, 512], F32)
    for nb in range(8):
        c0 = nb * 512
        mm = k.do("tensor", lambda eng, c0=c0: eng.matmul(
            psw[0:64, :], cm.c_p128[0:64, 0:64],
            krope_sb[0:64, c0:c0 + 512], start=True, stop=True),
            inc=("pe", 1))
        k.do("vector", lambda eng, c0=c0: eng.tensor_tensor(
            t1b[:, :], krope_sb[0:64, c0:c0 + 512], C2[0:64, c0:c0 + 512],
            ALU.mult))
        k.wait("vector", "pe", mm)
        k.do("vector", lambda eng, c0=c0: eng.tensor_tensor(
            t2b[:, :], psw[0:64, :], S2[0:64, c0:c0 + 512], ALU.mult))
        vv = k.do("vector", lambda eng, c0=c0: eng.tensor_tensor(
            krope_sb[0:64, c0:c0 + 512], t1b[:, :], t2b[:, :], ALU.add))
        k.do("vector", lambda eng, c0=c0: eng.tensor_copy(
            krope_sb[64:128, c0:c0 + 512], krope_sb[0:64, c0:c0 + 512]))
        k.wait("tensor", "ve", vv)
    ar.free_to(m0)
    k.barrier()



def emit_attention(cm, wkvb_ap, kvn_sb, krope_sb, qn_sb, qr_sb, attn_sb):
    nc, k, ar = cm.nc, cm.k, cm.ar
    T = 1024
    TK = 4096
    m0 = ar.mark()
    wkb = [ar.sb(f"at_wb{j}", [128, 4 * 256], BF16)[0] for j in range(2)]
    kno, _ = ar.sb("at_kn", [128, TK], BF16)
    vfm, _ = ar.sb("at_vf", [128, TK], BF16)
    vtm, _ = ar.sb("at_vt", [128, TK], BF16)
    p_sb = [ar.sb(f"at_p{j}", [128, T], BF16)[0] for j in range(2)]
    rrow, _ = ar.sb("at_rr", [1, T], F32)
    rbc, _ = ar.sb("at_rbc", [128, T], F32)
    psS = [ar.ps(f"at_s{j}", [128, T], F32)[0] for j in range(2)]
    psO, _ = ar.ps("at_o", [128, T], F32)
    psR, _ = ar.ps("at_r", [1, T], F32)

    k.wait("tensor", "ve", cm.negc_ready)
    wld = {}

    def load_wb(h):
        par = h % 2
        if h >= 2:
            k.wait("gpsimd", "ac", kvb_evac[h - 2])
        wld[h] = k.dma("gpsimd", lambda eng, h=h, par=par: eng.dma_start(
            out=wkb[par][:, :].rearrange("p (a f) -> p a f", a=4),
            in_=wkvb_ap[:, h * 256:(h + 1) * 256].rearrange(
                "(a p) c -> p a c", p=128)), f"awb{par}")

    kvb_evac = {}
    load_wb(0)
    load_wb(1)
    for h in range(16):
        par = h % 2
        k.wait("tensor", f"awb{par}", wld[h])
        for which in range(2):
            dst = kno if which == 0 else vfm
            for nb in range(8):
                pp = psS[nb % 2]
                for j in range(4):
                    k.do("tensor", lambda eng, j=j, nb=nb, par=par,
                         which=which, pp=pp: eng.matmul(
                             pp[:, 0:512],
                             wkb[par][:, j * 256 + which * 128:
                                      j * 256 + which * 128 + 128],
                             kvn_sb[:, j * TK + nb * 512:
                                    j * TK + (nb + 1) * 512],
                             start=(j == 0), stop=(j == 3)),
                         inc=("pe", 1) if j == 3 else None)
                mmv = k.val("pe")
                k.wait("scalar", "pe", mmv)
                ae = k.do("scalar", lambda eng, nb=nb, dst=dst, pp=pp:
                          eng.activation(dst[:, nb * 512:(nb + 1) * 512],
                                         pp[:, 0:512], AF.Copy))
                k.wait("tensor", "ac", ae)
        kvb_evac[h] = k.val("ac")
        if h < 14:
            load_wb(h + 2)
        for g4 in range(8):
            pp = psS[g4 % 2]
            for jj in range(4):
                sc = g4 * 4 + jj
                k.do("tensor", lambda eng, sc=sc, jj=jj, pp=pp: eng.matmul(
                    pp[:, jj * 128:(jj + 1) * 128],
                    vfm[:, sc * 128:(sc + 1) * 128],
                    cm.c_eye_bf[:, :], start=True, stop=True),
                    inc=("pe", 1) if jj == 3 else None)
            mmv = k.val("pe")
            k.wait("scalar", "pe", mmv)
            ve = k.do("scalar", lambda eng, g4=g4, pp=pp: eng.activation(
                vtm[:, g4 * 512:(g4 + 1) * 512], pp[:, 0:512], AF.Copy))
            k.wait("tensor", "ac", ve)
        exp_ac = {}
        pv_pe = {}
        for sc in range(32):
            par2 = sc % 2
            pp = psS[par2]
            if sc >= 2:
                k.wait("tensor", "ac", exp_ac[sc - 2])
            for half in range(2):
                c0 = half * 512
                k.do("tensor", lambda eng, sc=sc, c0=c0, h=h, pp=pp:
                     eng.matmul(pp[:, c0:c0 + 512],
                                kno[:, sc * 128:(sc + 1) * 128],
                                qn_sb[:, h * T + c0:h * T + c0 + 512],
                                start=True, stop=False))
                r0 = 64 * (h % 2)
                qrs = qr_sb[r0:r0 + 64,
                            (h // 2) * T + c0:(h // 2) * T + c0 + 512]
                krs = krope_sb[r0:r0 + 64, sc * 128:(sc + 1) * 128]
                k.do("tensor", lambda eng, qrs=qrs, krs=krs, c0=c0, pp=pp:
                     eng.matmul(pp[:, c0:c0 + 512], krs,
                                qrs, start=False, stop=True),
                     inc=("pe", 1) if c0 else None)
            mmv = k.val("pe")
            k.wait("scalar", "pe", mmv)
            if sc >= 2:
                k.wait("scalar", "pe", pv_pe[sc - 2])
            ea = k.do("scalar", lambda eng, sc=sc, par2=par2, pp=pp:
                      eng.activation(p_sb[par2][:, :], pp[:, :], AF.Exp,
                                     bias=cm.c_negC[:, 0:1], scale=SCALE))
            exp_ac[sc] = ea
            k.wait("tensor", "ac", ea)
            for half in range(2):
                c0 = half * 512
                k.do("tensor", lambda eng, sc=sc, c0=c0, par2=par2:
                     eng.matmul(psR[:, c0:c0 + 512], cm.c_ones_col[:, :],
                                p_sb[par2][:, c0:c0 + 512],
                                start=(sc == 0), stop=(sc == 31)))
                k.do("tensor", lambda eng, sc=sc, c0=c0, par2=par2:
                     eng.matmul(psO[:, c0:c0 + 512],
                                vtm[:, sc * 128:(sc + 1) * 128],
                                p_sb[par2][:, c0:c0 + 512],
                                start=(sc == 0), stop=(sc == 31)),
                     inc=("pe", 1) if c0 else None)
            pv_pe[sc] = k.val("pe")
        k.wait("vector", "pe", k.val("pe"))
        vv = k.do("vector", lambda eng: eng.reciprocal(rrow[:, :], psR[:, :]))
        k.wait("tensor", "ve", vv)
        bmm = k.do("tensor", lambda eng: eng.matmul(
            psS[0][:, 0:512], cm.c_ones_row[:, :], rrow[:, 0:512],
            start=True, stop=True), inc=("pe", 1))
        bmm2 = k.do("tensor", lambda eng: eng.matmul(
            psS[1][:, 0:512], cm.c_ones_row[:, :], rrow[:, 512:1024],
            start=True, stop=True), inc=("pe", 1))
        k.wait("vector", "pe", bmm2)
        k.do("vector", lambda eng: eng.tensor_copy(rbc[:, 0:512],
                                                   psS[0][:, 0:512]))
        k.do("vector", lambda eng: eng.tensor_copy(rbc[:, 512:1024],
                                                   psS[1][:, 0:512]))
        av = k.do("vector", lambda eng, h=h: eng.tensor_tensor(
            attn_sb[:, h * T:(h + 1) * T], psO[:, :], rbc[:, :], ALU.mult))
        k.wait("tensor", "ve", av)
    ar.free_to(m0)
    k.barrier()



def emit_wout(cm, wout_ap, attn_sb, x1fm, x2fm):
    nc, k, ar = cm.nc, cm.k, cm.ar
    T = 1024
    m0 = ar.mark()
    wsb, _ = ar.sb("wo_w", [128, 16 * D], BF16)
    xr = [ar.sb(f"wo_xr{j}", [128, T], F32)[0] for j in range(2)]
    xo = [ar.sb(f"wo_xo{j}", [128, T], F32)[0] for j in range(2)]
    pw = [ar.ps(f"wo_p{j}", [128, T], F32)[0] for j in range(2)]

    wl = k.dma("gpsimd", lambda eng: eng.dma_start(
        out=wsb[:, :].rearrange("p (a f) -> p a f", a=16),
        in_=wout_ap.rearrange("(a p) c -> p a c", p=128)), "wo")
    k.wait("tensor", "wo", wl)

    xl, mmd, evd, std = {}, {}, {}, {}

    def load_x(d):
        par = d % 2
        if d >= 2:
            k.wait("scalar", "ve", evd[d - 2])
        xl[d] = k.dma("scalar", lambda eng, d=d, par=par: eng.dma_start(
            out=xr[par][:, :], in_=x1fm[d * 128:(d + 1) * 128, 0:T]),
            f"wx{par}")

    load_x(0)
    load_x(1)
    for d in range(16):
        par = d % 2
        if d >= 2:
            k.wait("tensor", "ve", evd[d - 2])
        for h in range(16):
            for half in range(2):
                c0 = half * 512
                k.do("tensor", lambda eng, d=d, h=h, c0=c0, par=par:
                     eng.matmul(pw[par][:, c0:c0 + 512],
                                wsb[:, h * D + d * 128:h * D + (d + 1) * 128],
                                attn_sb[:, h * T + c0:h * T + c0 + 512],
                                start=(h == 0), stop=(h == 15)),
                     inc=("pe", 1) if (h == 15 and half == 1) else None)
        mmd[d] = k.val("pe")
        k.wait("vector", "pe", mmd[d])
        k.wait("vector", f"wx{par}", xl[d])
        if d >= 2:
            k.wait("vector", f"wst{par}", std[d - 2])
        evd[d] = k.do("vector", lambda eng, d=d, par=par: eng.tensor_tensor(
            xo[par][:, :], pw[par][:, :], xr[par][:, :], ALU.add))
        k.wait("scalar", "ve", evd[d])
        std[d] = k.dma("scalar", lambda eng, d=d, par=par: eng.dma_start(
            out=x2fm[d * 128:(d + 1) * 128, :], in_=xo[par][:, :]),
            f"wst{par}")
        if d < 14:
            load_x(d + 2)
    res = (k.val("wst0"), k.val("wst1"))
    ar.free_to(m0)
    k.barrier()
    return res



def build_L2(stop_after=None):
    nc = bass.Bass("TRN2", target_bir_lowering=False)
    cm = Common(nc)
    k, ar = cm.k, cm.ar

    inputs = {}
    inputs["x1fm"] = nc.dram_tensor("x1fm", [D, S], F32,
                                    kind="ExternalInput").ap()
    inputs["fr"] = nc.dram_tensor("fr", [S, DR // 2], F32,
                                  kind="ExternalInput").ap()
    inputs["Wq"] = nc.dram_tensor("Wq", [D, NH * DQK], F32,
                                  kind="ExternalInput").ap()
    inputs["Wkv_a"] = nc.dram_tensor("Wkv_a", [D, RANK + DR], F32,
                                     kind="ExternalInput").ap()
    inputs["kv_gamma"] = nc.dram_tensor("kv_gamma", [RANK], F32,
                                        kind="ExternalInput").ap()
    inputs["kv_beta"] = nc.dram_tensor("kv_beta", [RANK], F32,
                                       kind="ExternalInput").ap()
    inputs["Wkv_b"] = nc.dram_tensor("Wkv_b", [RANK, NH * (DN + DV)], F32,
                                     kind="ExternalInput").ap()
    inputs["Wout"] = nc.dram_tensor("Wout", [NH * DV, D], F32,
                                    kind="ExternalInput").ap()
    for nm, shp, dt in (("gate_w", [IF, D], I32), ("up_w", [IF, D], I32),
                        ("down_w", [D, IF], I32), ("gate_scale", [IF], F32),
                        ("up_scale", [IF], F32), ("down_scale", [D], F32)):
        inputs[nm] = nc.dram_tensor(nm, shp, dt, kind="ExternalInput").ap()
    cm.inputs = inputs

    out_tm = nc.dram_tensor("out_tm", [TOWN, D], F32,
                            kind="ExternalOutput").ap()
    ln2d = nc.dram_tensor("ln2d", [D, S], BF16, kind="Internal").ap()
    x2fm = nc.dram_tensor("x2fm", [D, TOWN], F32, kind="Internal").ap()
    x3fm = nc.dram_tensor("x3fm", [D, TOWN], F32, kind="Internal").ap()

    def _truncate():
        k.barrier()
        r = k.dma("gpsimd", lambda eng: eng.dma_start(
            out=out_tm, in_=inputs["x1fm"][0:TOWN, 0:D]), "outd")
        k.wait("sync", "outd", r)
        cm.k.finish()
        return nc

    cm.load_consts(with_ffn_scales=True)
    cm.gcol, _ = ar.sb("gcol", [128, 4], F32)
    cm.bcol, _ = ar.sb("bcol", [128, 4], F32)
    emit_cols(cm, inputs["kv_gamma"], 4, cm.gcol)
    emit_cols(cm, inputs["kv_beta"], 4, cm.bcol)
    if stop_after == "consts":
        return _truncate()
    emit_ffn_weight_prep(cm)
    k.barrier()
    if stop_after == "prep":
        return _truncate()

    m_pre = ar.mark()
    attn_sb, _ = ar.sb("attn_sb", [128, 16 * 1024], BF16)
    m_attn = ar.mark()
    qn_sb, _ = ar.sb("qn_sb", [128, 16 * 1024], BF16)
    qr_sb, _ = ar.sb("qr_sb", [128, 8 * 1024], BF16)
    C2, _ = ar.sb("C2", [128, S], F32)
    S2, _ = ar.sb("S2", [128, S], F32)

    emit_rope_tables(cm, inputs["fr"], C2, S2)
    if stop_after == "theta":
        return _truncate()

    for c in range(1, 4):
        emit_ln_chunk(cm, inputs["x1fm"], c * 1024, None, also_dram=ln2d)
        k.barrier()
    m_ln2 = ar.mark()
    ln2_own, _ = ar.sb("ln2_own", [128, 16 * 1024], BF16)
    emit_ln_chunk(cm, inputs["x1fm"], 0, ln2_own, also_dram=ln2d)
    k.barrier()
    if stop_after == "ln2":
        return _truncate()

    emit_qproj(cm, ln2_own, inputs["Wq"], C2, S2, qn_sb, qr_sb)
    ar.free_to(m_ln2)
    k.barrier()
    if stop_after == "qproj":
        return _truncate()
    kvn_sb, _ = ar.sb("kvn_sb", [128, 4 * S], BF16)
    krope_sb, _ = ar.sb("krope_sb", [128, S], BF16)
    emit_kv(cm, ln2d, inputs["Wkv_a"], cm.gcol, cm.bcol, C2, S2,
            kvn_sb, krope_sb)
    if stop_after == "kv":
        return _truncate()
    emit_attention(cm, inputs["Wkv_b"], kvn_sb, krope_sb, qn_sb, qr_sb,
                   attn_sb)
    ar.free_to(m_attn)
    k.barrier()
    if stop_after == "attn":
        return _truncate()
    r2 = emit_wout(cm, inputs["Wout"], attn_sb, inputs["x1fm"], x2fm)
    ar.free_to(m_pre)
    k.barrier()
    if stop_after == "wout":
        return _truncate()
    if stop_after == "x2":
        k.barrier()
        r = k.dma("gpsimd", lambda eng: eng.dma_start(
            out=out_tm.rearrange("a b -> (a b)"),
            in_=x2fm.rearrange("a b -> (a b)")), "outd")
        k.wait("sync", "outd", r)
        cm.k.finish()
        return nc
    r3 = emit_ffn(cm, x2fm, 0, x3fm)
    k.barrier()
    rout = emit_fm_to_tm_out(cm, x3fm, out_tm)
    k.wait("sync", "outd", rout)
    cm.k.finish()
    return nc



def build_L1():
    nc = bass.Bass("TRN2", target_bir_lowering=False)
    cm = Common(nc)
    k, ar = cm.k, cm.ar

    inputs = {}
    inputs["x_own"] = nc.dram_tensor("x_own", [TOWN, D], F32,
                                     kind="ExternalInput").ap()
    inputs["gate_w"] = nc.dram_tensor("gate_w", [IF, D], I32,
                                      kind="ExternalInput").ap()
    inputs["up_w"] = nc.dram_tensor("up_w", [IF, D], I32,
                                    kind="ExternalInput").ap()
    inputs["down_w"] = nc.dram_tensor("down_w", [D, IF], I32,
                                      kind="ExternalInput").ap()
    inputs["gate_scale"] = nc.dram_tensor("gate_scale", [IF], F32,
                                          kind="ExternalInput").ap()
    inputs["up_scale"] = nc.dram_tensor("up_scale", [IF], F32,
                                        kind="ExternalInput").ap()
    inputs["down_scale"] = nc.dram_tensor("down_scale", [D], F32,
                                          kind="ExternalInput").ap()
    cm.inputs = inputs

    x1fm = nc.dram_tensor("x1fm", [D, TOWN], F32, kind="ExternalOutput").ap()
    x0fm = nc.dram_tensor("x0fm", [D, TOWN], F32, kind="Internal").ap()

    cm.load_consts(with_ffn_scales=True)
    emit_ffn_weight_prep(cm)
    k.barrier()
    emit_tm_to_fm(cm, inputs["x_own"], x0fm, TOWN)
    k.barrier()

    r2 = emit_ffn(cm, x0fm, 0, x1fm)

    k.wait("sync", "fo0", r2["x_fo"][0])
    k.wait("sync", "fo1", r2["x_fo"][1])
    cm.k.finish()
    return nc


def build_FUSED(stop_after=None, dump_chunk=0, n_ffn1=4):
    nc = bass.Bass("TRN2", target_bir_lowering=False)
    cm = Common(nc)
    k, ar = cm.k, cm.ar

    inputs = {}
    inputs["x_full"] = nc.dram_tensor("x_full", [S, D], F32,
                                      kind="ExternalInput").ap()
    inputs["fr"] = nc.dram_tensor("fr", [S, DR // 2], F32,
                                  kind="ExternalInput").ap()
    inputs["Wq"] = nc.dram_tensor("Wq", [D, NH * DQK], F32,
                                  kind="ExternalInput").ap()
    inputs["Wkv_a"] = nc.dram_tensor("Wkv_a", [D, RANK + DR], F32,
                                     kind="ExternalInput").ap()
    inputs["kv_gamma"] = nc.dram_tensor("kv_gamma", [RANK], F32,
                                        kind="ExternalInput").ap()
    inputs["kv_beta"] = nc.dram_tensor("kv_beta", [RANK], F32,
                                       kind="ExternalInput").ap()
    inputs["Wkv_b"] = nc.dram_tensor("Wkv_b", [RANK, NH * (DN + DV)], F32,
                                     kind="ExternalInput").ap()
    inputs["Wout"] = nc.dram_tensor("Wout", [NH * DV, D], F32,
                                    kind="ExternalInput").ap()
    for nm, shp, dt in (("gate_w", [IF, D], I32), ("up_w", [IF, D], I32),
                        ("down_w", [D, IF], I32), ("gate_scale", [IF], F32),
                        ("up_scale", [IF], F32), ("down_scale", [D], F32)):
        inputs[nm] = nc.dram_tensor(nm, shp, dt, kind="ExternalInput").ap()
    cm.inputs = inputs

    out_tm = nc.dram_tensor("out_tm", [TOWN, D], F32,
                            kind="ExternalOutput").ap()
    x0fm = nc.dram_tensor("x0fm", [D, S], F32, kind="Internal").ap()
    x1fm = nc.dram_tensor("x1fm", [D, S], F32, kind="Internal").ap()
    ln2d = nc.dram_tensor("ln2d", [D, S], BF16, kind="Internal").ap()
    x2fm = nc.dram_tensor("x2fm", [D, TOWN], F32, kind="Internal").ap()
    x3fm = nc.dram_tensor("x3fm", [D, TOWN], F32, kind="Internal").ap()

    def _dump(src_fm):
        k.barrier()
        ov = out_tm.rearrange("a b -> (a b)").rearrange(
            "(c d) -> c d", c=D)
        r = k.dma("gpsimd", lambda eng: eng.dma_start(
            out=ov,
            in_=src_fm[:, dump_chunk * 1024:(dump_chunk + 1) * 1024]),
            "outd")
        k.wait("sync", "outd", r)
        cm.k.finish()
        return nc

    cm.load_consts(with_ffn_scales=True)
    cm.gcol, _ = ar.sb("gcol", [128, 4], F32)
    cm.bcol, _ = ar.sb("bcol", [128, 4], F32)
    emit_cols(cm, inputs["kv_gamma"], 4, cm.gcol)
    emit_cols(cm, inputs["kv_beta"], 4, cm.bcol)

    ln1d = nc.dram_tensor("ln1d", [D, S], BF16, kind="Internal").ap()
    for c in range(4):
        emit_tm_to_fm(cm, inputs["x_full"], x0fm, 1024, t0=c * 1024)
        k.barrier()
    if stop_after == "x0":
        return _dump(x0fm)
    m_gu = emit_prep_pe(cm, "gu")
    k.wait("tensor", "ac", k.val("ac"))
    k.wait("gpsimd", "t2fo", k.val("t2fo"))
    emit_ln_pipeline(cm, x0fm, [0, 1024, 2048, 3072], ln1d)
    if stop_after == "lnd":
        return _dump(ln1d)
    k.wait("vector", "ptw0", cm.wprep_gu[0])
    k.wait("vector", "ptw1", cm.wprep_gu[1])
    k.wait("vector", "lnst0", k.val("lnst0"))
    k.wait("vector", "lnst1", k.val("lnst1"))

    t0list = (n_ffn1 if isinstance(n_ffn1, (list, tuple))
              else [i * 1024 for i in range(n_ffn1)])
    for ji, t0_ in enumerate(t0list):
        if ji == 0:
            emit_ffn(cm, x0fm, t0_, x1fm, ln_dram=ln1d,
                     gu_gate=cm.wprep_gu_m,
                     mid_emit=lambda: emit_prep_pe(cm, "d"))
        else:
            emit_ffn(cm, x0fm, t0_, x1fm, ln_dram=ln1d)
        k.barrier()
        if ji == 0:
            ar.free_to(m_gu)
    if stop_after == "x1":
        return _dump(x1fm)

    m_pre = ar.mark()
    attn_sb, _ = ar.sb("attn_sb", [128, 16 * 1024], BF16)
    m_attn = ar.mark()
    qn_sb, _ = ar.sb("qn_sb", [128, 16 * 1024], BF16)
    qr_sb, _ = ar.sb("qr_sb", [128, 8 * 1024], BF16)
    C2, _ = ar.sb("C2", [128, S], F32)
    S2, _ = ar.sb("S2", [128, S], F32)

    emit_rope_tables(cm, inputs["fr"], C2, S2)

    m_ln2 = ar.mark()
    lnres = emit_ln_pipeline(cm, x1fm, [1024, 2048, 3072, 0], ln2d,
                             last_sbuf=attn_sb, keep_mark=True)
    emit_qproj(cm, attn_sb, inputs["Wq"], C2, S2, qn_sb, qr_sb)
    k.wait("vector", "pe", k.val("pe"))
    ar.free_to(m_ln2)
    kvn_sb, _ = ar.sb("kvn_sb", [128, 4 * S], BF16)
    krope_sb, _ = ar.sb("krope_sb", [128, S], BF16)
    for e in ("sync", "gpsimd"):
        k.wait(e, "pe", k.val("pe"))
        k.wait(e, "ve", k.val("ve"))
        k.wait(e, "ac", k.val("ac"))
        k.wait(e, "lnst0", lnres["st0"])
        k.wait(e, "lnst1", lnres["st1"])
    emit_kv(cm, ln2d, inputs["Wkv_a"], cm.gcol, cm.bcol, C2, S2,
            kvn_sb, krope_sb)
    emit_attention(cm, inputs["Wkv_b"], kvn_sb, krope_sb, qn_sb, qr_sb,
                   attn_sb)
    ar.free_to(m_attn)
    k.barrier()
    emit_wout(cm, inputs["Wout"], attn_sb, x1fm, x2fm)
    ar.free_to(m_pre)
    k.barrier()
    emit_ffn(cm, x2fm, 0, x3fm)
    k.barrier()
    rout = emit_fm_to_tm_out(cm, x3fm, out_tm)
    k.wait("sync", "outd", rout)
    cm.k.finish()
    return nc



def _ffn_weight_inputs(inputs):
    return dict(
        gate_w=np.ascontiguousarray(inputs["L0_ffn_gate_w"]),
        up_w=np.ascontiguousarray(inputs["L0_ffn_up_w"]),
        down_w=np.ascontiguousarray(inputs["L0_ffn_down_w"]),
        gate_scale=np.ascontiguousarray(
            np.asarray(inputs["L0_ffn_gate_scale"], np.float32)),
        up_scale=np.ascontiguousarray(
            np.asarray(inputs["L0_ffn_up_scale"], np.float32)),
        down_scale=np.ascontiguousarray(
            np.asarray(inputs["L0_ffn_down_scale"], np.float32)),
    )


def kernel(**inputs):
    return kernel_fused(**inputs)


def kernel_fused(**inputs):
    x = np.asarray(inputs["L0_attn_hidden_states"], np.float32)
    wff = _ffn_weight_inputs(inputs)
    freqs = np.ascontiguousarray(
        np.asarray(inputs["L0_attn_freqs_cis"], np.float32))
    watt = dict(
        Wq=np.ascontiguousarray(np.asarray(inputs["L0_attn_Wq"], np.float32)),
        Wkv_a=np.ascontiguousarray(
            np.asarray(inputs["L0_attn_Wkv_a"], np.float32)),
        kv_gamma=np.ascontiguousarray(
            np.asarray(inputs["L0_attn_kv_gamma"], np.float32)),
        kv_beta=np.ascontiguousarray(
            np.asarray(inputs["L0_attn_kv_beta"], np.float32)),
        Wkv_b=np.ascontiguousarray(
            np.asarray(inputs["L0_attn_Wkv_b"], np.float32)),
        Wout=np.ascontiguousarray(
            np.asarray(inputs["L0_attn_Wout"], np.float32)),
    )
    nc = build_FUSED()
    in_maps = []
    for c in range(NC_):
        g, qc = c // 4, c % 4
        rows = (qc * 1024 + np.arange(S)) % S
        xr = np.ascontiguousarray(x[g][rows])
        fr = np.ascontiguousarray(freqs[rows % 2048])
        m = dict(wff)
        m.update(watt)
        m["x_full"] = xr
        m["fr"] = fr
        in_maps.append(m)
    res = run_bass_kernel_spmd(nc, in_maps, core_ids=list(range(NC_)))
    out = np.empty((B, S, D), np.float32)
    for c in range(NC_):
        g, qc = c // 4, c % 4
        out[g, qc * TOWN:(qc + 1) * TOWN] = res.results[c]["out_tm"]
    return out


def kernel_unfused(**inputs):
    x = np.ascontiguousarray(
        np.asarray(inputs["L0_attn_hidden_states"], np.float32)
    ).reshape(B * S, D)
    wff = _ffn_weight_inputs(inputs)

    nc1 = build_L1()
    in_maps = [dict(wff, x_own=np.ascontiguousarray(x[c * TOWN:(c + 1) * TOWN]))
               for c in range(NC_)]
    res1 = run_bass_kernel_spmd(nc1, in_maps, core_ids=list(range(NC_)))
    x1 = [np.asarray(res1.results[c]["x1fm"]) for c in range(NC_)]

    freqs = np.ascontiguousarray(
        np.asarray(inputs["L0_attn_freqs_cis"], np.float32))
    watt = dict(
        Wq=np.ascontiguousarray(np.asarray(inputs["L0_attn_Wq"], np.float32)),
        Wkv_a=np.ascontiguousarray(
            np.asarray(inputs["L0_attn_Wkv_a"], np.float32)),
        kv_gamma=np.ascontiguousarray(
            np.asarray(inputs["L0_attn_kv_gamma"], np.float32)),
        kv_beta=np.ascontiguousarray(
            np.asarray(inputs["L0_attn_kv_beta"], np.float32)),
        Wkv_b=np.ascontiguousarray(
            np.asarray(inputs["L0_attn_Wkv_b"], np.float32)),
        Wout=np.ascontiguousarray(
            np.asarray(inputs["L0_attn_Wout"], np.float32)),
    )
    nc2 = build_L2()
    in_maps2 = []
    for c in range(NC_):
        g, qc = c // 4, c % 4
        order = [g * 4 + ((qc + j) % 4) for j in range(4)]
        x1full = np.ascontiguousarray(
            np.concatenate([x1[o] for o in order], axis=1))
        pos = (qc * 1024 + np.arange(S)) % S
        fr = np.ascontiguousarray(freqs[pos % 2048])
        m = dict(wff)
        m.update(watt)
        m["x1fm"] = x1full
        m["fr"] = fr
        in_maps2.append(m)
    res2 = run_bass_kernel_spmd(nc2, in_maps2, core_ids=list(range(NC_)))
    out = np.empty((B, S, D), np.float32)
    for c in range(NC_):
        g, qc = c // 4, c % 4
        out[g, qc * TOWN:(qc + 1) * TOWN] = res2.results[c]["out_tm"]
    return out


# version 17
# speedup vs baseline: 1.5448x; 1.5005x over previous
import math

import numpy as np
import ml_dtypes

import concourse.bass as bass
import concourse.mybir as mybir
from concourse.bass_utils import run_bass_kernel_spmd

F32 = mybir.dt.float32
BF16 = mybir.dt.bfloat16
I32 = mybir.dt.int32
AF = mybir.ActivationFunctionType
ALU = mybir.AluOpType
AX = mybir.AxisListType
bfdt = ml_dtypes.bfloat16

B, S, D = 2, 4096, 2048
NH, DN, DR, DV, RANK = 16, 128, 64, 128, 512
DQK = DN + DR
SCALE = DQK ** -0.5
EPS = 1e-5
IF = 8192
NC_ = 8
TOWN = 1024
PI = math.pi

ENGINES = ("sync", "scalar", "vector", "tensor", "gpsimd")


class K:

    def __init__(self, nc):
        self.nc = nc
        self.q = {e: [] for e in ENGINES}
        self.sems = {}
        self.cnt = {}
        self.waited = {e: {} for e in ENGINES}

    def sem(self, name):
        self.sems[name] = self.nc.alloc_semaphore(name)
        self.cnt[name] = 0

    def wait(self, e, sem, val):
        if val is None or val <= 0:
            return
        if self.waited[e].get(sem, 0) >= val:
            return
        self.waited[e][sem] = val
        h = self.sems[sem]
        self.q[e].append(lambda eng, h=h, v=val: eng.wait_ge(h, v))

    SELF_SEM = {"vector": "ve", "scalar": "ac", "gpsimd": "po"}

    def do(self, e, fn, inc=None):
        ssem = self.SELF_SEM.get(e)
        if ssem is not None:
            self.wait(e, ssem, self.cnt[ssem])
            self.cnt[ssem] += 1
            h = self.sems[ssem]
            if inc is None:
                self.q[e].append(
                    lambda eng, fn=fn, h=h: fn(eng).then_inc(h, 1))
                return self.cnt[ssem]
            sem2, amt2 = inc
            assert sem2 == ssem and amt2 == 1, (e, inc)
            self.q[e].append(lambda eng, fn=fn, h=h: fn(eng).then_inc(h, 1))
            return self.cnt[ssem]
        if inc is None:
            self.q[e].append(fn)
            return None
        sem, amt = inc
        self.cnt[sem] += amt
        h = self.sems[sem]
        self.q[e].append(lambda eng, fn=fn, h=h, a=amt: fn(eng).then_inc(h, a))
        return self.cnt[sem]

    def dma(self, e, fn, stream):
        if stream not in self.sems:
            self.sem(stream)
        self.cnt[stream] += 16
        h = self.sems[stream]
        self.q[e].append(lambda eng, fn=fn, h=h: fn(eng).then_inc(h, 16))
        return self.cnt[stream]

    def val(self, sem):
        return self.cnt.get(sem, 0)

    def barrier(self, engines=ENGINES, exclude=()):
        for e in engines:
            for s, v in self.cnt.items():
                if s in exclude:
                    continue
                self.wait(e, s, v)

    def finish(self):
        with self.nc.Block() as block:
            block.sync(lambda eng: [f(eng) for f in self.q["sync"]])
            block.scalar(lambda eng: [f(eng) for f in self.q["scalar"]])
            block.vector(lambda eng: [f(eng) for f in self.q["vector"]])
            block.tensor(lambda eng: [f(eng) for f in self.q["tensor"]])
            block.gpsimd(lambda eng: [f(eng) for f in self.q["gpsimd"]])


class Arena:

    def __init__(self, nc):
        self.nc = nc
        self.stack = []
        self.uid = 0

    def _nm(self, name):
        self.uid += 1
        return f"{name}_{self.uid}"

    def sb(self, name, shape, dt):
        g = self.nc.sbuf_tensor(self._nm(name), shape, dt)
        h = g.__enter__()
        self.stack.append(g)
        return h, len(self.stack) - 1

    def ps(self, name, shape, dt=F32):
        g = self.nc.psum_tensor(self._nm(name), shape, dt)
        h = g.__enter__()
        self.stack.append(g)
        return h, len(self.stack) - 1

    def free_to(self, mark):
        while len(self.stack) > mark:
            g = self.stack.pop()
            g.__exit__(None, None, None)

    def mark(self):
        return len(self.stack)


def np_consts():
    eye_bf = np.eye(128, dtype=bfdt)
    eye_f32 = np.eye(128, dtype=np.float32)
    ones_col_bf = np.ones((128, 1), dtype=bfdt)
    ones_row_f32 = np.ones((1, 128), dtype=np.float32)
    p64 = np.zeros((64, 64), dtype=bfdt)
    for j in range(32):
        p64[2 * j, 2 * j + 1] = 1.0
        p64[2 * j + 1, 2 * j] = 1.0
    dup = np.zeros((32, 64), dtype=np.float32)
    dups = np.zeros((32, 64), dtype=np.float32)
    for j in range(32):
        dup[j, 2 * j] = 1.0
        dup[j, 2 * j + 1] = 1.0
        dups[j, 2 * j] = -1.0
        dups[j, 2 * j + 1] = 1.0
    p128 = np.zeros((128, 128), dtype=bfdt)
    for j in range(64):
        p128[2 * j, 2 * j + 1] = 1.0
        p128[2 * j + 1, 2 * j] = 1.0
    dup128 = np.zeros((32, 128), dtype=np.float32)
    dups128 = np.zeros((32, 128), dtype=np.float32)
    for j in range(32):
        for base in (0, 64):
            dup128[j, base + 2 * j] = 1.0
            dup128[j, base + 2 * j + 1] = 1.0
            dups128[j, base + 2 * j] = -1.0
            dups128[j, base + 2 * j + 1] = 1.0
    return dict(eye_bf=eye_bf, eye_f32=eye_f32, ones_col_bf=ones_col_bf,
                ones_row_f32=ones_row_f32, p64=p64, dup=dup, dups=dups,
                p128=p128, dup128=dup128, dups128=dups128)


class Common:

    def __init__(self, nc):
        self.nc = nc
        self.k = K(nc)
        self.ar = Arena(nc)
        for s in ("pe", "ve", "ac", "po"):
            self.k.sem(s)

    def load_consts(self, with_ffn_scales=True, with_attn=False):
        nc, k, ar = self.nc, self.k, self.ar
        cn = np_consts()
        self.c_eye_bf, _ = ar.sb("c_eye_bf", [128, 128], BF16)
        self.c_eye_f32, _ = ar.sb("c_eye_f32", [128, 128], F32)
        self.c_ones_col, _ = ar.sb("c_ones_col", [128, 1], BF16)
        self.c_ones_row, _ = ar.sb("c_ones_row", [1, 128], F32)
        self.c_p128, _ = ar.sb("c_p128", [128, 128], BF16)
        self.c_dup128, _ = ar.sb("c_dup128", [32, 128], F32)
        self.c_dups128, _ = ar.sb("c_dups128", [32, 128], F32)
        self.c_negC, _ = ar.sb("c_negC", [128, 1], F32)
        self.negc_ready = self.k.do(
            "vector", lambda eng: eng.memset(self.c_negC[:, :], -30.0))
        pairs = [
            (self.c_eye_bf, cn["eye_bf"]), (self.c_eye_f32, cn["eye_f32"]),
            (self.c_ones_col, cn["ones_col_bf"]),
            (self.c_ones_row, cn["ones_row_f32"]),
            (self.c_p128, cn["p128"]), (self.c_dup128, cn["dup128"]),
            (self.c_dups128, cn["dups128"]),
        ]
        for sb, arr in pairs:
            dh = nc.inline_tensor(np.ascontiguousarray(arr))
            self.k.dma("sync", lambda eng, sb=sb, dh=dh: eng.dma_start(
                out=sb[:, :], in_=dh.ap()), "c0")
        self.consts_ready = self.k.val("c0")
        if with_ffn_scales:
            self.gsc, _ = ar.sb("gsc", [128, 64], F32)
            self.usc, _ = ar.sb("usc", [128, 64], F32)
            self.dsc_half, _ = ar.sb("dsc_half", [128, 16], F32)
            m0 = ar.mark()
            tmp, _ = ar.sb("sc_tmp", [64, 128], F32)
            pscale, _ = ar.ps("sc_ps", [128, 64], F32)
            jobs = [("gate_scale", 64, self.gsc), ("up_scale", 64, self.usc),
                    ("down_scale", 16, self.dsc_half)]
            for name, n, dst in jobs:
                src = self.in_ap(name)
                k.wait("sync", "pe", self.k.val("pe"))
                ld = k.dma("sync", lambda eng, src=src, n=n: eng.dma_start(
                    out=tmp[0:n, :], in_=src.rearrange("(a p) -> a p", p=128)),
                    "c0")
                k.wait("tensor", "c0", ld)
                k.wait("tensor", "ve", self.k.val("ve"))
                mm = k.do("tensor", lambda eng, n=n: eng.matmul(
                    pscale[:, 0:n], tmp[0:n, :], self.c_eye_f32[0:n, 0:n],
                    start=True, stop=True), inc=("pe", 1))
                k.wait("vector", "pe", mm)
                k.do("vector", lambda eng, n=n, dst=dst: eng.tensor_copy(
                    dst[:, 0:n], pscale[:, 0:n]), inc=("ve", 1))
            k.do("vector", lambda eng: eng.tensor_scalar_mul(
                self.dsc_half[:, :], self.dsc_half[:, :], 0.5),
                inc=("ve", 1))
            self.dsc_ready = self.k.val("ve")
            ar.free_to(m0)
            self.k.barrier()
        else:
            self.dsc_ready = 0

    def in_ap(self, name):
        return self.inputs[name]



def emit_prep_cast(cm):
    nc, k = cm.nc, cm.k
    cm.wg_bf = nc.dram_tensor("wg_bf", [IF, D], BF16, kind="Internal").ap()
    cm.wu_bf = nc.dram_tensor("wu_bf", [IF, D], BF16, kind="Internal").ap()
    cm.wd_bf = nc.dram_tensor("wd_bf", [D, IF], BF16, kind="Internal").ap()
    cm.pcast_done = {}
    for (name, bf_out, rows) in (("gate_w", cm.wg_bf, IF),
                                 ("up_w", cm.wu_bf, IF),
                                 ("down_w", cm.wd_bf, D)):
        srcw = cm.in_ap(name)
        vals = []
        for hh in range(2):
            r0, r1 = hh * rows // 2, (hh + 1) * rows // 2
            k.dma("gpsimd", lambda eng, srcw=srcw, bf=bf_out, r0=r0, r1=r1:
                  eng.dma_start(out=bf[r0:r1, :], in_=srcw[r0:r1, :]),
                  f"pcast{hh}")
            vals.append(k.val(f"pcast{hh}"))
        cm.pcast_done[name] = tuple(vals)


def emit_prep_tile(cm, phase="all"):
    nc, k, ar = cm.nc, cm.k, cm.ar
    if phase in ("all", "gu"):
        cm.wgT = nc.dram_tensor("wgT", [64, 16, 128, 128], BF16,
                                kind="Internal").ap()
        cm.wuT = nc.dram_tensor("wuT", [64, 16, 128, 128], BF16,
                                kind="Internal").ap()
        cm.wdT = nc.dram_tensor("wdT", [16, 64, 128, 128], BF16,
                                kind="Internal").ap()
        cm._pt_slab = [ar.sb(f"wprep_s{jj}", [128, IF], BF16)[0]
                       for jj in range(2)]
        cm._pt_slab2 = [ar.sb(f"wprep_t{jj}", [128, IF], BF16)[0]
                        for jj in range(2)]
        cm._pt_state = ({}, {})
    slab, slab2 = cm._pt_slab, cm._pt_slab2
    sh_done, wr_done = cm._pt_state

    alljobs = [("g", kc) for kc in range(16)] + \
              [("u", kc) for kc in range(16)] + \
              [("d", ic) for ic in range(64)]
    if phase == "gu":
        tjobs = list(enumerate(alljobs))[:32]
    elif phase == "d":
        tjobs = list(enumerate(alljobs))[32:]
    else:
        tjobs = list(enumerate(alljobs))
    gate = {"g": "gate_w", "u": "up_w", "d": "down_w"}
    for j2, (which, idx) in tjobs:
        par = j2 % 2
        cols = IF if which in ("g", "u") else D
        for hh in range(2):
            k.wait("sync", f"pcast{hh}", cm.pcast_done[gate[which]][hh])
        if j2 >= 2:
            k.wait("sync", "ve", sh_done[j2 - 2])
        bf = {"g": cm.wg_bf, "u": cm.wu_bf, "d": cm.wd_bf}[which]
        t = k.dma("sync", lambda eng, bf=bf, idx=idx, par=par, cols=cols:
                  eng.dma_start_transpose(
                      slab[par][:, 0:cols],
                      bf[:, idx * 128:(idx + 1) * 128]),
                  f"ptr{par}")
        k.wait("vector", f"ptr{par}", t)
        if j2 >= 2:
            k.wait("vector", f"ptw{par}", wr_done[j2 - 2])
        sh = k.do("vector", lambda eng, par=par, cols=cols: eng.tensor_scalar(
            slab2[par][:, 0:cols], slab[par][:, 0:cols], -128.0, None,
            ALU.add))
        sh_done[j2] = sh
        k.wait("scalar", "ve", sh)
        if which in ("g", "u"):
            dst = cm.wgT if which == "g" else cm.wuT
            w = k.dma("scalar", lambda eng, dst=dst, idx=idx, par=par:
                      eng.dma_start(
                          out=dst[:, idx, :, :].rearrange("m h i -> h m i"),
                          in_=slab2[par][:, 0:IF].rearrange(
                              "h (m i) -> h m i", m=64)),
                      f"ptw{par}")
        else:
            w = k.dma("scalar", lambda eng, idx=idx, par=par:
                      eng.dma_start(
                          out=cm.wdT[:, idx, :, :].rearrange("d i c -> i d c"),
                          in_=slab2[par][:, 0:D].rearrange(
                              "i (d c) -> i d c", d=16)),
                      f"ptw{par}")
        wr_done[j2] = w
        if j2 == 31:
            cm.wprep_gu = (k.val("ptw0"), k.val("ptw1"))
    if phase in ("all", "d"):
        cm.wprep_ptw = (k.val("ptw0"), k.val("ptw1"))


def emit_prep_pe(cm, phase):
    nc, k, ar = cm.nc, cm.k, cm.ar
    if phase == "gu":
        cm.wgT = nc.dram_tensor("wgT", [64, 16, 128, 128], BF16,
                                kind="Internal").ap()
        cm.wuT = nc.dram_tensor("wuT", [64, 16, 128, 128], BF16,
                                kind="Internal").ap()
        cm.wdT = nc.dram_tensor("wdT", [16, 64, 128, 128], BF16,
                                kind="Internal").ap()
        jobs = []
        for b in range(64):
            jobs.append(("g", b))
            jobs.append(("u", b))
        cm.wprep_gu_m = []
    else:
        jobs = [("d", j) for j in range(64)]
    m0 = ar.mark()
    if phase == "gu":
        lw = [ar.sb(f"pp_l{j}", [128, 2048], BF16)[0] for j in range(2)]
        tb = [ar.sb(f"pp_t{j}", [128, 2048], BF16)[0] for j in range(2)]
        cm._pp_bufs = (lw, tb)
        seed_pe, seed_ac = k.val("pe"), k.val("ac")
        seed_wr = (0, 0)
    else:
        lw, tb = cm._pp_bufs
        seed_pe, seed_ac = cm._pp_pe_final, cm._pp_ac_final
        seed_wr = cm.wprep_gu
    mps = ar.mark()
    pst = [ar.ps(f"pp_p{phase}{j}", [128, 2048], F32)[0] for j in range(2)]
    srcs = {"g": cm.in_ap("gate_w"), "u": cm.in_ap("up_w"),
            "d": cm.in_ap("down_w")}
    lds, mm_done, ev_done, wr_done = {}, {}, {}, {}
    k.wait("tensor", "c0", cm.consts_ready)
    if phase == "d":
        k.wait("tensor", "ac", k.val("ac"))
    for j2, (which, idx) in enumerate(jobs):
        par = j2 % 2
        if which == "d":
            r0, c0 = (idx // 4) * 128, (idx % 4) * 2048
        else:
            r0, c0 = idx * 128, 0
        if j2 >= 2:
            k.wait("gpsimd", "pe", mm_done[j2 - 2])
        else:
            k.wait("gpsimd", "pe", seed_pe)
            k.wait("gpsimd", "ac", seed_ac)
        lds[j2] = k.dma("gpsimd", lambda eng, which=which, r0=r0, c0=c0,
                        par=par: eng.dma_start(
                            out=lw[par][:, :],
                            in_=srcs[which][r0:r0 + 128, c0:c0 + 2048]),
                        f"pls{par}")
        k.wait("tensor", f"pls{par}", lds[j2])
        if j2 >= 2:
            k.wait("tensor", "ac", ev_done[j2 - 2])
        for kc in range(16):
            k.do("tensor", lambda eng, kc=kc, par=par: eng.matmul(
                pst[par][:, kc * 128:(kc + 1) * 128],
                lw[par][:, kc * 128:(kc + 1) * 128],
                cm.c_eye_bf[:, :], start=True, stop=True),
                inc=("pe", 1) if kc == 15 else None)
        mm_done[j2] = k.val("pe")
        k.wait("scalar", "pe", mm_done[j2])
        if j2 >= 2:
            k.wait("scalar", f"ptw{par}", wr_done[j2 - 2])
        else:
            k.wait("scalar", f"ptw{par}", seed_wr[par])
        for q in range(4):
            k.do("scalar", lambda eng, q=q, par=par: eng.activation(
                tb[par][:, q * 512:(q + 1) * 512],
                pst[par][:, q * 512:(q + 1) * 512],
                AF.Copy, bias=-128.0),
                inc=("ac", 1) if q == 3 else None)
        ev_done[j2] = k.val("ac")
        k.wait("sync", "ac", ev_done[j2])
        if which in ("g", "u"):
            dst = cm.wgT if which == "g" else cm.wuT
            wr_done[j2] = k.dma("sync", lambda eng, dst=dst, idx=idx,
                                par=par: eng.dma_start(
                                    out=dst[idx].rearrange("a d h -> d a h"),
                                    in_=tb[par][:, :].rearrange(
                                        "d (a h) -> d a h", a=16)),
                                f"ptw{par}")
        else:
            wr_done[j2] = k.dma("sync", lambda eng, idx=idx, par=par:
                                eng.dma_start(
                                    out=cm.wdT[idx // 4,
                                               (idx % 4) * 16:
                                               (idx % 4 + 1) * 16].rearrange(
                                                   "m h c -> h m c"),
                                    in_=tb[par][:, :].rearrange(
                                        "h (m c) -> h m c", m=16)),
                                f"ptw{par}")
        if phase == "gu" and which == "u":
            cm.wprep_gu_m.append((k.val("ptw0"), k.val("ptw1")))
    if phase == "gu":
        cm.wprep_gu = (k.val("ptw0"), k.val("ptw1"))
        cm._pp_pe_final = k.val("pe")
        cm._pp_ac_final = k.val("ac")
    else:
        cm.wprep_ptw = (k.val("ptw0"), k.val("ptw1"))
    ar.free_to(mps)
    return m0


def emit_ffn_weight_prep(cm):
    m0 = cm.ar.mark()
    emit_prep_cast(cm)
    emit_prep_tile(cm)
    cm.k.barrier()
    cm.ar.free_to(m0)



def emit_tm_to_fm(cm, src_tm, dst_fm, T, t0=0):
    nc, k, ar = cm.nc, cm.k, cm.ar
    assert T % 128 == 0
    nt = T // 128
    m0 = ar.mark()
    xt = [ar.sb(f"t2f_x{t}", [128, D], F32)[0] for t in range(nt)]
    xo = [ar.sb(f"t2f_o{d}", [128, T], F32)[0] for d in range(16)]
    pt, _ = ar.ps("t2f_ps", [128, 512], F32)
    pt2, _ = ar.ps("t2f_ps2", [128, 512], F32)

    for t in range(nt):
        k.dma("sync", lambda eng, t=t: eng.dma_start(
            out=xt[t][:, :],
            in_=src_tm[t0 + t * 128:t0 + (t + 1) * 128, :]), "t2f")
    ld_all = k.val("t2f")
    evac_cnt = {}
    for d in range(16):
        for t in range(nt):
            par = (d * nt + t) // 4 % 2
            ps = pt if par == 0 else pt2
            k.wait("tensor", "t2f", ld_all)
            grp = (d * nt + t) // 4
            if grp >= 2:
                k.wait("tensor", "ve", evac_cnt.get(grp - 2, 0))
            k.do("tensor", lambda eng, d=d, t=t, ps=ps: eng.matmul(
                ps[:, (t % 4) * 128:(t % 4 + 1) * 128],
                xt[t][:, d * 128:(d + 1) * 128],
                cm.c_eye_f32[:, :], start=True, stop=True),
                inc=("pe", 1) if (d * nt + t) % 4 == 3 or t == nt - 1 else None)
            if (d * nt + t) % 4 == 3 or t == nt - 1:
                pedone = k.val("pe")
                k.wait("vector", "pe", pedone)
                lo = (t // 4) * 4
                nn = (t - lo + 1) * 128
                ev = k.do("vector", lambda eng, d=d, lo=lo, nn=nn, ps=ps:
                          eng.tensor_copy(
                              xo[d][:, lo * 128:lo * 128 + nn],
                              ps[:, 0:nn]),
                          inc=("ve", 1))
                evac_cnt[grp] = ev
    for d in range(16):
        k.wait("scalar", "ve", k.val("ve"))
        k.dma("scalar", lambda eng, d=d: eng.dma_start(
            out=dst_fm[d * 128:(d + 1) * 128, t0:t0 + T],
            in_=xo[d][:, :]), "t2fo")
    out_dA = k.val("t2fo")
    ar.free_to(m0)
    return ("t2fo", out_dA)



def emit_ln_pipeline(cm, x_fm, t0s, also_dram, last_sbuf=None,
                     keep_mark=False):
    nc, k, ar = cm.nc, cm.k, cm.ar
    T = 1024
    m0 = ar.mark()
    xbb = [ar.sb(f"lp_x{j}", [128, T], BF16)[0] for j in range(2)]
    xq = [ar.sb(f"lp_q{j}", [128, T], BF16)[0] for j in range(2)]
    xab = [ar.sb(f"lp_a{j}", [128, T], BF16)[0] for j in range(2)]
    stage = [ar.sb(f"lp_st{j}", [128, T], BF16)[0] for j in range(2)]
    rowbuf, _ = ar.sb("lp_rows", [1, 3 * T], F32)
    abc, _ = ar.sb("lp_abc", [128, T], BF16)
    bbc, _ = ar.sb("lp_bbc", [128, T], BF16)
    mpsum = ar.mark()
    prow, _ = ar.ps("lp_prow", [1, 2 * T], F32)
    pbc, _ = ar.ps("lp_pbc", [128, T], F32)
    pbc2, _ = ar.ps("lp_pbc2", [128, T], F32)
    mu = rowbuf[:, 0:T]
    arow = rowbuf[:, T:2 * T]
    brow = rowbuf[:, 2 * T:3 * T]

    k.wait("tensor", "c0", cm.consts_ready)
    st = dict(stats_pe=k.val("pe"), sq_ac=k.val("ac"), ve=k.val("ve"),
              prow_ve=k.val("ve"), bc_pe=k.val("pe"), evac_ve=k.val("ve"),
              app_ve=k.val("ve"), st0=k.val("lnst0"), st1=k.val("lnst1"))

    for ci, t0 in enumerate(t0s):
        is_last = (ci == len(t0s) - 1) and last_sbuf is not None
        p_stats, p_sq, p_ve = st["stats_pe"], st["sq_ac"], st["ve"]
        p_prow, p_bc, p_evac = st["prow_ve"], st["bc_pe"], st["evac_ve"]
        p_app, p_st0, p_st1 = st["app_ve"], st["st0"], st["st1"]
        lds, lds2, sq_ac, mmq = {}, {}, {}, {}

        def load_x(kc):
            par = kc % 2
            if kc >= 2:
                k.wait("gpsimd", "pe", mmq[kc - 2])
                k.wait("gpsimd", "ac", sq_ac[kc - 2])
            else:
                k.wait("gpsimd", "pe", p_stats)
                k.wait("gpsimd", "ac", p_sq)
                k.wait("gpsimd", "ve", p_ve)
            lds[kc] = k.dma("gpsimd", lambda eng, kc=kc, par=par, t0=t0:
                            eng.dma_start(
                                out=xbb[par][:, :],
                                in_=x_fm[kc * 128:(kc + 1) * 128,
                                         t0:t0 + T]), f"lpx{par}")

        def load_x2(kc):
            par = kc % 2
            if kc >= 2:
                k.wait("gpsimd", "ve", app_ve[kc - 2])
            else:
                k.wait("gpsimd", "ve", p_app)
            lds2[kc] = k.dma("gpsimd", lambda eng, kc=kc, par=par, t0=t0:
                             eng.dma_start(
                                 out=xab[par][:, :],
                                 in_=x_fm[kc * 128:(kc + 1) * 128,
                                          t0:t0 + T]), f"lpa{par}")

        load_x(0)
        load_x(1)
        for kc in range(16):
            par = kc % 2
            k.wait("scalar", f"lpx{par}", lds[kc])
            if kc >= 2:
                k.wait("scalar", "pe", mmq[kc - 2])
            else:
                k.wait("scalar", "pe", p_stats)
            sq_ac[kc] = k.do("scalar", lambda eng, kc=kc, par=par:
                             eng.activation(xq[par][:, :], xbb[par][:, :],
                                            AF.Square), inc=("ac", 1))
            k.wait("tensor", f"lpx{par}", lds[kc])
            k.wait("tensor", "ac", sq_ac[kc])
            if kc == 0:
                k.wait("tensor", "ve", p_prow)
            for nb in range(T // 512):
                k.do("tensor", lambda eng, nb=nb, kc=kc, par=par: eng.matmul(
                    prow[:, nb * 512:(nb + 1) * 512],
                    cm.c_ones_col[:, :],
                    xbb[par][:, nb * 512:(nb + 1) * 512],
                    start=(kc == 0), stop=(kc == 15)), inc=None)
                k.do("tensor", lambda eng, nb=nb, kc=kc, par=par: eng.matmul(
                    prow[:, T + nb * 512:T + (nb + 1) * 512],
                    cm.c_ones_col[:, :],
                    xq[par][:, nb * 512:(nb + 1) * 512],
                    start=(kc == 0), stop=(kc == 15)),
                    inc=("pe", 1))
            mmq[kc] = k.val("pe")
            if kc < 14:
                load_x(kc + 2)
            elif kc == 14:
                load_x2(0)
            else:
                load_x2(1)
        stats_pe = k.val("pe")
        k.wait("vector", "pe", stats_pe)
        k.do("vector", lambda eng: eng.tensor_scalar_mul(
            mu, prow[:, 0:T], 1.0 / D))
        k.do("vector", lambda eng: eng.tensor_scalar_mul(
            arow, prow[:, T:2 * T], 1.0 / D), inc=("ve", 1))
        prow_ve = k.val("ve")
        k.do("vector", lambda eng: eng.scalar_tensor_tensor(
            brow, mu, -1.0, mu, ALU.mult, ALU.mult))
        k.do("vector", lambda eng: eng.tensor_add(arow, arow, brow))
        vq = k.do("vector", lambda eng: eng.tensor_scalar_add(
            arow, arow, EPS), inc=("ve", 1))
        k.wait("scalar", "ve", vq)
        k.wait("scalar", "pe", p_bc)
        k.do("scalar", lambda eng: eng.activation(arow, arow, AF.Ln))
        rstd_ac = k.do("scalar", lambda eng: eng.activation(
            arow, arow, AF.Exp, scale=-0.5), inc=("ac", 1))
        k.wait("vector", "ac", rstd_ac)
        k.do("vector", lambda eng: eng.scalar_tensor_tensor(
            brow, mu, -1.0, arow, ALU.mult, ALU.mult), inc=("ve", 1))
        rowdone = k.val("ve")
        k.wait("tensor", "ve", rowdone)
        for nb in range(T // 512):
            k.do("tensor", lambda eng, nb=nb: eng.matmul(
                pbc[:, nb * 512:(nb + 1) * 512], cm.c_ones_row[:, :],
                arow[:, nb * 512:(nb + 1) * 512], start=True, stop=True))
            k.do("tensor", lambda eng, nb=nb: eng.matmul(
                pbc2[:, nb * 512:(nb + 1) * 512], cm.c_ones_row[:, :],
                brow[:, nb * 512:(nb + 1) * 512], start=True, stop=True),
                inc=("pe", 1) if nb == T // 512 - 1 else None)
        bc_pe = k.val("pe")
        k.wait("vector", "pe", bc_pe)
        k.do("vector", lambda eng: eng.tensor_copy(abc[:, :], pbc[:, :]))
        k.do("vector", lambda eng: eng.tensor_copy(bbc[:, :], pbc2[:, :]),
             inc=("ve", 1))
        evac_ve = k.val("ve")
        app_ve, st_dma = {}, {}
        for kc in range(16):
            par = kc % 2
            k.wait("vector", f"lpa{par}", lds2[kc])
            if is_last:
                dst = last_sbuf[:, kc * T:(kc + 1) * T]
            else:
                if kc >= 2:
                    k.wait("vector", f"lnst{par}", st_dma[kc - 2])
                elif par == 0:
                    k.wait("vector", "lnst0", p_st0)
                else:
                    k.wait("vector", "lnst1", p_st1)
                dst = stage[par][:, :]
            k.do("vector", lambda eng, par=par: eng.tensor_mul(
                xab[par][:, :], xab[par][:, :], abc[:, :]))
            app_ve[kc] = k.do("vector", lambda eng, dst=dst, par=par:
                              eng.tensor_tensor(
                                  dst, xab[par][:, :], bbc[:, :], ALU.add),
                              inc=("ve", 1))
            if kc < 14:
                load_x2(kc + 2)
            k.wait("scalar", "ve", app_ve[kc])
            srcap = (last_sbuf[:, kc * T:(kc + 1) * T] if is_last
                     else stage[par][:, :])
            st_dma[kc] = k.dma("scalar", lambda eng, kc=kc, srcap=srcap,
                               t0=t0: eng.dma_start(
                                   out=also_dram[kc * 128:(kc + 1) * 128,
                                                 t0:t0 + T],
                                   in_=srcap), f"lnst{par}")
        st = dict(stats_pe=stats_pe, sq_ac=sq_ac[15], ve=k.val("ve"),
                  prow_ve=prow_ve, bc_pe=bc_pe, evac_ve=evac_ve,
                  app_ve=app_ve[15], st0=k.val("lnst0"), st1=k.val("lnst1"))
    ar.free_to(mpsum if keep_mark else m0)
    return dict(ln_ve=st["app_ve"], st0=st["st0"], st1=st["st1"],
                pe=st["bc_pe"])


def emit_ln_chunk(cm, x_fm, t0, lnsb, also_dram=None, ldq="sync"):
    nc, k, ar = cm.nc, cm.k, cm.ar
    T = 1024
    m0 = ar.mark()
    stage = None
    if lnsb is None:
        assert also_dram is not None
        stage = [ar.sb(f"ln_st{j}", [128, T], BF16)[0] for j in range(2)]
    xs = [ar.sb(f"ln_x{j}", [128, T], F32)[0] for j in range(2)]
    xb = [ar.sb(f"ln_b{j}", [128, T], BF16)[0] for j in range(2)]
    xq = [ar.sb(f"ln_q{j}", [128, T], BF16)[0] for j in range(2)]
    rowbuf, _ = ar.sb("ln_rows", [1, 2 * T], F32)
    abc, _ = ar.sb("ln_abc", [128, T], F32)
    bbc, _ = ar.sb("ln_bbc", [128, T], F32)
    prow, _ = ar.ps("ln_prow", [1, 2 * T], F32)
    pbc, _ = ar.ps("ln_pbc", [128, T], F32)
    pbc2, _ = ar.ps("ln_pbc2", [128, T], F32)

    mu = rowbuf[:, 0:T]
    arow = rowbuf[:, T:2 * T]
    msq = arow
    var = arow
    sd = arow
    brow = abc[0:1, 0:T]

    k.wait("tensor", "c0", cm.consts_ready)
    lds, cast_ve, sq_ac, mmq = {}, {}, {}, {}

    def load_x(kc):
        par = kc % 2
        if kc >= 2:
            k.wait(ldq, "ve", cast_ve[kc - 2])
        lds[kc] = k.dma(ldq, lambda eng, kc=kc, par=par: eng.dma_start(
            out=xs[par][:, :],
            in_=x_fm[kc * 128:(kc + 1) * 128, t0:t0 + T]), f"lnx{par}")

    load_x(0)
    load_x(1)
    for kc in range(16):
        par = kc % 2
        k.wait("vector", f"lnx{par}", lds[kc])
        if kc >= 2:
            k.wait("vector", "pe", mmq[kc - 2])
        cast_ve[kc] = k.do("vector", lambda eng, kc=kc, par=par:
                           eng.tensor_copy(xb[par][:, :], xs[par][:, :]),
                           inc=("ve", 1))
        if kc < 14:
            load_x(kc + 2)
        k.wait("scalar", "ve", cast_ve[kc])
        if kc >= 2:
            k.wait("scalar", "pe", mmq[kc - 2])
        sq_ac[kc] = k.do("scalar", lambda eng, kc=kc, par=par: eng.activation(
            xq[par][:, :], xb[par][:, :], AF.Square), inc=("ac", 1))
        k.wait("tensor", "ac", sq_ac[kc])
        for nb in range(T // 512):
            k.do("tensor", lambda eng, nb=nb, kc=kc, par=par: eng.matmul(
                prow[:, nb * 512:(nb + 1) * 512],
                cm.c_ones_col[:, :],
                xb[par][:, nb * 512:(nb + 1) * 512],
                start=(kc == 0), stop=(kc == 15)), inc=None)
            k.do("tensor", lambda eng, nb=nb, kc=kc, par=par: eng.matmul(
                prow[:, T + nb * 512:T + (nb + 1) * 512],
                cm.c_ones_col[:, :],
                xq[par][:, nb * 512:(nb + 1) * 512],
                start=(kc == 0), stop=(kc == 15)),
                inc=("pe", 1) if nb == T // 512 - 1 else None)
        mmq[kc] = k.val("pe")
    stats_pe = k.val("pe")
    k.wait("vector", "pe", stats_pe)
    k.do("vector", lambda eng: eng.tensor_scalar_mul(mu, prow[:, 0:T], 1.0 / D))
    k.do("vector", lambda eng: eng.tensor_scalar_mul(
        msq, prow[:, T:2 * T], 1.0 / D))
    k.do("vector", lambda eng: eng.scalar_tensor_tensor(
        brow, mu, -1.0, mu, ALU.mult, ALU.mult))
    k.do("vector", lambda eng: eng.tensor_add(var, msq, brow))
    k.do("vector", lambda eng: eng.tensor_scalar_add(var, var, EPS),
         inc=("ve", 1))
    k.wait("scalar", "ve", k.val("ve"))
    k.do("scalar", lambda eng: eng.activation(sd, var, AF.Ln))
    k.do("scalar", lambda eng: eng.activation(arow, sd, AF.Exp, scale=-0.5),
         inc=("ac", 1))
    k.wait("vector", "ac", k.val("ac"))
    k.do("vector", lambda eng: eng.scalar_tensor_tensor(
        brow, mu, -1.0, arow, ALU.mult, ALU.mult), inc=("ve", 1))
    rowdone = k.val("ve")
    k.wait("tensor", "ve", rowdone)
    for nb in range(T // 512):
        k.do("tensor", lambda eng, nb=nb: eng.matmul(
            pbc[:, nb * 512:(nb + 1) * 512], cm.c_ones_row[:, :],
            arow[:, nb * 512:(nb + 1) * 512], start=True, stop=True))
        k.do("tensor", lambda eng, nb=nb: eng.matmul(
            pbc2[:, nb * 512:(nb + 1) * 512], cm.c_ones_row[:, :],
            brow[:, nb * 512:(nb + 1) * 512], start=True, stop=True),
            inc=("pe", 1) if nb == T // 512 - 1 else None)
    bc_pe = k.val("pe")
    k.wait("vector", "pe", bc_pe)
    k.do("vector", lambda eng: eng.tensor_copy(abc[:, :], pbc[:, :]))
    k.do("vector", lambda eng: eng.tensor_copy(bbc[:, :], pbc2[:, :]),
         inc=("ve", 1))
    bc_ve = k.val("ve")
    lds2, app_ve = {}, {}

    def load_x2(kc):
        par = kc % 2
        if kc >= 2:
            k.wait(ldq, "ve", app_ve[kc - 2])
        lds2[kc] = k.dma(ldq, lambda eng, kc=kc, par=par: eng.dma_start(
            out=xs[par][:, :],
            in_=x_fm[kc * 128:(kc + 1) * 128, t0:t0 + T]), f"lnx{par}")

    load_x2(0)
    load_x2(1)
    st_dma = {}
    for kc in range(16):
        par = kc % 2
        k.wait("vector", f"lnx{par}", lds2[kc])
        k.wait("vector", "ve", bc_ve)
        k.do("vector", lambda eng, kc=kc, par=par: eng.tensor_mul(
            xs[par][:, :], xs[par][:, :], abc[:, :]))
        if stage is not None:
            if kc >= 2:
                k.wait("vector", f"lnst{par}", st_dma[kc - 2])
            dst = stage[par][:, :]
        else:
            dst = lnsb[:, kc * T:(kc + 1) * T]
        app_ve[kc] = k.do("vector", lambda eng, dst=dst, par=par:
                          eng.tensor_tensor(
                              dst, xs[par][:, :],
                              bbc[:, :], ALU.add))
        if kc < 14:
            load_x2(kc + 2)
        if also_dram is not None:
            k.wait("scalar", "ve", app_ve[kc])
            srcap = stage[par][:, :] if stage is not None \
                else lnsb[:, kc * T:(kc + 1) * T]
            st_dma[kc] = k.dma("scalar", lambda eng, kc=kc, srcap=srcap,
                               t0=t0: eng.dma_start(
                                   out=also_dram[kc * 128:(kc + 1) * 128,
                                                 t0:t0 + T],
                                   in_=srcap), f"lnst{par}")
    res = dict(ln_ve=k.val("ve"))
    ar.free_to(m0)
    return res



def emit_ffn(cm, x_fm, t0, xout_fm, dbg_ln=None, ln_dram=None,
             gu_gate=None, mid_emit=None):
    nc, k, ar = cm.nc, cm.k, cm.ar
    T = 1024
    m0 = ar.mark()
    hsb, _ = ar.sb("ffn_h", [128, 64 * T], BF16)
    mln = ar.mark()
    lnsb, _ = ar.sb("ffn_ln", [128, 16 * T], BF16)
    if ln_dram is None:
        lnres = emit_ln_chunk(cm, x_fm, t0, lnsb, also_dram=dbg_ln)
        ln_ready_ve = lnres["ln_ve"]
        k.barrier()
    else:
        k.barrier(("sync",), exclude=("pcast0", "pcast1", "ptr0", "ptr1",
                                      "ptw0", "ptw1"))
        for kc in range(16):
            k.dma("sync", lambda eng, kc=kc: eng.dma_start(
                out=lnsb[:, kc * T:(kc + 1) * T],
                in_=ln_dram[kc * 128:(kc + 1) * 128, t0:t0 + T]), "fl")
        fl_all = k.val("fl")
        k.wait("tensor", "fl", fl_all)
        ln_ready_ve = 0
    wsg = [ar.sb(f"ffn_wg{j}", [128, 16 * 128], BF16)[0] for j in range(2)]
    wsu = [ar.sb(f"ffn_wu{j}", [128, 16 * 128], BF16)[0] for j in range(2)]
    sg = [ar.sb(f"ffn_sg{j}", [128, T], BF16)[0] for j in range(2)]
    su = [ar.sb(f"ffn_su{j}", [128, T], BF16)[0] for j in range(2)]
    pg = [ar.ps(f"ffn_pg{j}", [128, T], F32)[0] for j in range(2)]
    pu = [ar.ps(f"ffn_pu{j}", [128, T], F32)[0] for j in range(2)]

    k.wait("tensor", "ve", ln_ready_ve)
    k.wait("tensor", "ve", cm.dsc_ready)
    if gu_gate is None:
        k.wait("sync", "ptw0", cm.wprep_gu[0])
        k.wait("sync", "ptw1", cm.wprep_gu[1])

    wld, mm_done, act_done, h_done = {}, {}, {}, {}

    def load_w(m):
        par = m % 2
        if m >= 2:
            k.wait("sync", "pe", mm_done[m - 2])
        if gu_gate is not None:
            k.wait("sync", "ptw0", gu_gate[m][0])
            k.wait("sync", "ptw1", gu_gate[m][1])
        k.dma("sync", lambda eng, m=m, par=par: eng.dma_start(
            out=wsg[par][:, :].rearrange("p (a f) -> p a f", a=16),
            in_=cm.wgT[m].rearrange("a h i -> h a i")), f"fw{par}")
        wld[m] = k.dma("sync", lambda eng, m=m, par=par: eng.dma_start(
            out=wsu[par][:, :].rearrange("p (a f) -> p a f", a=16),
            in_=cm.wuT[m].rearrange("a h i -> h a i")), f"fw{par}")

    load_w(0)
    load_w(1)
    for m in range(64):
        par = m % 2
        k.wait("tensor", f"fw{par}", wld[m])
        if m >= 2:
            k.wait("tensor", "ac", act_done[m - 2])
        for which in range(2):
            ps = pg[par] if which == 0 else pu[par]
            ws = wsg[par] if which == 0 else wsu[par]
            for half in range(2):
                c0 = half * 512
                for kc in range(16):
                    last = (which == 1 and half == 1 and kc == 15)
                    k.do("tensor", lambda eng, ps=ps, ws=ws, kc=kc, c0=c0:
                         eng.matmul(ps[:, c0:c0 + 512],
                                    ws[:, kc * 128:(kc + 1) * 128],
                                    lnsb[:, kc * T + c0:kc * T + c0 + 512],
                                    start=(kc == 0), stop=(kc == 15)),
                         inc=("pe", 1) if last else None)
        mm_done[m] = k.val("pe")
        if m < 62:
            load_w(m + 2)
        k.wait("scalar", "pe", mm_done[m])
        if m >= 2:
            k.wait("scalar", "ve", h_done[m - 2])
        k.do("scalar", lambda eng, m=m, par=par: eng.activation(
            sg[par][:, :], pg[par][:, :], AF.Silu,
            scale=cm.gsc[:, m:m + 1]))
        act_done[m] = k.do("scalar", lambda eng, m=m, par=par: eng.activation(
            su[par][:, :], pu[par][:, :], AF.Copy,
            scale=cm.usc[:, m:m + 1]), inc=("ac", 1))
        k.wait("vector", "ac", act_done[m])
        h_done[m] = k.do("vector", lambda eng, m=m, par=par: eng.tensor_mul(
            hsb[:, m * T:(m + 1) * T], sg[par][:, :], su[par][:, :]),
            inc=("ve", 1))
    h_all = k.val("ve")

    ar.free_to(mln)
    if mid_emit is not None:
        mid_emit()
    wsd = [ar.sb(f"ffn_wd{j}", [128, 64 * 128], BF16)[0] for j in range(2)]
    xr = [ar.sb(f"ffn_xr{j}", [128, T], F32)[0] for j in range(2)]
    xo = [ar.sb(f"ffn_xo{j}", [128, T], F32)[0] for j in range(2)]
    pd = [ar.ps(f"ffn_pd{j}", [128, T], F32)[0] for j in range(2)]

    dld, xld, dmm_done, dev_done, dst_done = {}, {}, {}, {}, {}

    def load_d(d):
        par = d % 2
        k.wait("sync", "ptw0", cm.wprep_ptw[0])
        k.wait("sync", "ptw1", cm.wprep_ptw[1])
        if d < 2:
            k.wait("scalar", "ptw0", cm.wprep_ptw[0])
            k.wait("scalar", "ptw1", cm.wprep_ptw[1])
        if d >= 2:
            k.wait("sync", "pe", dmm_done[d - 2])
            k.wait("scalar", "ve", dev_done[d - 2])
        else:
            k.wait("sync", "pe", mm_done[63])
            k.wait("sync", "ve", h_all)
            k.wait("scalar", "ve", h_all)
        dld[d] = k.dma("sync", lambda eng, d=d, par=par: eng.dma_start(
            out=wsd[par][:, :].rearrange("p (a f) -> p a f", a=64),
            in_=cm.wdT[d].rearrange("m i c -> i m c")), f"fd{par}")
        xld[d] = k.dma("scalar", lambda eng, d=d, par=par: eng.dma_start(
            out=xr[par][:, :],
            in_=x_fm[d * 128:(d + 1) * 128, t0:t0 + T]), f"fx{par}")

    load_d(0)
    load_d(1)
    k.wait("tensor", "ve", h_all)
    k.wait("tensor", "ac", k.val("ac"))
    for d in range(16):
        par = d % 2
        k.wait("tensor", f"fd{par}", dld[d])
        if d >= 2:
            k.wait("tensor", "ve", dev_done[d - 2])
        for half in range(2):
            c0 = half * 512
            for m in range(64):
                last = (half == 1 and m == 63)
                k.do("tensor", lambda eng, d=d, par=par, m=m, c0=c0:
                     eng.matmul(pd[par][:, c0:c0 + 512],
                                wsd[par][:, m * 128:(m + 1) * 128],
                                hsb[:, m * T + c0:m * T + c0 + 512],
                                start=(m == 0), stop=(m == 63)),
                     inc=("pe", 1) if last else None)
        dmm_done[d] = k.val("pe")
        k.wait("vector", "pe", dmm_done[d])
        k.wait("vector", f"fx{par}", xld[d])
        if d >= 2:
            k.wait("vector", f"fo{par}", dst_done[d - 2])
        dev_done[d] = k.do("vector", lambda eng, d=d, par=par:
                           eng.scalar_tensor_tensor(
                               xo[par][:, :], pd[par][:, :],
                               cm.dsc_half[:, d:d + 1], xr[par][:, :],
                               ALU.mult, ALU.add), inc=("ve", 1))
        k.wait("scalar", "ve", dev_done[d])
        dst_done[d] = k.dma("scalar", lambda eng, d=d, par=par: eng.dma_start(
            out=xout_fm[d * 128:(d + 1) * 128, t0:t0 + T],
            in_=xo[par][:, :]), f"fo{par}")
        if d < 14:
            load_d(d + 2)
    res = dict(x_fo=(k.val("fo0"), k.val("fo1")))
    ar.free_to(m0)
    return res



def emit_fm_to_tm_out(cm, src_fm, dst_tm):
    nc, k, ar = cm.nc, cm.k, cm.ar
    m0 = ar.mark()
    xs = [ar.sb(f"f2t_x{d}", [128, 1024], F32)[0] for d in range(16)]
    ot = [ar.sb(f"f2t_o{t}", [128, D], F32)[0] for t in range(8)]
    ps = [ar.ps(f"f2t_p{j}", [128, 512], F32)[0] for j in range(2)]

    for d in range(16):
        k.dma("sync", lambda eng, d=d: eng.dma_start(
            out=xs[d][:, :], in_=src_fm[d * 128:(d + 1) * 128, :]), "f2t")
    ld_all = k.val("f2t")
    ev = {}
    for t in range(8):
        for d in range(16):
            g = (t * 16 + d) // 4
            par = g % 2
            k.wait("tensor", "f2t", ld_all)
            if g >= 2:
                k.wait("tensor", "ve", ev.get(g - 2, 0))
            k.do("tensor", lambda eng, t=t, d=d, par=par: eng.matmul(
                ps[par][:, (d % 4) * 128:(d % 4 + 1) * 128],
                xs[d][:, t * 128:(t + 1) * 128],
                cm.c_eye_f32[:, :], start=True, stop=True),
                inc=("pe", 1) if d % 4 == 3 else None)
            if d % 4 == 3:
                k.wait("vector", "pe", k.val("pe"))
                ev[g] = k.do("vector", lambda eng, t=t, d=d, par=par:
                             eng.tensor_copy(
                                 ot[t][:, (d - 3) * 128:(d + 1) * 128],
                                 ps[par][:, :]), inc=("ve", 1))
    for t in range(8):
        k.wait("gpsimd", "ve", k.val("ve"))
        k.dma("gpsimd", lambda eng, t=t: eng.dma_start(
            out=dst_tm[t * 128:(t + 1) * 128, :], in_=ot[t][:, :]), "outd")
    res = k.val("outd")
    ar.free_to(m0)
    return res



def emit_cols(cm, src1d, n, dst_sb, scale_after=None):
    nc, k, ar = cm.nc, cm.k, cm.ar
    m0 = ar.mark()
    tmp, _ = ar.sb("col_tmp", [64, 128], F32)
    pscale, _ = ar.ps("col_ps", [128, 64], F32)
    k.wait("sync", "pe", k.val("pe"))
    ld = k.dma("sync", lambda eng: eng.dma_start(
        out=tmp[0:n, :], in_=src1d.rearrange("(a p) -> a p", p=128)),
        "c0")
    k.wait("tensor", "c0", ld)
    k.wait("tensor", "ve", k.val("ve"))
    mm = k.do("tensor", lambda eng: eng.matmul(
        pscale[:, 0:n], tmp[0:n, :], cm.c_eye_f32[0:n, 0:n],
        start=True, stop=True), inc=("pe", 1))
    k.wait("vector", "pe", mm)
    r = k.do("vector", lambda eng: eng.tensor_copy(dst_sb[:, 0:n],
                                                   pscale[:, 0:n]))
    if scale_after is not None:
        r = k.do("vector", lambda eng: eng.tensor_scalar_mul(
            dst_sb[:, 0:n], dst_sb[:, 0:n], scale_after))
    ar.free_to(m0)
    k.barrier()
    return r



def emit_rope_tables(cm, fr_ap, C2, S2, TK=4096):
    nc, k, ar = cm.nc, cm.k, cm.ar
    m0 = ar.mark()
    th_tm = [ar.sb(f"th_tm{j}", [128, 32], F32)[0] for j in range(2)]
    th, _ = ar.sb("th_fm", [32, TK], F32)
    s2, _ = ar.sb("th_s2", [32, TK], F32)
    c2, _ = ar.sb("th_c2", [32, TK], F32)
    pth, _ = ar.ps("th_ps", [32, 512], F32)
    pbc, _ = ar.ps("th_pbc", [128, 512], F32)

    k.wait("tensor", "c0", cm.consts_ready)
    lds, mms, grp_ev = {}, {}, {}
    for t in range(TK // 128):
        par = t % 2
        if t >= 2:
            k.wait("sync", "pe", mms[t - 2])
        lds[t] = k.dma("sync", lambda eng, t=t, par=par: eng.dma_start(
            out=th_tm[par][:, :], in_=fr_ap[t * 128:(t + 1) * 128, :]),
            f"thl{par}")
        k.wait("tensor", f"thl{par}", lds[t])
        if t % 4 == 0 and t >= 4:
            k.wait("tensor", "ve", grp_ev[t // 4 - 1])
        mms[t] = k.do("tensor", lambda eng, t=t, par=par: eng.matmul(
            pth[0:32, (t % 4) * 128:(t % 4 + 1) * 128],
            th_tm[par][:, :], cm.c_eye_f32[:, :], start=True, stop=True),
            inc=("pe", 1))
        if t % 4 == 3:
            k.wait("vector", "pe", mms[t])
            grp_ev[t // 4] = k.do("vector", lambda eng, t=t: eng.tensor_copy(
                th[:, (t - 3) * 128:(t + 1) * 128], pth[0:32, :]))
    k.wait("vector", "pe", k.val("pe"))
    k.do("vector", lambda eng: eng.tensor_scalar(
        th[:, :], th[:, :], 0.5, -PI / 2, ALU.mult, ALU.add))
    k.wait("scalar", "ve", k.val("ve"))
    k.do("scalar", lambda eng: eng.activation(s2[:, :], th[:, :], AF.Sin))
    k.do("vector", lambda eng: eng.tensor_scalar(
        c2[:, :], th[:, :], -1.0, PI / 2, ALU.mult, ALU.add))
    k.wait("scalar", "ve", k.val("ve"))
    a1 = k.do("scalar", lambda eng: eng.activation(c2[:, :], c2[:, :], AF.Sin))
    k.wait("vector", "ac", a1)
    k.do("vector", lambda eng: eng.scalar_tensor_tensor(
        c2[:, :], s2[:, :], -2.0, c2[:, :], ALU.mult, ALU.mult))
    k.do("vector", lambda eng: eng.tensor_mul(s2[:, :], s2[:, :], s2[:, :]))
    vv = k.do("vector", lambda eng: eng.tensor_scalar(
        s2[:, :], s2[:, :], 2.0, -1.0, ALU.mult, ALU.add))
    k.wait("tensor", "ve", vv)
    for nb in range(TK // 512):
        m1 = k.do("tensor", lambda eng, nb=nb: eng.matmul(
            pbc[:, :], cm.c_dup128[:, :], s2[:, nb * 512:(nb + 1) * 512],
            start=True, stop=True), inc=("pe", 1))
        k.wait("vector", "pe", m1)
        cv = k.do("vector", lambda eng, nb=nb: eng.tensor_copy(
            C2[:, nb * 512:(nb + 1) * 512], pbc[:, :]))
        k.wait("tensor", "ve", cv)
        m2 = k.do("tensor", lambda eng, nb=nb: eng.matmul(
            pbc[:, :], cm.c_dups128[:, :], c2[:, nb * 512:(nb + 1) * 512],
            start=True, stop=True), inc=("pe", 1))
        k.wait("vector", "pe", m2)
        sv = k.do("vector", lambda eng, nb=nb: eng.tensor_copy(
            S2[:, nb * 512:(nb + 1) * 512], pbc[:, :]))
        k.wait("tensor", "ve", sv)
    ar.free_to(m0)


def emit_rope_apply(cm, io_bf, psw, t1, t2, C2s, S2s, W, out_ap):
    k = cm.k
    for c0 in range(0, W, 512):
        mm = k.do("tensor", lambda eng, c0=c0: eng.matmul(
            psw[:, c0:c0 + 512], cm.c_p128[:, :], io_bf[:, c0:c0 + 512],
            start=True, stop=True), inc=("pe", 1))
    k.do("vector", lambda eng: eng.tensor_tensor(
        t1[:, 0:W], io_bf[:, 0:W], C2s, ALU.mult))
    k.wait("vector", "pe", mm)
    k.do("vector", lambda eng: eng.tensor_tensor(
        t2[:, 0:W], psw[:, 0:W], S2s, ALU.mult))
    return k.do("vector", lambda eng: eng.tensor_tensor(
        out_ap, t1[:, 0:W], t2[:, 0:W], ALU.add))



def emit_qproj(cm, ln2_own, wq_ap, C2, S2, qn_sb, qr_sb):
    nc, k, ar = cm.nc, cm.k, cm.ar
    T = 1024
    m0 = ar.mark()
    wqb = [ar.sb(f"qp_w{j}", [128, 16 * 384], BF16)[0] for j in range(2)]
    qrb, _ = ar.sb("qp_qrb", [128, T], BF16)
    t1, _ = ar.sb("qp_t1", [128, T], F32)
    t2, _ = ar.sb("qp_t2", [128, T], F32)
    pqa, _ = ar.ps("qp_pa", [128, T], F32)
    pqb, _ = ar.ps("qp_pb", [128, T], F32)
    pqr, _ = ar.ps("qp_pr", [128, T], F32)
    psw, _ = ar.ps("qp_psw", [128, T], F32)

    wld, pair_done = {}, {}

    def load_wq(p):
        par = p % 2
        if p >= 2:
            k.wait("gpsimd", "pe", pair_done[p - 2])
        wld[p] = k.dma("gpsimd", lambda eng, p=p, par=par: eng.dma_start(
            out=wqb[par][:, :].rearrange("p (a f) -> p a f", a=16),
            in_=wq_ap[:, p * 384:(p + 1) * 384].rearrange(
                "(a p) c -> p a c", p=128)), f"qw{par}")

    load_wq(0)
    load_wq(1)
    for p in range(8):
        par = p % 2
        h = 2 * p
        k.wait("tensor", f"qw{par}", wld[p])
        k.wait("tensor", "ve", k.val("ve"))
        for kc in range(16):
            for half in range(2):
                c0 = half * 512
                k.do("tensor", lambda eng, kc=kc, c0=c0, par=par: eng.matmul(
                    pqa[:, c0:c0 + 512],
                    wqb[par][:, kc * 384:kc * 384 + 128],
                    ln2_own[:, kc * T + c0:kc * T + c0 + 512],
                    start=(kc == 0), stop=(kc == 15)))
                k.do("tensor", lambda eng, kc=kc, c0=c0, par=par: eng.matmul(
                    pqb[:, c0:c0 + 512],
                    wqb[par][:, kc * 384 + 192:kc * 384 + 320],
                    ln2_own[:, kc * T + c0:kc * T + c0 + 512],
                    start=(kc == 0), stop=(kc == 15)))
                k.do("tensor", lambda eng, kc=kc, c0=c0, par=par: eng.matmul(
                    pqr[0:64, c0:c0 + 512],
                    wqb[par][:, kc * 384 + 128:kc * 384 + 192],
                    ln2_own[:, kc * T + c0:kc * T + c0 + 512],
                    start=(kc == 0), stop=(kc == 15)))
                k.do("tensor", lambda eng, kc=kc, c0=c0, par=par: eng.matmul(
                    pqr[64:128, c0:c0 + 512],
                    wqb[par][:, kc * 384 + 320:kc * 384 + 384],
                    ln2_own[:, kc * T + c0:kc * T + c0 + 512],
                    start=(kc == 0), stop=(kc == 15)),
                    inc=("pe", 1) if (kc == 15 and half == 1) else None)
        mm = k.val("pe")
        k.wait("scalar", "pe", mm)
        k.do("scalar", lambda eng, h=h, par=par: eng.activation(
            qn_sb[:, h * T:(h + 1) * T], pqa[:, :], AF.Copy))
        k.do("scalar", lambda eng, h=h, par=par: eng.activation(
            qn_sb[:, (h + 1) * T:(h + 2) * T], pqb[:, :], AF.Copy))
        aa = k.do("scalar", lambda eng, par=par: eng.activation(
            qrb[:, :], pqr[:, :], AF.Copy))
        k.wait("tensor", "ac", aa)
        k.wait("vector", "ac", aa)
        emit_rope_apply(cm, qrb, psw, t1, t2,
                        C2[:, 0:T], S2[:, 0:T], T,
                        qr_sb[:, p * T:(p + 1) * T])
        pair_done[p] = k.val("pe")
        if p < 6:
            load_wq(p + 2)
        k.wait("tensor", "ac", aa)
        k.wait("tensor", "ve", k.val("ve"))
    ar.free_to(m0)
    k.barrier()



def emit_kv(cm, ln2d, wkva_ap, gcol, bcol, C2, S2, kvn_sb, krope_sb):
    nc, k, ar = cm.nc, cm.k, cm.ar
    TK = 4096
    m0 = ar.mark()
    kvc = kvn_sb
    m_kva = ar.mark()
    wka = [ar.sb(f"kv_w{j}", [128, 576], BF16)[0] for j in range(2)]
    blk = [ar.sb(f"kv_b{j}", [128, 512], BF16)[0] for j in range(2)]
    pm = [ar.ps(f"kv_p{j}", [128, 512], F32)[0] for j in range(5)]

    ev_cnt = {}
    for nb in range(8):
        lds = {}
        for kc in range(16):
            par = kc % 2
            if kc >= 2:
                k.wait("sync", "pe", lds[kc - 2][1])
                k.wait("gpsimd", "pe", lds[kc - 2][1])
            ld = k.dma("sync", lambda eng, kc=kc, nb=nb, par=par:
                       eng.dma_start(out=blk[par][:, :],
                                     in_=ln2d[kc * 128:(kc + 1) * 128,
                                              nb * 512:(nb + 1) * 512]),
                       f"kvb{par}")
            wl = k.dma("gpsimd", lambda eng, kc=kc, par=par: eng.dma_start(
                out=wka[par][:, :],
                in_=wkva_ap[kc * 128:(kc + 1) * 128, :]), f"kvw{par}")
            k.wait("tensor", f"kvb{par}", ld)
            k.wait("tensor", f"kvw{par}", wl)
            if nb > 0 and kc == 0:
                k.wait("tensor", "ac", ev_cnt[nb - 1])
            for m in range(5):
                kk = 128 if m < 4 else 64
                out_ap = pm[m][:, :] if m < 4 else pm[m][0:64, :]
                k.do("tensor", lambda eng, m=m, kc=kc, par=par, kk=kk,
                     out_ap=out_ap: eng.matmul(
                         out_ap,
                         wka[par][:, m * 128:m * 128 + kk],
                         blk[par][:, :],
                         start=(kc == 0), stop=(kc == 15)),
                     inc=("pe", 1) if (m == 4 and kc == 15) else None)
            lds[kc] = (ld, k.val("pe"))
        mmv = k.val("pe")
        k.wait("scalar", "pe", mmv)
        for m in range(4):
            k.do("scalar", lambda eng, m=m, nb=nb: eng.activation(
                kvc[:, m * TK + nb * 512:m * TK + (nb + 1) * 512],
                pm[m][:, :], AF.Copy))
        ev = k.do("scalar", lambda eng, nb=nb: eng.activation(
            krope_sb[0:64, nb * 512:(nb + 1) * 512], pm[4][0:64, :],
            AF.Copy))
        ev_cnt[nb] = ev
    kva_done_ac = k.val("ac")
    ar.free_to(m_kva)

    xq = [ar.sb(f"kvl_q{j}", [128, 512], BF16)[0] for j in range(2)]
    rb, _ = ar.sb("kvl_rows", [1, 2 * 512], F32)
    ab, _ = ar.sb("kvl_ab", [128, 512], F32)
    bb, _ = ar.sb("kvl_bb", [128, 512], F32)
    t1, _ = ar.sb("kvl_t1", [128, 512], F32)
    prow, _ = ar.ps("kvl_prow", [1, 2 * 512], F32)
    pb1, _ = ar.ps("kvl_pb1", [128, 512], F32)
    pb2, _ = ar.ps("kvl_pb2", [128, 512], F32)
    k.barrier()
    RK = float(RANK)
    bcast_ve = {}
    for nb in range(8):
        sqa = {}
        for kc in range(4):
            par = kc % 2
            if kc >= 2:
                k.wait("scalar", "pe", sqa[kc - 2])
            sq = k.do("scalar", lambda eng, kc=kc, nb=nb, par=par:
                      eng.activation(xq[par][:, :],
                                     kvc[:, kc * TK + nb * 512:
                                         kc * TK + (nb + 1) * 512],
                                     AF.Square))
            k.wait("tensor", "ac", sq)
            if nb > 0 and kc == 0:
                k.wait("tensor", "ve", bcast_ve[nb - 1])
            k.do("tensor", lambda eng, kc=kc, nb=nb, par=par: eng.matmul(
                prow[:, 0:512], cm.c_ones_col[:, :],
                kvc[:, kc * TK + nb * 512:kc * TK + (nb + 1) * 512],
                start=(kc == 0), stop=(kc == 3)))
            mmk = k.do("tensor", lambda eng, kc=kc, par=par: eng.matmul(
                prow[:, 512:1024], cm.c_ones_col[:, :], xq[par][:, :],
                start=(kc == 0), stop=(kc == 3)),
                inc=("pe", 1))
            sqa[kc] = mmk
        k.wait("vector", "pe", k.val("pe"))
        k.do("vector", lambda eng: eng.tensor_scalar_mul(
            rb[:, 0:512], prow[:, 0:512], 1.0 / RK))
        k.do("vector", lambda eng: eng.tensor_scalar_mul(
            rb[:, 512:1024], prow[:, 512:1024], 1.0 / RK))
        k.do("vector", lambda eng: eng.scalar_tensor_tensor(
            ab[0:1, 0:512], rb[:, 0:512], -1.0, rb[:, 0:512],
            ALU.mult, ALU.mult))
        k.do("vector", lambda eng: eng.tensor_add(
            rb[:, 512:1024], rb[:, 512:1024], ab[0:1, 0:512]))
        vq = k.do("vector", lambda eng: eng.tensor_scalar_add(
            rb[:, 512:1024], rb[:, 512:1024], EPS))
        k.wait("scalar", "ve", vq)
        k.do("scalar", lambda eng: eng.activation(
            rb[:, 512:1024], rb[:, 512:1024], AF.Ln))
        sdq = k.do("scalar", lambda eng: eng.activation(
            rb[:, 512:1024], rb[:, 512:1024], AF.Exp, scale=-0.5))
        k.wait("vector", "ac", sdq)
        vr = k.do("vector", lambda eng: eng.scalar_tensor_tensor(
            ab[0:1, 0:512], rb[:, 0:512], -1.0, rb[:, 512:1024],
            ALU.mult, ALU.mult))
        k.wait("tensor", "ve", vr)
        k.do("tensor", lambda eng: eng.matmul(
            pb1[:, :], cm.c_ones_row[:, :], rb[:, 512:1024],
            start=True, stop=True))
        bmm = k.do("tensor", lambda eng: eng.matmul(
            pb2[:, :], cm.c_ones_row[:, :], ab[0:1, 0:512],
            start=True, stop=True), inc=("pe", 1))
        k.wait("vector", "pe", bmm)
        k.do("vector", lambda eng: eng.tensor_copy(ab[:, :], pb1[:, :]))
        bcv = k.do("vector", lambda eng: eng.tensor_copy(bb[:, :], pb2[:, :]))
        for kc in range(4):
            k.do("vector", lambda eng, kc=kc, nb=nb: eng.tensor_tensor(
                t1[:, :], kvc[:, kc * TK + nb * 512:kc * TK + (nb + 1) * 512],
                ab[:, :], ALU.mult))
            k.do("vector", lambda eng: eng.tensor_tensor(
                t1[:, :], t1[:, :], bb[:, :], ALU.add))
            k.do("vector", lambda eng, kc=kc, nb=nb: eng.tensor_scalar(
                kvn_sb[:, kc * TK + nb * 512:kc * TK + (nb + 1) * 512],
                t1[:, :], gcol[:, kc:kc + 1], bcol[:, kc:kc + 1],
                ALU.mult, ALU.add))
        bcast_ve[nb] = k.val("ve")
    k.barrier()
    psw, _ = ar.ps("kv_psw", [128, 512], F32)
    t2b, _ = ar.sb("kv_t2", [64, 512], F32)
    t1b, _ = ar.sb("kv_t1b", [64, 512], F32)
    for nb in range(8):
        c0 = nb * 512
        mm = k.do("tensor", lambda eng, c0=c0: eng.matmul(
            psw[0:64, :], cm.c_p128[0:64, 0:64],
            krope_sb[0:64, c0:c0 + 512], start=True, stop=True),
            inc=("pe", 1))
        k.do("vector", lambda eng, c0=c0: eng.tensor_tensor(
            t1b[:, :], krope_sb[0:64, c0:c0 + 512], C2[0:64, c0:c0 + 512],
            ALU.mult))
        k.wait("vector", "pe", mm)
        k.do("vector", lambda eng, c0=c0: eng.tensor_tensor(
            t2b[:, :], psw[0:64, :], S2[0:64, c0:c0 + 512], ALU.mult))
        vv = k.do("vector", lambda eng, c0=c0: eng.tensor_tensor(
            krope_sb[0:64, c0:c0 + 512], t1b[:, :], t2b[:, :], ALU.add))
        k.do("vector", lambda eng, c0=c0: eng.tensor_copy(
            krope_sb[64:128, c0:c0 + 512], krope_sb[0:64, c0:c0 + 512]))
        k.wait("tensor", "ve", vv)
    ar.free_to(m0)
    k.barrier()



def emit_attention(cm, wkvb_ap, kvn_sb, krope_sb, qn_sb, qr_sb, attn_sb):
    nc, k, ar = cm.nc, cm.k, cm.ar
    T = 1024
    TK = 4096
    m0 = ar.mark()
    wkb = [ar.sb(f"at_wb{j}", [128, 4 * 256], BF16)[0] for j in range(2)]
    kno, _ = ar.sb("at_kn", [128, TK], BF16)
    vfm, _ = ar.sb("at_vf", [128, TK], BF16)
    vtm, _ = ar.sb("at_vt", [128, TK], BF16)
    p_sb = [ar.sb(f"at_p{j}", [128, T], BF16)[0] for j in range(2)]
    rrow, _ = ar.sb("at_rr", [1, T], F32)
    rbc, _ = ar.sb("at_rbc", [128, T], F32)
    psS = [ar.ps(f"at_s{j}", [128, T], F32)[0] for j in range(2)]
    psO, _ = ar.ps("at_o", [128, T], F32)
    psR, _ = ar.ps("at_r", [1, T], F32)

    k.wait("tensor", "ve", cm.negc_ready)
    wld = {}

    def load_wb(h):
        par = h % 2
        if h >= 2:
            k.wait("gpsimd", "ac", kvb_evac[h - 2])
        wld[h] = k.dma("gpsimd", lambda eng, h=h, par=par: eng.dma_start(
            out=wkb[par][:, :].rearrange("p (a f) -> p a f", a=4),
            in_=wkvb_ap[:, h * 256:(h + 1) * 256].rearrange(
                "(a p) c -> p a c", p=128)), f"awb{par}")

    kvb_evac = {}
    load_wb(0)
    load_wb(1)
    for h in range(16):
        par = h % 2
        k.wait("tensor", f"awb{par}", wld[h])
        for which in range(2):
            dst = kno if which == 0 else vfm
            for nb in range(8):
                pp = psS[nb % 2]
                for j in range(4):
                    k.do("tensor", lambda eng, j=j, nb=nb, par=par,
                         which=which, pp=pp: eng.matmul(
                             pp[:, 0:512],
                             wkb[par][:, j * 256 + which * 128:
                                      j * 256 + which * 128 + 128],
                             kvn_sb[:, j * TK + nb * 512:
                                    j * TK + (nb + 1) * 512],
                             start=(j == 0), stop=(j == 3)),
                         inc=("pe", 1) if j == 3 else None)
                mmv = k.val("pe")
                k.wait("scalar", "pe", mmv)
                ae = k.do("scalar", lambda eng, nb=nb, dst=dst, pp=pp:
                          eng.activation(dst[:, nb * 512:(nb + 1) * 512],
                                         pp[:, 0:512], AF.Copy))
                k.wait("tensor", "ac", ae)
        kvb_evac[h] = k.val("ac")
        if h < 14:
            load_wb(h + 2)
        for g4 in range(8):
            pp = psS[g4 % 2]
            for jj in range(4):
                sc = g4 * 4 + jj
                k.do("tensor", lambda eng, sc=sc, jj=jj, pp=pp: eng.matmul(
                    pp[:, jj * 128:(jj + 1) * 128],
                    vfm[:, sc * 128:(sc + 1) * 128],
                    cm.c_eye_bf[:, :], start=True, stop=True),
                    inc=("pe", 1) if jj == 3 else None)
            mmv = k.val("pe")
            k.wait("scalar", "pe", mmv)
            ve = k.do("scalar", lambda eng, g4=g4, pp=pp: eng.activation(
                vtm[:, g4 * 512:(g4 + 1) * 512], pp[:, 0:512], AF.Copy))
            k.wait("tensor", "ac", ve)
        exp_ac = {}
        pv_pe = {}
        for sc in range(32):
            par2 = sc % 2
            pp = psS[par2]
            if sc >= 2:
                k.wait("tensor", "ac", exp_ac[sc - 2])
            for half in range(2):
                c0 = half * 512
                k.do("tensor", lambda eng, sc=sc, c0=c0, h=h, pp=pp:
                     eng.matmul(pp[:, c0:c0 + 512],
                                kno[:, sc * 128:(sc + 1) * 128],
                                qn_sb[:, h * T + c0:h * T + c0 + 512],
                                start=True, stop=False))
                r0 = 64 * (h % 2)
                qrs = qr_sb[r0:r0 + 64,
                            (h // 2) * T + c0:(h // 2) * T + c0 + 512]
                krs = krope_sb[r0:r0 + 64, sc * 128:(sc + 1) * 128]
                k.do("tensor", lambda eng, qrs=qrs, krs=krs, c0=c0, pp=pp:
                     eng.matmul(pp[:, c0:c0 + 512], krs,
                                qrs, start=False, stop=True),
                     inc=("pe", 1) if c0 else None)
            mmv = k.val("pe")
            k.wait("scalar", "pe", mmv)
            if sc >= 2:
                k.wait("scalar", "pe", pv_pe[sc - 2])
            ea = k.do("scalar", lambda eng, sc=sc, par2=par2, pp=pp:
                      eng.activation(p_sb[par2][:, :], pp[:, :], AF.Exp,
                                     bias=cm.c_negC[:, 0:1], scale=SCALE))
            exp_ac[sc] = ea
            k.wait("tensor", "ac", ea)
            for half in range(2):
                c0 = half * 512
                k.do("tensor", lambda eng, sc=sc, c0=c0, par2=par2:
                     eng.matmul(psR[:, c0:c0 + 512], cm.c_ones_col[:, :],
                                p_sb[par2][:, c0:c0 + 512],
                                start=(sc == 0), stop=(sc == 31)))
                k.do("tensor", lambda eng, sc=sc, c0=c0, par2=par2:
                     eng.matmul(psO[:, c0:c0 + 512],
                                vtm[:, sc * 128:(sc + 1) * 128],
                                p_sb[par2][:, c0:c0 + 512],
                                start=(sc == 0), stop=(sc == 31)),
                     inc=("pe", 1) if c0 else None)
            pv_pe[sc] = k.val("pe")
        k.wait("vector", "pe", k.val("pe"))
        vv = k.do("vector", lambda eng: eng.reciprocal(rrow[:, :], psR[:, :]))
        k.wait("tensor", "ve", vv)
        bmm = k.do("tensor", lambda eng: eng.matmul(
            psS[0][:, 0:512], cm.c_ones_row[:, :], rrow[:, 0:512],
            start=True, stop=True), inc=("pe", 1))
        bmm2 = k.do("tensor", lambda eng: eng.matmul(
            psS[1][:, 0:512], cm.c_ones_row[:, :], rrow[:, 512:1024],
            start=True, stop=True), inc=("pe", 1))
        k.wait("vector", "pe", bmm2)
        k.do("vector", lambda eng: eng.tensor_copy(rbc[:, 0:512],
                                                   psS[0][:, 0:512]))
        k.do("vector", lambda eng: eng.tensor_copy(rbc[:, 512:1024],
                                                   psS[1][:, 0:512]))
        av = k.do("vector", lambda eng, h=h: eng.tensor_tensor(
            attn_sb[:, h * T:(h + 1) * T], psO[:, :], rbc[:, :], ALU.mult))
        k.wait("tensor", "ve", av)
    ar.free_to(m0)
    k.barrier()



RG = [[0, 1, 2, 3], [4, 5, 6, 7]]
AGR = NH * (DN + DV) + DR


def emit_kv_own(cm, ln_own, wkva_ap, gcol, bcol, C2, S2, kvn_sb, krope_sb):
    nc, k, ar = cm.nc, cm.k, cm.ar
    TK = 1024
    m0 = ar.mark()
    kvc = kvn_sb
    m_kva = ar.mark()
    wka = [ar.sb(f"kv_w{j}", [128, 576], BF16)[0] for j in range(2)]
    pm = [ar.ps(f"kv_p{j}", [128, 512], F32)[0] for j in range(5)]

    ev_cnt = {}
    mmc = {}
    for nb in range(2):
        for kc in range(16):
            j = nb * 16 + kc
            par = j % 2
            if j >= 2:
                k.wait("gpsimd", "pe", mmc[j - 2])
            wl = k.dma("gpsimd", lambda eng, kc=kc, par=par: eng.dma_start(
                out=wka[par][:, :],
                in_=wkva_ap[kc * 128:(kc + 1) * 128, :]), f"kvw{par}")
            k.wait("tensor", f"kvw{par}", wl)
            if nb > 0 and kc == 0:
                k.wait("tensor", "ac", ev_cnt[nb - 1])
            for m in range(5):
                kk = 128 if m < 4 else 64
                out_ap = pm[m][:, :] if m < 4 else pm[m][0:64, :]
                k.do("tensor", lambda eng, m=m, kc=kc, nb=nb, par=par, kk=kk,
                     out_ap=out_ap: eng.matmul(
                         out_ap,
                         wka[par][:, m * 128:m * 128 + kk],
                         ln_own[:, kc * TK + nb * 512:kc * TK + (nb + 1) * 512],
                         start=(kc == 0), stop=(kc == 15)),
                     inc=("pe", 1) if m == 4 else None)
            mmc[j] = k.val("pe")
        mmv = k.val("pe")
        k.wait("scalar", "pe", mmv)
        for m in range(4):
            k.do("scalar", lambda eng, m=m, nb=nb: eng.activation(
                kvc[:, m * TK + nb * 512:m * TK + (nb + 1) * 512],
                pm[m][:, :], AF.Copy))
        ev = k.do("scalar", lambda eng, nb=nb: eng.activation(
            krope_sb[0:64, nb * 512:(nb + 1) * 512], pm[4][0:64, :],
            AF.Copy), inc=("ac", 1))
        ev_cnt[nb] = ev
    ar.free_to(m_kva)

    xq = [ar.sb(f"kvl_q{j}", [128, 512], BF16)[0] for j in range(2)]
    rb, _ = ar.sb("kvl_rows", [1, 2 * 512], F32)
    ab, _ = ar.sb("kvl_ab", [128, 512], F32)
    bb, _ = ar.sb("kvl_bb", [128, 512], F32)
    t1, _ = ar.sb("kvl_t1", [128, 512], F32)
    prow, _ = ar.ps("kvl_prow", [1, 2 * 512], F32)
    pb1, _ = ar.ps("kvl_pb1", [128, 512], F32)
    pb2, _ = ar.ps("kvl_pb2", [128, 512], F32)
    k.barrier()
    RK = float(RANK)
    bcast_ve = {}
    for nb in range(2):
        sqa = {}
        for kc in range(4):
            par = kc % 2
            if kc >= 2:
                k.wait("scalar", "pe", sqa[kc - 2])
            sq = k.do("scalar", lambda eng, kc=kc, nb=nb, par=par:
                      eng.activation(xq[par][:, :],
                                     kvc[:, kc * TK + nb * 512:
                                         kc * TK + (nb + 1) * 512],
                                     AF.Square))
            k.wait("tensor", "ac", sq)
            if nb > 0 and kc == 0:
                k.wait("tensor", "ve", bcast_ve[nb - 1])
            k.do("tensor", lambda eng, kc=kc, nb=nb, par=par: eng.matmul(
                prow[:, 0:512], cm.c_ones_col[:, :],
                kvc[:, kc * TK + nb * 512:kc * TK + (nb + 1) * 512],
                start=(kc == 0), stop=(kc == 3)))
            mmk = k.do("tensor", lambda eng, kc=kc, par=par: eng.matmul(
                prow[:, 512:1024], cm.c_ones_col[:, :], xq[par][:, :],
                start=(kc == 0), stop=(kc == 3)),
                inc=("pe", 1))
            sqa[kc] = mmk
        k.wait("vector", "pe", k.val("pe"))
        k.do("vector", lambda eng: eng.tensor_scalar_mul(
            rb[:, 0:512], prow[:, 0:512], 1.0 / RK))
        k.do("vector", lambda eng: eng.tensor_scalar_mul(
            rb[:, 512:1024], prow[:, 512:1024], 1.0 / RK))
        k.do("vector", lambda eng: eng.scalar_tensor_tensor(
            ab[0:1, 0:512], rb[:, 0:512], -1.0, rb[:, 0:512],
            ALU.mult, ALU.mult))
        k.do("vector", lambda eng: eng.tensor_add(
            rb[:, 512:1024], rb[:, 512:1024], ab[0:1, 0:512]))
        vq = k.do("vector", lambda eng: eng.tensor_scalar_add(
            rb[:, 512:1024], rb[:, 512:1024], EPS))
        k.wait("scalar", "ve", vq)
        k.do("scalar", lambda eng: eng.activation(
            rb[:, 512:1024], rb[:, 512:1024], AF.Ln))
        sdq = k.do("scalar", lambda eng: eng.activation(
            rb[:, 512:1024], rb[:, 512:1024], AF.Exp, scale=-0.5))
        k.wait("vector", "ac", sdq)
        vr = k.do("vector", lambda eng: eng.scalar_tensor_tensor(
            ab[0:1, 0:512], rb[:, 0:512], -1.0, rb[:, 512:1024],
            ALU.mult, ALU.mult))
        k.wait("tensor", "ve", vr)
        k.do("tensor", lambda eng: eng.matmul(
            pb1[:, :], cm.c_ones_row[:, :], rb[:, 512:1024],
            start=True, stop=True))
        bmm = k.do("tensor", lambda eng: eng.matmul(
            pb2[:, :], cm.c_ones_row[:, :], ab[0:1, 0:512],
            start=True, stop=True), inc=("pe", 1))
        k.wait("vector", "pe", bmm)
        k.do("vector", lambda eng: eng.tensor_copy(ab[:, :], pb1[:, :]))
        k.do("vector", lambda eng: eng.tensor_copy(bb[:, :], pb2[:, :]))
        for kc in range(4):
            k.do("vector", lambda eng, kc=kc, nb=nb: eng.tensor_tensor(
                t1[:, :], kvc[:, kc * TK + nb * 512:kc * TK + (nb + 1) * 512],
                ab[:, :], ALU.mult))
            k.do("vector", lambda eng: eng.tensor_tensor(
                t1[:, :], t1[:, :], bb[:, :], ALU.add))
            k.do("vector", lambda eng, kc=kc, nb=nb: eng.tensor_scalar(
                kvn_sb[:, kc * TK + nb * 512:kc * TK + (nb + 1) * 512],
                t1[:, :], gcol[:, kc:kc + 1], bcol[:, kc:kc + 1],
                ALU.mult, ALU.add))
        bcast_ve[nb] = k.val("ve")
    k.barrier()
    psw, _ = ar.ps("kv_psw", [128, 512], F32)
    t2b, _ = ar.sb("kv_t2", [64, 512], F32)
    t1b, _ = ar.sb("kv_t1b", [64, 512], F32)
    for nb in range(2):
        c0 = nb * 512
        mm = k.do("tensor", lambda eng, c0=c0: eng.matmul(
            psw[0:64, :], cm.c_p128[0:64, 0:64],
            krope_sb[0:64, c0:c0 + 512], start=True, stop=True),
            inc=("pe", 1))
        k.do("vector", lambda eng, c0=c0: eng.tensor_tensor(
            t1b[:, :], krope_sb[0:64, c0:c0 + 512], C2[0:64, c0:c0 + 512],
            ALU.mult))
        k.wait("vector", "pe", mm)
        k.do("vector", lambda eng, c0=c0: eng.tensor_tensor(
            t2b[:, :], psw[0:64, :], S2[0:64, c0:c0 + 512], ALU.mult))
        vv = k.do("vector", lambda eng, c0=c0: eng.tensor_tensor(
            krope_sb[0:64, c0:c0 + 512], t1b[:, :], t2b[:, :], ALU.add))
        k.wait("tensor", "ve", vv)
    ar.free_to(m0)
    k.barrier()


def emit_kvb_ag(cm, wkvb_ap, kvn_sb, krope_sb, agin):
    nc, k, ar = cm.nc, cm.k, cm.ar
    T = 1024
    m0 = ar.mark()
    wb, _ = ar.sb("kb_w", [128, 4 * 4096], BF16)
    st = [ar.sb(f"kb_s{j}", [128, T], BF16)[0] for j in range(2)]
    pb = [ar.ps(f"kb_p{j}", [128, T], F32)[0] for j in range(2)]
    wl = k.dma("gpsimd", lambda eng: eng.dma_start(
        out=wb[:, :].rearrange("p (a f) -> p a f", a=4),
        in_=wkvb_ap.rearrange("(a p) c -> p a c", p=128)), "kbw")
    k.wait("tensor", "kbw", wl)
    ev_d, st_d = {}, {}
    for f in range(32):
        par = f % 2
        if f >= 2:
            k.wait("tensor", "ac", ev_d[f - 2])
        for r in range(4):
            for half in range(2):
                c0 = half * 512
                k.do("tensor", lambda eng, f=f, r=r, par=par, c0=c0:
                     eng.matmul(
                         pb[par][:, c0:c0 + 512],
                         wb[:, r * 4096 + f * 128:r * 4096 + (f + 1) * 128],
                         kvn_sb[:, r * T + c0:r * T + c0 + 512],
                         start=(r == 0), stop=(r == 3)),
                     inc=("pe", 1) if (r == 3 and half == 1) else None)
        mm = k.val("pe")
        k.wait("scalar", "pe", mm)
        if f >= 2:
            k.wait("scalar", f"agst{par}", st_d[f - 2])
        ev_d[f] = k.do("scalar", lambda eng, par=par: eng.activation(
            st[par][:, :], pb[par][:, :], AF.Copy), inc=("ac", 1))
        k.wait("sync", "ac", ev_d[f])
        st_d[f] = k.dma("sync", lambda eng, f=f, par=par: eng.dma_start(
            out=agin[f * 128:(f + 1) * 128, :], in_=st[par][:, :]),
            f"agst{par}")
    k.wait("sync", "ve", k.val("ve"))
    k.dma("sync", lambda eng: eng.dma_start(
        out=agin[4096:4160, :], in_=krope_sb[0:64, :]), "agstk")
    ar.free_to(m0)


def emit_ag_trigger(cm, agin_ap, agout_ap):
    k = cm.k
    if "cc" not in k.sems:
        k.sem("cc")
    h = k.sems["cc"]
    k.wait("gpsimd", "agst0", k.val("agst0"))
    k.wait("gpsimd", "agst1", k.val("agst1"))
    k.wait("gpsimd", "agstk", k.val("agstk"))
    k.q["gpsimd"].append(lambda eng: eng.collective_compute(
        "AllGather", ALU.bypass, replica_groups=RG,
        ins=[agin_ap.opt()], outs=[agout_ap.opt()]).then_inc(h, 1))
    k.cnt["cc"] = 1


def emit_attention_ag(cm, agout, qn_sb, qr_sb, attn_sb):
    nc, k, ar = cm.nc, cm.k, cm.ar
    T = 1024
    TK = 4096
    m0 = ar.mark()
    kno = [ar.sb(f"at_kn{j}", [128, TK], BF16)[0] for j in range(2)]
    vfm = [ar.sb(f"at_vf{j}", [128, TK], BF16)[0] for j in range(2)]
    vtm, _ = ar.sb("at_vt", [128, TK], BF16)
    krope_sb, _ = ar.sb("at_kr", [128, TK], BF16)
    p_sb = [ar.sb(f"at_p{j}", [128, T], BF16)[0] for j in range(2)]
    rrow, _ = ar.sb("at_rr", [1, T], F32)
    rbc, _ = ar.sb("at_rbc", [128, T], F32)
    psS = [ar.ps(f"at_s{j}", [128, T], F32)[0] for j in range(2)]
    psO, _ = ar.ps("at_o", [128, T], F32)
    psR, _ = ar.ps("at_r", [1, T], F32)

    k.wait("tensor", "ve", cm.negc_ready)
    k.wait("sync", "cc", 1)
    k.wait("gpsimd", "cc", 1)
    for r in range(4):
        for half in range(2):
            k.dma("sync", lambda eng, r=r, half=half: eng.dma_start(
                out=krope_sb[half * 64:(half + 1) * 64, r * T:(r + 1) * T],
                in_=agout[r * AGR + 4096:r * AGR + 4160, :]), "akr")
    krl = k.val("akr")
    k.wait("tensor", "akr", krl)

    kld, head_pe = {}, {}

    def load_h(h):
        par = h % 2
        if h >= 2:
            k.wait("sync", "pe", head_pe[h - 2])
            k.wait("gpsimd", "pe", head_pe[h - 2])
        for r in range(4):
            k.dma("sync", lambda eng, h=h, r=r, par=par: eng.dma_start(
                out=kno[par][:, r * T:(r + 1) * T],
                in_=agout[r * AGR + h * 256:r * AGR + h * 256 + 128, :]),
                f"akn{par}")
            k.dma("gpsimd", lambda eng, h=h, r=r, par=par: eng.dma_start(
                out=vfm[par][:, r * T:(r + 1) * T],
                in_=agout[r * AGR + h * 256 + 128:
                          r * AGR + h * 256 + 256, :]), f"avf{par}")
        kld[h] = (k.val(f"akn{par}"), k.val(f"avf{par}"))

    load_h(0)
    load_h(1)
    for h in range(16):
        par = h % 2
        k.wait("tensor", f"akn{par}", kld[h][0])
        k.wait("tensor", f"avf{par}", kld[h][1])
        for g4 in range(8):
            pp = psS[g4 % 2]
            for jj in range(4):
                sc = g4 * 4 + jj
                k.do("tensor", lambda eng, sc=sc, jj=jj, pp=pp, par=par:
                     eng.matmul(
                         pp[:, jj * 128:(jj + 1) * 128],
                         vfm[par][:, sc * 128:(sc + 1) * 128],
                         cm.c_eye_bf[:, :], start=True, stop=True),
                     inc=("pe", 1) if jj == 3 else None)
            mmv = k.val("pe")
            k.wait("scalar", "pe", mmv)
            ve = k.do("scalar", lambda eng, g4=g4, pp=pp: eng.activation(
                vtm[:, g4 * 512:(g4 + 1) * 512], pp[:, 0:512], AF.Copy))
            k.wait("tensor", "ac", ve)
        exp_ac = {}
        pv_pe = {}
        for sc in range(32):
            par2 = sc % 2
            pp = psS[par2]
            if sc >= 2:
                k.wait("tensor", "ac", exp_ac[sc - 2])
            for half in range(2):
                c0 = half * 512
                k.do("tensor", lambda eng, sc=sc, c0=c0, h=h, pp=pp, par=par:
                     eng.matmul(pp[:, c0:c0 + 512],
                                kno[par][:, sc * 128:(sc + 1) * 128],
                                qn_sb[:, h * T + c0:h * T + c0 + 512],
                                start=True, stop=False))
                r0 = 64 * (h % 2)
                qrs = qr_sb[r0:r0 + 64,
                            (h // 2) * T + c0:(h // 2) * T + c0 + 512]
                krs = krope_sb[r0:r0 + 64, sc * 128:(sc + 1) * 128]
                k.do("tensor", lambda eng, qrs=qrs, krs=krs, c0=c0, pp=pp:
                     eng.matmul(pp[:, c0:c0 + 512], krs,
                                qrs, start=False, stop=True),
                     inc=("pe", 1) if c0 else None)
            mmv = k.val("pe")
            k.wait("scalar", "pe", mmv)
            if sc >= 2:
                k.wait("scalar", "pe", pv_pe[sc - 2])
            ea = k.do("scalar", lambda eng, sc=sc, par2=par2, pp=pp:
                      eng.activation(p_sb[par2][:, :], pp[:, :], AF.Exp,
                                     bias=cm.c_negC[:, 0:1], scale=SCALE))
            exp_ac[sc] = ea
            k.wait("tensor", "ac", ea)
            for half in range(2):
                c0 = half * 512
                k.do("tensor", lambda eng, sc=sc, c0=c0, par2=par2:
                     eng.matmul(psR[:, c0:c0 + 512], cm.c_ones_col[:, :],
                                p_sb[par2][:, c0:c0 + 512],
                                start=(sc == 0), stop=(sc == 31)))
                k.do("tensor", lambda eng, sc=sc, c0=c0, par2=par2:
                     eng.matmul(psO[:, c0:c0 + 512],
                                vtm[:, sc * 128:(sc + 1) * 128],
                                p_sb[par2][:, c0:c0 + 512],
                                start=(sc == 0), stop=(sc == 31)),
                     inc=("pe", 1) if c0 else None)
            pv_pe[sc] = k.val("pe")
        head_pe[h] = k.val("pe")
        if h < 14:
            load_h(h + 2)
        k.wait("vector", "pe", k.val("pe"))
        vv = k.do("vector", lambda eng: eng.reciprocal(rrow[:, :], psR[:, :]))
        k.wait("tensor", "ve", vv)
        bmm = k.do("tensor", lambda eng: eng.matmul(
            psS[0][:, 0:512], cm.c_ones_row[:, :], rrow[:, 0:512],
            start=True, stop=True), inc=("pe", 1))
        bmm2 = k.do("tensor", lambda eng: eng.matmul(
            psS[1][:, 0:512], cm.c_ones_row[:, :], rrow[:, 512:1024],
            start=True, stop=True), inc=("pe", 1))
        k.wait("vector", "pe", bmm2)
        k.do("vector", lambda eng: eng.tensor_copy(rbc[:, 0:512],
                                                   psS[0][:, 0:512]))
        k.do("vector", lambda eng: eng.tensor_copy(rbc[:, 512:1024],
                                                   psS[1][:, 0:512]))
        av = k.do("vector", lambda eng, h=h: eng.tensor_tensor(
            attn_sb[:, h * T:(h + 1) * T], psO[:, :], rbc[:, :], ALU.mult))
        k.wait("tensor", "ve", av)
    ar.free_to(m0)
    k.barrier()



def emit_wout(cm, wout_ap, attn_sb, x1fm, x2fm):
    nc, k, ar = cm.nc, cm.k, cm.ar
    T = 1024
    m0 = ar.mark()
    wsb, _ = ar.sb("wo_w", [128, 16 * D], BF16)
    xr = [ar.sb(f"wo_xr{j}", [128, T], F32)[0] for j in range(2)]
    xo = [ar.sb(f"wo_xo{j}", [128, T], F32)[0] for j in range(2)]
    pw = [ar.ps(f"wo_p{j}", [128, T], F32)[0] for j in range(2)]

    wl = k.dma("gpsimd", lambda eng: eng.dma_start(
        out=wsb[:, :].rearrange("p (a f) -> p a f", a=16),
        in_=wout_ap.rearrange("(a p) c -> p a c", p=128)), "wo")
    k.wait("tensor", "wo", wl)

    xl, mmd, evd, std = {}, {}, {}, {}

    def load_x(d):
        par = d % 2
        if d >= 2:
            k.wait("scalar", "ve", evd[d - 2])
        xl[d] = k.dma("scalar", lambda eng, d=d, par=par: eng.dma_start(
            out=xr[par][:, :], in_=x1fm[d * 128:(d + 1) * 128, 0:T]),
            f"wx{par}")

    load_x(0)
    load_x(1)
    for d in range(16):
        par = d % 2
        if d >= 2:
            k.wait("tensor", "ve", evd[d - 2])
        for h in range(16):
            for half in range(2):
                c0 = half * 512
                k.do("tensor", lambda eng, d=d, h=h, c0=c0, par=par:
                     eng.matmul(pw[par][:, c0:c0 + 512],
                                wsb[:, h * D + d * 128:h * D + (d + 1) * 128],
                                attn_sb[:, h * T + c0:h * T + c0 + 512],
                                start=(h == 0), stop=(h == 15)),
                     inc=("pe", 1) if (h == 15 and half == 1) else None)
        mmd[d] = k.val("pe")
        k.wait("vector", "pe", mmd[d])
        k.wait("vector", f"wx{par}", xl[d])
        if d >= 2:
            k.wait("vector", f"wst{par}", std[d - 2])
        evd[d] = k.do("vector", lambda eng, d=d, par=par: eng.tensor_tensor(
            xo[par][:, :], pw[par][:, :], xr[par][:, :], ALU.add))
        k.wait("scalar", "ve", evd[d])
        std[d] = k.dma("scalar", lambda eng, d=d, par=par: eng.dma_start(
            out=x2fm[d * 128:(d + 1) * 128, :], in_=xo[par][:, :]),
            f"wst{par}")
        if d < 14:
            load_x(d + 2)
    res = (k.val("wst0"), k.val("wst1"))
    ar.free_to(m0)
    k.barrier()
    return res



def build_L2(stop_after=None):
    nc = bass.Bass("TRN2", target_bir_lowering=False)
    cm = Common(nc)
    k, ar = cm.k, cm.ar

    inputs = {}
    inputs["x1fm"] = nc.dram_tensor("x1fm", [D, S], F32,
                                    kind="ExternalInput").ap()
    inputs["fr"] = nc.dram_tensor("fr", [S, DR // 2], F32,
                                  kind="ExternalInput").ap()
    inputs["Wq"] = nc.dram_tensor("Wq", [D, NH * DQK], F32,
                                  kind="ExternalInput").ap()
    inputs["Wkv_a"] = nc.dram_tensor("Wkv_a", [D, RANK + DR], F32,
                                     kind="ExternalInput").ap()
    inputs["kv_gamma"] = nc.dram_tensor("kv_gamma", [RANK], F32,
                                        kind="ExternalInput").ap()
    inputs["kv_beta"] = nc.dram_tensor("kv_beta", [RANK], F32,
                                       kind="ExternalInput").ap()
    inputs["Wkv_b"] = nc.dram_tensor("Wkv_b", [RANK, NH * (DN + DV)], F32,
                                     kind="ExternalInput").ap()
    inputs["Wout"] = nc.dram_tensor("Wout", [NH * DV, D], F32,
                                    kind="ExternalInput").ap()
    for nm, shp, dt in (("gate_w", [IF, D], I32), ("up_w", [IF, D], I32),
                        ("down_w", [D, IF], I32), ("gate_scale", [IF], F32),
                        ("up_scale", [IF], F32), ("down_scale", [D], F32)):
        inputs[nm] = nc.dram_tensor(nm, shp, dt, kind="ExternalInput").ap()
    cm.inputs = inputs

    out_tm = nc.dram_tensor("out_tm", [TOWN, D], F32,
                            kind="ExternalOutput").ap()
    ln2d = nc.dram_tensor("ln2d", [D, S], BF16, kind="Internal").ap()
    x2fm = nc.dram_tensor("x2fm", [D, TOWN], F32, kind="Internal").ap()
    x3fm = nc.dram_tensor("x3fm", [D, TOWN], F32, kind="Internal").ap()

    def _truncate():
        k.barrier()
        r = k.dma("gpsimd", lambda eng: eng.dma_start(
            out=out_tm, in_=inputs["x1fm"][0:TOWN, 0:D]), "outd")
        k.wait("sync", "outd", r)
        cm.k.finish()
        return nc

    cm.load_consts(with_ffn_scales=True)
    cm.gcol, _ = ar.sb("gcol", [128, 4], F32)
    cm.bcol, _ = ar.sb("bcol", [128, 4], F32)
    emit_cols(cm, inputs["kv_gamma"], 4, cm.gcol)
    emit_cols(cm, inputs["kv_beta"], 4, cm.bcol)
    if stop_after == "consts":
        return _truncate()
    emit_ffn_weight_prep(cm)
    k.barrier()
    if stop_after == "prep":
        return _truncate()

    m_pre = ar.mark()
    attn_sb, _ = ar.sb("attn_sb", [128, 16 * 1024], BF16)
    m_attn = ar.mark()
    qn_sb, _ = ar.sb("qn_sb", [128, 16 * 1024], BF16)
    qr_sb, _ = ar.sb("qr_sb", [128, 8 * 1024], BF16)
    C2, _ = ar.sb("C2", [128, S], F32)
    S2, _ = ar.sb("S2", [128, S], F32)

    emit_rope_tables(cm, inputs["fr"], C2, S2)
    if stop_after == "theta":
        return _truncate()

    for c in range(1, 4):
        emit_ln_chunk(cm, inputs["x1fm"], c * 1024, None, also_dram=ln2d)
        k.barrier()
    m_ln2 = ar.mark()
    ln2_own, _ = ar.sb("ln2_own", [128, 16 * 1024], BF16)
    emit_ln_chunk(cm, inputs["x1fm"], 0, ln2_own, also_dram=ln2d)
    k.barrier()
    if stop_after == "ln2":
        return _truncate()

    emit_qproj(cm, ln2_own, inputs["Wq"], C2, S2, qn_sb, qr_sb)
    ar.free_to(m_ln2)
    k.barrier()
    if stop_after == "qproj":
        return _truncate()
    kvn_sb, _ = ar.sb("kvn_sb", [128, 4 * S], BF16)
    krope_sb, _ = ar.sb("krope_sb", [128, S], BF16)
    emit_kv(cm, ln2d, inputs["Wkv_a"], cm.gcol, cm.bcol, C2, S2,
            kvn_sb, krope_sb)
    if stop_after == "kv":
        return _truncate()
    emit_attention(cm, inputs["Wkv_b"], kvn_sb, krope_sb, qn_sb, qr_sb,
                   attn_sb)
    ar.free_to(m_attn)
    k.barrier()
    if stop_after == "attn":
        return _truncate()
    r2 = emit_wout(cm, inputs["Wout"], attn_sb, inputs["x1fm"], x2fm)
    ar.free_to(m_pre)
    k.barrier()
    if stop_after == "wout":
        return _truncate()
    if stop_after == "x2":
        k.barrier()
        r = k.dma("gpsimd", lambda eng: eng.dma_start(
            out=out_tm.rearrange("a b -> (a b)"),
            in_=x2fm.rearrange("a b -> (a b)")), "outd")
        k.wait("sync", "outd", r)
        cm.k.finish()
        return nc
    r3 = emit_ffn(cm, x2fm, 0, x3fm)
    k.barrier()
    rout = emit_fm_to_tm_out(cm, x3fm, out_tm)
    k.wait("sync", "outd", rout)
    cm.k.finish()
    return nc



def build_L1():
    nc = bass.Bass("TRN2", target_bir_lowering=False)
    cm = Common(nc)
    k, ar = cm.k, cm.ar

    inputs = {}
    inputs["x_own"] = nc.dram_tensor("x_own", [TOWN, D], F32,
                                     kind="ExternalInput").ap()
    inputs["gate_w"] = nc.dram_tensor("gate_w", [IF, D], I32,
                                      kind="ExternalInput").ap()
    inputs["up_w"] = nc.dram_tensor("up_w", [IF, D], I32,
                                    kind="ExternalInput").ap()
    inputs["down_w"] = nc.dram_tensor("down_w", [D, IF], I32,
                                      kind="ExternalInput").ap()
    inputs["gate_scale"] = nc.dram_tensor("gate_scale", [IF], F32,
                                          kind="ExternalInput").ap()
    inputs["up_scale"] = nc.dram_tensor("up_scale", [IF], F32,
                                        kind="ExternalInput").ap()
    inputs["down_scale"] = nc.dram_tensor("down_scale", [D], F32,
                                          kind="ExternalInput").ap()
    cm.inputs = inputs

    x1fm = nc.dram_tensor("x1fm", [D, TOWN], F32, kind="ExternalOutput").ap()
    x0fm = nc.dram_tensor("x0fm", [D, TOWN], F32, kind="Internal").ap()

    cm.load_consts(with_ffn_scales=True)
    emit_ffn_weight_prep(cm)
    k.barrier()
    emit_tm_to_fm(cm, inputs["x_own"], x0fm, TOWN)
    k.barrier()

    r2 = emit_ffn(cm, x0fm, 0, x1fm)

    k.wait("sync", "fo0", r2["x_fo"][0])
    k.wait("sync", "fo1", r2["x_fo"][1])
    cm.k.finish()
    return nc


def build_FUSED(stop_after=None, dump_chunk=0, n_ffn1=4):
    nc = bass.Bass("TRN2", target_bir_lowering=False)
    cm = Common(nc)
    k, ar = cm.k, cm.ar

    inputs = {}
    inputs["x_full"] = nc.dram_tensor("x_full", [S, D], F32,
                                      kind="ExternalInput").ap()
    inputs["fr"] = nc.dram_tensor("fr", [S, DR // 2], F32,
                                  kind="ExternalInput").ap()
    inputs["Wq"] = nc.dram_tensor("Wq", [D, NH * DQK], F32,
                                  kind="ExternalInput").ap()
    inputs["Wkv_a"] = nc.dram_tensor("Wkv_a", [D, RANK + DR], F32,
                                     kind="ExternalInput").ap()
    inputs["kv_gamma"] = nc.dram_tensor("kv_gamma", [RANK], F32,
                                        kind="ExternalInput").ap()
    inputs["kv_beta"] = nc.dram_tensor("kv_beta", [RANK], F32,
                                       kind="ExternalInput").ap()
    inputs["Wkv_b"] = nc.dram_tensor("Wkv_b", [RANK, NH * (DN + DV)], F32,
                                     kind="ExternalInput").ap()
    inputs["Wout"] = nc.dram_tensor("Wout", [NH * DV, D], F32,
                                    kind="ExternalInput").ap()
    for nm, shp, dt in (("gate_w", [IF, D], I32), ("up_w", [IF, D], I32),
                        ("down_w", [D, IF], I32), ("gate_scale", [IF], F32),
                        ("up_scale", [IF], F32), ("down_scale", [D], F32)):
        inputs[nm] = nc.dram_tensor(nm, shp, dt, kind="ExternalInput").ap()
    cm.inputs = inputs

    out_tm = nc.dram_tensor("out_tm", [TOWN, D], F32,
                            kind="ExternalOutput").ap()
    x0fm = nc.dram_tensor("x0fm", [D, S], F32, kind="Internal").ap()
    x1fm = nc.dram_tensor("x1fm", [D, S], F32, kind="Internal").ap()
    ln2d = nc.dram_tensor("ln2d", [D, S], BF16, kind="Internal").ap()
    x2fm = nc.dram_tensor("x2fm", [D, TOWN], F32, kind="Internal").ap()
    x3fm = nc.dram_tensor("x3fm", [D, TOWN], F32, kind="Internal").ap()

    def _dump(src_fm):
        k.barrier()
        ov = out_tm.rearrange("a b -> (a b)").rearrange(
            "(c d) -> c d", c=D)
        r = k.dma("gpsimd", lambda eng: eng.dma_start(
            out=ov,
            in_=src_fm[:, dump_chunk * 1024:(dump_chunk + 1) * 1024]),
            "outd")
        k.wait("sync", "outd", r)
        cm.k.finish()
        return nc

    cm.load_consts(with_ffn_scales=True)
    cm.gcol, _ = ar.sb("gcol", [128, 4], F32)
    cm.bcol, _ = ar.sb("bcol", [128, 4], F32)
    emit_cols(cm, inputs["kv_gamma"], 4, cm.gcol)
    emit_cols(cm, inputs["kv_beta"], 4, cm.bcol)

    ln1d = nc.dram_tensor("ln1d", [D, S], BF16, kind="Internal").ap()
    for c in range(4):
        emit_tm_to_fm(cm, inputs["x_full"], x0fm, 1024, t0=c * 1024)
        k.barrier()
    if stop_after == "x0":
        return _dump(x0fm)
    m_gu = emit_prep_pe(cm, "gu")
    k.wait("tensor", "ac", k.val("ac"))
    k.wait("gpsimd", "t2fo", k.val("t2fo"))
    emit_ln_pipeline(cm, x0fm, [0, 1024, 2048, 3072], ln1d)
    if stop_after == "lnd":
        return _dump(ln1d)
    k.wait("vector", "ptw0", cm.wprep_gu[0])
    k.wait("vector", "ptw1", cm.wprep_gu[1])
    k.wait("vector", "lnst0", k.val("lnst0"))
    k.wait("vector", "lnst1", k.val("lnst1"))

    t0list = (n_ffn1 if isinstance(n_ffn1, (list, tuple))
              else [i * 1024 for i in range(n_ffn1)])
    for ji, t0_ in enumerate(t0list):
        if ji == 0:
            emit_ffn(cm, x0fm, t0_, x1fm, ln_dram=ln1d,
                     gu_gate=cm.wprep_gu_m,
                     mid_emit=lambda: emit_prep_pe(cm, "d"))
        else:
            emit_ffn(cm, x0fm, t0_, x1fm, ln_dram=ln1d)
        k.barrier()
        if ji == 0:
            ar.free_to(m_gu)
    if stop_after == "x1":
        return _dump(x1fm)

    m_pre = ar.mark()
    attn_sb, _ = ar.sb("attn_sb", [128, 16 * 1024], BF16)
    m_attn = ar.mark()
    qn_sb, _ = ar.sb("qn_sb", [128, 16 * 1024], BF16)
    qr_sb, _ = ar.sb("qr_sb", [128, 8 * 1024], BF16)
    C2, _ = ar.sb("C2", [128, S], F32)
    S2, _ = ar.sb("S2", [128, S], F32)

    emit_rope_tables(cm, inputs["fr"], C2, S2)

    m_ln2 = ar.mark()
    lnres = emit_ln_pipeline(cm, x1fm, [1024, 2048, 3072, 0], ln2d,
                             last_sbuf=attn_sb, keep_mark=True)
    emit_qproj(cm, attn_sb, inputs["Wq"], C2, S2, qn_sb, qr_sb)
    k.wait("vector", "pe", k.val("pe"))
    ar.free_to(m_ln2)
    kvn_sb, _ = ar.sb("kvn_sb", [128, 4 * S], BF16)
    krope_sb, _ = ar.sb("krope_sb", [128, S], BF16)
    for e in ("sync", "gpsimd"):
        k.wait(e, "pe", k.val("pe"))
        k.wait(e, "ve", k.val("ve"))
        k.wait(e, "ac", k.val("ac"))
        k.wait(e, "lnst0", lnres["st0"])
        k.wait(e, "lnst1", lnres["st1"])
    emit_kv(cm, ln2d, inputs["Wkv_a"], cm.gcol, cm.bcol, C2, S2,
            kvn_sb, krope_sb)
    emit_attention(cm, inputs["Wkv_b"], kvn_sb, krope_sb, qn_sb, qr_sb,
                   attn_sb)
    ar.free_to(m_attn)
    k.barrier()
    emit_wout(cm, inputs["Wout"], attn_sb, x1fm, x2fm)
    ar.free_to(m_pre)
    k.barrier()
    emit_ffn(cm, x2fm, 0, x3fm)
    k.barrier()
    rout = emit_fm_to_tm_out(cm, x3fm, out_tm)
    k.wait("sync", "outd", rout)
    cm.k.finish()
    return nc


def build_AG(stop_after=None, dump_chunk=0):
    nc = bass.Bass("TRN2", target_bir_lowering=False)
    cm = Common(nc)
    k, ar = cm.k, cm.ar

    inputs = {}
    inputs["x_own"] = nc.dram_tensor("x_own", [TOWN, D], F32,
                                     kind="ExternalInput").ap()
    inputs["fr"] = nc.dram_tensor("fr", [TOWN, DR // 2], F32,
                                  kind="ExternalInput").ap()
    inputs["Wq"] = nc.dram_tensor("Wq", [D, NH * DQK], F32,
                                  kind="ExternalInput").ap()
    inputs["Wkv_a"] = nc.dram_tensor("Wkv_a", [D, RANK + DR], F32,
                                     kind="ExternalInput").ap()
    inputs["kv_gamma"] = nc.dram_tensor("kv_gamma", [RANK], F32,
                                        kind="ExternalInput").ap()
    inputs["kv_beta"] = nc.dram_tensor("kv_beta", [RANK], F32,
                                       kind="ExternalInput").ap()
    inputs["Wkv_b"] = nc.dram_tensor("Wkv_b", [RANK, NH * (DN + DV)], F32,
                                     kind="ExternalInput").ap()
    inputs["Wout"] = nc.dram_tensor("Wout", [NH * DV, D], F32,
                                    kind="ExternalInput").ap()
    for nm, shp, dt in (("gate_w", [IF, D], I32), ("up_w", [IF, D], I32),
                        ("down_w", [D, IF], I32), ("gate_scale", [IF], F32),
                        ("up_scale", [IF], F32), ("down_scale", [D], F32)):
        inputs[nm] = nc.dram_tensor(nm, shp, dt, kind="ExternalInput").ap()
    cm.inputs = inputs

    out_tm = nc.dram_tensor("out_tm", [TOWN, D], F32,
                            kind="ExternalOutput").ap()
    x0fm = nc.dram_tensor("x0fm", [D, TOWN], F32, kind="Internal").ap()
    x1fm = nc.dram_tensor("x1fm", [D, TOWN], F32, kind="Internal").ap()
    ln1d = nc.dram_tensor("ln1d", [D, TOWN], BF16, kind="Internal").ap()
    x2fm = nc.dram_tensor("x2fm", [D, TOWN], F32, kind="Internal").ap()
    x3fm = nc.dram_tensor("x3fm", [D, TOWN], F32, kind="Internal").ap()
    agin = nc.dram_tensor("agin", [AGR, TOWN], BF16, kind="Internal").ap()
    agout = nc.dram_tensor("agout", [4 * AGR, TOWN], BF16,
                           kind="Internal").ap()

    def _dump(src_fm):
        k.barrier()
        ov = out_tm.rearrange("a b -> (a b)").rearrange(
            "(c d) -> c d", c=D)
        r = k.dma("gpsimd", lambda eng: eng.dma_start(
            out=ov,
            in_=src_fm[:, dump_chunk * 1024:(dump_chunk + 1) * 1024]),
            "outd")
        k.wait("sync", "outd", r)
        cm.k.finish()
        return nc

    cm.load_consts(with_ffn_scales=True)
    cm.gcol, _ = ar.sb("gcol", [128, 4], F32)
    cm.bcol, _ = ar.sb("bcol", [128, 4], F32)
    emit_cols(cm, inputs["kv_gamma"], 4, cm.gcol)
    emit_cols(cm, inputs["kv_beta"], 4, cm.bcol)

    emit_tm_to_fm(cm, inputs["x_own"], x0fm, TOWN)
    k.barrier()
    if stop_after == "x0":
        return _dump(x0fm)
    m_gu = emit_prep_pe(cm, "gu")
    k.wait("tensor", "ac", k.val("ac"))
    k.wait("gpsimd", "t2fo", k.val("t2fo"))
    emit_ln_pipeline(cm, x0fm, [0], ln1d)
    if stop_after == "lnd":
        return _dump(ln1d)
    k.wait("vector", "ptw0", cm.wprep_gu[0])
    k.wait("vector", "ptw1", cm.wprep_gu[1])
    k.wait("vector", "lnst0", k.val("lnst0"))
    k.wait("vector", "lnst1", k.val("lnst1"))

    emit_ffn(cm, x0fm, 0, x1fm, ln_dram=ln1d,
             gu_gate=cm.wprep_gu_m,
             mid_emit=lambda: emit_prep_pe(cm, "d"))
    k.barrier()
    ar.free_to(m_gu)
    if stop_after == "x1":
        return _dump(x1fm)

    m_pre = ar.mark()
    attn_sb, _ = ar.sb("attn_sb", [128, 16 * 1024], BF16)
    m_attn = ar.mark()
    qn_sb, _ = ar.sb("qn_sb", [128, 16 * 1024], BF16)
    qr_sb, _ = ar.sb("qr_sb", [128, 8 * 1024], BF16)
    C2, _ = ar.sb("C2", [128, TOWN], F32)
    S2, _ = ar.sb("S2", [128, TOWN], F32)

    emit_rope_tables(cm, inputs["fr"], C2, S2, TK=TOWN)

    m_ln2 = ar.mark()
    lnres = emit_ln_pipeline(cm, x1fm, [0], ln1d,
                             last_sbuf=attn_sb, keep_mark=True)
    for e in ("sync", "gpsimd"):
        k.wait(e, "pe", k.val("pe"))
        k.wait(e, "ve", k.val("ve"))
        k.wait(e, "ac", k.val("ac"))
        k.wait(e, "lnst0", lnres["st0"])
        k.wait(e, "lnst1", lnres["st1"])
    ar.free_to(m_ln2)
    if stop_after == "ln2":
        return _dump(ln1d)

    m_kv = ar.mark()
    kvn_own, _ = ar.sb("kvn_own", [128, 4 * TOWN], BF16)
    krope_own, _ = ar.sb("krope_own", [128, TOWN], BF16)
    emit_kv_own(cm, attn_sb, inputs["Wkv_a"], cm.gcol, cm.bcol, C2, S2,
                kvn_own, krope_own)
    emit_kvb_ag(cm, inputs["Wkv_b"], kvn_own, krope_own, agin)
    emit_ag_trigger(cm, agin, agout)
    k.barrier(exclude=("cc",))
    ar.free_to(m_kv)
    emit_qproj(cm, attn_sb, inputs["Wq"], C2, S2, qn_sb, qr_sb)
    k.wait("vector", "pe", k.val("pe"))

    emit_attention_ag(cm, agout, qn_sb, qr_sb, attn_sb)
    ar.free_to(m_attn)
    k.barrier()
    if stop_after == "attn":
        return _dump(x1fm)
    emit_wout(cm, inputs["Wout"], attn_sb, x1fm, x2fm)
    ar.free_to(m_pre)
    k.barrier()
    if stop_after == "x2":
        return _dump(x2fm)
    emit_ffn(cm, x2fm, 0, x3fm)
    k.barrier()
    rout = emit_fm_to_tm_out(cm, x3fm, out_tm)
    k.wait("sync", "outd", rout)
    cm.k.finish()
    return nc



def _ffn_weight_inputs(inputs):
    return dict(
        gate_w=np.ascontiguousarray(inputs["L0_ffn_gate_w"]),
        up_w=np.ascontiguousarray(inputs["L0_ffn_up_w"]),
        down_w=np.ascontiguousarray(inputs["L0_ffn_down_w"]),
        gate_scale=np.ascontiguousarray(
            np.asarray(inputs["L0_ffn_gate_scale"], np.float32)),
        up_scale=np.ascontiguousarray(
            np.asarray(inputs["L0_ffn_up_scale"], np.float32)),
        down_scale=np.ascontiguousarray(
            np.asarray(inputs["L0_ffn_down_scale"], np.float32)),
    )


def kernel(**inputs):
    return kernel_ag(**inputs)


def _attn_weight_inputs(inputs):
    return dict(
        Wq=np.ascontiguousarray(np.asarray(inputs["L0_attn_Wq"], np.float32)),
        Wkv_a=np.ascontiguousarray(
            np.asarray(inputs["L0_attn_Wkv_a"], np.float32)),
        kv_gamma=np.ascontiguousarray(
            np.asarray(inputs["L0_attn_kv_gamma"], np.float32)),
        kv_beta=np.ascontiguousarray(
            np.asarray(inputs["L0_attn_kv_beta"], np.float32)),
        Wkv_b=np.ascontiguousarray(
            np.asarray(inputs["L0_attn_Wkv_b"], np.float32)),
        Wout=np.ascontiguousarray(
            np.asarray(inputs["L0_attn_Wout"], np.float32)),
    )


def kernel_ag(**inputs):
    x = np.asarray(inputs["L0_attn_hidden_states"], np.float32)
    wff = _ffn_weight_inputs(inputs)
    watt = _attn_weight_inputs(inputs)
    freqs = np.ascontiguousarray(
        np.asarray(inputs["L0_attn_freqs_cis"], np.float32))
    nc = build_AG()
    in_maps = []
    for c in range(NC_):
        g, qc = c // 4, c % 4
        rows = qc * TOWN + np.arange(TOWN)
        m = dict(wff)
        m.update(watt)
        m["x_own"] = np.ascontiguousarray(x[g, qc * TOWN:(qc + 1) * TOWN])
        m["fr"] = np.ascontiguousarray(freqs[rows % 2048])
        in_maps.append(m)
    res = run_bass_kernel_spmd(nc, in_maps, core_ids=list(range(NC_)))
    out = np.empty((B, S, D), np.float32)
    for c in range(NC_):
        g, qc = c // 4, c % 4
        out[g, qc * TOWN:(qc + 1) * TOWN] = res.results[c]["out_tm"]
    return out


def kernel_fused(**inputs):
    x = np.asarray(inputs["L0_attn_hidden_states"], np.float32)
    wff = _ffn_weight_inputs(inputs)
    freqs = np.ascontiguousarray(
        np.asarray(inputs["L0_attn_freqs_cis"], np.float32))
    watt = dict(
        Wq=np.ascontiguousarray(np.asarray(inputs["L0_attn_Wq"], np.float32)),
        Wkv_a=np.ascontiguousarray(
            np.asarray(inputs["L0_attn_Wkv_a"], np.float32)),
        kv_gamma=np.ascontiguousarray(
            np.asarray(inputs["L0_attn_kv_gamma"], np.float32)),
        kv_beta=np.ascontiguousarray(
            np.asarray(inputs["L0_attn_kv_beta"], np.float32)),
        Wkv_b=np.ascontiguousarray(
            np.asarray(inputs["L0_attn_Wkv_b"], np.float32)),
        Wout=np.ascontiguousarray(
            np.asarray(inputs["L0_attn_Wout"], np.float32)),
    )
    nc = build_FUSED()
    in_maps = []
    for c in range(NC_):
        g, qc = c // 4, c % 4
        rows = (qc * 1024 + np.arange(S)) % S
        xr = np.ascontiguousarray(x[g][rows])
        fr = np.ascontiguousarray(freqs[rows % 2048])
        m = dict(wff)
        m.update(watt)
        m["x_full"] = xr
        m["fr"] = fr
        in_maps.append(m)
    res = run_bass_kernel_spmd(nc, in_maps, core_ids=list(range(NC_)))
    out = np.empty((B, S, D), np.float32)
    for c in range(NC_):
        g, qc = c // 4, c % 4
        out[g, qc * TOWN:(qc + 1) * TOWN] = res.results[c]["out_tm"]
    return out


def kernel_unfused(**inputs):
    x = np.ascontiguousarray(
        np.asarray(inputs["L0_attn_hidden_states"], np.float32)
    ).reshape(B * S, D)
    wff = _ffn_weight_inputs(inputs)

    nc1 = build_L1()
    in_maps = [dict(wff, x_own=np.ascontiguousarray(x[c * TOWN:(c + 1) * TOWN]))
               for c in range(NC_)]
    res1 = run_bass_kernel_spmd(nc1, in_maps, core_ids=list(range(NC_)))
    x1 = [np.asarray(res1.results[c]["x1fm"]) for c in range(NC_)]

    freqs = np.ascontiguousarray(
        np.asarray(inputs["L0_attn_freqs_cis"], np.float32))
    watt = dict(
        Wq=np.ascontiguousarray(np.asarray(inputs["L0_attn_Wq"], np.float32)),
        Wkv_a=np.ascontiguousarray(
            np.asarray(inputs["L0_attn_Wkv_a"], np.float32)),
        kv_gamma=np.ascontiguousarray(
            np.asarray(inputs["L0_attn_kv_gamma"], np.float32)),
        kv_beta=np.ascontiguousarray(
            np.asarray(inputs["L0_attn_kv_beta"], np.float32)),
        Wkv_b=np.ascontiguousarray(
            np.asarray(inputs["L0_attn_Wkv_b"], np.float32)),
        Wout=np.ascontiguousarray(
            np.asarray(inputs["L0_attn_Wout"], np.float32)),
    )
    nc2 = build_L2()
    in_maps2 = []
    for c in range(NC_):
        g, qc = c // 4, c % 4
        order = [g * 4 + ((qc + j) % 4) for j in range(4)]
        x1full = np.ascontiguousarray(
            np.concatenate([x1[o] for o in order], axis=1))
        pos = (qc * 1024 + np.arange(S)) % S
        fr = np.ascontiguousarray(freqs[pos % 2048])
        m = dict(wff)
        m.update(watt)
        m["x1fm"] = x1full
        m["fr"] = fr
        in_maps2.append(m)
    res2 = run_bass_kernel_spmd(nc2, in_maps2, core_ids=list(range(NC_)))
    out = np.empty((B, S, D), np.float32)
    for c in range(NC_):
        g, qc = c // 4, c % 4
        out[g, qc * TOWN:(qc + 1) * TOWN] = res2.results[c]["out_tm"]
    return out


# version 19
# speedup vs baseline: 1.6804x; 1.0878x over previous
import math

import numpy as np
import ml_dtypes

import concourse.bass as bass
import concourse.mybir as mybir
from concourse.bass_utils import run_bass_kernel_spmd

F32 = mybir.dt.float32
BF16 = mybir.dt.bfloat16
I32 = mybir.dt.int32
AF = mybir.ActivationFunctionType
ALU = mybir.AluOpType
AX = mybir.AxisListType
bfdt = ml_dtypes.bfloat16

B, S, D = 2, 4096, 2048
NH, DN, DR, DV, RANK = 16, 128, 64, 128, 512
DQK = DN + DR
SCALE = DQK ** -0.5
EPS = 1e-5
IF = 8192
NC_ = 8
TOWN = 1024
PI = math.pi

ENGINES = ("sync", "scalar", "vector", "tensor", "gpsimd")


class K:

    def __init__(self, nc):
        self.nc = nc
        self.q = {e: [] for e in ENGINES}
        self.sems = {}
        self.cnt = {}
        self.waited = {e: {} for e in ENGINES}

    def sem(self, name):
        self.sems[name] = self.nc.alloc_semaphore(name)
        self.cnt[name] = 0

    def wait(self, e, sem, val):
        if val is None or val <= 0:
            return
        if self.waited[e].get(sem, 0) >= val:
            return
        self.waited[e][sem] = val
        h = self.sems[sem]
        self.q[e].append(lambda eng, h=h, v=val: eng.wait_ge(h, v))

    SELF_SEM = {"vector": "ve", "scalar": "ac", "gpsimd": "po"}

    def do(self, e, fn, inc=None):
        ssem = self.SELF_SEM.get(e)
        if ssem is not None:
            self.wait(e, ssem, self.cnt[ssem])
            self.cnt[ssem] += 1
            h = self.sems[ssem]
            if inc is None:
                self.q[e].append(
                    lambda eng, fn=fn, h=h: fn(eng).then_inc(h, 1))
                return self.cnt[ssem]
            sem2, amt2 = inc
            assert sem2 == ssem and amt2 == 1, (e, inc)
            self.q[e].append(lambda eng, fn=fn, h=h: fn(eng).then_inc(h, 1))
            return self.cnt[ssem]
        if inc is None:
            self.q[e].append(fn)
            return None
        sem, amt = inc
        self.cnt[sem] += amt
        h = self.sems[sem]
        self.q[e].append(lambda eng, fn=fn, h=h, a=amt: fn(eng).then_inc(h, a))
        return self.cnt[sem]

    def dma(self, e, fn, stream):
        if stream not in self.sems:
            self.sem(stream)
        self.cnt[stream] += 16
        h = self.sems[stream]
        self.q[e].append(lambda eng, fn=fn, h=h: fn(eng).then_inc(h, 16))
        return self.cnt[stream]

    def val(self, sem):
        return self.cnt.get(sem, 0)

    def barrier(self, engines=ENGINES, exclude=()):
        for e in engines:
            for s, v in self.cnt.items():
                if s in exclude:
                    continue
                self.wait(e, s, v)

    def finish(self):
        with self.nc.Block() as block:
            block.sync(lambda eng: [f(eng) for f in self.q["sync"]])
            block.scalar(lambda eng: [f(eng) for f in self.q["scalar"]])
            block.vector(lambda eng: [f(eng) for f in self.q["vector"]])
            block.tensor(lambda eng: [f(eng) for f in self.q["tensor"]])
            block.gpsimd(lambda eng: [f(eng) for f in self.q["gpsimd"]])


class Arena:

    def __init__(self, nc):
        self.nc = nc
        self.stack = []
        self.uid = 0

    def _nm(self, name):
        self.uid += 1
        return f"{name}_{self.uid}"

    def sb(self, name, shape, dt):
        g = self.nc.sbuf_tensor(self._nm(name), shape, dt)
        h = g.__enter__()
        self.stack.append(g)
        return h, len(self.stack) - 1

    def ps(self, name, shape, dt=F32):
        g = self.nc.psum_tensor(self._nm(name), shape, dt)
        h = g.__enter__()
        self.stack.append(g)
        return h, len(self.stack) - 1

    def free_to(self, mark):
        while len(self.stack) > mark:
            g = self.stack.pop()
            g.__exit__(None, None, None)

    def mark(self):
        return len(self.stack)


def np_consts():
    eye_bf = np.eye(128, dtype=bfdt)
    eye_f32 = np.eye(128, dtype=np.float32)
    ones_col_bf = np.ones((128, 1), dtype=bfdt)
    ones_row_f32 = np.ones((1, 128), dtype=np.float32)
    p64 = np.zeros((64, 64), dtype=bfdt)
    for j in range(32):
        p64[2 * j, 2 * j + 1] = 1.0
        p64[2 * j + 1, 2 * j] = 1.0
    dup = np.zeros((32, 64), dtype=np.float32)
    dups = np.zeros((32, 64), dtype=np.float32)
    for j in range(32):
        dup[j, 2 * j] = 1.0
        dup[j, 2 * j + 1] = 1.0
        dups[j, 2 * j] = -1.0
        dups[j, 2 * j + 1] = 1.0
    p128 = np.zeros((128, 128), dtype=bfdt)
    for j in range(64):
        p128[2 * j, 2 * j + 1] = 1.0
        p128[2 * j + 1, 2 * j] = 1.0
    dup128 = np.zeros((32, 128), dtype=np.float32)
    dups128 = np.zeros((32, 128), dtype=np.float32)
    for j in range(32):
        for base in (0, 64):
            dup128[j, base + 2 * j] = 1.0
            dup128[j, base + 2 * j + 1] = 1.0
            dups128[j, base + 2 * j] = -1.0
            dups128[j, base + 2 * j + 1] = 1.0
    return dict(eye_bf=eye_bf, eye_f32=eye_f32, ones_col_bf=ones_col_bf,
                ones_row_f32=ones_row_f32, p64=p64, dup=dup, dups=dups,
                p128=p128, dup128=dup128, dups128=dups128)


class Common:

    def __init__(self, nc):
        self.nc = nc
        self.k = K(nc)
        self.ar = Arena(nc)
        for s in ("pe", "ve", "ac", "po"):
            self.k.sem(s)

    def load_consts(self, with_ffn_scales=True, with_attn=False):
        nc, k, ar = self.nc, self.k, self.ar
        cn = np_consts()
        self.c_eye_bf, _ = ar.sb("c_eye_bf", [128, 128], BF16)
        self.c_eye_f32, _ = ar.sb("c_eye_f32", [128, 128], F32)
        self.c_ones_col, _ = ar.sb("c_ones_col", [128, 1], BF16)
        self.c_ones_row, _ = ar.sb("c_ones_row", [1, 128], F32)
        self.c_p128, _ = ar.sb("c_p128", [128, 128], BF16)
        self.c_dup128, _ = ar.sb("c_dup128", [32, 128], F32)
        self.c_dups128, _ = ar.sb("c_dups128", [32, 128], F32)
        self.c_negC, _ = ar.sb("c_negC", [128, 1], F32)
        self.negc_ready = self.k.do(
            "vector", lambda eng: eng.memset(self.c_negC[:, :], -30.0))
        pairs = [
            (self.c_eye_bf, cn["eye_bf"]), (self.c_eye_f32, cn["eye_f32"]),
            (self.c_ones_col, cn["ones_col_bf"]),
            (self.c_ones_row, cn["ones_row_f32"]),
            (self.c_p128, cn["p128"]), (self.c_dup128, cn["dup128"]),
            (self.c_dups128, cn["dups128"]),
        ]
        for sb, arr in pairs:
            dh = nc.inline_tensor(np.ascontiguousarray(arr))
            self.k.dma("sync", lambda eng, sb=sb, dh=dh: eng.dma_start(
                out=sb[:, :], in_=dh.ap()), "c0")
        self.consts_ready = self.k.val("c0")
        if with_ffn_scales:
            self.gsc, _ = ar.sb("gsc", [128, 64], F32)
            self.usc, _ = ar.sb("usc", [128, 64], F32)
            self.dsc_half, _ = ar.sb("dsc_half", [128, 16], F32)
            m0 = ar.mark()
            tmp, _ = ar.sb("sc_tmp", [64, 128], F32)
            pscale, _ = ar.ps("sc_ps", [128, 64], F32)
            jobs = [("gate_scale", 64, self.gsc), ("up_scale", 64, self.usc),
                    ("down_scale", 16, self.dsc_half)]
            for name, n, dst in jobs:
                src = self.in_ap(name)
                k.wait("sync", "pe", self.k.val("pe"))
                ld = k.dma("sync", lambda eng, src=src, n=n: eng.dma_start(
                    out=tmp[0:n, :], in_=src.rearrange("(a p) -> a p", p=128)),
                    "c0")
                k.wait("tensor", "c0", ld)
                k.wait("tensor", "ve", self.k.val("ve"))
                mm = k.do("tensor", lambda eng, n=n: eng.matmul(
                    pscale[:, 0:n], tmp[0:n, :], self.c_eye_f32[0:n, 0:n],
                    start=True, stop=True), inc=("pe", 1))
                k.wait("vector", "pe", mm)
                k.do("vector", lambda eng, n=n, dst=dst: eng.tensor_copy(
                    dst[:, 0:n], pscale[:, 0:n]), inc=("ve", 1))
            k.do("vector", lambda eng: eng.tensor_scalar_mul(
                self.dsc_half[:, :], self.dsc_half[:, :], 0.5),
                inc=("ve", 1))
            self.dsc_ready = self.k.val("ve")
            ar.free_to(m0)
            self.k.barrier()
        else:
            self.dsc_ready = 0

    def in_ap(self, name):
        return self.inputs[name]



def emit_prep_cast(cm):
    nc, k = cm.nc, cm.k
    cm.wg_bf = nc.dram_tensor("wg_bf", [IF, D], BF16, kind="Internal").ap()
    cm.wu_bf = nc.dram_tensor("wu_bf", [IF, D], BF16, kind="Internal").ap()
    cm.wd_bf = nc.dram_tensor("wd_bf", [D, IF], BF16, kind="Internal").ap()
    cm.pcast_done = {}
    for (name, bf_out, rows) in (("gate_w", cm.wg_bf, IF),
                                 ("up_w", cm.wu_bf, IF),
                                 ("down_w", cm.wd_bf, D)):
        srcw = cm.in_ap(name)
        vals = []
        for hh in range(2):
            r0, r1 = hh * rows // 2, (hh + 1) * rows // 2
            k.dma("gpsimd", lambda eng, srcw=srcw, bf=bf_out, r0=r0, r1=r1:
                  eng.dma_start(out=bf[r0:r1, :], in_=srcw[r0:r1, :]),
                  f"pcast{hh}")
            vals.append(k.val(f"pcast{hh}"))
        cm.pcast_done[name] = tuple(vals)


def emit_prep_tile(cm, phase="all"):
    nc, k, ar = cm.nc, cm.k, cm.ar
    if phase in ("all", "gu"):
        cm.wgT = nc.dram_tensor("wgT", [64, 16, 128, 128], BF16,
                                kind="Internal").ap()
        cm.wuT = nc.dram_tensor("wuT", [64, 16, 128, 128], BF16,
                                kind="Internal").ap()
        cm.wdT = nc.dram_tensor("wdT", [16, 64, 128, 128], BF16,
                                kind="Internal").ap()
        cm._pt_slab = [ar.sb(f"wprep_s{jj}", [128, IF], BF16)[0]
                       for jj in range(2)]
        cm._pt_slab2 = [ar.sb(f"wprep_t{jj}", [128, IF], BF16)[0]
                        for jj in range(2)]
        cm._pt_state = ({}, {})
    slab, slab2 = cm._pt_slab, cm._pt_slab2
    sh_done, wr_done = cm._pt_state

    alljobs = [("g", kc) for kc in range(16)] + \
              [("u", kc) for kc in range(16)] + \
              [("d", ic) for ic in range(64)]
    if phase == "gu":
        tjobs = list(enumerate(alljobs))[:32]
    elif phase == "d":
        tjobs = list(enumerate(alljobs))[32:]
    else:
        tjobs = list(enumerate(alljobs))
    gate = {"g": "gate_w", "u": "up_w", "d": "down_w"}
    for j2, (which, idx) in tjobs:
        par = j2 % 2
        cols = IF if which in ("g", "u") else D
        for hh in range(2):
            k.wait("sync", f"pcast{hh}", cm.pcast_done[gate[which]][hh])
        if j2 >= 2:
            k.wait("sync", "ve", sh_done[j2 - 2])
        bf = {"g": cm.wg_bf, "u": cm.wu_bf, "d": cm.wd_bf}[which]
        t = k.dma("sync", lambda eng, bf=bf, idx=idx, par=par, cols=cols:
                  eng.dma_start_transpose(
                      slab[par][:, 0:cols],
                      bf[:, idx * 128:(idx + 1) * 128]),
                  f"ptr{par}")
        k.wait("vector", f"ptr{par}", t)
        if j2 >= 2:
            k.wait("vector", f"ptw{par}", wr_done[j2 - 2])
        sh = k.do("vector", lambda eng, par=par, cols=cols: eng.tensor_scalar(
            slab2[par][:, 0:cols], slab[par][:, 0:cols], -128.0, None,
            ALU.add))
        sh_done[j2] = sh
        k.wait("scalar", "ve", sh)
        if which in ("g", "u"):
            dst = cm.wgT if which == "g" else cm.wuT
            w = k.dma("scalar", lambda eng, dst=dst, idx=idx, par=par:
                      eng.dma_start(
                          out=dst[:, idx, :, :].rearrange("m h i -> h m i"),
                          in_=slab2[par][:, 0:IF].rearrange(
                              "h (m i) -> h m i", m=64)),
                      f"ptw{par}")
        else:
            w = k.dma("scalar", lambda eng, idx=idx, par=par:
                      eng.dma_start(
                          out=cm.wdT[:, idx, :, :].rearrange("d i c -> i d c"),
                          in_=slab2[par][:, 0:D].rearrange(
                              "i (d c) -> i d c", d=16)),
                      f"ptw{par}")
        wr_done[j2] = w
        if j2 == 31:
            cm.wprep_gu = (k.val("ptw0"), k.val("ptw1"))
    if phase in ("all", "d"):
        cm.wprep_ptw = (k.val("ptw0"), k.val("ptw1"))


def emit_prep_pe(cm, phase):
    nc, k, ar = cm.nc, cm.k, cm.ar
    if phase == "gu":
        cm.wgT = nc.dram_tensor("wgT", [64, 16, 128, 128], BF16,
                                kind="Internal").ap()
        cm.wuT = nc.dram_tensor("wuT", [64, 16, 128, 128], BF16,
                                kind="Internal").ap()
        cm.wdT = nc.dram_tensor("wdT", [16, 64, 128, 128], BF16,
                                kind="Internal").ap()
        jobs = []
        for b in range(64):
            jobs.append(("g", b))
            jobs.append(("u", b))
        cm.wprep_gu_m = []
    else:
        jobs = [("d", j) for j in range(64)]
    m0 = ar.mark()
    if phase == "gu":
        lw = [ar.sb(f"pp_l{j}", [128, 2048], BF16)[0] for j in range(2)]
        tb = [ar.sb(f"pp_t{j}", [128, 2048], BF16)[0] for j in range(2)]
        cm._pp_bufs = (lw, tb)
        seed_pe, seed_ac = k.val("pe"), k.val("ac")
        seed_wr = (0, 0)
    else:
        lw, tb = cm._pp_bufs
        seed_pe, seed_ac = cm._pp_pe_final, cm._pp_ac_final
        seed_wr = cm.wprep_gu
    mps = ar.mark()
    pst = [ar.ps(f"pp_p{phase}{j}", [128, 2048], F32)[0] for j in range(2)]
    srcs = {"g": cm.in_ap("gate_w"), "u": cm.in_ap("up_w"),
            "d": cm.in_ap("down_w")}
    lds, mm_done, ev_done, wr_done = {}, {}, {}, {}
    k.wait("tensor", "c0", cm.consts_ready)
    if phase == "d":
        k.wait("tensor", "ac", k.val("ac"))
    for j2, (which, idx) in enumerate(jobs):
        par = j2 % 2
        if which == "d":
            r0, c0 = (idx // 4) * 128, (idx % 4) * 2048
        else:
            r0, c0 = idx * 128, 0
        if j2 >= 2:
            k.wait("gpsimd", "pe", mm_done[j2 - 2])
        else:
            k.wait("gpsimd", "pe", seed_pe)
            k.wait("gpsimd", "ac", seed_ac)
        lds[j2] = k.dma("gpsimd", lambda eng, which=which, r0=r0, c0=c0,
                        par=par: eng.dma_start(
                            out=lw[par][:, :],
                            in_=srcs[which][r0:r0 + 128, c0:c0 + 2048]),
                        f"pls{par}")
        k.wait("tensor", f"pls{par}", lds[j2])
        if j2 >= 2:
            k.wait("tensor", "ac", ev_done[j2 - 2])
        for kc in range(16):
            k.do("tensor", lambda eng, kc=kc, par=par: eng.matmul(
                pst[par][:, kc * 128:(kc + 1) * 128],
                lw[par][:, kc * 128:(kc + 1) * 128],
                cm.c_eye_bf[:, :], start=True, stop=True),
                inc=("pe", 1) if kc == 15 else None)
        mm_done[j2] = k.val("pe")
        k.wait("scalar", "pe", mm_done[j2])
        if j2 >= 2:
            k.wait("scalar", f"ptw{par}", wr_done[j2 - 2])
        else:
            k.wait("scalar", f"ptw{par}", seed_wr[par])
        for q in range(4):
            k.do("scalar", lambda eng, q=q, par=par: eng.activation(
                tb[par][:, q * 512:(q + 1) * 512],
                pst[par][:, q * 512:(q + 1) * 512],
                AF.Copy, bias=-128.0),
                inc=("ac", 1) if q == 3 else None)
        ev_done[j2] = k.val("ac")
        k.wait("sync", "ac", ev_done[j2])
        if which in ("g", "u"):
            dst = cm.wgT if which == "g" else cm.wuT
            wr_done[j2] = k.dma("sync", lambda eng, dst=dst, idx=idx,
                                par=par: eng.dma_start(
                                    out=dst[idx].rearrange("a d h -> d a h"),
                                    in_=tb[par][:, :].rearrange(
                                        "d (a h) -> d a h", a=16)),
                                f"ptw{par}")
        else:
            wr_done[j2] = k.dma("sync", lambda eng, idx=idx, par=par:
                                eng.dma_start(
                                    out=cm.wdT[idx // 4,
                                               (idx % 4) * 16:
                                               (idx % 4 + 1) * 16].rearrange(
                                                   "m h c -> h m c"),
                                    in_=tb[par][:, :].rearrange(
                                        "h (m c) -> h m c", m=16)),
                                f"ptw{par}")
        if phase == "gu" and which == "u":
            cm.wprep_gu_m.append((k.val("ptw0"), k.val("ptw1")))
    if phase == "gu":
        cm.wprep_gu = (k.val("ptw0"), k.val("ptw1"))
        cm._pp_pe_final = k.val("pe")
        cm._pp_ac_final = k.val("ac")
    else:
        cm.wprep_ptw = (k.val("ptw0"), k.val("ptw1"))
    ar.free_to(mps)
    return m0


def emit_ffn_weight_prep(cm):
    m0 = cm.ar.mark()
    emit_prep_cast(cm)
    emit_prep_tile(cm)
    cm.k.barrier()
    cm.ar.free_to(m0)



def emit_tm_to_fm(cm, src_tm, dst_fm, T, t0=0):
    nc, k, ar = cm.nc, cm.k, cm.ar
    assert T % 128 == 0
    nt = T // 128
    m0 = ar.mark()
    xt = [ar.sb(f"t2f_x{t}", [128, D], F32)[0] for t in range(nt)]
    xo = [ar.sb(f"t2f_o{d}", [128, T], F32)[0] for d in range(16)]
    pt, _ = ar.ps("t2f_ps", [128, 512], F32)
    pt2, _ = ar.ps("t2f_ps2", [128, 512], F32)

    for t in range(nt):
        k.dma("sync", lambda eng, t=t: eng.dma_start(
            out=xt[t][:, :],
            in_=src_tm[t0 + t * 128:t0 + (t + 1) * 128, :]), "t2f")
    ld_all = k.val("t2f")
    evac_cnt = {}
    for d in range(16):
        for t in range(nt):
            par = (d * nt + t) // 4 % 2
            ps = pt if par == 0 else pt2
            k.wait("tensor", "t2f", ld_all)
            grp = (d * nt + t) // 4
            if grp >= 2:
                k.wait("tensor", "ve", evac_cnt.get(grp - 2, 0))
            k.do("tensor", lambda eng, d=d, t=t, ps=ps: eng.matmul(
                ps[:, (t % 4) * 128:(t % 4 + 1) * 128],
                xt[t][:, d * 128:(d + 1) * 128],
                cm.c_eye_f32[:, :], start=True, stop=True),
                inc=("pe", 1) if (d * nt + t) % 4 == 3 or t == nt - 1 else None)
            if (d * nt + t) % 4 == 3 or t == nt - 1:
                pedone = k.val("pe")
                k.wait("vector", "pe", pedone)
                lo = (t // 4) * 4
                nn = (t - lo + 1) * 128
                ev = k.do("vector", lambda eng, d=d, lo=lo, nn=nn, ps=ps:
                          eng.tensor_copy(
                              xo[d][:, lo * 128:lo * 128 + nn],
                              ps[:, 0:nn]),
                          inc=("ve", 1))
                evac_cnt[grp] = ev
    for d in range(16):
        k.wait("scalar", "ve", k.val("ve"))
        k.dma("scalar", lambda eng, d=d: eng.dma_start(
            out=dst_fm[d * 128:(d + 1) * 128, t0:t0 + T],
            in_=xo[d][:, :]), "t2fo")
    out_dA = k.val("t2fo")
    ar.free_to(m0)
    return ("t2fo", out_dA)



def emit_ln_pipeline(cm, x_fm, t0s, also_dram, last_sbuf=None,
                     keep_mark=False):
    nc, k, ar = cm.nc, cm.k, cm.ar
    T = 1024
    m0 = ar.mark()
    xbb = [ar.sb(f"lp_x{j}", [128, T], BF16)[0] for j in range(2)]
    xq = [ar.sb(f"lp_q{j}", [128, T], BF16)[0] for j in range(2)]
    xab = [ar.sb(f"lp_a{j}", [128, T], BF16)[0] for j in range(2)]
    stage = [ar.sb(f"lp_st{j}", [128, T], BF16)[0] for j in range(2)]
    rowbuf, _ = ar.sb("lp_rows", [1, 3 * T], F32)
    abc, _ = ar.sb("lp_abc", [128, T], BF16)
    bbc, _ = ar.sb("lp_bbc", [128, T], BF16)
    mpsum = ar.mark()
    prow, _ = ar.ps("lp_prow", [1, 2 * T], F32)
    pbc, _ = ar.ps("lp_pbc", [128, T], F32)
    pbc2, _ = ar.ps("lp_pbc2", [128, T], F32)
    mu = rowbuf[:, 0:T]
    arow = rowbuf[:, T:2 * T]
    brow = rowbuf[:, 2 * T:3 * T]

    k.wait("tensor", "c0", cm.consts_ready)
    st = dict(stats_pe=k.val("pe"), sq_ac=k.val("ac"), ve=k.val("ve"),
              prow_ve=k.val("ve"), bc_pe=k.val("pe"), evac_ve=k.val("ve"),
              app_ve=k.val("ve"), st0=k.val("lnst0"), st1=k.val("lnst1"))

    for ci, t0 in enumerate(t0s):
        is_last = (ci == len(t0s) - 1) and last_sbuf is not None
        p_stats, p_sq, p_ve = st["stats_pe"], st["sq_ac"], st["ve"]
        p_prow, p_bc, p_evac = st["prow_ve"], st["bc_pe"], st["evac_ve"]
        p_app, p_st0, p_st1 = st["app_ve"], st["st0"], st["st1"]
        lds, lds2, sq_ac, mmq = {}, {}, {}, {}

        def load_x(kc):
            par = kc % 2
            if kc >= 2:
                k.wait("gpsimd", "pe", mmq[kc - 2])
                k.wait("gpsimd", "ac", sq_ac[kc - 2])
            else:
                k.wait("gpsimd", "pe", p_stats)
                k.wait("gpsimd", "ac", p_sq)
                k.wait("gpsimd", "ve", p_ve)
            lds[kc] = k.dma("gpsimd", lambda eng, kc=kc, par=par, t0=t0:
                            eng.dma_start(
                                out=xbb[par][:, :],
                                in_=x_fm[kc * 128:(kc + 1) * 128,
                                         t0:t0 + T]), f"lpx{par}")

        def load_x2(kc):
            par = kc % 2
            if kc >= 2:
                k.wait("gpsimd", "ve", app_ve[kc - 2])
            else:
                k.wait("gpsimd", "ve", p_app)
            lds2[kc] = k.dma("gpsimd", lambda eng, kc=kc, par=par, t0=t0:
                             eng.dma_start(
                                 out=xab[par][:, :],
                                 in_=x_fm[kc * 128:(kc + 1) * 128,
                                          t0:t0 + T]), f"lpa{par}")

        load_x(0)
        load_x(1)
        for kc in range(16):
            par = kc % 2
            k.wait("scalar", f"lpx{par}", lds[kc])
            if kc >= 2:
                k.wait("scalar", "pe", mmq[kc - 2])
            else:
                k.wait("scalar", "pe", p_stats)
            sq_ac[kc] = k.do("scalar", lambda eng, kc=kc, par=par:
                             eng.activation(xq[par][:, :], xbb[par][:, :],
                                            AF.Square), inc=("ac", 1))
            k.wait("tensor", f"lpx{par}", lds[kc])
            k.wait("tensor", "ac", sq_ac[kc])
            if kc == 0:
                k.wait("tensor", "ve", p_prow)
            for nb in range(T // 512):
                k.do("tensor", lambda eng, nb=nb, kc=kc, par=par: eng.matmul(
                    prow[:, nb * 512:(nb + 1) * 512],
                    cm.c_ones_col[:, :],
                    xbb[par][:, nb * 512:(nb + 1) * 512],
                    start=(kc == 0), stop=(kc == 15)), inc=None)
                k.do("tensor", lambda eng, nb=nb, kc=kc, par=par: eng.matmul(
                    prow[:, T + nb * 512:T + (nb + 1) * 512],
                    cm.c_ones_col[:, :],
                    xq[par][:, nb * 512:(nb + 1) * 512],
                    start=(kc == 0), stop=(kc == 15)),
                    inc=("pe", 1))
            mmq[kc] = k.val("pe")
            if kc < 14:
                load_x(kc + 2)
            elif kc == 14:
                load_x2(0)
            else:
                load_x2(1)
        stats_pe = k.val("pe")
        k.wait("vector", "pe", stats_pe)
        k.do("vector", lambda eng: eng.tensor_scalar_mul(
            mu, prow[:, 0:T], 1.0 / D))
        k.do("vector", lambda eng: eng.tensor_scalar_mul(
            arow, prow[:, T:2 * T], 1.0 / D), inc=("ve", 1))
        prow_ve = k.val("ve")
        k.do("vector", lambda eng: eng.scalar_tensor_tensor(
            brow, mu, -1.0, mu, ALU.mult, ALU.mult))
        k.do("vector", lambda eng: eng.tensor_add(arow, arow, brow))
        vq = k.do("vector", lambda eng: eng.tensor_scalar_add(
            arow, arow, EPS), inc=("ve", 1))
        k.wait("scalar", "ve", vq)
        k.wait("scalar", "pe", p_bc)
        k.do("scalar", lambda eng: eng.activation(arow, arow, AF.Ln))
        rstd_ac = k.do("scalar", lambda eng: eng.activation(
            arow, arow, AF.Exp, scale=-0.5), inc=("ac", 1))
        k.wait("vector", "ac", rstd_ac)
        k.do("vector", lambda eng: eng.scalar_tensor_tensor(
            brow, mu, -1.0, arow, ALU.mult, ALU.mult), inc=("ve", 1))
        rowdone = k.val("ve")
        k.wait("tensor", "ve", rowdone)
        for nb in range(T // 512):
            k.do("tensor", lambda eng, nb=nb: eng.matmul(
                pbc[:, nb * 512:(nb + 1) * 512], cm.c_ones_row[:, :],
                arow[:, nb * 512:(nb + 1) * 512], start=True, stop=True))
            k.do("tensor", lambda eng, nb=nb: eng.matmul(
                pbc2[:, nb * 512:(nb + 1) * 512], cm.c_ones_row[:, :],
                brow[:, nb * 512:(nb + 1) * 512], start=True, stop=True),
                inc=("pe", 1) if nb == T // 512 - 1 else None)
        bc_pe = k.val("pe")
        k.wait("vector", "pe", bc_pe)
        k.do("vector", lambda eng: eng.tensor_copy(abc[:, :], pbc[:, :]))
        k.do("vector", lambda eng: eng.tensor_copy(bbc[:, :], pbc2[:, :]),
             inc=("ve", 1))
        evac_ve = k.val("ve")
        app_ve, st_dma = {}, {}
        for kc in range(16):
            par = kc % 2
            k.wait("vector", f"lpa{par}", lds2[kc])
            if is_last:
                dst = last_sbuf[:, kc * T:(kc + 1) * T]
            else:
                if kc >= 2:
                    k.wait("vector", f"lnst{par}", st_dma[kc - 2])
                elif par == 0:
                    k.wait("vector", "lnst0", p_st0)
                else:
                    k.wait("vector", "lnst1", p_st1)
                dst = stage[par][:, :]
            k.do("vector", lambda eng, par=par: eng.tensor_mul(
                xab[par][:, :], xab[par][:, :], abc[:, :]))
            app_ve[kc] = k.do("vector", lambda eng, dst=dst, par=par:
                              eng.tensor_tensor(
                                  dst, xab[par][:, :], bbc[:, :], ALU.add),
                              inc=("ve", 1))
            if kc < 14:
                load_x2(kc + 2)
            k.wait("scalar", "ve", app_ve[kc])
            srcap = (last_sbuf[:, kc * T:(kc + 1) * T] if is_last
                     else stage[par][:, :])
            st_dma[kc] = k.dma("scalar", lambda eng, kc=kc, srcap=srcap,
                               t0=t0: eng.dma_start(
                                   out=also_dram[kc * 128:(kc + 1) * 128,
                                                 t0:t0 + T],
                                   in_=srcap), f"lnst{par}")
        st = dict(stats_pe=stats_pe, sq_ac=sq_ac[15], ve=k.val("ve"),
                  prow_ve=prow_ve, bc_pe=bc_pe, evac_ve=evac_ve,
                  app_ve=app_ve[15], st0=k.val("lnst0"), st1=k.val("lnst1"))
    ar.free_to(mpsum if keep_mark else m0)
    return dict(ln_ve=st["app_ve"], st0=st["st0"], st1=st["st1"],
                pe=st["bc_pe"])


def emit_ln_chunk(cm, x_fm, t0, lnsb, also_dram=None, ldq="sync"):
    nc, k, ar = cm.nc, cm.k, cm.ar
    T = 1024
    m0 = ar.mark()
    stage = None
    if lnsb is None:
        assert also_dram is not None
        stage = [ar.sb(f"ln_st{j}", [128, T], BF16)[0] for j in range(2)]
    xs = [ar.sb(f"ln_x{j}", [128, T], F32)[0] for j in range(2)]
    xb = [ar.sb(f"ln_b{j}", [128, T], BF16)[0] for j in range(2)]
    xq = [ar.sb(f"ln_q{j}", [128, T], BF16)[0] for j in range(2)]
    rowbuf, _ = ar.sb("ln_rows", [1, 2 * T], F32)
    abc, _ = ar.sb("ln_abc", [128, T], F32)
    bbc, _ = ar.sb("ln_bbc", [128, T], F32)
    prow, _ = ar.ps("ln_prow", [1, 2 * T], F32)
    pbc, _ = ar.ps("ln_pbc", [128, T], F32)
    pbc2, _ = ar.ps("ln_pbc2", [128, T], F32)

    mu = rowbuf[:, 0:T]
    arow = rowbuf[:, T:2 * T]
    msq = arow
    var = arow
    sd = arow
    brow = abc[0:1, 0:T]

    k.wait("tensor", "c0", cm.consts_ready)
    lds, cast_ve, sq_ac, mmq = {}, {}, {}, {}

    def load_x(kc):
        par = kc % 2
        if kc >= 2:
            k.wait(ldq, "ve", cast_ve[kc - 2])
        lds[kc] = k.dma(ldq, lambda eng, kc=kc, par=par: eng.dma_start(
            out=xs[par][:, :],
            in_=x_fm[kc * 128:(kc + 1) * 128, t0:t0 + T]), f"lnx{par}")

    load_x(0)
    load_x(1)
    for kc in range(16):
        par = kc % 2
        k.wait("vector", f"lnx{par}", lds[kc])
        if kc >= 2:
            k.wait("vector", "pe", mmq[kc - 2])
        cast_ve[kc] = k.do("vector", lambda eng, kc=kc, par=par:
                           eng.tensor_copy(xb[par][:, :], xs[par][:, :]),
                           inc=("ve", 1))
        if kc < 14:
            load_x(kc + 2)
        k.wait("scalar", "ve", cast_ve[kc])
        if kc >= 2:
            k.wait("scalar", "pe", mmq[kc - 2])
        sq_ac[kc] = k.do("scalar", lambda eng, kc=kc, par=par: eng.activation(
            xq[par][:, :], xb[par][:, :], AF.Square), inc=("ac", 1))
        k.wait("tensor", "ac", sq_ac[kc])
        for nb in range(T // 512):
            k.do("tensor", lambda eng, nb=nb, kc=kc, par=par: eng.matmul(
                prow[:, nb * 512:(nb + 1) * 512],
                cm.c_ones_col[:, :],
                xb[par][:, nb * 512:(nb + 1) * 512],
                start=(kc == 0), stop=(kc == 15)), inc=None)
            k.do("tensor", lambda eng, nb=nb, kc=kc, par=par: eng.matmul(
                prow[:, T + nb * 512:T + (nb + 1) * 512],
                cm.c_ones_col[:, :],
                xq[par][:, nb * 512:(nb + 1) * 512],
                start=(kc == 0), stop=(kc == 15)),
                inc=("pe", 1) if nb == T // 512 - 1 else None)
        mmq[kc] = k.val("pe")
    stats_pe = k.val("pe")
    k.wait("vector", "pe", stats_pe)
    k.do("vector", lambda eng: eng.tensor_scalar_mul(mu, prow[:, 0:T], 1.0 / D))
    k.do("vector", lambda eng: eng.tensor_scalar_mul(
        msq, prow[:, T:2 * T], 1.0 / D))
    k.do("vector", lambda eng: eng.scalar_tensor_tensor(
        brow, mu, -1.0, mu, ALU.mult, ALU.mult))
    k.do("vector", lambda eng: eng.tensor_add(var, msq, brow))
    k.do("vector", lambda eng: eng.tensor_scalar_add(var, var, EPS),
         inc=("ve", 1))
    k.wait("scalar", "ve", k.val("ve"))
    k.do("scalar", lambda eng: eng.activation(sd, var, AF.Ln))
    k.do("scalar", lambda eng: eng.activation(arow, sd, AF.Exp, scale=-0.5),
         inc=("ac", 1))
    k.wait("vector", "ac", k.val("ac"))
    k.do("vector", lambda eng: eng.scalar_tensor_tensor(
        brow, mu, -1.0, arow, ALU.mult, ALU.mult), inc=("ve", 1))
    rowdone = k.val("ve")
    k.wait("tensor", "ve", rowdone)
    for nb in range(T // 512):
        k.do("tensor", lambda eng, nb=nb: eng.matmul(
            pbc[:, nb * 512:(nb + 1) * 512], cm.c_ones_row[:, :],
            arow[:, nb * 512:(nb + 1) * 512], start=True, stop=True))
        k.do("tensor", lambda eng, nb=nb: eng.matmul(
            pbc2[:, nb * 512:(nb + 1) * 512], cm.c_ones_row[:, :],
            brow[:, nb * 512:(nb + 1) * 512], start=True, stop=True),
            inc=("pe", 1) if nb == T // 512 - 1 else None)
    bc_pe = k.val("pe")
    k.wait("vector", "pe", bc_pe)
    k.do("vector", lambda eng: eng.tensor_copy(abc[:, :], pbc[:, :]))
    k.do("vector", lambda eng: eng.tensor_copy(bbc[:, :], pbc2[:, :]),
         inc=("ve", 1))
    bc_ve = k.val("ve")
    lds2, app_ve = {}, {}

    def load_x2(kc):
        par = kc % 2
        if kc >= 2:
            k.wait(ldq, "ve", app_ve[kc - 2])
        lds2[kc] = k.dma(ldq, lambda eng, kc=kc, par=par: eng.dma_start(
            out=xs[par][:, :],
            in_=x_fm[kc * 128:(kc + 1) * 128, t0:t0 + T]), f"lnx{par}")

    load_x2(0)
    load_x2(1)
    st_dma = {}
    for kc in range(16):
        par = kc % 2
        k.wait("vector", f"lnx{par}", lds2[kc])
        k.wait("vector", "ve", bc_ve)
        k.do("vector", lambda eng, kc=kc, par=par: eng.tensor_mul(
            xs[par][:, :], xs[par][:, :], abc[:, :]))
        if stage is not None:
            if kc >= 2:
                k.wait("vector", f"lnst{par}", st_dma[kc - 2])
            dst = stage[par][:, :]
        else:
            dst = lnsb[:, kc * T:(kc + 1) * T]
        app_ve[kc] = k.do("vector", lambda eng, dst=dst, par=par:
                          eng.tensor_tensor(
                              dst, xs[par][:, :],
                              bbc[:, :], ALU.add))
        if kc < 14:
            load_x2(kc + 2)
        if also_dram is not None:
            k.wait("scalar", "ve", app_ve[kc])
            srcap = stage[par][:, :] if stage is not None \
                else lnsb[:, kc * T:(kc + 1) * T]
            st_dma[kc] = k.dma("scalar", lambda eng, kc=kc, srcap=srcap,
                               t0=t0: eng.dma_start(
                                   out=also_dram[kc * 128:(kc + 1) * 128,
                                                 t0:t0 + T],
                                   in_=srcap), f"lnst{par}")
    res = dict(ln_ve=k.val("ve"))
    ar.free_to(m0)
    return res



def emit_ffn(cm, x_fm, t0, xout_fm, dbg_ln=None, ln_dram=None,
             gu_gate=None, mid_emit=None):
    nc, k, ar = cm.nc, cm.k, cm.ar
    T = 1024
    m0 = ar.mark()
    hsb, _ = ar.sb("ffn_h", [128, 64 * T], BF16)
    mln = ar.mark()
    lnsb, _ = ar.sb("ffn_ln", [128, 16 * T], BF16)
    if ln_dram is None:
        lnres = emit_ln_chunk(cm, x_fm, t0, lnsb, also_dram=dbg_ln)
        ln_ready_ve = lnres["ln_ve"]
        k.barrier()
    else:
        k.barrier(("sync",), exclude=("pcast0", "pcast1", "ptr0", "ptr1",
                                      "ptw0", "ptw1"))
        for kc in range(16):
            k.dma("sync", lambda eng, kc=kc: eng.dma_start(
                out=lnsb[:, kc * T:(kc + 1) * T],
                in_=ln_dram[kc * 128:(kc + 1) * 128, t0:t0 + T]), "fl")
        fl_all = k.val("fl")
        k.wait("tensor", "fl", fl_all)
        ln_ready_ve = 0
    wsg = [ar.sb(f"ffn_wg{j}", [128, 16 * 128], BF16)[0] for j in range(2)]
    wsu = [ar.sb(f"ffn_wu{j}", [128, 16 * 128], BF16)[0] for j in range(2)]
    sg = [ar.sb(f"ffn_sg{j}", [128, T], BF16)[0] for j in range(2)]
    su = [ar.sb(f"ffn_su{j}", [128, T], BF16)[0] for j in range(2)]
    pg = [ar.ps(f"ffn_pg{j}", [128, T], F32)[0] for j in range(2)]
    pu = [ar.ps(f"ffn_pu{j}", [128, T], F32)[0] for j in range(2)]

    k.wait("tensor", "ve", ln_ready_ve)
    k.wait("tensor", "ve", cm.dsc_ready)
    if gu_gate is None:
        k.wait("sync", "ptw0", cm.wprep_gu[0])
        k.wait("sync", "ptw1", cm.wprep_gu[1])

    wld, mm_done, act_done, h_done = {}, {}, {}, {}

    def load_w(m):
        par = m % 2
        if m >= 2:
            k.wait("sync", "pe", mm_done[m - 2])
        if gu_gate is not None:
            k.wait("sync", "ptw0", gu_gate[m][0])
            k.wait("sync", "ptw1", gu_gate[m][1])
        k.dma("sync", lambda eng, m=m, par=par: eng.dma_start(
            out=wsg[par][:, :].rearrange("p (a f) -> p a f", a=16),
            in_=cm.wgT[m].rearrange("a h i -> h a i")), f"fw{par}")
        wld[m] = k.dma("sync", lambda eng, m=m, par=par: eng.dma_start(
            out=wsu[par][:, :].rearrange("p (a f) -> p a f", a=16),
            in_=cm.wuT[m].rearrange("a h i -> h a i")), f"fw{par}")

    load_w(0)
    load_w(1)
    for m in range(64):
        par = m % 2
        k.wait("tensor", f"fw{par}", wld[m])
        if m >= 2:
            k.wait("tensor", "ac", act_done[m - 2])
        for which in range(2):
            ps = pg[par] if which == 0 else pu[par]
            ws = wsg[par] if which == 0 else wsu[par]
            for half in range(2):
                c0 = half * 512
                for kc in range(16):
                    last = (which == 1 and half == 1 and kc == 15)
                    k.do("tensor", lambda eng, ps=ps, ws=ws, kc=kc, c0=c0:
                         eng.matmul(ps[:, c0:c0 + 512],
                                    ws[:, kc * 128:(kc + 1) * 128],
                                    lnsb[:, kc * T + c0:kc * T + c0 + 512],
                                    start=(kc == 0), stop=(kc == 15)),
                         inc=("pe", 1) if last else None)
        mm_done[m] = k.val("pe")
        if m < 62:
            load_w(m + 2)
        k.wait("scalar", "pe", mm_done[m])
        if m >= 2:
            k.wait("scalar", "ve", h_done[m - 2])
        k.do("scalar", lambda eng, m=m, par=par: eng.activation(
            sg[par][:, :], pg[par][:, :], AF.Silu,
            scale=cm.gsc[:, m:m + 1]))
        act_done[m] = k.do("scalar", lambda eng, m=m, par=par: eng.activation(
            su[par][:, :], pu[par][:, :], AF.Copy,
            scale=cm.usc[:, m:m + 1]), inc=("ac", 1))
        k.wait("vector", "ac", act_done[m])
        h_done[m] = k.do("vector", lambda eng, m=m, par=par: eng.tensor_mul(
            hsb[:, m * T:(m + 1) * T], sg[par][:, :], su[par][:, :]),
            inc=("ve", 1))
    h_all = k.val("ve")

    ar.free_to(mln)
    if mid_emit is not None:
        mid_emit()
    wsd = [ar.sb(f"ffn_wd{j}", [128, 64 * 128], BF16)[0] for j in range(2)]
    xr = [ar.sb(f"ffn_xr{j}", [128, T], F32)[0] for j in range(2)]
    xo = [ar.sb(f"ffn_xo{j}", [128, T], F32)[0] for j in range(2)]
    pd = [ar.ps(f"ffn_pd{j}", [128, T], F32)[0] for j in range(2)]

    dld, xld, dmm_done, dev_done, dst_done = {}, {}, {}, {}, {}

    def load_d(d):
        par = d % 2
        k.wait("sync", "ptw0", cm.wprep_ptw[0])
        k.wait("sync", "ptw1", cm.wprep_ptw[1])
        if d < 2:
            k.wait("scalar", "ptw0", cm.wprep_ptw[0])
            k.wait("scalar", "ptw1", cm.wprep_ptw[1])
        if d >= 2:
            k.wait("sync", "pe", dmm_done[d - 2])
            k.wait("scalar", "ve", dev_done[d - 2])
        else:
            k.wait("sync", "pe", mm_done[63])
            k.wait("sync", "ve", h_all)
            k.wait("scalar", "ve", h_all)
        dld[d] = k.dma("sync", lambda eng, d=d, par=par: eng.dma_start(
            out=wsd[par][:, :].rearrange("p (a f) -> p a f", a=64),
            in_=cm.wdT[d].rearrange("m i c -> i m c")), f"fd{par}")
        xld[d] = k.dma("scalar", lambda eng, d=d, par=par: eng.dma_start(
            out=xr[par][:, :],
            in_=x_fm[d * 128:(d + 1) * 128, t0:t0 + T]), f"fx{par}")

    load_d(0)
    load_d(1)
    k.wait("tensor", "ve", h_all)
    k.wait("tensor", "ac", k.val("ac"))
    for d in range(16):
        par = d % 2
        k.wait("tensor", f"fd{par}", dld[d])
        if d >= 2:
            k.wait("tensor", "ve", dev_done[d - 2])
        for half in range(2):
            c0 = half * 512
            for m in range(64):
                last = (half == 1 and m == 63)
                k.do("tensor", lambda eng, d=d, par=par, m=m, c0=c0:
                     eng.matmul(pd[par][:, c0:c0 + 512],
                                wsd[par][:, m * 128:(m + 1) * 128],
                                hsb[:, m * T + c0:m * T + c0 + 512],
                                start=(m == 0), stop=(m == 63)),
                     inc=("pe", 1) if last else None)
        dmm_done[d] = k.val("pe")
        k.wait("vector", "pe", dmm_done[d])
        k.wait("vector", f"fx{par}", xld[d])
        if d >= 2:
            k.wait("vector", f"fo{par}", dst_done[d - 2])
        dev_done[d] = k.do("vector", lambda eng, d=d, par=par:
                           eng.scalar_tensor_tensor(
                               xo[par][:, :], pd[par][:, :],
                               cm.dsc_half[:, d:d + 1], xr[par][:, :],
                               ALU.mult, ALU.add), inc=("ve", 1))
        k.wait("scalar", "ve", dev_done[d])
        dst_done[d] = k.dma("scalar", lambda eng, d=d, par=par: eng.dma_start(
            out=xout_fm[d * 128:(d + 1) * 128, t0:t0 + T],
            in_=xo[par][:, :]), f"fo{par}")
        if d < 14:
            load_d(d + 2)
    res = dict(x_fo=(k.val("fo0"), k.val("fo1")))
    ar.free_to(m0)
    return res



def emit_fm_to_tm_out(cm, src_fm, dst_tm):
    nc, k, ar = cm.nc, cm.k, cm.ar
    m0 = ar.mark()
    xs = [ar.sb(f"f2t_x{d}", [128, 1024], F32)[0] for d in range(16)]
    ot = [ar.sb(f"f2t_o{t}", [128, D], F32)[0] for t in range(8)]
    ps = [ar.ps(f"f2t_p{j}", [128, 512], F32)[0] for j in range(2)]

    for d in range(16):
        k.dma("sync", lambda eng, d=d: eng.dma_start(
            out=xs[d][:, :], in_=src_fm[d * 128:(d + 1) * 128, :]), "f2t")
    ld_all = k.val("f2t")
    ev = {}
    for t in range(8):
        for d in range(16):
            g = (t * 16 + d) // 4
            par = g % 2
            k.wait("tensor", "f2t", ld_all)
            if g >= 2:
                k.wait("tensor", "ve", ev.get(g - 2, 0))
            k.do("tensor", lambda eng, t=t, d=d, par=par: eng.matmul(
                ps[par][:, (d % 4) * 128:(d % 4 + 1) * 128],
                xs[d][:, t * 128:(t + 1) * 128],
                cm.c_eye_f32[:, :], start=True, stop=True),
                inc=("pe", 1) if d % 4 == 3 else None)
            if d % 4 == 3:
                k.wait("vector", "pe", k.val("pe"))
                ev[g] = k.do("vector", lambda eng, t=t, d=d, par=par:
                             eng.tensor_copy(
                                 ot[t][:, (d - 3) * 128:(d + 1) * 128],
                                 ps[par][:, :]), inc=("ve", 1))
    for t in range(8):
        k.wait("gpsimd", "ve", k.val("ve"))
        k.dma("gpsimd", lambda eng, t=t: eng.dma_start(
            out=dst_tm[t * 128:(t + 1) * 128, :], in_=ot[t][:, :]), "outd")
    res = k.val("outd")
    ar.free_to(m0)
    return res



def emit_cols(cm, src1d, n, dst_sb, scale_after=None):
    nc, k, ar = cm.nc, cm.k, cm.ar
    m0 = ar.mark()
    tmp, _ = ar.sb("col_tmp", [64, 128], F32)
    pscale, _ = ar.ps("col_ps", [128, 64], F32)
    k.wait("sync", "pe", k.val("pe"))
    ld = k.dma("sync", lambda eng: eng.dma_start(
        out=tmp[0:n, :], in_=src1d.rearrange("(a p) -> a p", p=128)),
        "c0")
    k.wait("tensor", "c0", ld)
    k.wait("tensor", "ve", k.val("ve"))
    mm = k.do("tensor", lambda eng: eng.matmul(
        pscale[:, 0:n], tmp[0:n, :], cm.c_eye_f32[0:n, 0:n],
        start=True, stop=True), inc=("pe", 1))
    k.wait("vector", "pe", mm)
    r = k.do("vector", lambda eng: eng.tensor_copy(dst_sb[:, 0:n],
                                                   pscale[:, 0:n]))
    if scale_after is not None:
        r = k.do("vector", lambda eng: eng.tensor_scalar_mul(
            dst_sb[:, 0:n], dst_sb[:, 0:n], scale_after))
    ar.free_to(m0)
    k.barrier()
    return r



def emit_rope_tables(cm, fr_ap, C2, S2, TK=4096):
    nc, k, ar = cm.nc, cm.k, cm.ar
    m0 = ar.mark()
    th_tm = [ar.sb(f"th_tm{j}", [128, 32], F32)[0] for j in range(2)]
    th, _ = ar.sb("th_fm", [32, TK], F32)
    s2, _ = ar.sb("th_s2", [32, TK], F32)
    c2, _ = ar.sb("th_c2", [32, TK], F32)
    pth, _ = ar.ps("th_ps", [32, 512], F32)
    pbc, _ = ar.ps("th_pbc", [128, 512], F32)

    k.wait("tensor", "c0", cm.consts_ready)
    lds, mms, grp_ev = {}, {}, {}
    for t in range(TK // 128):
        par = t % 2
        if t >= 2:
            k.wait("sync", "pe", mms[t - 2])
        lds[t] = k.dma("sync", lambda eng, t=t, par=par: eng.dma_start(
            out=th_tm[par][:, :], in_=fr_ap[t * 128:(t + 1) * 128, :]),
            f"thl{par}")
        k.wait("tensor", f"thl{par}", lds[t])
        if t % 4 == 0 and t >= 4:
            k.wait("tensor", "ve", grp_ev[t // 4 - 1])
        mms[t] = k.do("tensor", lambda eng, t=t, par=par: eng.matmul(
            pth[0:32, (t % 4) * 128:(t % 4 + 1) * 128],
            th_tm[par][:, :], cm.c_eye_f32[:, :], start=True, stop=True),
            inc=("pe", 1))
        if t % 4 == 3:
            k.wait("vector", "pe", mms[t])
            grp_ev[t // 4] = k.do("vector", lambda eng, t=t: eng.tensor_copy(
                th[:, (t - 3) * 128:(t + 1) * 128], pth[0:32, :]))
    k.wait("vector", "pe", k.val("pe"))
    k.do("vector", lambda eng: eng.tensor_scalar(
        th[:, :], th[:, :], 0.5, -PI / 2, ALU.mult, ALU.add))
    k.wait("scalar", "ve", k.val("ve"))
    k.do("scalar", lambda eng: eng.activation(s2[:, :], th[:, :], AF.Sin))
    k.do("vector", lambda eng: eng.tensor_scalar(
        c2[:, :], th[:, :], -1.0, PI / 2, ALU.mult, ALU.add))
    k.wait("scalar", "ve", k.val("ve"))
    a1 = k.do("scalar", lambda eng: eng.activation(c2[:, :], c2[:, :], AF.Sin))
    k.wait("vector", "ac", a1)
    k.do("vector", lambda eng: eng.scalar_tensor_tensor(
        c2[:, :], s2[:, :], -2.0, c2[:, :], ALU.mult, ALU.mult))
    k.do("vector", lambda eng: eng.tensor_mul(s2[:, :], s2[:, :], s2[:, :]))
    vv = k.do("vector", lambda eng: eng.tensor_scalar(
        s2[:, :], s2[:, :], 2.0, -1.0, ALU.mult, ALU.add))
    k.wait("tensor", "ve", vv)
    for nb in range(TK // 512):
        m1 = k.do("tensor", lambda eng, nb=nb: eng.matmul(
            pbc[:, :], cm.c_dup128[:, :], s2[:, nb * 512:(nb + 1) * 512],
            start=True, stop=True), inc=("pe", 1))
        k.wait("vector", "pe", m1)
        cv = k.do("vector", lambda eng, nb=nb: eng.tensor_copy(
            C2[:, nb * 512:(nb + 1) * 512], pbc[:, :]))
        k.wait("tensor", "ve", cv)
        m2 = k.do("tensor", lambda eng, nb=nb: eng.matmul(
            pbc[:, :], cm.c_dups128[:, :], c2[:, nb * 512:(nb + 1) * 512],
            start=True, stop=True), inc=("pe", 1))
        k.wait("vector", "pe", m2)
        sv = k.do("vector", lambda eng, nb=nb: eng.tensor_copy(
            S2[:, nb * 512:(nb + 1) * 512], pbc[:, :]))
        k.wait("tensor", "ve", sv)
    ar.free_to(m0)


def emit_rope_apply(cm, io_bf, psw, t1, t2, C2s, S2s, W, out_ap):
    k = cm.k
    for c0 in range(0, W, 512):
        mm = k.do("tensor", lambda eng, c0=c0: eng.matmul(
            psw[:, c0:c0 + 512], cm.c_p128[:, :], io_bf[:, c0:c0 + 512],
            start=True, stop=True), inc=("pe", 1))
    k.do("vector", lambda eng: eng.tensor_tensor(
        t1[:, 0:W], io_bf[:, 0:W], C2s, ALU.mult))
    k.wait("vector", "pe", mm)
    k.do("vector", lambda eng: eng.tensor_tensor(
        t2[:, 0:W], psw[:, 0:W], S2s, ALU.mult))
    return k.do("vector", lambda eng: eng.tensor_tensor(
        out_ap, t1[:, 0:W], t2[:, 0:W], ALU.add))



def emit_qproj(cm, ln2_own, wq_ap, C2, S2, qn_sb, qr_sb):
    nc, k, ar = cm.nc, cm.k, cm.ar
    T = 1024
    m0 = ar.mark()
    wqb = [ar.sb(f"qp_w{j}", [128, 16 * 384], BF16)[0] for j in range(2)]
    qrb, _ = ar.sb("qp_qrb", [128, T], BF16)
    t1, _ = ar.sb("qp_t1", [128, T], F32)
    t2, _ = ar.sb("qp_t2", [128, T], F32)
    pqa, _ = ar.ps("qp_pa", [128, T], F32)
    pqb, _ = ar.ps("qp_pb", [128, T], F32)
    pqr, _ = ar.ps("qp_pr", [128, T], F32)
    psw, _ = ar.ps("qp_psw", [128, T], F32)

    wld, pair_done = {}, {}

    def load_wq(p):
        par = p % 2
        if p >= 2:
            k.wait("gpsimd", "pe", pair_done[p - 2])
        wld[p] = k.dma("gpsimd", lambda eng, p=p, par=par: eng.dma_start(
            out=wqb[par][:, :].rearrange("p (a f) -> p a f", a=16),
            in_=wq_ap[:, p * 384:(p + 1) * 384].rearrange(
                "(a p) c -> p a c", p=128)), f"qw{par}")

    load_wq(0)
    load_wq(1)
    for p in range(8):
        par = p % 2
        h = 2 * p
        k.wait("tensor", f"qw{par}", wld[p])
        k.wait("tensor", "ve", k.val("ve"))
        for kc in range(16):
            for half in range(2):
                c0 = half * 512
                k.do("tensor", lambda eng, kc=kc, c0=c0, par=par: eng.matmul(
                    pqa[:, c0:c0 + 512],
                    wqb[par][:, kc * 384:kc * 384 + 128],
                    ln2_own[:, kc * T + c0:kc * T + c0 + 512],
                    start=(kc == 0), stop=(kc == 15)))
                k.do("tensor", lambda eng, kc=kc, c0=c0, par=par: eng.matmul(
                    pqb[:, c0:c0 + 512],
                    wqb[par][:, kc * 384 + 192:kc * 384 + 320],
                    ln2_own[:, kc * T + c0:kc * T + c0 + 512],
                    start=(kc == 0), stop=(kc == 15)))
                k.do("tensor", lambda eng, kc=kc, c0=c0, par=par: eng.matmul(
                    pqr[0:64, c0:c0 + 512],
                    wqb[par][:, kc * 384 + 128:kc * 384 + 192],
                    ln2_own[:, kc * T + c0:kc * T + c0 + 512],
                    start=(kc == 0), stop=(kc == 15)))
                k.do("tensor", lambda eng, kc=kc, c0=c0, par=par: eng.matmul(
                    pqr[64:128, c0:c0 + 512],
                    wqb[par][:, kc * 384 + 320:kc * 384 + 384],
                    ln2_own[:, kc * T + c0:kc * T + c0 + 512],
                    start=(kc == 0), stop=(kc == 15)),
                    inc=("pe", 1) if (kc == 15 and half == 1) else None)
        mm = k.val("pe")
        k.wait("scalar", "pe", mm)
        k.do("scalar", lambda eng, h=h, par=par: eng.activation(
            qn_sb[:, h * T:(h + 1) * T], pqa[:, :], AF.Copy))
        k.do("scalar", lambda eng, h=h, par=par: eng.activation(
            qn_sb[:, (h + 1) * T:(h + 2) * T], pqb[:, :], AF.Copy))
        aa = k.do("scalar", lambda eng, par=par: eng.activation(
            qrb[:, :], pqr[:, :], AF.Copy))
        k.wait("tensor", "ac", aa)
        k.wait("vector", "ac", aa)
        emit_rope_apply(cm, qrb, psw, t1, t2,
                        C2[:, 0:T], S2[:, 0:T], T,
                        qr_sb[:, p * T:(p + 1) * T])
        pair_done[p] = k.val("pe")
        if p < 6:
            load_wq(p + 2)
        k.wait("tensor", "ac", aa)
        k.wait("tensor", "ve", k.val("ve"))
    ar.free_to(m0)
    k.barrier()



def emit_kv(cm, ln2d, wkva_ap, gcol, bcol, C2, S2, kvn_sb, krope_sb):
    nc, k, ar = cm.nc, cm.k, cm.ar
    TK = 4096
    m0 = ar.mark()
    kvc = kvn_sb
    m_kva = ar.mark()
    wka = [ar.sb(f"kv_w{j}", [128, 576], BF16)[0] for j in range(2)]
    blk = [ar.sb(f"kv_b{j}", [128, 512], BF16)[0] for j in range(2)]
    pm = [ar.ps(f"kv_p{j}", [128, 512], F32)[0] for j in range(5)]

    ev_cnt = {}
    for nb in range(8):
        lds = {}
        for kc in range(16):
            par = kc % 2
            if kc >= 2:
                k.wait("sync", "pe", lds[kc - 2][1])
                k.wait("gpsimd", "pe", lds[kc - 2][1])
            ld = k.dma("sync", lambda eng, kc=kc, nb=nb, par=par:
                       eng.dma_start(out=blk[par][:, :],
                                     in_=ln2d[kc * 128:(kc + 1) * 128,
                                              nb * 512:(nb + 1) * 512]),
                       f"kvb{par}")
            wl = k.dma("gpsimd", lambda eng, kc=kc, par=par: eng.dma_start(
                out=wka[par][:, :],
                in_=wkva_ap[kc * 128:(kc + 1) * 128, :]), f"kvw{par}")
            k.wait("tensor", f"kvb{par}", ld)
            k.wait("tensor", f"kvw{par}", wl)
            if nb > 0 and kc == 0:
                k.wait("tensor", "ac", ev_cnt[nb - 1])
            for m in range(5):
                kk = 128 if m < 4 else 64
                out_ap = pm[m][:, :] if m < 4 else pm[m][0:64, :]
                k.do("tensor", lambda eng, m=m, kc=kc, par=par, kk=kk,
                     out_ap=out_ap: eng.matmul(
                         out_ap,
                         wka[par][:, m * 128:m * 128 + kk],
                         blk[par][:, :],
                         start=(kc == 0), stop=(kc == 15)),
                     inc=("pe", 1) if (m == 4 and kc == 15) else None)
            lds[kc] = (ld, k.val("pe"))
        mmv = k.val("pe")
        k.wait("scalar", "pe", mmv)
        for m in range(4):
            k.do("scalar", lambda eng, m=m, nb=nb: eng.activation(
                kvc[:, m * TK + nb * 512:m * TK + (nb + 1) * 512],
                pm[m][:, :], AF.Copy))
        ev = k.do("scalar", lambda eng, nb=nb: eng.activation(
            krope_sb[0:64, nb * 512:(nb + 1) * 512], pm[4][0:64, :],
            AF.Copy))
        ev_cnt[nb] = ev
    kva_done_ac = k.val("ac")
    ar.free_to(m_kva)

    xq = [ar.sb(f"kvl_q{j}", [128, 512], BF16)[0] for j in range(2)]
    rb, _ = ar.sb("kvl_rows", [1, 2 * 512], F32)
    ab, _ = ar.sb("kvl_ab", [128, 512], F32)
    bb, _ = ar.sb("kvl_bb", [128, 512], F32)
    t1, _ = ar.sb("kvl_t1", [128, 512], F32)
    prow, _ = ar.ps("kvl_prow", [1, 2 * 512], F32)
    pb1, _ = ar.ps("kvl_pb1", [128, 512], F32)
    pb2, _ = ar.ps("kvl_pb2", [128, 512], F32)
    k.barrier()
    RK = float(RANK)
    bcast_ve = {}
    for nb in range(8):
        sqa = {}
        for kc in range(4):
            par = kc % 2
            if kc >= 2:
                k.wait("scalar", "pe", sqa[kc - 2])
            sq = k.do("scalar", lambda eng, kc=kc, nb=nb, par=par:
                      eng.activation(xq[par][:, :],
                                     kvc[:, kc * TK + nb * 512:
                                         kc * TK + (nb + 1) * 512],
                                     AF.Square))
            k.wait("tensor", "ac", sq)
            if nb > 0 and kc == 0:
                k.wait("tensor", "ve", bcast_ve[nb - 1])
            k.do("tensor", lambda eng, kc=kc, nb=nb, par=par: eng.matmul(
                prow[:, 0:512], cm.c_ones_col[:, :],
                kvc[:, kc * TK + nb * 512:kc * TK + (nb + 1) * 512],
                start=(kc == 0), stop=(kc == 3)))
            mmk = k.do("tensor", lambda eng, kc=kc, par=par: eng.matmul(
                prow[:, 512:1024], cm.c_ones_col[:, :], xq[par][:, :],
                start=(kc == 0), stop=(kc == 3)),
                inc=("pe", 1))
            sqa[kc] = mmk
        k.wait("vector", "pe", k.val("pe"))
        k.do("vector", lambda eng: eng.tensor_scalar_mul(
            rb[:, 0:512], prow[:, 0:512], 1.0 / RK))
        k.do("vector", lambda eng: eng.tensor_scalar_mul(
            rb[:, 512:1024], prow[:, 512:1024], 1.0 / RK))
        k.do("vector", lambda eng: eng.scalar_tensor_tensor(
            ab[0:1, 0:512], rb[:, 0:512], -1.0, rb[:, 0:512],
            ALU.mult, ALU.mult))
        k.do("vector", lambda eng: eng.tensor_add(
            rb[:, 512:1024], rb[:, 512:1024], ab[0:1, 0:512]))
        vq = k.do("vector", lambda eng: eng.tensor_scalar_add(
            rb[:, 512:1024], rb[:, 512:1024], EPS))
        k.wait("scalar", "ve", vq)
        k.do("scalar", lambda eng: eng.activation(
            rb[:, 512:1024], rb[:, 512:1024], AF.Ln))
        sdq = k.do("scalar", lambda eng: eng.activation(
            rb[:, 512:1024], rb[:, 512:1024], AF.Exp, scale=-0.5))
        k.wait("vector", "ac", sdq)
        vr = k.do("vector", lambda eng: eng.scalar_tensor_tensor(
            ab[0:1, 0:512], rb[:, 0:512], -1.0, rb[:, 512:1024],
            ALU.mult, ALU.mult))
        k.wait("tensor", "ve", vr)
        k.do("tensor", lambda eng: eng.matmul(
            pb1[:, :], cm.c_ones_row[:, :], rb[:, 512:1024],
            start=True, stop=True))
        bmm = k.do("tensor", lambda eng: eng.matmul(
            pb2[:, :], cm.c_ones_row[:, :], ab[0:1, 0:512],
            start=True, stop=True), inc=("pe", 1))
        k.wait("vector", "pe", bmm)
        k.do("vector", lambda eng: eng.tensor_copy(ab[:, :], pb1[:, :]))
        bcv = k.do("vector", lambda eng: eng.tensor_copy(bb[:, :], pb2[:, :]))
        for kc in range(4):
            k.do("vector", lambda eng, kc=kc, nb=nb: eng.tensor_tensor(
                t1[:, :], kvc[:, kc * TK + nb * 512:kc * TK + (nb + 1) * 512],
                ab[:, :], ALU.mult))
            k.do("vector", lambda eng: eng.tensor_tensor(
                t1[:, :], t1[:, :], bb[:, :], ALU.add))
            k.do("vector", lambda eng, kc=kc, nb=nb: eng.tensor_scalar(
                kvn_sb[:, kc * TK + nb * 512:kc * TK + (nb + 1) * 512],
                t1[:, :], gcol[:, kc:kc + 1], bcol[:, kc:kc + 1],
                ALU.mult, ALU.add))
        bcast_ve[nb] = k.val("ve")
    k.barrier()
    psw, _ = ar.ps("kv_psw", [128, 512], F32)
    t2b, _ = ar.sb("kv_t2", [64, 512], F32)
    t1b, _ = ar.sb("kv_t1b", [64, 512], F32)
    for nb in range(8):
        c0 = nb * 512
        mm = k.do("tensor", lambda eng, c0=c0: eng.matmul(
            psw[0:64, :], cm.c_p128[0:64, 0:64],
            krope_sb[0:64, c0:c0 + 512], start=True, stop=True),
            inc=("pe", 1))
        k.do("vector", lambda eng, c0=c0: eng.tensor_tensor(
            t1b[:, :], krope_sb[0:64, c0:c0 + 512], C2[0:64, c0:c0 + 512],
            ALU.mult))
        k.wait("vector", "pe", mm)
        k.do("vector", lambda eng, c0=c0: eng.tensor_tensor(
            t2b[:, :], psw[0:64, :], S2[0:64, c0:c0 + 512], ALU.mult))
        vv = k.do("vector", lambda eng, c0=c0: eng.tensor_tensor(
            krope_sb[0:64, c0:c0 + 512], t1b[:, :], t2b[:, :], ALU.add))
        k.do("vector", lambda eng, c0=c0: eng.tensor_copy(
            krope_sb[64:128, c0:c0 + 512], krope_sb[0:64, c0:c0 + 512]))
        k.wait("tensor", "ve", vv)
    ar.free_to(m0)
    k.barrier()



def emit_attention(cm, wkvb_ap, kvn_sb, krope_sb, qn_sb, qr_sb, attn_sb):
    nc, k, ar = cm.nc, cm.k, cm.ar
    T = 1024
    TK = 4096
    m0 = ar.mark()
    wkb = [ar.sb(f"at_wb{j}", [128, 4 * 256], BF16)[0] for j in range(2)]
    kno, _ = ar.sb("at_kn", [128, TK], BF16)
    vfm, _ = ar.sb("at_vf", [128, TK], BF16)
    vtm, _ = ar.sb("at_vt", [128, TK], BF16)
    p_sb = [ar.sb(f"at_p{j}", [128, T], BF16)[0] for j in range(2)]
    rrow, _ = ar.sb("at_rr", [1, T], F32)
    rbc, _ = ar.sb("at_rbc", [128, T], F32)
    psS = [ar.ps(f"at_s{j}", [128, T], F32)[0] for j in range(2)]
    psO, _ = ar.ps("at_o", [128, T], F32)
    psR, _ = ar.ps("at_r", [1, T], F32)

    k.wait("tensor", "ve", cm.negc_ready)
    wld = {}

    def load_wb(h):
        par = h % 2
        if h >= 2:
            k.wait("gpsimd", "ac", kvb_evac[h - 2])
        wld[h] = k.dma("gpsimd", lambda eng, h=h, par=par: eng.dma_start(
            out=wkb[par][:, :].rearrange("p (a f) -> p a f", a=4),
            in_=wkvb_ap[:, h * 256:(h + 1) * 256].rearrange(
                "(a p) c -> p a c", p=128)), f"awb{par}")

    kvb_evac = {}
    load_wb(0)
    load_wb(1)
    for h in range(16):
        par = h % 2
        k.wait("tensor", f"awb{par}", wld[h])
        for which in range(2):
            dst = kno if which == 0 else vfm
            for nb in range(8):
                pp = psS[nb % 2]
                for j in range(4):
                    k.do("tensor", lambda eng, j=j, nb=nb, par=par,
                         which=which, pp=pp: eng.matmul(
                             pp[:, 0:512],
                             wkb[par][:, j * 256 + which * 128:
                                      j * 256 + which * 128 + 128],
                             kvn_sb[:, j * TK + nb * 512:
                                    j * TK + (nb + 1) * 512],
                             start=(j == 0), stop=(j == 3)),
                         inc=("pe", 1) if j == 3 else None)
                mmv = k.val("pe")
                k.wait("scalar", "pe", mmv)
                ae = k.do("scalar", lambda eng, nb=nb, dst=dst, pp=pp:
                          eng.activation(dst[:, nb * 512:(nb + 1) * 512],
                                         pp[:, 0:512], AF.Copy))
                k.wait("tensor", "ac", ae)
        kvb_evac[h] = k.val("ac")
        if h < 14:
            load_wb(h + 2)
        for g4 in range(8):
            pp = psS[g4 % 2]
            for jj in range(4):
                sc = g4 * 4 + jj
                k.do("tensor", lambda eng, sc=sc, jj=jj, pp=pp: eng.matmul(
                    pp[:, jj * 128:(jj + 1) * 128],
                    vfm[:, sc * 128:(sc + 1) * 128],
                    cm.c_eye_bf[:, :], start=True, stop=True),
                    inc=("pe", 1) if jj == 3 else None)
            mmv = k.val("pe")
            k.wait("scalar", "pe", mmv)
            ve = k.do("scalar", lambda eng, g4=g4, pp=pp: eng.activation(
                vtm[:, g4 * 512:(g4 + 1) * 512], pp[:, 0:512], AF.Copy))
            k.wait("tensor", "ac", ve)
        exp_ac = {}
        pv_pe = {}
        for sc in range(32):
            par2 = sc % 2
            pp = psS[par2]
            if sc >= 2:
                k.wait("tensor", "ac", exp_ac[sc - 2])
            for half in range(2):
                c0 = half * 512
                k.do("tensor", lambda eng, sc=sc, c0=c0, h=h, pp=pp:
                     eng.matmul(pp[:, c0:c0 + 512],
                                kno[:, sc * 128:(sc + 1) * 128],
                                qn_sb[:, h * T + c0:h * T + c0 + 512],
                                start=True, stop=False))
                r0 = 64 * (h % 2)
                qrs = qr_sb[r0:r0 + 64,
                            (h // 2) * T + c0:(h // 2) * T + c0 + 512]
                krs = krope_sb[r0:r0 + 64, sc * 128:(sc + 1) * 128]
                k.do("tensor", lambda eng, qrs=qrs, krs=krs, c0=c0, pp=pp:
                     eng.matmul(pp[:, c0:c0 + 512], krs,
                                qrs, start=False, stop=True),
                     inc=("pe", 1) if c0 else None)
            mmv = k.val("pe")
            k.wait("scalar", "pe", mmv)
            if sc >= 2:
                k.wait("scalar", "pe", pv_pe[sc - 2])
            ea = k.do("scalar", lambda eng, sc=sc, par2=par2, pp=pp:
                      eng.activation(p_sb[par2][:, :], pp[:, :], AF.Exp,
                                     bias=cm.c_negC[:, 0:1], scale=SCALE))
            exp_ac[sc] = ea
            k.wait("tensor", "ac", ea)
            for half in range(2):
                c0 = half * 512
                k.do("tensor", lambda eng, sc=sc, c0=c0, par2=par2:
                     eng.matmul(psR[:, c0:c0 + 512], cm.c_ones_col[:, :],
                                p_sb[par2][:, c0:c0 + 512],
                                start=(sc == 0), stop=(sc == 31)))
                k.do("tensor", lambda eng, sc=sc, c0=c0, par2=par2:
                     eng.matmul(psO[:, c0:c0 + 512],
                                vtm[:, sc * 128:(sc + 1) * 128],
                                p_sb[par2][:, c0:c0 + 512],
                                start=(sc == 0), stop=(sc == 31)),
                     inc=("pe", 1) if c0 else None)
            pv_pe[sc] = k.val("pe")
        k.wait("vector", "pe", k.val("pe"))
        vv = k.do("vector", lambda eng: eng.reciprocal(rrow[:, :], psR[:, :]))
        k.wait("tensor", "ve", vv)
        bmm = k.do("tensor", lambda eng: eng.matmul(
            psS[0][:, 0:512], cm.c_ones_row[:, :], rrow[:, 0:512],
            start=True, stop=True), inc=("pe", 1))
        bmm2 = k.do("tensor", lambda eng: eng.matmul(
            psS[1][:, 0:512], cm.c_ones_row[:, :], rrow[:, 512:1024],
            start=True, stop=True), inc=("pe", 1))
        k.wait("vector", "pe", bmm2)
        k.do("vector", lambda eng: eng.tensor_copy(rbc[:, 0:512],
                                                   psS[0][:, 0:512]))
        k.do("vector", lambda eng: eng.tensor_copy(rbc[:, 512:1024],
                                                   psS[1][:, 0:512]))
        av = k.do("vector", lambda eng, h=h: eng.tensor_tensor(
            attn_sb[:, h * T:(h + 1) * T], psO[:, :], rbc[:, :], ALU.mult))
        k.wait("tensor", "ve", av)
    ar.free_to(m0)
    k.barrier()



RG = [[0, 1, 2, 3], [4, 5, 6, 7]]
AGR = NH * (DN + DV) + DR


def emit_kv_own(cm, ln_own, wkva_ap, gcol, bcol, C2, S2, kvn_sb, krope_sb):
    nc, k, ar = cm.nc, cm.k, cm.ar
    TK = 1024
    m0 = ar.mark()
    kvc = kvn_sb
    m_kva = ar.mark()
    wka = [ar.sb(f"kv_w{j}", [128, 576], BF16)[0] for j in range(2)]
    pm = [ar.ps(f"kv_p{j}", [128, 512], F32)[0] for j in range(5)]

    ev_cnt = {}
    mmc = {}
    for nb in range(2):
        for kc in range(16):
            j = nb * 16 + kc
            par = j % 2
            if j >= 2:
                k.wait("gpsimd", "pe", mmc[j - 2])
            wl = k.dma("gpsimd", lambda eng, kc=kc, par=par: eng.dma_start(
                out=wka[par][:, :],
                in_=wkva_ap[kc * 128:(kc + 1) * 128, :]), f"kvw{par}")
            k.wait("tensor", f"kvw{par}", wl)
            if nb > 0 and kc == 0:
                k.wait("tensor", "ac", ev_cnt[nb - 1])
            for m in range(5):
                kk = 128 if m < 4 else 64
                out_ap = pm[m][:, :] if m < 4 else pm[m][0:64, :]
                k.do("tensor", lambda eng, m=m, kc=kc, nb=nb, par=par, kk=kk,
                     out_ap=out_ap: eng.matmul(
                         out_ap,
                         wka[par][:, m * 128:m * 128 + kk],
                         ln_own[:, kc * TK + nb * 512:kc * TK + (nb + 1) * 512],
                         start=(kc == 0), stop=(kc == 15)),
                     inc=("pe", 1) if m == 4 else None)
            mmc[j] = k.val("pe")
        mmv = k.val("pe")
        k.wait("scalar", "pe", mmv)
        for m in range(4):
            k.do("scalar", lambda eng, m=m, nb=nb: eng.activation(
                kvc[:, m * TK + nb * 512:m * TK + (nb + 1) * 512],
                pm[m][:, :], AF.Copy))
        ev = k.do("scalar", lambda eng, nb=nb: eng.activation(
            krope_sb[0:64, nb * 512:(nb + 1) * 512], pm[4][0:64, :],
            AF.Copy), inc=("ac", 1))
        ev_cnt[nb] = ev
    ar.free_to(m_kva)

    xq = [ar.sb(f"kvl_q{j}", [128, 512], BF16)[0] for j in range(2)]
    rb, _ = ar.sb("kvl_rows", [1, 2 * 512], F32)
    ab, _ = ar.sb("kvl_ab", [128, 512], F32)
    bb, _ = ar.sb("kvl_bb", [128, 512], F32)
    t1, _ = ar.sb("kvl_t1", [128, 512], F32)
    prow, _ = ar.ps("kvl_prow", [1, 2 * 512], F32)
    pb1, _ = ar.ps("kvl_pb1", [128, 512], F32)
    pb2, _ = ar.ps("kvl_pb2", [128, 512], F32)
    k.barrier()
    RK = float(RANK)
    bcast_ve = {}
    for nb in range(2):
        sqa = {}
        for kc in range(4):
            par = kc % 2
            if kc >= 2:
                k.wait("scalar", "pe", sqa[kc - 2])
            sq = k.do("scalar", lambda eng, kc=kc, nb=nb, par=par:
                      eng.activation(xq[par][:, :],
                                     kvc[:, kc * TK + nb * 512:
                                         kc * TK + (nb + 1) * 512],
                                     AF.Square))
            k.wait("tensor", "ac", sq)
            if nb > 0 and kc == 0:
                k.wait("tensor", "ve", bcast_ve[nb - 1])
            k.do("tensor", lambda eng, kc=kc, nb=nb, par=par: eng.matmul(
                prow[:, 0:512], cm.c_ones_col[:, :],
                kvc[:, kc * TK + nb * 512:kc * TK + (nb + 1) * 512],
                start=(kc == 0), stop=(kc == 3)))
            mmk = k.do("tensor", lambda eng, kc=kc, par=par: eng.matmul(
                prow[:, 512:1024], cm.c_ones_col[:, :], xq[par][:, :],
                start=(kc == 0), stop=(kc == 3)),
                inc=("pe", 1))
            sqa[kc] = mmk
        k.wait("vector", "pe", k.val("pe"))
        k.do("vector", lambda eng: eng.tensor_scalar_mul(
            rb[:, 0:512], prow[:, 0:512], 1.0 / RK))
        k.do("vector", lambda eng: eng.tensor_scalar_mul(
            rb[:, 512:1024], prow[:, 512:1024], 1.0 / RK))
        k.do("vector", lambda eng: eng.scalar_tensor_tensor(
            ab[0:1, 0:512], rb[:, 0:512], -1.0, rb[:, 0:512],
            ALU.mult, ALU.mult))
        k.do("vector", lambda eng: eng.tensor_add(
            rb[:, 512:1024], rb[:, 512:1024], ab[0:1, 0:512]))
        vq = k.do("vector", lambda eng: eng.tensor_scalar_add(
            rb[:, 512:1024], rb[:, 512:1024], EPS))
        k.wait("scalar", "ve", vq)
        k.do("scalar", lambda eng: eng.activation(
            rb[:, 512:1024], rb[:, 512:1024], AF.Ln))
        sdq = k.do("scalar", lambda eng: eng.activation(
            rb[:, 512:1024], rb[:, 512:1024], AF.Exp, scale=-0.5))
        k.wait("vector", "ac", sdq)
        vr = k.do("vector", lambda eng: eng.scalar_tensor_tensor(
            ab[0:1, 0:512], rb[:, 0:512], -1.0, rb[:, 512:1024],
            ALU.mult, ALU.mult))
        k.wait("tensor", "ve", vr)
        k.do("tensor", lambda eng: eng.matmul(
            pb1[:, :], cm.c_ones_row[:, :], rb[:, 512:1024],
            start=True, stop=True))
        bmm = k.do("tensor", lambda eng: eng.matmul(
            pb2[:, :], cm.c_ones_row[:, :], ab[0:1, 0:512],
            start=True, stop=True), inc=("pe", 1))
        k.wait("vector", "pe", bmm)
        k.do("vector", lambda eng: eng.tensor_copy(ab[:, :], pb1[:, :]))
        k.do("vector", lambda eng: eng.tensor_copy(bb[:, :], pb2[:, :]))
        for kc in range(4):
            k.do("vector", lambda eng, kc=kc, nb=nb: eng.tensor_tensor(
                t1[:, :], kvc[:, kc * TK + nb * 512:kc * TK + (nb + 1) * 512],
                ab[:, :], ALU.mult))
            k.do("vector", lambda eng: eng.tensor_tensor(
                t1[:, :], t1[:, :], bb[:, :], ALU.add))
            k.do("vector", lambda eng, kc=kc, nb=nb: eng.tensor_scalar(
                kvn_sb[:, kc * TK + nb * 512:kc * TK + (nb + 1) * 512],
                t1[:, :], gcol[:, kc:kc + 1], bcol[:, kc:kc + 1],
                ALU.mult, ALU.add))
        bcast_ve[nb] = k.val("ve")
    k.barrier()
    psw, _ = ar.ps("kv_psw", [128, 512], F32)
    t2b, _ = ar.sb("kv_t2", [64, 512], F32)
    t1b, _ = ar.sb("kv_t1b", [64, 512], F32)
    for nb in range(2):
        c0 = nb * 512
        mm = k.do("tensor", lambda eng, c0=c0: eng.matmul(
            psw[0:64, :], cm.c_p128[0:64, 0:64],
            krope_sb[0:64, c0:c0 + 512], start=True, stop=True),
            inc=("pe", 1))
        k.do("vector", lambda eng, c0=c0: eng.tensor_tensor(
            t1b[:, :], krope_sb[0:64, c0:c0 + 512], C2[0:64, c0:c0 + 512],
            ALU.mult))
        k.wait("vector", "pe", mm)
        k.do("vector", lambda eng, c0=c0: eng.tensor_tensor(
            t2b[:, :], psw[0:64, :], S2[0:64, c0:c0 + 512], ALU.mult))
        vv = k.do("vector", lambda eng, c0=c0: eng.tensor_tensor(
            krope_sb[0:64, c0:c0 + 512], t1b[:, :], t2b[:, :], ALU.add))
        k.wait("tensor", "ve", vv)
    ar.free_to(m0)
    k.barrier()


def emit_kvb_ag(cm, wkvb_ap, kvn_sb, krope_sb, agin):
    nc, k, ar = cm.nc, cm.k, cm.ar
    T = 1024
    m0 = ar.mark()
    wb, _ = ar.sb("kb_w", [128, 4 * 4096], BF16)
    st = [ar.sb(f"kb_s{j}", [128, T], BF16)[0] for j in range(2)]
    pb = [ar.ps(f"kb_p{j}", [128, T], F32)[0] for j in range(2)]
    wl = k.dma("gpsimd", lambda eng: eng.dma_start(
        out=wb[:, :].rearrange("p (a f) -> p a f", a=4),
        in_=wkvb_ap.rearrange("(a p) c -> p a c", p=128)), "kbw")
    k.wait("tensor", "kbw", wl)
    ev_d, st_d = {}, {}
    for f in range(32):
        par = f % 2
        if f >= 2:
            k.wait("tensor", "ac", ev_d[f - 2])
        for r in range(4):
            for half in range(2):
                c0 = half * 512
                k.do("tensor", lambda eng, f=f, r=r, par=par, c0=c0:
                     eng.matmul(
                         pb[par][:, c0:c0 + 512],
                         wb[:, r * 4096 + f * 128:r * 4096 + (f + 1) * 128],
                         kvn_sb[:, r * T + c0:r * T + c0 + 512],
                         start=(r == 0), stop=(r == 3)),
                     inc=("pe", 1) if (r == 3 and half == 1) else None)
        mm = k.val("pe")
        k.wait("scalar", "pe", mm)
        if f >= 2:
            k.wait("scalar", f"agst{par}", st_d[f - 2])
        ev_d[f] = k.do("scalar", lambda eng, par=par: eng.activation(
            st[par][:, :], pb[par][:, :], AF.Copy), inc=("ac", 1))
        k.wait("sync", "ac", ev_d[f])
        st_d[f] = k.dma("sync", lambda eng, f=f, par=par: eng.dma_start(
            out=agin[f * 128:(f + 1) * 128, :], in_=st[par][:, :]),
            f"agst{par}")
    k.wait("sync", "ve", k.val("ve"))
    k.dma("sync", lambda eng: eng.dma_start(
        out=agin[4096:4160, :], in_=krope_sb[0:64, :]), "agstk")
    ar.free_to(m0)


def emit_ag_trigger(cm, agin_ap, agout_ap):
    k = cm.k
    if "cc" not in k.sems:
        k.sem("cc")
    h = k.sems["cc"]
    k.wait("gpsimd", "agst0", k.val("agst0"))
    k.wait("gpsimd", "agst1", k.val("agst1"))
    k.wait("gpsimd", "agstk", k.val("agstk"))
    k.q["gpsimd"].append(lambda eng: eng.collective_compute(
        "AllGather", ALU.bypass, replica_groups=RG,
        ins=[agin_ap.opt()], outs=[agout_ap.opt()]).then_inc(h, 1))
    k.cnt["cc"] = 1


def emit_attention_ag(cm, agout, qn_sb, qr_sb, attn_sb):
    nc, k, ar = cm.nc, cm.k, cm.ar
    T = 1024
    TK = 4096
    m0 = ar.mark()
    kno = [ar.sb(f"at_kn{j}", [128, TK], BF16)[0] for j in range(2)]
    vfm = [ar.sb(f"at_vf{j}", [128, TK], BF16)[0] for j in range(2)]
    vtm, _ = ar.sb("at_vt", [128, TK], BF16)
    krope_sb, _ = ar.sb("at_kr", [128, TK], BF16)
    p_sb = [ar.sb(f"at_p{j}", [128, T], BF16)[0] for j in range(2)]
    rs_sb, _ = ar.sb("at_rs", [1, 16 * T], F32)
    psS = [ar.ps(f"at_s{j}", [128, T], F32)[0] for j in range(2)]
    psO, _ = ar.ps("at_o", [128, T], F32)
    psR, _ = ar.ps("at_r", [1, T], F32)

    k.wait("tensor", "ve", cm.negc_ready)
    k.wait("sync", "cc", 1)
    k.wait("gpsimd", "cc", 1)
    for r in range(4):
        for half in range(2):
            k.dma("sync", lambda eng, r=r, half=half: eng.dma_start(
                out=krope_sb[half * 64:(half + 1) * 64, r * T:(r + 1) * T],
                in_=agout[r * AGR + 4096:r * AGR + 4160, :]), "akr")
    krl = k.val("akr")
    k.wait("tensor", "akr", krl)

    kld, head_pe = {}, {}

    def load_h(h):
        par = h % 2
        if h >= 2:
            k.wait("sync", "pe", head_pe[h - 2])
            k.wait("gpsimd", "pe", head_pe[h - 2])
        for r in range(4):
            k.dma("sync", lambda eng, h=h, r=r, par=par: eng.dma_start(
                out=kno[par][:, r * T:(r + 1) * T],
                in_=agout[r * AGR + h * 256:r * AGR + h * 256 + 128, :]),
                f"akn{par}")
            k.dma("gpsimd", lambda eng, h=h, r=r, par=par: eng.dma_start(
                out=vfm[par][:, r * T:(r + 1) * T],
                in_=agout[r * AGR + h * 256 + 128:
                          r * AGR + h * 256 + 256, :]), f"avf{par}")
        kld[h] = (k.val(f"akn{par}"), k.val(f"avf{par}"))

    load_h(0)
    load_h(1)
    oev_ac, rsv_ve = {}, {}
    for h in range(16):
        par = h % 2
        k.wait("tensor", f"akn{par}", kld[h][0])
        k.wait("tensor", f"avf{par}", kld[h][1])
        for g4 in range(8):
            pp = psS[g4 % 2]
            for jj in range(4):
                sc = g4 * 4 + jj
                k.do("tensor", lambda eng, sc=sc, jj=jj, pp=pp, par=par:
                     eng.matmul(
                         pp[:, jj * 128:(jj + 1) * 128],
                         vfm[par][:, sc * 128:(sc + 1) * 128],
                         cm.c_eye_bf[:, :], start=True, stop=True),
                     inc=("pe", 1) if jj == 3 else None)
            mmv = k.val("pe")
            k.wait("scalar", "pe", mmv)
            ve = k.do("scalar", lambda eng, g4=g4, pp=pp: eng.activation(
                vtm[:, g4 * 512:(g4 + 1) * 512], pp[:, 0:512], AF.Copy))
            k.wait("tensor", "ac", ve)
        exp_ac = {}
        pv_pe = {}
        if h > 0:
            k.wait("tensor", "ac", oev_ac[h - 1])
            k.wait("tensor", "ve", rsv_ve[h - 1])
        for sc in range(32):
            par2 = sc % 2
            pp = psS[par2]
            if sc >= 2:
                k.wait("tensor", "ac", exp_ac[sc - 2])
            for half in range(2):
                c0 = half * 512
                k.do("tensor", lambda eng, sc=sc, c0=c0, h=h, pp=pp, par=par:
                     eng.matmul(pp[:, c0:c0 + 512],
                                kno[par][:, sc * 128:(sc + 1) * 128],
                                qn_sb[:, h * T + c0:h * T + c0 + 512],
                                start=True, stop=False))
                r0 = 64 * (h % 2)
                qrs = qr_sb[r0:r0 + 64,
                            (h // 2) * T + c0:(h // 2) * T + c0 + 512]
                krs = krope_sb[r0:r0 + 64, sc * 128:(sc + 1) * 128]
                k.do("tensor", lambda eng, qrs=qrs, krs=krs, c0=c0, pp=pp:
                     eng.matmul(pp[:, c0:c0 + 512], krs,
                                qrs, start=False, stop=True),
                     inc=("pe", 1) if c0 else None)
            mmv = k.val("pe")
            k.wait("scalar", "pe", mmv)
            if sc >= 2:
                k.wait("scalar", "pe", pv_pe[sc - 2])
            ea = k.do("scalar", lambda eng, sc=sc, par2=par2, pp=pp:
                      eng.activation(p_sb[par2][:, :], pp[:, :], AF.Exp,
                                     bias=cm.c_negC[:, 0:1], scale=SCALE))
            exp_ac[sc] = ea
            k.wait("tensor", "ac", ea)
            for half in range(2):
                c0 = half * 512
                k.do("tensor", lambda eng, sc=sc, c0=c0, par2=par2:
                     eng.matmul(psR[:, c0:c0 + 512], cm.c_ones_col[:, :],
                                p_sb[par2][:, c0:c0 + 512],
                                start=(sc == 0), stop=(sc == 31)))
                k.do("tensor", lambda eng, sc=sc, c0=c0, par2=par2:
                     eng.matmul(psO[:, c0:c0 + 512],
                                vtm[:, sc * 128:(sc + 1) * 128],
                                p_sb[par2][:, c0:c0 + 512],
                                start=(sc == 0), stop=(sc == 31)),
                     inc=("pe", 1) if c0 else None)
            pv_pe[sc] = k.val("pe")
        head_pe[h] = k.val("pe")
        if h < 14:
            load_h(h + 2)
        k.wait("scalar", "pe", head_pe[h])
        oev_ac[h] = k.do("scalar", lambda eng, h=h: eng.activation(
            attn_sb[:, h * T:(h + 1) * T], psO[:, :], AF.Copy),
            inc=("ac", 1))
        k.wait("vector", "pe", head_pe[h])
        rsv_ve[h] = k.do("vector", lambda eng, h=h: eng.tensor_copy(
            rs_sb[:, h * T:(h + 1) * T], psR[:, :]), inc=("ve", 1))
    rcp = k.do("vector", lambda eng: eng.reciprocal(rs_sb[:, :], rs_sb[:, :]),
               inc=("ve", 1))
    k.wait("tensor", "ve", rcp)
    k.wait("tensor", "ac", oev_ac[15])
    nbm, nmul = {}, {}
    for h in range(16):
        par = h % 2
        if h >= 2:
            k.wait("tensor", "ve", nmul[h - 2])
        for half in range(2):
            c0 = half * 512
            k.do("tensor", lambda eng, h=h, c0=c0, par=par: eng.matmul(
                psS[par][:, c0:c0 + 512], cm.c_ones_row[:, :],
                rs_sb[:, h * T + c0:h * T + c0 + 512],
                start=True, stop=True),
                inc=("pe", 1) if c0 else None)
        nbm[h] = k.val("pe")
        k.wait("vector", "pe", nbm[h])
        for half in range(2):
            c0 = half * 512
            k.do("vector", lambda eng, h=h, c0=c0, par=par: eng.tensor_tensor(
                attn_sb[:, h * T + c0:h * T + c0 + 512],
                attn_sb[:, h * T + c0:h * T + c0 + 512],
                psS[par][:, c0:c0 + 512], ALU.mult),
                inc=("ve", 1) if c0 else None)
        nmul[h] = k.val("ve")
    ar.free_to(m0)
    k.barrier()



def emit_wout(cm, wout_ap, attn_sb, x1fm, x2fm):
    nc, k, ar = cm.nc, cm.k, cm.ar
    T = 1024
    m0 = ar.mark()
    wsb, _ = ar.sb("wo_w", [128, 16 * D], BF16)
    xr = [ar.sb(f"wo_xr{j}", [128, T], F32)[0] for j in range(2)]
    xo = [ar.sb(f"wo_xo{j}", [128, T], F32)[0] for j in range(2)]
    pw = [ar.ps(f"wo_p{j}", [128, T], F32)[0] for j in range(2)]

    wl = k.dma("gpsimd", lambda eng: eng.dma_start(
        out=wsb[:, :].rearrange("p (a f) -> p a f", a=16),
        in_=wout_ap.rearrange("(a p) c -> p a c", p=128)), "wo")
    k.wait("tensor", "wo", wl)

    xl, mmd, evd, std = {}, {}, {}, {}

    def load_x(d):
        par = d % 2
        if d >= 2:
            k.wait("scalar", "ve", evd[d - 2])
        xl[d] = k.dma("scalar", lambda eng, d=d, par=par: eng.dma_start(
            out=xr[par][:, :], in_=x1fm[d * 128:(d + 1) * 128, 0:T]),
            f"wx{par}")

    load_x(0)
    load_x(1)
    for d in range(16):
        par = d % 2
        if d >= 2:
            k.wait("tensor", "ve", evd[d - 2])
        for h in range(16):
            for half in range(2):
                c0 = half * 512
                k.do("tensor", lambda eng, d=d, h=h, c0=c0, par=par:
                     eng.matmul(pw[par][:, c0:c0 + 512],
                                wsb[:, h * D + d * 128:h * D + (d + 1) * 128],
                                attn_sb[:, h * T + c0:h * T + c0 + 512],
                                start=(h == 0), stop=(h == 15)),
                     inc=("pe", 1) if (h == 15 and half == 1) else None)
        mmd[d] = k.val("pe")
        k.wait("vector", "pe", mmd[d])
        k.wait("vector", f"wx{par}", xl[d])
        if d >= 2:
            k.wait("vector", f"wst{par}", std[d - 2])
        evd[d] = k.do("vector", lambda eng, d=d, par=par: eng.tensor_tensor(
            xo[par][:, :], pw[par][:, :], xr[par][:, :], ALU.add))
        k.wait("scalar", "ve", evd[d])
        std[d] = k.dma("scalar", lambda eng, d=d, par=par: eng.dma_start(
            out=x2fm[d * 128:(d + 1) * 128, :], in_=xo[par][:, :]),
            f"wst{par}")
        if d < 14:
            load_x(d + 2)
    res = (k.val("wst0"), k.val("wst1"))
    ar.free_to(m0)
    k.barrier()
    return res



def build_L2(stop_after=None):
    nc = bass.Bass("TRN2", target_bir_lowering=False)
    cm = Common(nc)
    k, ar = cm.k, cm.ar

    inputs = {}
    inputs["x1fm"] = nc.dram_tensor("x1fm", [D, S], F32,
                                    kind="ExternalInput").ap()
    inputs["fr"] = nc.dram_tensor("fr", [S, DR // 2], F32,
                                  kind="ExternalInput").ap()
    inputs["Wq"] = nc.dram_tensor("Wq", [D, NH * DQK], F32,
                                  kind="ExternalInput").ap()
    inputs["Wkv_a"] = nc.dram_tensor("Wkv_a", [D, RANK + DR], F32,
                                     kind="ExternalInput").ap()
    inputs["kv_gamma"] = nc.dram_tensor("kv_gamma", [RANK], F32,
                                        kind="ExternalInput").ap()
    inputs["kv_beta"] = nc.dram_tensor("kv_beta", [RANK], F32,
                                       kind="ExternalInput").ap()
    inputs["Wkv_b"] = nc.dram_tensor("Wkv_b", [RANK, NH * (DN + DV)], F32,
                                     kind="ExternalInput").ap()
    inputs["Wout"] = nc.dram_tensor("Wout", [NH * DV, D], F32,
                                    kind="ExternalInput").ap()
    for nm, shp, dt in (("gate_w", [IF, D], I32), ("up_w", [IF, D], I32),
                        ("down_w", [D, IF], I32), ("gate_scale", [IF], F32),
                        ("up_scale", [IF], F32), ("down_scale", [D], F32)):
        inputs[nm] = nc.dram_tensor(nm, shp, dt, kind="ExternalInput").ap()
    cm.inputs = inputs

    out_tm = nc.dram_tensor("out_tm", [TOWN, D], F32,
                            kind="ExternalOutput").ap()
    ln2d = nc.dram_tensor("ln2d", [D, S], BF16, kind="Internal").ap()
    x2fm = nc.dram_tensor("x2fm", [D, TOWN], F32, kind="Internal").ap()
    x3fm = nc.dram_tensor("x3fm", [D, TOWN], F32, kind="Internal").ap()

    def _truncate():
        k.barrier()
        r = k.dma("gpsimd", lambda eng: eng.dma_start(
            out=out_tm, in_=inputs["x1fm"][0:TOWN, 0:D]), "outd")
        k.wait("sync", "outd", r)
        cm.k.finish()
        return nc

    cm.load_consts(with_ffn_scales=True)
    cm.gcol, _ = ar.sb("gcol", [128, 4], F32)
    cm.bcol, _ = ar.sb("bcol", [128, 4], F32)
    emit_cols(cm, inputs["kv_gamma"], 4, cm.gcol)
    emit_cols(cm, inputs["kv_beta"], 4, cm.bcol)
    if stop_after == "consts":
        return _truncate()
    emit_ffn_weight_prep(cm)
    k.barrier()
    if stop_after == "prep":
        return _truncate()

    m_pre = ar.mark()
    attn_sb, _ = ar.sb("attn_sb", [128, 16 * 1024], BF16)
    m_attn = ar.mark()
    qn_sb, _ = ar.sb("qn_sb", [128, 16 * 1024], BF16)
    qr_sb, _ = ar.sb("qr_sb", [128, 8 * 1024], BF16)
    C2, _ = ar.sb("C2", [128, S], F32)
    S2, _ = ar.sb("S2", [128, S], F32)

    emit_rope_tables(cm, inputs["fr"], C2, S2)
    if stop_after == "theta":
        return _truncate()

    for c in range(1, 4):
        emit_ln_chunk(cm, inputs["x1fm"], c * 1024, None, also_dram=ln2d)
        k.barrier()
    m_ln2 = ar.mark()
    ln2_own, _ = ar.sb("ln2_own", [128, 16 * 1024], BF16)
    emit_ln_chunk(cm, inputs["x1fm"], 0, ln2_own, also_dram=ln2d)
    k.barrier()
    if stop_after == "ln2":
        return _truncate()

    emit_qproj(cm, ln2_own, inputs["Wq"], C2, S2, qn_sb, qr_sb)
    ar.free_to(m_ln2)
    k.barrier()
    if stop_after == "qproj":
        return _truncate()
    kvn_sb, _ = ar.sb("kvn_sb", [128, 4 * S], BF16)
    krope_sb, _ = ar.sb("krope_sb", [128, S], BF16)
    emit_kv(cm, ln2d, inputs["Wkv_a"], cm.gcol, cm.bcol, C2, S2,
            kvn_sb, krope_sb)
    if stop_after == "kv":
        return _truncate()
    emit_attention(cm, inputs["Wkv_b"], kvn_sb, krope_sb, qn_sb, qr_sb,
                   attn_sb)
    ar.free_to(m_attn)
    k.barrier()
    if stop_after == "attn":
        return _truncate()
    r2 = emit_wout(cm, inputs["Wout"], attn_sb, inputs["x1fm"], x2fm)
    ar.free_to(m_pre)
    k.barrier()
    if stop_after == "wout":
        return _truncate()
    if stop_after == "x2":
        k.barrier()
        r = k.dma("gpsimd", lambda eng: eng.dma_start(
            out=out_tm.rearrange("a b -> (a b)"),
            in_=x2fm.rearrange("a b -> (a b)")), "outd")
        k.wait("sync", "outd", r)
        cm.k.finish()
        return nc
    r3 = emit_ffn(cm, x2fm, 0, x3fm)
    k.barrier()
    rout = emit_fm_to_tm_out(cm, x3fm, out_tm)
    k.wait("sync", "outd", rout)
    cm.k.finish()
    return nc



def build_L1():
    nc = bass.Bass("TRN2", target_bir_lowering=False)
    cm = Common(nc)
    k, ar = cm.k, cm.ar

    inputs = {}
    inputs["x_own"] = nc.dram_tensor("x_own", [TOWN, D], F32,
                                     kind="ExternalInput").ap()
    inputs["gate_w"] = nc.dram_tensor("gate_w", [IF, D], I32,
                                      kind="ExternalInput").ap()
    inputs["up_w"] = nc.dram_tensor("up_w", [IF, D], I32,
                                    kind="ExternalInput").ap()
    inputs["down_w"] = nc.dram_tensor("down_w", [D, IF], I32,
                                      kind="ExternalInput").ap()
    inputs["gate_scale"] = nc.dram_tensor("gate_scale", [IF], F32,
                                          kind="ExternalInput").ap()
    inputs["up_scale"] = nc.dram_tensor("up_scale", [IF], F32,
                                        kind="ExternalInput").ap()
    inputs["down_scale"] = nc.dram_tensor("down_scale", [D], F32,
                                          kind="ExternalInput").ap()
    cm.inputs = inputs

    x1fm = nc.dram_tensor("x1fm", [D, TOWN], F32, kind="ExternalOutput").ap()
    x0fm = nc.dram_tensor("x0fm", [D, TOWN], F32, kind="Internal").ap()

    cm.load_consts(with_ffn_scales=True)
    emit_ffn_weight_prep(cm)
    k.barrier()
    emit_tm_to_fm(cm, inputs["x_own"], x0fm, TOWN)
    k.barrier()

    r2 = emit_ffn(cm, x0fm, 0, x1fm)

    k.wait("sync", "fo0", r2["x_fo"][0])
    k.wait("sync", "fo1", r2["x_fo"][1])
    cm.k.finish()
    return nc


def build_FUSED(stop_after=None, dump_chunk=0, n_ffn1=4):
    nc = bass.Bass("TRN2", target_bir_lowering=False)
    cm = Common(nc)
    k, ar = cm.k, cm.ar

    inputs = {}
    inputs["x_full"] = nc.dram_tensor("x_full", [S, D], F32,
                                      kind="ExternalInput").ap()
    inputs["fr"] = nc.dram_tensor("fr", [S, DR // 2], F32,
                                  kind="ExternalInput").ap()
    inputs["Wq"] = nc.dram_tensor("Wq", [D, NH * DQK], F32,
                                  kind="ExternalInput").ap()
    inputs["Wkv_a"] = nc.dram_tensor("Wkv_a", [D, RANK + DR], F32,
                                     kind="ExternalInput").ap()
    inputs["kv_gamma"] = nc.dram_tensor("kv_gamma", [RANK], F32,
                                        kind="ExternalInput").ap()
    inputs["kv_beta"] = nc.dram_tensor("kv_beta", [RANK], F32,
                                       kind="ExternalInput").ap()
    inputs["Wkv_b"] = nc.dram_tensor("Wkv_b", [RANK, NH * (DN + DV)], F32,
                                     kind="ExternalInput").ap()
    inputs["Wout"] = nc.dram_tensor("Wout", [NH * DV, D], F32,
                                    kind="ExternalInput").ap()
    for nm, shp, dt in (("gate_w", [IF, D], I32), ("up_w", [IF, D], I32),
                        ("down_w", [D, IF], I32), ("gate_scale", [IF], F32),
                        ("up_scale", [IF], F32), ("down_scale", [D], F32)):
        inputs[nm] = nc.dram_tensor(nm, shp, dt, kind="ExternalInput").ap()
    cm.inputs = inputs

    out_tm = nc.dram_tensor("out_tm", [TOWN, D], F32,
                            kind="ExternalOutput").ap()
    x0fm = nc.dram_tensor("x0fm", [D, S], F32, kind="Internal").ap()
    x1fm = nc.dram_tensor("x1fm", [D, S], F32, kind="Internal").ap()
    ln2d = nc.dram_tensor("ln2d", [D, S], BF16, kind="Internal").ap()
    x2fm = nc.dram_tensor("x2fm", [D, TOWN], F32, kind="Internal").ap()
    x3fm = nc.dram_tensor("x3fm", [D, TOWN], F32, kind="Internal").ap()

    def _dump(src_fm):
        k.barrier()
        ov = out_tm.rearrange("a b -> (a b)").rearrange(
            "(c d) -> c d", c=D)
        r = k.dma("gpsimd", lambda eng: eng.dma_start(
            out=ov,
            in_=src_fm[:, dump_chunk * 1024:(dump_chunk + 1) * 1024]),
            "outd")
        k.wait("sync", "outd", r)
        cm.k.finish()
        return nc

    cm.load_consts(with_ffn_scales=True)
    cm.gcol, _ = ar.sb("gcol", [128, 4], F32)
    cm.bcol, _ = ar.sb("bcol", [128, 4], F32)
    emit_cols(cm, inputs["kv_gamma"], 4, cm.gcol)
    emit_cols(cm, inputs["kv_beta"], 4, cm.bcol)

    ln1d = nc.dram_tensor("ln1d", [D, S], BF16, kind="Internal").ap()
    for c in range(4):
        emit_tm_to_fm(cm, inputs["x_full"], x0fm, 1024, t0=c * 1024)
        k.barrier()
    if stop_after == "x0":
        return _dump(x0fm)
    m_gu = emit_prep_pe(cm, "gu")
    k.wait("tensor", "ac", k.val("ac"))
    k.wait("gpsimd", "t2fo", k.val("t2fo"))
    emit_ln_pipeline(cm, x0fm, [0, 1024, 2048, 3072], ln1d)
    if stop_after == "lnd":
        return _dump(ln1d)
    k.wait("vector", "ptw0", cm.wprep_gu[0])
    k.wait("vector", "ptw1", cm.wprep_gu[1])
    k.wait("vector", "lnst0", k.val("lnst0"))
    k.wait("vector", "lnst1", k.val("lnst1"))

    t0list = (n_ffn1 if isinstance(n_ffn1, (list, tuple))
              else [i * 1024 for i in range(n_ffn1)])
    for ji, t0_ in enumerate(t0list):
        if ji == 0:
            emit_ffn(cm, x0fm, t0_, x1fm, ln_dram=ln1d,
                     gu_gate=cm.wprep_gu_m,
                     mid_emit=lambda: emit_prep_pe(cm, "d"))
        else:
            emit_ffn(cm, x0fm, t0_, x1fm, ln_dram=ln1d)
        k.barrier()
        if ji == 0:
            ar.free_to(m_gu)
    if stop_after == "x1":
        return _dump(x1fm)

    m_pre = ar.mark()
    attn_sb, _ = ar.sb("attn_sb", [128, 16 * 1024], BF16)
    m_attn = ar.mark()
    qn_sb, _ = ar.sb("qn_sb", [128, 16 * 1024], BF16)
    qr_sb, _ = ar.sb("qr_sb", [128, 8 * 1024], BF16)
    C2, _ = ar.sb("C2", [128, S], F32)
    S2, _ = ar.sb("S2", [128, S], F32)

    emit_rope_tables(cm, inputs["fr"], C2, S2)

    m_ln2 = ar.mark()
    lnres = emit_ln_pipeline(cm, x1fm, [1024, 2048, 3072, 0], ln2d,
                             last_sbuf=attn_sb, keep_mark=True)
    emit_qproj(cm, attn_sb, inputs["Wq"], C2, S2, qn_sb, qr_sb)
    k.wait("vector", "pe", k.val("pe"))
    ar.free_to(m_ln2)
    kvn_sb, _ = ar.sb("kvn_sb", [128, 4 * S], BF16)
    krope_sb, _ = ar.sb("krope_sb", [128, S], BF16)
    for e in ("sync", "gpsimd"):
        k.wait(e, "pe", k.val("pe"))
        k.wait(e, "ve", k.val("ve"))
        k.wait(e, "ac", k.val("ac"))
        k.wait(e, "lnst0", lnres["st0"])
        k.wait(e, "lnst1", lnres["st1"])
    emit_kv(cm, ln2d, inputs["Wkv_a"], cm.gcol, cm.bcol, C2, S2,
            kvn_sb, krope_sb)
    emit_attention(cm, inputs["Wkv_b"], kvn_sb, krope_sb, qn_sb, qr_sb,
                   attn_sb)
    ar.free_to(m_attn)
    k.barrier()
    emit_wout(cm, inputs["Wout"], attn_sb, x1fm, x2fm)
    ar.free_to(m_pre)
    k.barrier()
    emit_ffn(cm, x2fm, 0, x3fm)
    k.barrier()
    rout = emit_fm_to_tm_out(cm, x3fm, out_tm)
    k.wait("sync", "outd", rout)
    cm.k.finish()
    return nc


def build_AG(stop_after=None, dump_chunk=0):
    nc = bass.Bass("TRN2", target_bir_lowering=False)
    cm = Common(nc)
    k, ar = cm.k, cm.ar

    inputs = {}
    inputs["x_own"] = nc.dram_tensor("x_own", [TOWN, D], F32,
                                     kind="ExternalInput").ap()
    inputs["fr"] = nc.dram_tensor("fr", [TOWN, DR // 2], F32,
                                  kind="ExternalInput").ap()
    inputs["Wq"] = nc.dram_tensor("Wq", [D, NH * DQK], F32,
                                  kind="ExternalInput").ap()
    inputs["Wkv_a"] = nc.dram_tensor("Wkv_a", [D, RANK + DR], F32,
                                     kind="ExternalInput").ap()
    inputs["kv_gamma"] = nc.dram_tensor("kv_gamma", [RANK], F32,
                                        kind="ExternalInput").ap()
    inputs["kv_beta"] = nc.dram_tensor("kv_beta", [RANK], F32,
                                       kind="ExternalInput").ap()
    inputs["Wkv_b"] = nc.dram_tensor("Wkv_b", [RANK, NH * (DN + DV)], F32,
                                     kind="ExternalInput").ap()
    inputs["Wout"] = nc.dram_tensor("Wout", [NH * DV, D], F32,
                                    kind="ExternalInput").ap()
    for nm, shp, dt in (("gate_w", [IF, D], I32), ("up_w", [IF, D], I32),
                        ("down_w", [D, IF], I32), ("gate_scale", [IF], F32),
                        ("up_scale", [IF], F32), ("down_scale", [D], F32)):
        inputs[nm] = nc.dram_tensor(nm, shp, dt, kind="ExternalInput").ap()
    cm.inputs = inputs

    out_tm = nc.dram_tensor("out_tm", [TOWN, D], F32,
                            kind="ExternalOutput").ap()
    x0fm = nc.dram_tensor("x0fm", [D, TOWN], F32, kind="Internal").ap()
    x1fm = nc.dram_tensor("x1fm", [D, TOWN], F32, kind="Internal").ap()
    ln1d = nc.dram_tensor("ln1d", [D, TOWN], BF16, kind="Internal").ap()
    x2fm = nc.dram_tensor("x2fm", [D, TOWN], F32, kind="Internal").ap()
    x3fm = nc.dram_tensor("x3fm", [D, TOWN], F32, kind="Internal").ap()
    agin = nc.dram_tensor("agin", [AGR, TOWN], BF16, kind="Internal").ap()
    agout = nc.dram_tensor("agout", [4 * AGR, TOWN], BF16,
                           kind="Internal").ap()

    def _dump(src_fm):
        k.barrier()
        ov = out_tm.rearrange("a b -> (a b)").rearrange(
            "(c d) -> c d", c=D)
        r = k.dma("gpsimd", lambda eng: eng.dma_start(
            out=ov,
            in_=src_fm[:, dump_chunk * 1024:(dump_chunk + 1) * 1024]),
            "outd")
        k.wait("sync", "outd", r)
        cm.k.finish()
        return nc

    cm.load_consts(with_ffn_scales=True)
    cm.gcol, _ = ar.sb("gcol", [128, 4], F32)
    cm.bcol, _ = ar.sb("bcol", [128, 4], F32)
    emit_cols(cm, inputs["kv_gamma"], 4, cm.gcol)
    emit_cols(cm, inputs["kv_beta"], 4, cm.bcol)

    emit_tm_to_fm(cm, inputs["x_own"], x0fm, TOWN)
    k.barrier()
    if stop_after == "x0":
        return _dump(x0fm)
    m_gu = emit_prep_pe(cm, "gu")
    k.wait("tensor", "ac", k.val("ac"))
    k.wait("gpsimd", "t2fo", k.val("t2fo"))
    emit_ln_pipeline(cm, x0fm, [0], ln1d)
    if stop_after == "lnd":
        return _dump(ln1d)
    k.wait("vector", "ptw0", cm.wprep_gu[0])
    k.wait("vector", "ptw1", cm.wprep_gu[1])
    k.wait("vector", "lnst0", k.val("lnst0"))
    k.wait("vector", "lnst1", k.val("lnst1"))

    emit_ffn(cm, x0fm, 0, x1fm, ln_dram=ln1d,
             gu_gate=cm.wprep_gu_m,
             mid_emit=lambda: emit_prep_pe(cm, "d"))
    k.barrier()
    ar.free_to(m_gu)
    if stop_after == "x1":
        return _dump(x1fm)

    m_pre = ar.mark()
    attn_sb, _ = ar.sb("attn_sb", [128, 16 * 1024], BF16)
    m_attn = ar.mark()
    qn_sb, _ = ar.sb("qn_sb", [128, 16 * 1024], BF16)
    qr_sb, _ = ar.sb("qr_sb", [128, 8 * 1024], BF16)
    C2, _ = ar.sb("C2", [128, TOWN], F32)
    S2, _ = ar.sb("S2", [128, TOWN], F32)

    emit_rope_tables(cm, inputs["fr"], C2, S2, TK=TOWN)

    m_ln2 = ar.mark()
    lnres = emit_ln_pipeline(cm, x1fm, [0], ln1d,
                             last_sbuf=attn_sb, keep_mark=True)
    for e in ("sync", "gpsimd"):
        k.wait(e, "pe", k.val("pe"))
        k.wait(e, "ve", k.val("ve"))
        k.wait(e, "ac", k.val("ac"))
        k.wait(e, "lnst0", lnres["st0"])
        k.wait(e, "lnst1", lnres["st1"])
    ar.free_to(m_ln2)
    if stop_after == "ln2":
        return _dump(ln1d)

    m_kv = ar.mark()
    kvn_own, _ = ar.sb("kvn_own", [128, 4 * TOWN], BF16)
    krope_own, _ = ar.sb("krope_own", [128, TOWN], BF16)
    emit_kv_own(cm, attn_sb, inputs["Wkv_a"], cm.gcol, cm.bcol, C2, S2,
                kvn_own, krope_own)
    emit_kvb_ag(cm, inputs["Wkv_b"], kvn_own, krope_own, agin)
    emit_ag_trigger(cm, agin, agout)
    k.barrier(exclude=("cc",))
    ar.free_to(m_kv)
    emit_qproj(cm, attn_sb, inputs["Wq"], C2, S2, qn_sb, qr_sb)
    k.wait("vector", "pe", k.val("pe"))

    emit_attention_ag(cm, agout, qn_sb, qr_sb, attn_sb)
    ar.free_to(m_attn)
    k.barrier()
    if stop_after == "attn":
        return _dump(x1fm)
    emit_wout(cm, inputs["Wout"], attn_sb, x1fm, x2fm)
    ar.free_to(m_pre)
    k.barrier()
    if stop_after == "x2":
        return _dump(x2fm)
    emit_ffn(cm, x2fm, 0, x3fm)
    k.barrier()
    rout = emit_fm_to_tm_out(cm, x3fm, out_tm)
    k.wait("sync", "outd", rout)
    cm.k.finish()
    return nc



def _ffn_weight_inputs(inputs):
    return dict(
        gate_w=np.ascontiguousarray(inputs["L0_ffn_gate_w"]),
        up_w=np.ascontiguousarray(inputs["L0_ffn_up_w"]),
        down_w=np.ascontiguousarray(inputs["L0_ffn_down_w"]),
        gate_scale=np.ascontiguousarray(
            np.asarray(inputs["L0_ffn_gate_scale"], np.float32)),
        up_scale=np.ascontiguousarray(
            np.asarray(inputs["L0_ffn_up_scale"], np.float32)),
        down_scale=np.ascontiguousarray(
            np.asarray(inputs["L0_ffn_down_scale"], np.float32)),
    )


def kernel(**inputs):
    return kernel_ag(**inputs)


def _attn_weight_inputs(inputs):
    return dict(
        Wq=np.ascontiguousarray(np.asarray(inputs["L0_attn_Wq"], np.float32)),
        Wkv_a=np.ascontiguousarray(
            np.asarray(inputs["L0_attn_Wkv_a"], np.float32)),
        kv_gamma=np.ascontiguousarray(
            np.asarray(inputs["L0_attn_kv_gamma"], np.float32)),
        kv_beta=np.ascontiguousarray(
            np.asarray(inputs["L0_attn_kv_beta"], np.float32)),
        Wkv_b=np.ascontiguousarray(
            np.asarray(inputs["L0_attn_Wkv_b"], np.float32)),
        Wout=np.ascontiguousarray(
            np.asarray(inputs["L0_attn_Wout"], np.float32)),
    )


def kernel_ag(**inputs):
    x = np.asarray(inputs["L0_attn_hidden_states"], np.float32)
    wff = _ffn_weight_inputs(inputs)
    watt = _attn_weight_inputs(inputs)
    freqs = np.ascontiguousarray(
        np.asarray(inputs["L0_attn_freqs_cis"], np.float32))
    nc = build_AG()
    in_maps = []
    for c in range(NC_):
        g, qc = c // 4, c % 4
        rows = qc * TOWN + np.arange(TOWN)
        m = dict(wff)
        m.update(watt)
        m["x_own"] = np.ascontiguousarray(x[g, qc * TOWN:(qc + 1) * TOWN])
        m["fr"] = np.ascontiguousarray(freqs[rows % 2048])
        in_maps.append(m)
    res = run_bass_kernel_spmd(nc, in_maps, core_ids=list(range(NC_)))
    out = np.empty((B, S, D), np.float32)
    for c in range(NC_):
        g, qc = c // 4, c % 4
        out[g, qc * TOWN:(qc + 1) * TOWN] = res.results[c]["out_tm"]
    return out


def kernel_fused(**inputs):
    x = np.asarray(inputs["L0_attn_hidden_states"], np.float32)
    wff = _ffn_weight_inputs(inputs)
    freqs = np.ascontiguousarray(
        np.asarray(inputs["L0_attn_freqs_cis"], np.float32))
    watt = dict(
        Wq=np.ascontiguousarray(np.asarray(inputs["L0_attn_Wq"], np.float32)),
        Wkv_a=np.ascontiguousarray(
            np.asarray(inputs["L0_attn_Wkv_a"], np.float32)),
        kv_gamma=np.ascontiguousarray(
            np.asarray(inputs["L0_attn_kv_gamma"], np.float32)),
        kv_beta=np.ascontiguousarray(
            np.asarray(inputs["L0_attn_kv_beta"], np.float32)),
        Wkv_b=np.ascontiguousarray(
            np.asarray(inputs["L0_attn_Wkv_b"], np.float32)),
        Wout=np.ascontiguousarray(
            np.asarray(inputs["L0_attn_Wout"], np.float32)),
    )
    nc = build_FUSED()
    in_maps = []
    for c in range(NC_):
        g, qc = c // 4, c % 4
        rows = (qc * 1024 + np.arange(S)) % S
        xr = np.ascontiguousarray(x[g][rows])
        fr = np.ascontiguousarray(freqs[rows % 2048])
        m = dict(wff)
        m.update(watt)
        m["x_full"] = xr
        m["fr"] = fr
        in_maps.append(m)
    res = run_bass_kernel_spmd(nc, in_maps, core_ids=list(range(NC_)))
    out = np.empty((B, S, D), np.float32)
    for c in range(NC_):
        g, qc = c // 4, c % 4
        out[g, qc * TOWN:(qc + 1) * TOWN] = res.results[c]["out_tm"]
    return out


def kernel_unfused(**inputs):
    x = np.ascontiguousarray(
        np.asarray(inputs["L0_attn_hidden_states"], np.float32)
    ).reshape(B * S, D)
    wff = _ffn_weight_inputs(inputs)

    nc1 = build_L1()
    in_maps = [dict(wff, x_own=np.ascontiguousarray(x[c * TOWN:(c + 1) * TOWN]))
               for c in range(NC_)]
    res1 = run_bass_kernel_spmd(nc1, in_maps, core_ids=list(range(NC_)))
    x1 = [np.asarray(res1.results[c]["x1fm"]) for c in range(NC_)]

    freqs = np.ascontiguousarray(
        np.asarray(inputs["L0_attn_freqs_cis"], np.float32))
    watt = dict(
        Wq=np.ascontiguousarray(np.asarray(inputs["L0_attn_Wq"], np.float32)),
        Wkv_a=np.ascontiguousarray(
            np.asarray(inputs["L0_attn_Wkv_a"], np.float32)),
        kv_gamma=np.ascontiguousarray(
            np.asarray(inputs["L0_attn_kv_gamma"], np.float32)),
        kv_beta=np.ascontiguousarray(
            np.asarray(inputs["L0_attn_kv_beta"], np.float32)),
        Wkv_b=np.ascontiguousarray(
            np.asarray(inputs["L0_attn_Wkv_b"], np.float32)),
        Wout=np.ascontiguousarray(
            np.asarray(inputs["L0_attn_Wout"], np.float32)),
    )
    nc2 = build_L2()
    in_maps2 = []
    for c in range(NC_):
        g, qc = c // 4, c % 4
        order = [g * 4 + ((qc + j) % 4) for j in range(4)]
        x1full = np.ascontiguousarray(
            np.concatenate([x1[o] for o in order], axis=1))
        pos = (qc * 1024 + np.arange(S)) % S
        fr = np.ascontiguousarray(freqs[pos % 2048])
        m = dict(wff)
        m.update(watt)
        m["x1fm"] = x1full
        m["fr"] = fr
        in_maps2.append(m)
    res2 = run_bass_kernel_spmd(nc2, in_maps2, core_ids=list(range(NC_)))
    out = np.empty((B, S, D), np.float32)
    for c in range(NC_):
        g, qc = c // 4, c % 4
        out[g, qc * TOWN:(qc + 1) * TOWN] = res2.results[c]["out_tm"]
    return out
